# version 03
# baseline (speedup 1.0000x reference)
import contextlib
import numpy as np
import concourse.bass as bass
import concourse.mybir as mybir

F32 = mybir.dt.float32
BF16 = mybir.dt.bfloat16
F16 = mybir.dt.float16
AF = mybir.ActivationFunctionType
ALU = mybir.AluOpType
AX = mybir.AxisListType.X

S = 2048
DM = 2048
KD = 16
INTER = 4096
CONVD = 4352
PROJ = 8512
NO2 = 67
NH = 64
HD = 64
DST = 128
L = 128
NCH = 16
HFF = 8192
TH = 1024
TQ = 512
MAGIC = 12582912.0
CLIP1 = 1.4990234375
LNEPS = 1e-5
RMSEPS = 1e-6
QEPS = 1e-5



def _tile_lhsT(w, no, nk):
    out_r, in_r = w.shape
    assert in_r == nk * 128
    if out_r < no * 128:
        w = np.concatenate([w, np.zeros((no * 128 - out_r, in_r), w.dtype)], 0)
    t = np.empty((no, 128, nk * 128), np.float32)
    for o in range(no):
        blk = w[o * 128:(o + 1) * 128, :]
        t[o] = np.ascontiguousarray(
            blk.reshape(128, nk, 128).transpose(2, 1, 0)).reshape(128, nk * 128)
    return t


def host_prep(inputs):
    x = np.ascontiguousarray(np.asarray(inputs["L0_attn_hidden_states"], np.float32))
    sh = {}
    sh["win"] = _tile_lhsT(np.asarray(inputs["L0_attn_in_proj_w"], np.float32), NO2, KD)
    sh["wout"] = _tile_lhsT(np.asarray(inputs["L0_attn_out_proj_w"], np.float32), 16, 32)
    sh["w1"] = _tile_lhsT(np.asarray(inputs["L0_ffn_w1"], np.float32), 64, KD)
    sh["w2"] = _tile_lhsT(np.asarray(inputs["L0_ffn_w2"], np.float32), 16, 64)
    cw = np.asarray(inputs["L0_attn_conv_w"], np.float32).reshape(CONVD, 4)
    sh["convw"] = np.ascontiguousarray(
        cw.reshape(34, 128, 4).transpose(1, 0, 2).reshape(128, 136))
    cb = np.asarray(inputs["L0_attn_conv_b"], np.float32)
    sh["convb"] = np.ascontiguousarray(cb.reshape(34, 128).T)
    sh["dtb"] = np.ascontiguousarray(
        np.asarray(inputs["L0_attn_dt_bias"], np.float32).reshape(64, 1))
    sh["alog"] = np.ascontiguousarray(
        np.asarray(inputs["L0_attn_A_log"], np.float32).reshape(64, 1))
    dd = np.asarray(inputs["L0_attn_D"], np.float32)
    ch = np.arange(INTER)
    sh["dpp"] = np.ascontiguousarray(dd[(ch // HD)].reshape(32, 128).T)
    nw = np.asarray(inputs["L0_attn_norm_weight"], np.float32)
    sh["nwpp"] = np.ascontiguousarray(nw.reshape(32, 128).T)
    sh["idn"] = np.eye(128, dtype=np.float32)
    sh["ones"] = np.ones((128, 128), np.float32)
    sh["mask"] = np.ascontiguousarray(
        (np.arange(128)[None, :] >= np.arange(128)[:, None]).astype(np.float32))
    in_maps = []
    for c in range(x.shape[0]):
        d = dict(sh)
        d["xt"] = np.ascontiguousarray(x[c].T)
        in_maps.append(d)
    return in_maps



class Sem:
    def __init__(self, stack, nc, name):
        self.h = stack.enter_context(nc.semaphore(name))
        self.n = 0
        self.name = name


class Prog:
    ENGS = ("t", "v", "s", "g", "q")

    def __init__(self):
        self.ops = {k: [] for k in self.ENGS}
        self.trace = {k: [] for k in self.ENGS}
        self.eng_sems = {}

    def emit(self, ek, build, wait=(), inc=(), dma=None):
        if dma is None:
            dma = any(k >= 16 for _, k in inc)
        esem = self.eng_sems.get(ek)
        if esem is not None:
            wait = list(wait) + [(esem, esem.n)]
            if not dma and not any(sm is esem for sm, _ in inc):
                inc = list(inc) + [(esem, 1)]
        for sm, k in inc:
            sm.n += k
        inc_h = tuple((sm.h, int(k)) for sm, k in inc)
        wait_h = tuple((sm.h, int(v)) for sm, v in wait if v > 0)

        def op(e, build=build, wait_h=wait_h, inc_h=inc_h):
            for hh, v in wait_h:
                e.wait_ge(hh, v)
            r = build(e)
            for hh, k in inc_h:
                r = r.then_inc(hh, k)
        self.ops[ek].append(op)
        self.trace[ek].append((tuple((sm.name, int(v)) for sm, v in wait if v > 0),
                               tuple((sm.name, int(k)) for sm, k in inc)))

    def check(self):
        pos = {k: 0 for k in self.ENGS}
        cnt = {}
        for k in self.ENGS:
            for w, i in self.trace[k]:
                for n, _ in list(w) + list(i):
                    cnt.setdefault(n, 0)
        progressed = True
        while progressed:
            progressed = False
            for k in self.ENGS:
                while pos[k] < len(self.trace[k]):
                    w, i = self.trace[k][pos[k]]
                    if all(cnt[n] >= v for n, v in w):
                        for n, kk in i:
                            cnt[n] += kk
                        pos[k] += 1
                        progressed = True
                    else:
                        break
        stuck = {k: pos[k] for k in self.ENGS if pos[k] < len(self.trace[k])}
        if stuck:
            lines = []
            for k, p in stuck.items():
                lines.append(f"engine {k} stuck at op {p}/{len(self.trace[k])}"
                             f" waiting {self.trace[k][p][0]} counts="
                             f"{[(n, cnt[n]) for n, _ in self.trace[k][p][0]]}")
            raise RuntimeError("DEADLOCK:\n" + "\n".join(lines))
        return {k: len(self.trace[k]) for k in self.ENGS}



def build(taps=(), last_phase=7, p2_os=None):
    nc = bass.Bass("TRN2", target_bir_lowering=False)
    stack = contextlib.ExitStack()
    P = Prog()
    D = {}

    ap = lambda t: t.ap()
    xt = ap(nc.dram_tensor("xt", [DM, S], F32, kind="ExternalInput"))
    win = ap(nc.dram_tensor("win", [NO2, 128, DM], F32, kind="ExternalInput"))
    wout = ap(nc.dram_tensor("wout", [16, 128, INTER], F32, kind="ExternalInput"))
    w1 = ap(nc.dram_tensor("w1", [64, 128, DM], F32, kind="ExternalInput"))
    w2 = ap(nc.dram_tensor("w2", [16, 128, HFF], F32, kind="ExternalInput"))
    convw = ap(nc.dram_tensor("convw", [128, 136], F32, kind="ExternalInput"))
    convb = ap(nc.dram_tensor("convb", [128, 34], F32, kind="ExternalInput"))
    dtb = ap(nc.dram_tensor("dtb", [64, 1], F32, kind="ExternalInput"))
    alog = ap(nc.dram_tensor("alog", [64, 1], F32, kind="ExternalInput"))
    dpp = ap(nc.dram_tensor("dpp", [128, 32], F32, kind="ExternalInput"))
    nwpp = ap(nc.dram_tensor("nwpp", [128, 32], F32, kind="ExternalInput"))
    idn = ap(nc.dram_tensor("idn", [128, 128], F32, kind="ExternalInput"))
    onesd = ap(nc.dram_tensor("ones", [128, 128], F32, kind="ExternalInput"))
    maskd = ap(nc.dram_tensor("mask", [128, 128], F32, kind="ExternalInput"))
    out = ap(nc.dram_tensor("out", [DM, S], F32, kind="ExternalOutput"))

    d_xqi = ap(nc.dram_tensor("d_xqi", [KD, 128, S], BF16, kind="Internal"))
    d_gate = ap(nc.dram_tensor("d_gate", [NCH, 32, 128, L], BF16, kind="Internal"))
    d_xssm = ap(nc.dram_tensor("d_xssm", [NCH, 32, 128, L], BF16, kind="Internal"))
    d_hready = ap(nc.dram_tensor("d_hready", [32, 128, S], BF16, kind="Internal"))
    d_h1 = ap(nc.dram_tensor("d_h1", [64, 128, S], F16, kind="Internal"))
    d_wq2 = ap(nc.dram_tensor("d_wq2", [16, 128, HFF], F16, kind="Internal"))
    d_stil = ap(nc.dram_tensor("d_stil", [16, 128, S], BF16, kind="Internal"))
    d_g = ap(nc.dram_tensor("d_g", [16, 128, S], BF16, kind="Internal"))
    d_rows = ap(nc.dram_tensor("d_rows", [4, 2048], F32, kind="Internal"))

    tap_out = {}
    TAPSHAPES = {
        "xqi": ([KD, 128, S], BF16), "gate": ([NCH, 32, 128, L], BF16),
        "xssm": ([NCH, 32, 128, L], BF16), "bsb": ([128, S], BF16),
        "csb": ([128, S], BF16), "dtsp": ([64, S], F32),
        "hready": ([32, 128, S], BF16), "stil": ([16, 128, S], BF16),
        "h1": ([64, 128, S], F16), "g": ([16, 128, S], BF16),
        "rows": ([4, 2048], F32), "carry": ([128, NH * HD], F32),
        "sw": ([1, 4], F32), "xn": ([2, 128, KD * TH], F16),
        "hq0": ([2, 128, 32 * TQ], F16), "sch0": ([2, TQ], F32),
    }
    for t in taps:
        if t not in TAPSHAPES:
            continue
        shp, dt = TAPSHAPES[t]
        tap_out[t] = ap(nc.dram_tensor("tap_" + t, shp, dt, kind="ExternalOutput"))

    def arena(name, nbytes):
        return stack.enter_context(nc.sbuf_tensor(name, [128, nbytes // 2], BF16))

    AR1 = arena("AR1", 32768)
    AR2 = arena("AR2", 32768)
    AR3 = arena("AR3", 41472)
    AR4 = arena("AR4", 22528)
    AR5 = arena("AR5", 16384)
    AR7 = arena("AR7", 30720)
    AR8 = arena("AR8", 25088)

    def V(at, byte_off, p0, pcnt, shape_free, dtype=BF16):
        esz = 4 if dtype == F32 else 2
        n = 1
        for d in shape_free:
            n *= int(d)
        assert byte_off % 4 == 0
        a = at[p0:p0 + pcnt, byte_off // 2: byte_off // 2 + (esz * n) // 2]
        if dtype != BF16:
            a = a.bitcast(dtype)
        if len(shape_free) > 1:
            names = " ".join(f"d{i}" for i in range(len(shape_free)))
            a = a.rearrange(f"p ({names}) -> p {names}",
                            **{f"d{i}": int(shape_free[i]) for i in range(len(shape_free))})
        return a

    c_idn = stack.enter_context(nc.sbuf_tensor("c_idn", [128, 128], F32))
    c_idnb = stack.enter_context(nc.sbuf_tensor("c_idnb", [128, 128], BF16))
    c_idnh = stack.enter_context(nc.sbuf_tensor("c_idnh", [128, 128], F16))
    c_ones = stack.enter_context(nc.sbuf_tensor("c_ones", [128, 128], F32))
    c_mask = stack.enter_context(nc.sbuf_tensor("c_mask", [128, 128], F32))
    c_convw = stack.enter_context(nc.sbuf_tensor("c_convw", [128, 136], F32))
    c_convb = stack.enter_context(nc.sbuf_tensor("c_convb", [128, 34], F32))
    c_dpp = stack.enter_context(nc.sbuf_tensor("c_dpp", [128, 32], F32))
    c_nwpp = stack.enter_context(nc.sbuf_tensor("c_nwpp", [128, 32], F32))
    c_dtb = stack.enter_context(nc.sbuf_tensor("c_dtb", [64, 1], F32))
    c_alog = stack.enter_context(nc.sbuf_tensor("c_alog", [64, 1], F32))
    c_acol = stack.enter_context(nc.sbuf_tensor("c_acol", [64, 1], F32))
    c_tails = stack.enter_context(nc.sbuf_tensor("c_tails", [128, 102], F32))
    c_sw = stack.enter_context(nc.sbuf_tensor("c_sw", [1, 4], F32))
    c_swcol = stack.enter_context(nc.sbuf_tensor("c_swcol", [128, 2], F32))
    c_wacc = stack.enter_context(nc.sbuf_tensor("c_wacc", [128, 2], F32))
    c_wred = stack.enter_context(nc.sbuf_tensor("c_wred", [128, 1], F32))
    c_eps = stack.enter_context(nc.sbuf_tensor("c_eps", [1, 2], F32))

    xn_half = V(AR1, 0, 0, 128, (KD, TH), F16)
    hready_asm = [V(AR1, i * 8192, 0, 128, (32, L)) for i in range(2)]
    t1st = V(AR1, 16384, 0, 128, (2, L))
    gateg = [V(AR1, 17408 + i * 512, 0, 128, (2, L)) for i in range(2)]
    xssmg = [V(AR1, 19456 + i * 512, 0, 128, (2, L)) for i in range(2)]
    hr4_1 = V(AR1, 0, 0, 128, (16, TH))
    hq6_1 = V(AR1, 0, 0, 128, (32, TQ), F16)

    xqi_half = V(AR2, 0, 0, 128, (KD, TH))
    hr4_2 = V(AR2, 0, 0, 128, (16, TH))
    hq6_2 = V(AR2, 0, 0, 128, (32, TQ), F16)

    carry = V(AR3, 0, 0, 128, (NH * HD,), F32)
    carry_bf = V(AR3, 16384, 0, 128, (NH * HD,))
    xssmT = [V(AR3, 24576 + i * 8192, 0, 128, (INTER,)) for i in range(2)]
    p2wb = [V(AR3, i * 4096, 0, 128, (DM,), F16) for i in range(2)]
    w2q_raw = [V(AR3, i * 8192, 0, 128, (2048,), F32) for i in range(2)]
    w2q_stg = V(AR3, 16384, 0, 128, (4096,), F16)
    w2q_scr = V(AR3, 24576, 0, 128, (2048,), F32)
    wq2b = [V(AR3, i * 16384, 0, 128, (64, 128), F16) for i in range(2)]
    x7 = [V(AR3, i * 8192, 0, 128, (S,), F32) for i in range(2)]

    B_sb = V(AR4, 0, 0, 128, (S,))
    C_sb = V(AR4, 4096, 0, 128, (S,))
    dt_sp = V(AR4, 8192, 0, 64, (S,), F32)
    SC = 16384
    m1sc = V(AR4, SC, 0, 128, (L,), F32)
    edif = [V(AR4, SC + 512 + i * 256, 0, 128, (L,)) for i in range(2)]
    mtt = [V(AR4, SC + 1024 + i * 256, 0, 128, (L,)) for i in range(2)]
    cexp = [V(AR4, SC + 1536 + i * 256, 0, 128, (L,)) for i in range(2)]
    scoresM = V(AR4, SC + 2048, 0, 128, (L,))
    dsT_c = V(AR4, SC + 2304, 0, 128, (NH,), F32)
    cd_c = V(AR4, SC + 2560, 0, 128, (NH,), F32)
    ecs_c = V(AR4, SC + 2816, 0, 64, (L,), F32)
    cs_c = V(AR4, SC + 3328, 0, 64, (L,), F32)
    a_c = V(AR4, SC + 3840, 0, 64, (L,), F32)
    csT_c = V(AR4, SC + 4352, 0, 128, (NH,), F32)
    dtT_c = V(AR4, SC + 4608, 0, 128, (NH,), F32)
    B_t = V(AR4, SC + 4864, 0, 128, (DST,))
    xdtds_g = [V(AR4, SC + 5120 + i * 512, 0, 128, (4, HD)) for i in range(2)]
    spa = V(AR4, SC, 0, 64, (TH,), F32)
    dtrelu = V(AR3, 8192, 0, 64, (TH,), F32)

    xstg = [V(AR5, i * 4096, 0, 128, (TH,), F32) for i in range(2)]
    x2stg = [V(AR5, 8192 + i * 4096, 0, 128, (TH,), F32) for i in range(2)]
    cacc = [V(AR5, i * 4096, 0, 128, (TH,), F32) for i in range(2)]
    sstg = [V(AR5, 8192 + i * 2048, 0, 128, (TH,)) for i in range(2)]
    wob = [V(AR5, i * 8192, 0, 128, (INTER,)) for i in range(2)]
    stil7 = [V(AR5, i * 4096, 0, 128, (S,)) for i in range(2)]
    g7 = [V(AR5, 8192 + i * 4096, 0, 128, (S,)) for i in range(2)]

    w1bf = [V(AR7, i * 4096, 0, 128, (TH,), F32) for i in range(2)]
    w1b = [V(AR7, 12288 + i * 4096, 0, 128, (DM,)) for i in range(2)]
    wqscr = V(AR7, 8192, 0, 128, (TH,), F32)
    wq1b = [V(AR7, 12288 + i * 4096, 0, 128, (DM,)) for i in range(2)]
    h1stg = [V(AR7, 20480 + i * 2048, 0, 128, (TH,), F16) for i in range(2)]
    xqstg = [V(AR7, 20480 + i * 2048, 0, 128, (TH,)) for i in range(2)]
    mxcols = V(AR7, 24576, 0, 128, (8,), F32)
    invsbc = V(AR7, 26624, 0, 128, (TH,), F32)
    vacc4 = V(AR7, 0, 0, 128, (TH,), F32)
    stil4 = [V(AR7, 4096 + i * 2048, 0, 128, (TH,)) for i in range(2)]
    sq4 = [V(AR7, 8192 + i * 2048, 0, 128, (TH,)) for i in range(2)]
    gst6 = [V(AR7, i * 1024, 0, 128, (TQ,)) for i in range(2)]
    schbc = V(AR7, 4096, 0, 128, (TQ,), F32)
    i2bc = V(AR7, 6144, 0, 128, (TQ,), F32)
    rmq = V(AR7, 8192, 0, 128, (TQ,), F32)
    rmqn = V(AR7, 14400, 0, 128, (TQ,), F32)
    t6scr = [V(AR7, 10240 + i * 2048, 0, 128, (TQ,), F32) for i in range(2)]
    mx6 = V(AR7, 14336, 0, 128, (4,), F32)
    rmq1 = V(AR7, 20480, 0, 128, (TH,), F32)
    rmq1n = V(AR7, 24640, 0, 128, (TH,), F32)
    ostg7 = [V(AR7, i * 8192, 0, 128, (S,), F32) for i in range(2)]
    t7a = V(AR7, 16384, 0, 128, (TQ,))
    t7b = V(AR7, 17408, 0, 128, (TQ,))

    rowM = V(AR8, 0, 0, 1, (TH,), F32)
    rowR = V(AR8, 4096, 0, 1, (TH,), F32)
    rowS = V(AR8, 8192, 0, 1, (TH,), F32)
    csf = [V(AR8, i * 4096, 0, 1, (TH,), F32) for i in range(2)]
    ecsf = [V(AR8, 8192 + i * 4096, 0, 1, (TH,), F32) for i in range(2)]
    cslf = V(AR8, 16384, 0, 1, (NH,), F32)
    rowB = V(AR8, 16640, 0, 1, (S,), F32)
    inv1h = [V(AR8, 16640 + i * 4096, 0, 1, (TH,), F32) for i in range(2)]
    r7 = V(AR8, 0, 0, 1, (S,), F32)
    r7b = V(AR8, 8192, 0, 1, (S,), F32)
    r7c = V(AR8, 16640, 0, 1, (S,), F32)
    row6 = [V(AR8, i * 2048, 0, 1, (TQ,), F32) for i in range(4)]

    PS = [stack.enter_context(nc.psum_tensor(f"ps{i}", [128, 512], F32)) for i in range(8)]

    def PSb(i, lo, n):
        return PS[i][:, :].bitcast(BF16)[:, lo:lo + n]

    def PSh(i, lo, n):
        return PS[i][:, :].bitcast(F16)[:, lo:lo + n]

    _semd = {}

    def Q(name):
        if name not in _semd:
            _semd[name] = Sem(stack, nc, "q_" + name)
        return _semd[name]

    st = Sem(stack, nc, "st")
    sv = Sem(stack, nc, "sv")
    ss = Sem(stack, nc, "ss")
    sg = Sem(stack, nc, "sg")
    P.eng_sems = {"t": st, "v": sv, "s": ss, "g": sg}

    E = P.emit

    def row_assemble_dst(row_i, t0, ncols):
        base = d_rows[row_i:row_i + 1, t0:t0 + 1]
        return bass.AP(base.tensor, base.offset, [[1, 128], [128, ncols]])

    def init_consts():
        qc = Q("c")
        for dst, src in ((c_idn[:, :], idn), (c_ones[:, :], onesd), (c_mask[:, :], maskd),
                         (c_convw[:, :], convw), (c_convb[:, :], convb),
                         (c_dpp[:, :], dpp), (c_nwpp[:, :], nwpp),
                         (c_dtb[:, :], dtb), (c_alog[:, :], alog)):
            E("q", lambda e, d=dst, s=src: e.dma_start(d, s), inc=[(qc, 16)])
        ldn = (qc, qc.n)
        D["consts"] = ldn
        for ek in ("t", "v", "s", "g"):
            E(ek, lambda e: e.nop(), wait=[ldn])
        E("v", lambda e: e.tensor_copy(c_idnb[:, :], c_idn[:, :]),
          wait=[ldn], inc=[(sv, 1)])
        E("v", lambda e: e.tensor_copy(c_idnh[:, :], c_idn[:, :]), inc=[(sv, 1)])
        E("v", lambda e: e.memset(c_tails[:, :], 0.0), inc=[(sv, 1)])
        E("v", lambda e: e.memset(c_wacc[:, :], 0.0), inc=[(sv, 1)])
        E("v", lambda e: e.memset(c_eps[0:1, 0:1], LNEPS), inc=[(sv, 1)])
        E("v", lambda e: e.memset(c_eps[0:1, 1:2], RMSEPS), inc=[(sv, 1)])
        E("s", lambda e: e.activation(c_acol[:, :], c_alog[:, :], AF.Exp, scale=1.0),
          wait=[ldn], inc=[(ss, 1)])
        E("v", lambda e: e.tensor_scalar_mul(c_acol[:, :], c_acol[:, :], -1.0),
          wait=[(ss, ss.n)], inc=[(sv, 1)])
        D["init_v"] = sv.n

    def p0_dma(lo, hi):
        for i in range(lo, hi):
            if i < 128:
                src = w1[i // 2][:, (i % 2) * 1024:(i % 2 + 1) * 1024]
            else:
                j = i - 128
                src = w2[j // 8][:, (j % 8) * 1024:(j % 8 + 1) * 1024]
            qq = Q(f"wa{i % 2}")
            wv = [(sv, D.get(("p0r", i - 2), 0))]
            E("g", lambda e, d=w1bf[i % 2][:, :], s=src: e.dma_start(d, s),
              wait=wv, inc=[(qq, 16)])
            col = 0 if i < 128 else 1
            first = (i == 0 or i == 128)
            E("v", lambda e, b=w1bf[i % 2][:, :]: e.tensor_reduce(
                c_wred[:, :], b, AX, ALU.add, apply_absolute_value=True),
                wait=[(qq, qq.n), (sv, D["init_v"])], inc=[(sv, 1)])
            if first:
                E("v", lambda e, c=col: e.tensor_copy(c_wacc[:, c:c + 1], c_wred[:, :]),
                  inc=[(sv, 1)])
            else:
                E("v", lambda e, c=col: e.tensor_add(
                    c_wacc[:, c:c + 1], c_wacc[:, c:c + 1], c_wred[:, :]), inc=[(sv, 1)])
            D[("p0r", i)] = sv.n

    def p1_half(hf):
        t0 = hf * TH
        entry = [(sv, sv.n), (ss, ss.n), (Q("pa0"), Q("pa0").n),
                 (Q("pa1"), Q("pa1").n)]
        if "xntap0" in D:
            entry.append(D["xntap0"])
        for k in range(KD):
            qq = Q(f"a{k % 2}")
            wv = [(st, D.get(("p1mm", hf, k - 2), 0)), (ss, D.get(("p1sq", hf, k - 2), 0))]
            if k < 2:
                wv += entry
            E("q", lambda e, kk=k, b=xstg[k % 2]: e.dma_start(
                b[:, :], xt[kk * 128:(kk + 1) * 128, t0:t0 + TH]), wait=wv, inc=[(qq, 16)])
            ldn = (qq, qq.n)
            E("s", lambda e, b=xstg[k % 2], o=x2stg[k % 2]: e.activation(
                o[:, :], b[:, :], AF.Square, scale=1.0),
                wait=[ldn, (st, D.get(("p1mm2", hf, k - 2), 0))] +
                (entry if k < 2 else []), inc=[(ss, 1)])
            D[("p1sq", hf, k)] = ss.n
            sqn = ss.n
            for t4 in range(2):
                E("t", lambda e, b=xstg[k % 2], t4=t4, kk=k: e.matmul(
                    PS[t4][0:1, :], c_ones[:, 0:1], b[:, t4 * 512:(t4 + 1) * 512],
                    start=(kk == 0), stop=(kk == KD - 1)),
                    wait=([ldn] + (entry if k == 0 else [])) if t4 == 0 else [],
                    inc=[(st, 1)] if t4 == 1 else [])
            D[("p1mm", hf, k)] = st.n
            for t4 in range(2):
                E("t", lambda e, b=x2stg[k % 2], t4=t4, kk=k: e.matmul(
                    PS[2 + t4][0:1, :], c_ones[:, 0:1], b[:, t4 * 512:(t4 + 1) * 512],
                    start=(kk == 0), stop=(kk == KD - 1)),
                    wait=([(ss, sqn)] + (entry if k == 0 else [])) if t4 == 0 else [],
                    inc=[(st, 1)] if t4 == 1 else [])
            D[("p1mm2", hf, k)] = st.n
        fin = st.n
        E("v", lambda e: e.tensor_scalar(rowM[0:1, 0:512], PS[0][0:1, :], 1.0 / DM, None,
                                         ALU.mult), wait=[(st, fin)], inc=[(sv, 1)])
        E("v", lambda e: e.tensor_scalar(rowM[0:1, 512:1024], PS[1][0:1, :], 1.0 / DM,
                                         None, ALU.mult), inc=[(sv, 1)])
        E("v", lambda e: e.tensor_scalar(rowR[0:1, 0:512], PS[2][0:1, :], 1.0 / DM, None,
                                         ALU.mult), inc=[(sv, 1)])
        E("v", lambda e: e.tensor_scalar(rowR[0:1, 512:1024], PS[3][0:1, :], 1.0 / DM,
                                         None, ALU.mult), inc=[(sv, 1)])
        E("v", lambda e: e.tensor_mul(rowS[0:1, :], rowM[0:1, :], rowM[0:1, :]),
          inc=[(sv, 1)])
        E("v", lambda e: e.tensor_sub(rowR[0:1, :], rowR[0:1, :], rowS[0:1, :]),
          inc=[(sv, 1)])
        E("v", lambda e: e.tensor_scalar_add(rowS[0:1, :], rowR[0:1, :],
                                             c_eps[0:1, 0:1]),
          wait=[(sv, sv.n), (sv, D["init_v"])], inc=[(sv, 1)])
        E("s", lambda e: e.activation(rowR[0:1, :], rowS[0:1, :], AF.Sqrt, scale=1.0),
          wait=[(sv, sv.n)], inc=[(ss, 1)])
        E("v", lambda e: e.reciprocal(rowR[0:1, :], rowR[0:1, :]),
          wait=[(ss, ss.n)], inc=[(sv, 1)])
        E("v", lambda e: e.tensor_mul(ecsf[1][0:1, :], rowR[0:1, :], rowR[0:1, :]),
          inc=[(sv, 1)])
        E("v", lambda e: e.tensor_mul(ecsf[1][0:1, :], ecsf[1][0:1, :], rowS[0:1, :]),
          inc=[(sv, 1)])
        E("v", lambda e: e.tensor_scalar(ecsf[1][0:1, :], ecsf[1][0:1, :], -0.5, 1.5,
                                         ALU.mult, ALU.add), inc=[(sv, 1)])
        E("v", lambda e: e.tensor_mul(rowR[0:1, :], rowR[0:1, :], ecsf[1][0:1, :]),
          inc=[(sv, 1)])
        stn = sv.n
        if "lnstats" in taps:
            E("q", lambda e, t0=t0: e.dma_start(d_rows[2:3, t0:t0 + TH], rowM[0:1, :]),
              wait=[(sv, stn)], inc=[(Q(f"ls{hf}"), 16)])
            E("q", lambda e, t0=t0: e.dma_start(d_rows[3:4, t0:t0 + TH], rowR[0:1, :]),
              wait=[(sv, stn)], inc=[(Q(f"ls{hf}b"), 16)])
        for t4 in range(2):
            E("t", lambda e, t4=t4: e.matmul(PS[4 + t4][:, :], c_ones[0:1, :],
                                             rowM[0:1, t4 * 512:(t4 + 1) * 512],
                                             start=True, stop=True),
              wait=[(sv, stn)], inc=[(st, 1)])
            E("t", lambda e, t4=t4: e.matmul(PS[6 + t4][:, :], c_ones[0:1, :],
                                             rowR[0:1, t4 * 512:(t4 + 1) * 512],
                                             start=True, stop=True), inc=[(st, 1)])
        bcn = st.n
        for k in range(KD):
            qq = Q(f"a{k % 2}")
            E("q", lambda e, kk=k, b=xstg[k % 2]: e.dma_start(
                b[:, :], xt[kk * 128:(kk + 1) * 128, t0:t0 + TH]),
                wait=[(sv, D.get(("p1xn", hf, k - 2), 0)), (st, fin)], inc=[(qq, 16)])
            ldn = (qq, qq.n)
            for t4 in range(2):
                sl = slice(t4 * 512, (t4 + 1) * 512)
                E("v", lambda e, b=xstg[k % 2], o=x2stg[k % 2], t4=t4, sl=sl: e.tensor_sub(
                    o[:, sl], b[:, sl], PS[4 + t4][:, :]),
                    wait=[ldn, (st, bcn)], inc=[(sv, 1)])
                E("v", lambda e, o=x2stg[k % 2], kk=k, t4=t4, sl=sl: e.tensor_mul(
                    xn_half[:, kk, sl], o[:, sl], PS[6 + t4][:, :]), inc=[(sv, 1)])
            D[("p1xn", hf, k)] = sv.n

    def p1b_half(hf):
        t0 = hf * TH
        spg = [(Q("sa0"), Q("sa0").n), (Q("sa1"), Q("sa1").n)]
        E("v", lambda e: e.tensor_copy(rmq1[:, :], xn_half[:, 0, :]),
          wait=spg, inc=[(sv, 1)])
        E("v", lambda e: e.tensor_copy(rmq1n[:, :], xn_half[:, 0, :]),
          wait=spg, inc=[(sv, 1)])
        for k in range(1, KD):
            E("v", lambda e, kk=k: e.tensor_max(rmq1[:, :], rmq1[:, :],
                                                xn_half[:, kk, :]), inc=[(sv, 1)])
            E("v", lambda e, kk=k: e.tensor_tensor(rmq1n[:, :], rmq1n[:, :],
                                                   xn_half[:, kk, :], ALU.min),
              inc=[(sv, 1)])
        E("v", lambda e: e.tensor_scalar_mul(rmq1n[:, :], rmq1n[:, :], -1.0),
          inc=[(sv, 1)])
        E("v", lambda e: e.tensor_max(rmq1[:, :], rmq1[:, :], rmq1n[:, :]),
          inc=[(sv, 1)])
        amx = sv.n
        for i in range(8):
            E("t", lambda e, i=i: e.transpose(PS[i % 2][:, 0:128],
                                              rmq1[:, i * 128:(i + 1) * 128], c_idn[:, :]),
              wait=[(sv, amx if i < 2 else D[("p1t", hf, i - 2)])], inc=[(st, 1)])
            E("v", lambda e, i=i: e.reduce_max(mxcols[:, i:i + 1], PS[i % 2][:, 0:128], AX),
              wait=[(st, st.n)], inc=[(sv, 1)])
            D[("p1t", hf, i)] = sv.n
        qq = Q(f"r1{hf}")
        E("q", lambda e, dst=row_assemble_dst(0, t0, 8): e.dma_start(
            dst, mxcols[:, :], allow_slow_non_contiguous=True),
            wait=[(sv, sv.n)], inc=[(qq, 16)])
        E("q", lambda e: e.dma_start(rowS[0:1, :], d_rows[0:1, t0:t0 + TH]),
          wait=[(qq, qq.n)], inc=[(qq, 16)])
        E("v", lambda e: e.tensor_scalar_max(rowS[0:1, :], rowS[0:1, :], QEPS),
          wait=[(qq, qq.n)], inc=[(sv, 1)])
        E("v", lambda e, o=t0: e.tensor_copy(rowB[0:1, o:o + TH], rowS[0:1, :]),
          inc=[(sv, 1)])
        E("v", lambda e: e.reciprocal(rowM[0:1, :], rowS[0:1, :]), inc=[(sv, 1)])
        E("v", lambda e: e.tensor_scalar_mul(rowM[0:1, :], rowM[0:1, :], 127.0),
          inc=[(sv, 1)])
        for t4 in range(2):
            E("t", lambda e, t4=t4: e.matmul(PS[6 + t4][:, :], c_ones[0:1, :],
                                             rowM[0:1, t4 * 512:(t4 + 1) * 512],
                                             start=True, stop=True),
              wait=[(sv, sv.n)], inc=[(st, 1)])
        bc = st.n
        for k in range(KD):
            qq = Q(f"sa{k % 2}")
            for t4 in range(2):
                sl = slice(t4 * 512, (t4 + 1) * 512)
                E("v", lambda e, kk=k, t4=t4, sl=sl: e.tensor_mul(
                    x2stg[kk % 2][:, sl], xn_half[:, kk, sl], PS[6 + t4][:, :]),
                    wait=[(st, bc)] + ([D[("p1q", hf, k - 2)]] if ("p1q", hf, k - 2) in D
                                       else ([D[("p1q", 0, 14 + (k % 2))]] if hf == 1 else [])),
                    inc=[(sv, 1)])
            E("v", lambda e, kk=k: e.tensor_scalar(
                xqstg[kk % 2][:, :], x2stg[kk % 2][:, :], MAGIC, MAGIC,
                ALU.add, ALU.subtract), inc=[(sv, 1)])
            E("q", lambda e, kk=k: e.dma_start(d_xqi[kk, :, t0:t0 + TH],
                                               xqstg[kk % 2][:, :]),
              wait=[(sv, sv.n)], inc=[(qq, 16)])
            D[("p1q", hf, k)] = (qq, qq.n)

    def p2_half(hf):
        t0 = hf * TH
        for o in (range(NO2) if p2_os is None else p2_os):
            grp = o % 2
            pg = [PS[2 * grp], PS[2 * grp + 1]]
            qq = Q(f"wb{o % 2}")
            E("g", lambda e, oo=o, b=p2wb[o % 2]: e.dma_start(b[:, :], win[oo]),
              wait=[(st, D.get(("p2mm", hf, o - 2), 0))], inc=[(qq, 16)])
            wv = [(qq, qq.n), (sv, D.get(("p2ev", hf, o - 2), 0)),
                  (ss, D.get(("p2es", hf, o - 2), 0)),
                  (sv, D.get(("p1bend", hf), 0))]
            for k in range(KD):
                for t2 in range(2):
                    E("t", lambda e, kk=k, t2=t2, b=p2wb[o % 2], pg=pg: e.matmul(
                        pg[t2][:, :], b[:, kk * 128:(kk + 1) * 128],
                        xn_half[:, kk, t2 * 512:(t2 + 1) * 512],
                        start=(kk == 0), stop=(kk == KD - 1)),
                        wait=wv if (k == 0 and t2 == 0) else [],
                        inc=[(st, 1)] if (k == KD - 1 and t2 == 1) else [])
            D[("p2mm", hf, o)] = st.n
            mmn = st.n
            if o < 32:
                qs2 = Q(f"pa{o % 2}")
                for t2 in range(2):
                    E("s", lambda e, t2=t2, b=sstg[o % 2], pg=pg: e.activation(
                        b[:, t2 * 512:(t2 + 1) * 512], pg[t2][:, :], AF.Silu, scale=1.0),
                        wait=[(st, mmn)] + ([D[("p2dma", hf, o - 2)]]
                                            if ("p2dma", hf, o - 2) in D else []),
                        inc=[(ss, 1)])
                D[("p2es", hf, o)] = ss.n
                D[("p2ev", hf, o)] = sv.n
                dst = d_gate[hf * 8:(hf + 1) * 8, o].transpose([1, 0, 2])
                E("q", lambda e, b=sstg[o % 2], dst=dst: e.dma_start(
                    dst, b[:, :].rearrange("p (c l) -> p c l", c=8)),
                    wait=[(ss, ss.n)], inc=[(qs2, 16)])
                D[("p2dma", hf, o)] = (qs2, qs2.n)
            elif o < 66:
                cc = o - 32
                acc = cacc[o % 2]
                for t2 in range(2):
                    E("v", lambda e, t2=t2, acc=acc, cc=cc, pg=pg: e.tensor_scalar_mul(
                        acc[:, t2 * 512:(t2 + 1) * 512], pg[t2][:, :],
                        c_convw[:, cc * 4 + 3:cc * 4 + 4]),
                        wait=[(st, mmn), (ss, D.get(("p2es", hf, o - 2), 0))],
                        inc=[(sv, 1)])
                for sft in (1, 2, 3):
                    kk = 3 - sft
                    wc = c_convw[:, cc * 4 + kk:cc * 4 + kk + 1]
                    if hf == 1:
                        E("v", lambda e, sft=sft, acc=acc, cc=cc, wc=wc:
                          e.scalar_tensor_tensor(
                              acc[:, 0:sft], c_tails[:, cc * 3 + 3 - sft:cc * 3 + 3], wc,
                              acc[:, 0:sft], ALU.mult, ALU.add), inc=[(sv, 1)])
                    for t2 in range(2):
                        o_lo = max(sft, t2 * 512)
                        o_hi = (t2 + 1) * 512
                        i_lo = o_lo - sft
                        while o_lo < o_hi:
                            sb = i_lo // 512
                            i_in = i_lo % 512
                            n = min(o_hi - o_lo, 512 - i_in)
                            E("v", lambda e, acc=acc, o_lo=o_lo, n=n, sb=sb, i_in=i_in,
                              wc=wc, pg=pg: e.scalar_tensor_tensor(
                                  acc[:, o_lo:o_lo + n], pg[sb][:, i_in:i_in + n], wc,
                                  acc[:, o_lo:o_lo + n], ALU.mult, ALU.add),
                                inc=[(sv, 1)])
                            o_lo += n
                            i_lo += n
                if hf == 0:
                    E("v", lambda e, cc=cc, pg=pg: e.tensor_copy(
                        c_tails[:, cc * 3:cc * 3 + 3], pg[1][:, 509:512]), inc=[(sv, 1)])
                D[("p2ev", hf, o)] = sv.n
                evn = sv.n
                if cc < 32:
                    qs2 = Q(f"pa{o % 2}")
                    for t2 in range(2):
                        E("s", lambda e, t2=t2, b=sstg[o % 2], acc=acc, cc=cc: e.activation(
                            b[:, t2 * 512:(t2 + 1) * 512], acc[:, t2 * 512:(t2 + 1) * 512],
                            AF.Silu, bias=c_convb[:, cc:cc + 1], scale=1.0),
                            wait=[(sv, evn)] + ([D[("p2dma", hf, o - 2)]]
                                                if ("p2dma", hf, o - 2) in D else []),
                            inc=[(ss, 1)])
                    D[("p2es", hf, o)] = ss.n
                    dst = d_xssm[hf * 8:(hf + 1) * 8, cc].transpose([1, 0, 2])
                    E("q", lambda e, b=sstg[o % 2], dst=dst: e.dma_start(
                        dst, b[:, :].rearrange("p (c l) -> p c l", c=8)),
                        wait=[(ss, ss.n)], inc=[(qs2, 16)])
                    D[("p2dma", hf, o)] = (qs2, qs2.n)
                else:
                    dstv = B_sb if cc == 32 else C_sb
                    for t2 in range(2):
                        E("s", lambda e, t2=t2, dv=dstv, acc=acc, cc=cc: e.activation(
                            dv[:, t0 + t2 * 512:t0 + (t2 + 1) * 512],
                            acc[:, t2 * 512:(t2 + 1) * 512],
                            AF.Silu, bias=c_convb[:, cc:cc + 1], scale=1.0),
                            wait=[(sv, evn)], inc=[(ss, 1)])
                    D[("p2es", hf, o)] = ss.n
            else:
                for t2 in range(2):
                    E("v", lambda e, t2=t2, pg=pg: e.tensor_scalar_add(
                        dt_sp[:, t0 + t2 * 512:t0 + (t2 + 1) * 512], pg[t2][0:64, :],
                        c_dtb[:, :]), wait=[(st, mmn)], inc=[(sv, 1)])
                dts = dt_sp[:, t0:t0 + TH]
                E("s", lambda e, d2=dts: e.activation(spa[:, :], d2, AF.Abs, scale=1.0),
                  wait=[(sv, sv.n)], inc=[(ss, 1)])
                E("s", lambda e: e.activation(spa[:, :], spa[:, :], AF.Exp, scale=-1.0),
                  inc=[(ss, 1)])
                E("s", lambda e: e.activation(spa[:, :], spa[:, :], AF.Ln,
                                              bias=c_ones[0:64, 0:1], scale=1.0),
                  inc=[(ss, 1)])
                E("v", lambda e, d2=dts: e.tensor_scalar_max(dtrelu[:, :], d2, 0.0),
                  wait=[(ss, ss.n)], inc=[(sv, 1)])
                E("v", lambda e, d2=dts: e.tensor_add(d2, spa[:, :], dtrelu[:, :]),
                  inc=[(sv, 1)])
                D[("p2ev", hf, o)] = sv.n
                D[("dt", hf)] = sv.n

    def p0_finish():
        E("v", lambda e: e.memset(carry[:, :], 0.0),
          wait=[(st, D[("p2mm", 1, NO2 - 1)])], inc=[(sv, 1)])
        E("v", lambda e: e.memset(carry_bf[:, :], 0.0), inc=[(sv, 1)])
        E("t", lambda e: e.matmul(PS[4][0:2, 0:2], c_ones[:, 0:2], c_wacc[:, :],
                                  start=True, stop=True),
          wait=[(sv, sv.n)], inc=[(st, 1)])
        E("v", lambda e: e.tensor_scalar(c_sw[0:1, 0:2], PS[4][0:1, 0:2],
                                         1.0 / (HFF * DM), QEPS, ALU.mult, ALU.max),
          wait=[(st, st.n)], inc=[(sv, 1)])
        E("v", lambda e: e.reciprocal(c_sw[0:1, 2:4], c_sw[0:1, 0:2]), inc=[(sv, 1)])
        E("t", lambda e: e.matmul(PS[5][:, 0:2], c_ones[0:1, :], c_sw[0:1, 2:4],
                                  start=True, stop=True), wait=[(sv, sv.n)], inc=[(st, 1)])
        E("v", lambda e: e.tensor_copy(c_swcol[:, :], PS[5][:, 0:2]),
          wait=[(st, st.n)], inc=[(sv, 1)])
        for hf in range(2):
            E("v", lambda e, hf=hf: e.tensor_scalar(
                inv1h[hf][0:1, :], rowB[0:1, hf * TH:(hf + 1) * TH],
                c_sw[0:1, 0:1], 1.0 / 127.0, ALU.mult, ALU.mult), inc=[(sv, 1)])
        D["p0fin"] = sv.n

    def p3_loads(c):
        par = c % 2
        qq = Q(f"ta{par}")
        src = d_xssm[c].rearrange("a p l -> (a p) l")
        E("s", lambda e, par=par, src=src: e.dma_start(xssmT[par][:, :], src,
                                                       transpose=True),
          wait=[(sv, D.get(("xtfree", c - 2), 0))] + D["p2spill"],
          inc=[(qq, 16)])
        D[("xtq", c)] = (qq, qq.n)

    def p3_group_loads(c, g4):
        qq1 = Q(f"ga{g4 % 2}")
        qq2 = Q(f"gb{g4 % 2}")
        wv = [(sv, D.get(("grpfree", c, g4 - 2), 0)) if g4 >= 2 else
              (sv, D.get(("grpfree", c - 1, g4 + 14), 0)),
              (st, D.get("p2end_t", 0))] + \
            ([D["xntap1"]] if "xntap1" in D else [])
        E("q", lambda e, c=c, g4=g4: e.dma_start(
            gateg[g4 % 2][:, :, :], d_gate[c, 2 * g4:2 * g4 + 2].transpose([1, 0, 2])),
            wait=wv + D["p2spill"], inc=[(qq1, 16)])
        E("q", lambda e, c=c, g4=g4: e.dma_start(
            xssmg[g4 % 2][:, :, :], d_xssm[c, 2 * g4:2 * g4 + 2].transpose([1, 0, 2])),
            wait=wv, inc=[(qq2, 16)])
        D[("grpq", c, g4)] = [(qq1, qq1.n), (qq2, qq2.n)]

    def p3_chunk_prelude(c):
        par = c % 2
        E("v", lambda e, c=c: e.tensor_scalar_mul(a_c[:, :], dt_sp[:, c * L:(c + 1) * L],
                                                  c_acol[:, :]),
          wait=[(sv, D.get(("dt", c // 8), 0)),
                (ss, D.get(("presfree", c - 1), 0))], inc=[(sv, 1)])
        E("v", lambda e: e.tensor_tensor_scan(cs_c[:, :], c_ones[0:64, :], a_c[:, :],
                                              0.0, ALU.mult, ALU.add), inc=[(sv, 1)])
        csn = sv.n
        E("s", lambda e: e.activation(ecs_c[:, :], cs_c[:, :], AF.Exp, scale=1.0),
          wait=[(sv, csn)], inc=[(ss, 1)])
        ecn = ss.n
        wv_ps = [(sv, D.get("w1evlast", 0))]
        E("t", lambda e: e.transpose(PS[6][:, 0:64], cs_c[:, :], c_idn[0:64, 0:64]),
          wait=[(sv, csn)] + wv_ps, inc=[(st, 1)])
        E("v", lambda e: e.tensor_copy(csT_c[:, :], PS[6][:, 0:64]),
          wait=[(st, st.n)], inc=[(sv, 1)])
        E("t", lambda e, c=c: e.transpose(PS[7][:, 0:64], dt_sp[:, c * L:(c + 1) * L],
                                          c_idn[0:64, 0:64]), inc=[(st, 1)])
        E("v", lambda e: e.tensor_copy(dtT_c[:, :], PS[7][:, 0:64]),
          wait=[(st, st.n)], inc=[(sv, 1)])
        E("t", lambda e, c=c: e.transpose(PSb(6, 256, 128), B_sb[:, c * L:(c + 1) * L],
                                          c_idnb[:, :]), wait=[(sv, sv.n)], inc=[(st, 1)])
        E("v", lambda e: e.tensor_copy(B_t[:, :], PSb(6, 256, 128)),
          wait=[(st, st.n)], inc=[(sv, 1)])
        E("t", lambda e, c=c: e.matmul(PS[7][:, 64:192], B_sb[:, c * L:(c + 1) * L],
                                       C_sb[:, c * L:(c + 1) * L], start=True, stop=True),
          inc=[(st, 1)])
        E("v", lambda e: e.tensor_mul(scoresM[:, :], PS[7][:, 64:192], c_mask[:, :]),
          wait=[(st, st.n)], inc=[(sv, 1)])
        D[("csready", c)] = (csn, ecn)
        for g8 in range(2):
            qf1, qf2 = Q(f"tb{g8 % 2}"), Q(f"tc{g8 % 2}")
            wv = [(st, D.get(("flfree", c - 1), 0))]
            E("s", lambda e, g8=g8: e.dma_start(csf[g8 % 2][0:1, :],
                                                cs_c[8 * g8:8 * g8 + 8, :]),
              wait=[(sv, csn)] + wv, inc=[(qf1, 16)])
            E("s", lambda e, g8=g8: e.dma_start(ecsf[g8 % 2][0:1, :],
                                                ecs_c[8 * g8:8 * g8 + 8, :]),
              wait=[(ss, ecn)] + wv, inc=[(qf2, 16)])
            D[("flq", c, g8)] = [(qf1, qf1.n), (qf2, qf2.n)]
        qtd = Q("td")
        E("s", lambda e: e.dma_start(cslf[0:1, :], cs_c[:, 127:128]),
          wait=[(sv, csn), (st, D.get(("flfree", c - 1), 0))], inc=[(qtd, 16)])
        E("t", lambda e: e.matmul(PS[7][:, 192:256], c_ones[0:1, :], cslf[0:1, :],
                                  start=True, stop=True),
          wait=[(qtd, qtd.n)], inc=[(st, 1)])
        E("v", lambda e: e.tensor_sub(dsT_c[:, :], PS[7][:, 192:256], csT_c[:, :]),
          wait=[(st, st.n)], inc=[(sv, 1)])
        E("s", lambda e: e.activation(dsT_c[:, :], dsT_c[:, :], AF.Exp, scale=1.0),
          wait=[(sv, sv.n)], inc=[(ss, 1)])
        E("s", lambda e: e.activation(cd_c[:, :], PS[7][:, 192:256], AF.Exp, scale=1.0),
          inc=[(ss, 1)])
        E("v", lambda e, par=par: e.tensor_tensor(
            xssmT[par][:, :].rearrange("p (h d2) -> p h d2", h=NH),
            xssmT[par][:, :].rearrange("p (h d2) -> p h d2", h=NH),
            dtT_c[:, :].unsqueeze(2).broadcast_to([128, NH, HD]), ALU.mult),
            wait=[D[("xtq", c)]], inc=[(sv, 1)])
        D[("pre", c)] = (sv.n, ss.n, st.n)
        D["presv"] = sv.n
        D[("presfree", c)] = ss.n

    def p3_chunk(c):
        par = c % 2
        for g4 in range(16):
            p3_group_loads(c, g4)
            for j in range(4):
                h = g4 * 4 + j
                g8, pr = h // 8, h // 2
                bpar = pr % 2
                if h % 2 == 0:
                    wv = list(D[("flq", c, g8)])
                    key = ("prread", c, pr - 2) if pr >= 2 else ("prread", c - 1, pr + 30)
                    if key in D:
                        wv += [(ss, D[key][0]), (sv, D[key][1])]
                    E("t", lambda e, bpar=bpar, g8=g8, h=h: e.matmul(
                        PS[bpar][:, 0:256], c_ones[0:1, :],
                        csf[g8 % 2][0:1, (h % 8) * 128:(h % 8) * 128 + 256],
                        start=True, stop=True), wait=wv, inc=[(st, 1)])
                    E("t", lambda e, bpar=bpar, g8=g8, h=h: e.matmul(
                        PS[bpar][:, 256:512], c_ones[0:1, :],
                        ecsf[g8 % 2][0:1, (h % 8) * 128:(h % 8) * 128 + 256],
                        start=True, stop=True), inc=[(st, 1)])
                    D[("prbc", c, pr)] = st.n
                E("s", lambda e, bpar=bpar, h=h: e.activation(
                    m1sc[:, :], PS[bpar][:, (h % 2) * 128:(h % 2) * 128 + 128], AF.Relu,
                    bias=csT_c[:, h:h + 1], scale=-1.0),
                    wait=[(st, D[("prbc", c, pr)])], inc=[(ss, 1)])
                E("s", lambda e, h=h: e.activation(edif[h % 2][:, :], m1sc[:, :],
                                                   AF.Exp, scale=-1.0), inc=[(ss, 1)])
                edn = ss.n
                E("v", lambda e, h=h: e.tensor_mul(mtt[h % 2][:, :], edif[h % 2][:, :],
                                                   scoresM[:, :]),
                  wait=[(ss, edn)], inc=[(sv, 1)])
                E("v", lambda e, h=h, bpar=bpar, c=c: e.tensor_mul(
                    cexp[h % 2][:, :], C_sb[:, c * L:(c + 1) * L],
                    PS[bpar][:, 256 + (h % 2) * 128:256 + (h % 2) * 128 + 128]),
                    inc=[(sv, 1)])
                mtn = sv.n
                D[("prread", c, pr)] = (ss.n, sv.n)
                yb = 2 + (g4 % 2)
                pb, cl = 64 * (h % 2), 128 * ((h % 4) // 2)
                wv = [(sv, mtn)]
                key = ("yfree", c, g4 - 2) if g4 >= 2 else ("yfree", c - 1, g4 + 14)
                if key in D:
                    wv.append((sv, D[key]))
                E("t", lambda e, yb=yb, pb=pb, cl=cl, h=h, par=par: e.matmul(
                    PS[yb][pb:pb + 64, cl:cl + 128],
                    xssmT[par][:, h * HD:(h + 1) * HD], mtt[h % 2][:, :],
                    start=True, stop=False), wait=wv, inc=[(st, 1)])
                E("t", lambda e, yb=yb, pb=pb, cl=cl, h=h: e.matmul(
                    PS[yb][pb:pb + 64, cl:cl + 128],
                    carry_bf[:, h * HD:(h + 1) * HD], cexp[h % 2][:, :],
                    start=False, stop=True), inc=[(st, 1)])
                if h % 8 == 6 and h // 8 + 2 <= 7:
                    gnx = h // 8 + 2
                    qf1, qf2 = Q(f"tb{gnx % 2}"), Q(f"tc{gnx % 2}")
                    E("s", lambda e, gnx=gnx: e.dma_start(
                        csf[gnx % 2][0:1, :], cs_c[8 * gnx:8 * gnx + 8, :]),
                        wait=[(st, st.n)], inc=[(qf1, 16)])
                    E("s", lambda e, gnx=gnx: e.dma_start(
                        ecsf[gnx % 2][0:1, :], ecs_c[8 * gnx:8 * gnx + 8, :]),
                        wait=[(st, st.n)], inc=[(qf2, 16)])
                    D[("flq", c, gnx)] = [(qf1, qf1.n), (qf2, qf2.n)]
            E("v", lambda e, par=par, g4=g4: e.tensor_tensor(
                xdtds_g[g4 % 2][:, :, :],
                xssmT[par][:, g4 * 256:(g4 + 1) * 256].rearrange("p (a d2) -> p a d2", a=4),
                dsT_c[:, g4 * 4:g4 * 4 + 4].unsqueeze(2).broadcast_to([128, 4, HD]),
                ALU.mult), wait=[(st, D.get(("grpst", c, g4 - 2),
                                            D.get(("grpst", c - 1, 15), 0)))],
                inc=[(sv, 1)])
            E("t", lambda e, g4=g4: e.matmul(
                PS[2 + (g4 % 2)][:, 256:512], B_t[:, :],
                xdtds_g[g4 % 2][:, :, :].rearrange("p a d2 -> p (a d2)"),
                start=True, stop=True), wait=[(sv, sv.n)], inc=[(st, 1)])
            grp_st = st.n
            D[("grpst", c, g4)] = grp_st
            for j in range(4):
                h = g4 * 4 + j
                yb = 2 + (g4 % 2)
                pb = 64 * (h % 2)
                cl = 128 * ((h % 4) // 2)
                kb = h // 2
                E("v", lambda e, yb=yb, pb=pb, cl=cl, h=h, j=j, g4=g4, kb=kb:
                  e.scalar_tensor_tensor(
                      t1st[pb:pb + 64, (h % 4) // 2, :],
                      xssmg[g4 % 2][pb:pb + 64, j // 2, :],
                      c_dpp[pb:pb + 64, kb:kb + 1],
                      PS[yb][pb:pb + 64, cl:cl + 128], ALU.mult, ALU.add),
                    wait=[(st, grp_st)] + D[("grpq", c, g4)] +
                    ([D[("hrq", c - 2)]] if (g4 == 0 and j == 0 and ("hrq", c - 2) in D)
                     else []),
                    inc=[(sv, 1)])
                E("v", lambda e, par=par, pb=pb, h=h, j=j, g4=g4, kb=kb: e.tensor_mul(
                    hready_asm[par][pb:pb + 64, kb, :],
                    t1st[pb:pb + 64, (h % 4) // 2, :],
                    gateg[g4 % 2][pb:pb + 64, j // 2, :]), inc=[(sv, 1)])
                E("v", lambda e, yb=yb, h=h, j=j: e.scalar_tensor_tensor(
                    carry[:, h * HD:(h + 1) * HD], carry[:, h * HD:(h + 1) * HD],
                    cd_c[:, h:h + 1], PS[yb][:, 256 + j * 64:256 + (j + 1) * 64],
                    ALU.mult, ALU.add), inc=[(sv, 1)])
                E("v", lambda e, h=h: e.tensor_copy(carry_bf[:, h * HD:(h + 1) * HD],
                                                    carry[:, h * HD:(h + 1) * HD]),
                  inc=[(sv, 1)])
            D[("yfree", c, g4)] = sv.n
            D[("grpfree", c, g4)] = sv.n
        D[("xtfree", c)] = sv.n
        D[("flfree", c)] = st.n
        qq = Q(f"ha{c % 2}")
        E("q", lambda e, par=par, c=c: e.dma_start(
            d_hready[:, :, c * L:(c + 1) * L].transpose([1, 0, 2]),
            hready_asm[par][:, :, :]),
            wait=[(sv, sv.n)], inc=[(qq, 16)])
        D[("hrq", c)] = (qq, qq.n)

    def p5_init_half(hf):
        qq = Q("xq")
        wv = [(st, D[("w1mm", 0, 63)])] if hf == 1 else []
        E("q", lambda e: e.dma_start(
            xqi_half[:, :, :], d_xqi[:, :, hf * TH:(hf + 1) * TH].transpose([1, 0, 2])),
            wait=wv + [D[("p1q", hf, KD - 1)], D[("p1q", hf, KD - 2)]], inc=[(qq, 16)])
        D[("xqiq", hf)] = (qq, qq.n)
        for t4 in range(2):
            E("t", lambda e, hf=hf, t4=t4: e.matmul(
                PS[4 + t4][:, :], c_ones[0:1, :],
                inv1h[hf][0:1, t4 * 512:(t4 + 1) * 512], start=True, stop=True),
                wait=[(sv, D["p0fin"]), (sv, D.get("w1evlast", 0))], inc=[(st, 1)])
        bn = st.n
        E("v", lambda e: e.tensor_copy(invsbc[:, 0:512], PS[4][:, :]),
          wait=[(st, bn)], inc=[(sv, 1)])
        E("v", lambda e: e.tensor_copy(invsbc[:, 512:1024], PS[5][:, :]), inc=[(sv, 1)])
        D[("invn", hf)] = sv.n

    def p5_batch(hf, olo, ohi):
        for o in range(olo, ohi):
            ob = o % 2
            wmm_guard = (st, D.get(("w1mm", hf, o - 2),
                                   D.get(("w1mm", 0, 62 + (o % 2)), 0) if hf == 1
                                   else 0))
            for h2 in range(2):
                qq = Q(f"wa{h2}")
                sl = slice(h2 * 1024, (h2 + 1) * 1024)
                E("g", lambda e, o=o, b=w1bf[h2], sl=sl: e.dma_start(b[:, :], w1[o][:, sl]),
                  wait=[(sv, D.get(("w1qh", hf, o - 1, h2),
                                   D.get(("w1qh", 1 - hf, 63, h2), 0))),
                        (sv, D.get("p0done", 0))],
                  inc=[(qq, 16)])
                wn = (qq, qq.n)
                E("v", lambda e, b=w1bf[h2]: e.tensor_scalar(
                    wqscr[:, :], b[:, :], c_swcol[:, 0:1], CLIP1, ALU.mult, ALU.min),
                    wait=[wn, (sv, D["p0fin"]), wmm_guard], inc=[(sv, 1)])
                E("v", lambda e: e.tensor_scalar(wqscr[:, :], wqscr[:, :], -CLIP1, MAGIC,
                                                 ALU.max, ALU.add), inc=[(sv, 1)])
                E("v", lambda e, o=o, sl=sl: e.tensor_scalar(
                    wq1b[o % 2][:, sl], wqscr[:, :], MAGIC, None, ALU.subtract),
                    inc=[(sv, 1)])
                D[("w1qh", hf, o, h2)] = sv.n
            D[("w1q", hf, o)] = sv.n
            wv = [(sv, D[("w1q", hf, o)]), D[("xqiq", hf)], (sv, D["presv"]),
                  (sv, D.get(("w1ev", hf, o - 2), 0))]
            if o < 2:
                wv.append((sv, D[("invn", hf)]))
            pg = [PS[4 + 2 * (o % 2)], PS[5 + 2 * (o % 2)]]
            for k in range(KD):
                for t2 in range(2):
                    E("t", lambda e, o=o, k=k, t2=t2, pg=pg: e.matmul(
                        pg[t2][:, :], wq1b[o % 2][:, k * 128:(k + 1) * 128],
                        xqi_half[:, k, t2 * 512:(t2 + 1) * 512],
                        start=(k == 0), stop=(k == KD - 1)),
                        wait=wv if (k == 0 and t2 == 0) else [],
                        inc=[(st, 1)] if (k == KD - 1 and t2 == 1) else [])
            D[("w1mm", hf, o)] = st.n
            mmn = st.n
            qh = Q(f"h1{o % 2}")
            for t2 in range(2):
                E("v", lambda e, o=o, t2=t2, pg=pg: e.tensor_mul(
                    h1stg[o % 2][:, t2 * 512:(t2 + 1) * 512], pg[t2][:, :],
                    invsbc[:, t2 * 512:(t2 + 1) * 512]),
                    wait=[(st, mmn), (sv, D[("invn", hf)])] +
                    ([D[("h1q", hf, o - 2)]] if ("h1q", hf, o - 2) in D else
                     ([D[("h1q", 0, 62 + (o % 2))]] if hf == 1 and o < 2 else
                      [(Q("sa0"), Q("sa0").n), (Q("sa1"), Q("sa1").n)])),
                    inc=[(sv, 1)])
            D[("w1ev", hf, o)] = sv.n
            D["w1evlast"] = sv.n
            E("q", lambda e, o=o, hf=hf: e.dma_start(
                d_h1[o, :, hf * TH:(hf + 1) * TH], h1stg[o % 2][:, :]),
                wait=[(sv, sv.n)], inc=[(qh, 16)])
            D[("h1q", hf, o)] = (qh, qh.n)

    def p4_half(hf):
        t0 = hf * TH
        q1, q2 = Q("p40"), Q("p41")
        wv = [D[("hrq", NCH - 1)], D[("hrq", NCH - 2)], (st, D[("w1mm", 1, 63)])]
        if hf == 1:
            wv.append((st, D[("p4mm", 0, 15)]))
            wv.append((sv, D[("p4sq", 0, 31)]))
        E("q", lambda e: e.dma_start(
            hr4_1[:, :, :], d_hready[0:16, :, t0:t0 + TH].transpose([1, 0, 2])),
            wait=wv, inc=[(q1, 16)])
        E("q", lambda e: e.dma_start(
            hr4_2[:, :, :], d_hready[16:32, :, t0:t0 + TH].transpose([1, 0, 2])),
            wait=wv, inc=[(q2, 16)])
        ldq = [(q1, q1.n), (q2, q2.n)]
        E("v", lambda e: e.memset(vacc4[:, :], 0.0), inc=[(sv, 1)])
        for k in range(32):
            hr = hr4_1 if k < 16 else hr4_2
            kk = k % 16
            E("s", lambda e, hr=hr, kk=kk, k=k: e.activation(
                sq4[k % 2][:, :], hr[:, kk, :], AF.Square, scale=1.0),
                wait=ldq + [(sv, D.get(("p4sq", hf, k - 2), 0))], inc=[(ss, 1)])
            E("v", lambda e, k=k: e.tensor_add(vacc4[:, :], vacc4[:, :], sq4[k % 2][:, :]),
              wait=[(ss, ss.n)], inc=[(sv, 1)])
            E("v", lambda e, hr=hr, kk=kk, k=k: e.tensor_scalar_mul(
                hr[:, kk, :], hr[:, kk, :], c_nwpp[:, k:k + 1]), inc=[(sv, 1)])
            D[("p4sq", hf, k)] = sv.n
        nwn = sv.n
        for t2 in range(2):
            E("t", lambda e, t2=t2: e.matmul(PS[4 + t2][0:1, :], c_ones[:, 0:1],
                                             vacc4[:, t2 * 512:(t2 + 1) * 512],
                                             start=True, stop=True),
              wait=[(sv, nwn)], inc=[(st, 1)])
        E("v", lambda e: e.tensor_scalar_mul(rowM[0:1, 0:512], PS[4][0:1, :],
                                             1.0 / INTER),
          wait=[(st, st.n)] + ([D[("varq", hf - 1)]] if ("varq", hf - 1) in D else []),
          inc=[(sv, 1)])
        E("v", lambda e: e.tensor_scalar_mul(rowM[0:1, 512:1024], PS[5][0:1, :],
                                             1.0 / INTER), inc=[(sv, 1)])
        qr4 = Q(f"r4{hf}")
        E("q", lambda e, t0=t0: e.dma_start(d_rows[2:3, t0:t0 + TH], rowM[0:1, :]),
          wait=[(sv, sv.n)], inc=[(qr4, 16)])
        D[("varq", hf)] = (qr4, qr4.n)
        for o in range(16):
            pg = [PS[2 * (o % 2)], PS[2 * (o % 2) + 1]]
            qq = Q(f"wc{o % 2}")
            E("g", lambda e, o=o: e.dma_start(wob[o % 2][:, :], wout[o]),
              wait=[(st, D.get(("p4mm", hf, o - 2), 0))], inc=[(qq, 16)])
            wv2 = [(qq, qq.n), (sv, nwn), (sv, D.get(("p4ev", hf, o - 2), 0))]
            for k in range(32):
                hr = hr4_1 if k < 16 else hr4_2
                kk = k % 16
                for t2 in range(2):
                    E("t", lambda e, o=o, k=k, t2=t2, hr=hr, kk=kk, pg=pg: e.matmul(
                        pg[t2][:, :], wob[o % 2][:, k * 128:(k + 1) * 128],
                        hr[:, kk, t2 * 512:(t2 + 1) * 512],
                        start=(k == 0), stop=(k == 31)),
                        wait=wv2 if (k == 0 and t2 == 0) else [],
                        inc=[(st, 1)] if (k == 31 and t2 == 1) else [])
            D[("p4mm", hf, o)] = st.n
            qst = Q(f"st{o % 2}")
            for t2 in range(2):
                E("v", lambda e, o=o, t2=t2, pg=pg: e.tensor_copy(
                    stil4[o % 2][:, t2 * 512:(t2 + 1) * 512], pg[t2][:, :]),
                    wait=[(st, st.n)] + ([D[("stq", hf, o - 2)]]
                                         if ("stq", hf, o - 2) in D else
                                         ([D[("stq", 0, 14 + (o % 2))]]
                                          if hf == 1 and o < 2 else [])),
                    inc=[(sv, 1)])
            D[("p4ev", hf, o)] = sv.n
            E("q", lambda e, o=o, t0=t0: e.dma_start(
                d_stil[o, :, t0:t0 + TH], stil4[o % 2][:, :]),
                wait=[(sv, sv.n)], inc=[(qst, 16)])
            D[("stq", hf, o)] = (qst, qst.n)

    def p4_w2quant():
        for i in range(32):
            o, sub = i // 2, i % 2
            qwq = Q(f"wq{i % 2}")
            for h2 in range(2):
                j = i * 2 + h2
                qq = Q(f"wd{j % 2}")
                sl_src = slice(sub * 4096 + h2 * 2048, sub * 4096 + (h2 + 1) * 2048)
                sl_dst = slice(h2 * 2048, (h2 + 1) * 2048)
                E("g", lambda e, o=o, b=w2q_raw[j % 2], sl_src=sl_src: e.dma_start(
                    b[:, :], w2[o][:, sl_src]),
                    wait=[(sv, D.get(("w2qj", j - 2), 0))], inc=[(qq, 16)])
                wn = (qq, qq.n)
                E("v", lambda e, j=j, b=w2q_raw[j % 2]: e.tensor_scalar(
                    w2q_scr[:, :], b[:, :], c_swcol[:, 1:2], CLIP1, ALU.mult, ALU.min),
                    wait=[wn, (sv, D["p0fin"])] +
                    ([D[("wq2d", i - 1)]] if (h2 == 0 and ("wq2d", i - 1) in D) else []),
                    inc=[(sv, 1)])
                E("v", lambda e: e.tensor_scalar(w2q_scr[:, :], w2q_scr[:, :],
                                                 -CLIP1, MAGIC, ALU.max, ALU.add),
                  inc=[(sv, 1)])
                E("v", lambda e, sl_dst=sl_dst: e.tensor_scalar(
                    w2q_stg[:, sl_dst], w2q_scr[:, :], MAGIC, None, ALU.subtract),
                    inc=[(sv, 1)])
                D[("w2qj", j)] = sv.n
            D[("w2q", i)] = sv.n
            E("q", lambda e, o=o, sub=sub: e.dma_start(
                d_wq2[o][:, sub * 4096:(sub + 1) * 4096], w2q_stg[:, :]),
                wait=[(sv, sv.n)], inc=[(qwq, 16)])
            D[("wq2d", i)] = (qwq, qwq.n)

    def p6_quarter(q4):
        t0 = q4 * TQ
        q1, q2 = Q("p60"), Q("p61")
        wv = [D[("wq2d", 31)], D[("wq2d", 30)], D[("h1q", 1, 63)], D[("h1q", 1, 62)]]
        if q4 > 0:
            wv.append((st, D[("p6mm", q4 - 1, 15)]))
            wv.append((ss, D[("p6ev", q4 - 1, 15)]))
        else:
            wv.append((st, D[("p4mm", 1, 15)]))
            wv.append((sv, D[("p4ev", 1, 15)]))
        if "hq0tap" in D:
            wv += D["hq0tap"]
        if "sch0tap" in D:
            wv += D["sch0tap"]
        E("q", lambda e, q4=q4: e.dma_start(
            hq6_1[:, :, :], d_h1[0:32, :, t0:t0 + TQ].transpose([1, 0, 2])),
            wait=wv, inc=[(q1, 16)])
        E("q", lambda e, q4=q4: e.dma_start(
            hq6_2[:, :, :], d_h1[32:64, :, t0:t0 + TQ].transpose([1, 0, 2])),
            wait=wv, inc=[(q2, 16)])
        ldq = [(q1, q1.n), (q2, q2.n)]
        E("v", lambda e: e.memset(rmq[:, :], 0.0),
          wait=[D[("stq", 1, 15)], D[("stq", 1, 14)]] if q4 == 0 else [],
          inc=[(sv, 1)])
        E("v", lambda e: e.memset(rmqn[:, :], 0.0), inc=[(sv, 1)])
        for k in range(64):
            hq = hq6_1 if k < 32 else hq6_2
            kk = k % 32
            E("s", lambda e, hq=hq, kk=kk: e.activation(hq[:, kk, :], hq[:, kk, :],
                                                        AF.Gelu, scale=1.0),
              wait=ldq if k == 0 else [], inc=[(ss, 1)])
            E("v", lambda e, hq=hq, kk=kk: e.tensor_max(rmq[:, :], rmq[:, :],
                                                        hq[:, kk, :]),
              wait=[(ss, ss.n)], inc=[(sv, 1)])
            E("v", lambda e, hq=hq, kk=kk: e.tensor_tensor(rmqn[:, :], rmqn[:, :],
                                                           hq[:, kk, :], ALU.min),
              inc=[(sv, 1)])
        E("v", lambda e: e.tensor_scalar_mul(rmqn[:, :], rmqn[:, :], -1.0), inc=[(sv, 1)])
        E("v", lambda e: e.tensor_max(rmq[:, :], rmq[:, :], rmqn[:, :]), inc=[(sv, 1)])
        for i in range(4):
            E("t", lambda e, i=i: e.transpose(PS[6][:, 0:128],
                                              rmq[:, i * 128:(i + 1) * 128],
                                              c_idn[:, :]),
              wait=[(sv, sv.n)], inc=[(st, 1)])
            E("v", lambda e, i=i: e.reduce_max(mx6[:, i:i + 1], PS[6][:, 0:128], AX),
              wait=[(st, st.n)], inc=[(sv, 1)])
        qr6 = Q(f"r6{q4}")
        E("q", lambda e, dst=row_assemble_dst(1, t0, 4): e.dma_start(
            dst, mx6[:, :], allow_slow_non_contiguous=True),
            wait=[(sv, sv.n)], inc=[(qr6, 16)])
        E("q", lambda e, t0=t0: e.dma_start(row6[0][0:1, :], d_rows[1:2, t0:t0 + TQ]),
          wait=[(qr6, qr6.n)] + ([D[("varq", 1)], D[("varq", 0)]] if q4 == 0 else []),
          inc=[(qr6, 16)])
        E("v", lambda e: e.tensor_scalar_max(row6[0][0:1, :], row6[0][0:1, :], QEPS),
          wait=[(qr6, qr6.n)], inc=[(sv, 1)])
        E("v", lambda e: e.reciprocal(row6[1][0:1, :], row6[0][0:1, :]), inc=[(sv, 1)])
        E("v", lambda e: e.tensor_scalar_mul(row6[1][0:1, :], row6[1][0:1, :], 127.0),
          inc=[(sv, 1)])
        E("v", lambda e: e.tensor_scalar(row6[2][0:1, :], row6[0][0:1, :],
                                         c_sw[0:1, 1:2], 1.0 / 127.0, ALU.mult, ALU.mult),
          inc=[(sv, 1)])
        E("t", lambda e: e.matmul(PS[6][:, :], c_ones[0:1, :], row6[1][0:1, :],
                                  start=True, stop=True), wait=[(sv, sv.n)], inc=[(st, 1)])
        E("t", lambda e: e.matmul(PS[7][:, :], c_ones[0:1, :], row6[2][0:1, :],
                                  start=True, stop=True), inc=[(st, 1)])
        E("v", lambda e: e.tensor_copy(schbc[:, :], PS[6][:, :]),
          wait=[(st, st.n)], inc=[(sv, 1)])
        E("v", lambda e: e.tensor_copy(i2bc[:, :], PS[7][:, :]), inc=[(sv, 1)])
        bcn = sv.n
        for k in range(64):
            hq = hq6_1 if k < 32 else hq6_2
            kk = k % 32
            E("v", lambda e, hq=hq, kk=kk, k=k: e.tensor_mul(
                t6scr[k % 2][:, :], hq[:, kk, :], schbc[:, :]),
                wait=[(sv, bcn)] if k == 0 else [], inc=[(sv, 1)])
            E("v", lambda e, hq=hq, kk=kk, k=k: e.tensor_scalar(
                hq[:, kk, :], t6scr[k % 2][:, :], MAGIC, MAGIC, ALU.add, ALU.subtract),
                inc=[(sv, 1)])
        qn = sv.n
        if q4 == 0 and "hq0" in taps:
            E("q", lambda e: e.dma_start(tap_out["hq0"][0],
                                         hq6_1[:, :, :].rearrange("p a b -> p (a b)")),
              wait=[(sv, qn)], inc=[(Q("thq"), 16)])
            E("q", lambda e: e.dma_start(tap_out["hq0"][1],
                                         hq6_2[:, :, :].rearrange("p a b -> p (a b)")),
              wait=[(sv, qn)], inc=[(Q("thq2"), 16)])
            D["hq0tap"] = [(Q("thq"), Q("thq").n), (Q("thq2"), Q("thq2").n)]
        if q4 == 0 and "sch0" in taps:
            E("q", lambda e: e.dma_start(tap_out["sch0"][0:1, :], row6[1][0:1, :]),
              wait=[(sv, qn)], inc=[(Q("tsc"), 16)])
            E("q", lambda e: e.dma_start(tap_out["sch0"][1:2, :], row6[2][0:1, :]),
              wait=[(sv, qn)], inc=[(Q("tsc2"), 16)])
            D["sch0tap"] = [(Q("tsc"), Q("tsc").n), (Q("tsc2"), Q("tsc2").n)]
        for o in range(16):
            qq = Q(f"w2b{o % 2}")
            E("q", lambda e, o=o: e.dma_start(
                wq2b[o % 2][:, :, :], d_wq2[o].rearrange("p (a b) -> p a b", a=64)),
                wait=[(st, D.get(("p6mm", q4, o - 2), 0))] +
                ([D[("wq2d", 31)], D[("wq2d", 30)]] if (q4 == 0 and o < 2) else []),
                inc=[(qq, 16)])
            wv2 = [(qq, qq.n), (sv, qn), (ss, D.get(("p6ev", q4, o - 2), 0))]
            pb = PS[o % 2]
            for k in range(64):
                hq = hq6_1 if k < 32 else hq6_2
                kk = k % 32
                E("t", lambda e, o=o, k=k, hq=hq, kk=kk, pb=pb: e.matmul(
                    pb[:, :], wq2b[o % 2][:, k, :], hq[:, kk, :],
                    start=(k == 0), stop=(k == 63)),
                    wait=wv2 if k == 0 else [],
                    inc=[(st, 1)] if k == 63 else [])
            D[("p6mm", q4, o)] = st.n
            E("v", lambda e, o=o, pb=pb: e.tensor_mul(t6scr[o % 2][:, :], pb[:, :],
                                                      i2bc[:, :]),
              wait=[(st, st.n), (ss, D.get(("p6ev", q4, o - 2), 0))], inc=[(sv, 1)])
            qg = Q(f"g6{o % 2}")
            E("s", lambda e, o=o: e.activation(gst6[o % 2][:, :], t6scr[o % 2][:, :],
                                               AF.Silu, scale=1.0),
              wait=[(sv, sv.n)] + ([D[("gq", q4, o - 2)]] if ("gq", q4, o - 2) in D else
                                   ([D[("gq", q4 - 1, 14 + (o % 2))]] if q4 > 0 and o < 2
                                    else [])),
              inc=[(ss, 1)])
            D[("p6ev", q4, o)] = ss.n
            E("q", lambda e, o=o, t0=t0: e.dma_start(d_g[o, :, t0:t0 + TQ],
                                                     gst6[o % 2][:, :]),
              wait=[(ss, ss.n)], inc=[(qg, 16)])
            D[("gq", q4, o)] = (qg, qg.n)

    def p7():
        qr7 = Q("r7")
        E("q", lambda e: e.dma_start(r7[0:1, :], d_rows[2:3, :]),
          wait=[D[("varq", 1)], D[("varq", 0)], (ss, D[("p6ev", 3, 15)])],
          inc=[(qr7, 16)])
        E("v", lambda e: e.tensor_scalar_add(r7b[0:1, :], r7[0:1, :], c_eps[0:1, 1:2]),
          wait=[(qr7, qr7.n)], inc=[(sv, 1)])
        E("s", lambda e: e.activation(r7[0:1, :], r7b[0:1, :], AF.Sqrt, scale=1.0),
          wait=[(sv, sv.n)], inc=[(ss, 1)])
        E("v", lambda e: e.reciprocal(r7[0:1, :], r7[0:1, :]),
          wait=[(ss, ss.n)], inc=[(sv, 1)])
        E("v", lambda e: e.tensor_mul(r7c[0:1, :], r7[0:1, :], r7[0:1, :]), inc=[(sv, 1)])
        E("v", lambda e: e.tensor_mul(r7c[0:1, :], r7c[0:1, :], r7b[0:1, :]),
          inc=[(sv, 1)])
        E("v", lambda e: e.tensor_scalar(r7c[0:1, :], r7c[0:1, :], -0.5, 1.5,
                                         ALU.mult, ALU.add), inc=[(sv, 1)])
        E("v", lambda e: e.tensor_mul(r7[0:1, :], r7[0:1, :], r7c[0:1, :]),
          inc=[(sv, 1)])
        for t4 in range(4):
            E("t", lambda e, t4=t4: e.matmul(PS[4 + t4][:, :],
                                             c_ones[0:1, :],
                                             r7[0:1, t4 * 512:(t4 + 1) * 512],
                                             start=True, stop=True),
              wait=[(sv, sv.n)], inc=[(st, 1)])
        bcn = st.n
        for k in range(KD):
            qxk = Q(f"x7{k % 2}")
            qsk = Q(f"s7{k % 2}")
            qgk = Q(f"g7{k % 2}")
            qok = Q(f"o7{k % 2}")
            wv = [D[("gq", 3, 15)], D[("gq", 3, 14)]] if k < 2 else                  [D[("p7q", k - 2)]]
            E("q", lambda e, k=k: e.dma_start(x7[k % 2][:, :],
                                              xt[k * 128:(k + 1) * 128, :]),
              wait=wv, inc=[(qxk, 16)])
            E("q", lambda e, k=k: e.dma_start(stil7[k % 2][:, :], d_stil[k]),
              wait=wv + ([D[("stq", 1, k)], D[("stq", 0, k)]] if k < 2 else []),
              inc=[(qsk, 16)])
            E("q", lambda e, k=k: e.dma_start(g7[k % 2][:, :], d_g[k]),
              wait=wv, inc=[(qgk, 16)])
            ldq = [(qxk, qxk.n), (qsk, qsk.n), (qgk, qgk.n)]
            for t4 in range(4):
                sl = slice(t4 * 512, (t4 + 1) * 512)
                E("v", lambda e, k=k, t4=t4, sl=sl: e.tensor_mul(
                    t7a[:, :], g7[k % 2][:, sl], PS[4 + t4][:, :]),
                    wait=(ldq + [(st, bcn)]) if t4 == 0 else [], inc=[(sv, 1)])
                E("v", lambda e, k=k, sl=sl: e.tensor_mul(
                    t7b[:, :], t7a[:, :], stil7[k % 2][:, sl]), inc=[(sv, 1)])
                E("v", lambda e, k=k, sl=sl: e.tensor_add(
                    ostg7[k % 2][:, sl], x7[k % 2][:, sl], t7b[:, :]), inc=[(sv, 1)])
            E("q", lambda e, k=k: e.dma_start(out[k * 128:(k + 1) * 128, :],
                                              ostg7[k % 2][:, :]),
              wait=[(sv, sv.n)], inc=[(qok, 16)])
            D[("p7q", k)] = (qok, qok.n)

    init_consts()
    p0_dma(0, 48)
    if last_phase >= 1:
        p1_half(0)
        if "xn" in taps:
            E("q", lambda e: e.dma_start(tap_out["xn"][0], xn_half[:, :, :].rearrange(
                "p a b -> p (a b)")), wait=[(sv, sv.n)], inc=[(Q("xn0"), 16)])
            D["xntap0"] = (Q("xn0"), Q("xn0").n)
        p1b_half(0)
        D[("p1bend", 0)] = sv.n
    if last_phase >= 2:
        p2_half(0)
    p0_dma(48, 128)
    if last_phase >= 1:
        p1_half(1)
        if "xn" in taps:
            E("q", lambda e: e.dma_start(tap_out["xn"][1], xn_half[:, :, :].rearrange(
                "p a b -> p (a b)")), wait=[(sv, sv.n), (Q("xn0"), Q("xn0").n)],
              inc=[(Q("xn1"), 16)])
            D["xntap1"] = (Q("xn1"), Q("xn1").n)
        p1b_half(1)
        D[("p1bend", 1)] = sv.n
    if last_phase >= 2:
        p2_half(1)
        D["p2end_t"] = st.n
    p0_dma(128, 256)
    D["p0done"] = sv.n
    D["p2spill"] = [(Q("pa0"), Q("pa0").n), (Q("pa1"), Q("pa1").n)]
    if last_phase >= 3:
        p0_finish()
        if last_phase >= 5:
            p5_init_half(0)
        p3_loads(0)
        p3_chunk_prelude(0)
        for c in range(NCH):
            if c + 1 < NCH:
                p3_loads(c + 1)
            p3_chunk(c)
            if last_phase >= 5:
                if c == 8:
                    p5_init_half(1)
                hf = c // 8
                p5_batch(hf, 8 * (c % 8), 8 * (c % 8) + 8)
            if c + 1 < NCH:
                p3_chunk_prelude(c + 1)
    if last_phase >= 4:
        p4_half(0)
        p4_w2quant()
        p4_half(1)
    if last_phase >= 6:
        for q4 in range(4):
            p6_quarter(q4)
    if last_phase >= 7:
        p7()

    def all_done():
        w = [(sv, sv.n), (ss, ss.n), (st, st.n), (sg, sg.n)]
        w += [(s, s.n) for s in _semd.values()]
        return w

    qtap = Q("tap")
    for name, src in (("bsb", B_sb[:, :]), ("csb", C_sb[:, :]),
                      ("dtsp", dt_sp[:, :]), ("carry", carry[:, :]),
                      ("sw", c_sw[:, :])):
        if name in taps:
            E("q", lambda e, d=tap_out[name], s=src: e.dma_start(d, s),
              wait=all_done(), inc=[(qtap, 16)])
    for name, src in (("xqi", d_xqi), ("gate", d_gate), ("xssm", d_xssm),
                      ("hready", d_hready), ("h1", d_h1), ("stil", d_stil),
                      ("g", d_g), ("rows", d_rows)):
        if name in taps:
            E("q", lambda e, d=tap_out[name], s=src: e.dma_start(d, s),
              wait=all_done(), inc=[(qtap, 16)])

    E("q", lambda e: e.nop(), wait=all_done() + [(st, st.n), (sv, sv.n)])

    counts = P.check()

    with nc.Block() as block:
        @block.tensor
        def _(eng):
            for op in P.ops["t"]:
                op(eng)

        @block.vector
        def _(eng):
            for op in P.ops["v"]:
                op(eng)

        @block.scalar
        def _(eng):
            for op in P.ops["s"]:
                op(eng)

        @block.gpsimd
        def _(eng):
            for op in P.ops["g"]:
                op(eng)

        @block.sync
        def _(eng):
            for op in P.ops["q"]:
                op(eng)

    stack.close()
    return nc, counts



def kernel(**inputs):
    from concourse.bass_utils import run_bass_kernel_spmd
    in_maps = host_prep(inputs)
    nc, _counts = build(taps=(), last_phase=7)
    res = run_bass_kernel_spmd(nc, in_maps, core_ids=list(range(len(in_maps))))
    out = np.stack([np.ascontiguousarray(r["out"].T) for r in res.results], axis=0)
    return out.astype(np.float32)


# version 07
# speedup vs baseline: 1.0043x; 1.0043x over previous
import contextlib
import numpy as np
import concourse.bass as bass
import concourse.mybir as mybir

F32 = mybir.dt.float32
BF16 = mybir.dt.bfloat16
F16 = mybir.dt.float16
AF = mybir.ActivationFunctionType
ALU = mybir.AluOpType
AX = mybir.AxisListType.X

S = 2048
DM = 2048
KD = 16
INTER = 4096
CONVD = 4352
PROJ = 8512
NO2 = 67
NH = 64
HD = 64
DST = 128
L = 128
NCH = 16
HFF = 8192
TH = 1024
TQ = 512
MAGIC = 12582912.0
CLIP1 = 1.4990234375
LNEPS = 1e-5
RMSEPS = 1e-6
QEPS = 1e-5



def _tile_lhsT(w, no, nk):
    out_r, in_r = w.shape
    assert in_r == nk * 128
    if out_r < no * 128:
        w = np.concatenate([w, np.zeros((no * 128 - out_r, in_r), w.dtype)], 0)
    t = np.empty((no, 128, nk * 128), np.float32)
    for o in range(no):
        blk = w[o * 128:(o + 1) * 128, :]
        t[o] = np.ascontiguousarray(
            blk.reshape(128, nk, 128).transpose(2, 1, 0)).reshape(128, nk * 128)
    return t


def host_prep(inputs):
    x = np.ascontiguousarray(np.asarray(inputs["L0_attn_hidden_states"], np.float32))
    sh = {}
    sh["win"] = _tile_lhsT(np.asarray(inputs["L0_attn_in_proj_w"], np.float32), NO2, KD)
    sh["wout"] = _tile_lhsT(np.asarray(inputs["L0_attn_out_proj_w"], np.float32), 16, 32)
    sh["w1"] = _tile_lhsT(np.asarray(inputs["L0_ffn_w1"], np.float32), 64, KD)
    sh["w2"] = _tile_lhsT(np.asarray(inputs["L0_ffn_w2"], np.float32), 16, 64)
    cw = np.asarray(inputs["L0_attn_conv_w"], np.float32).reshape(CONVD, 4)
    sh["convw"] = np.ascontiguousarray(
        cw.reshape(34, 128, 4).transpose(1, 0, 2).reshape(128, 136))
    cb = np.asarray(inputs["L0_attn_conv_b"], np.float32)
    sh["convb"] = np.ascontiguousarray(cb.reshape(34, 128).T)
    sh["dtb"] = np.ascontiguousarray(
        np.asarray(inputs["L0_attn_dt_bias"], np.float32).reshape(64, 1))
    sh["alog"] = np.ascontiguousarray(
        np.asarray(inputs["L0_attn_A_log"], np.float32).reshape(64, 1))
    dd = np.asarray(inputs["L0_attn_D"], np.float32)
    ch = np.arange(INTER)
    sh["dpp"] = np.ascontiguousarray(dd[(ch // HD)].reshape(32, 128).T)
    nw = np.asarray(inputs["L0_attn_norm_weight"], np.float32)
    sh["nwpp"] = np.ascontiguousarray(nw.reshape(32, 128).T)
    sh["idn"] = np.eye(128, dtype=np.float32)
    sh["ones"] = np.ones((128, 128), np.float32)
    sh["mask"] = np.ascontiguousarray(
        (np.arange(128)[None, :] >= np.arange(128)[:, None]).astype(np.float32))
    in_maps = []
    for c in range(x.shape[0]):
        d = dict(sh)
        d["xt"] = np.ascontiguousarray(x[c].T)
        in_maps.append(d)
    return in_maps



class Sem:
    def __init__(self, stack, nc, name):
        self.h = stack.enter_context(nc.semaphore(name))
        self.n = 0
        self.name = name


class Prog:
    ENGS = ("t", "v", "s", "g", "q")

    def __init__(self):
        self.ops = {k: [] for k in self.ENGS}
        self.trace = {k: [] for k in self.ENGS}
        self.eng_sems = {}

    def emit(self, ek, build, wait=(), inc=(), dma=None, chain=False):
        if dma is None:
            dma = any(k >= 16 for _, k in inc)
        esem = self.eng_sems.get(ek)
        if esem is not None:
            lagv = esem.n
            wait = list(wait) + [(esem, lagv)]
            if not dma and not any(sm is esem for sm, _ in inc):
                inc = list(inc) + [(esem, 1)]
        for sm, k in inc:
            sm.n += k
        inc_h = tuple((sm.h, int(k)) for sm, k in inc)
        wait_h = tuple((sm.h, int(v)) for sm, v in wait if v > 0)

        def op(e, build=build, wait_h=wait_h, inc_h=inc_h):
            for hh, v in wait_h:
                e.wait_ge(hh, v)
            r = build(e)
            for hh, k in inc_h:
                r = r.then_inc(hh, k)
        self.ops[ek].append(op)
        self.trace[ek].append((tuple((sm.name, int(v)) for sm, v in wait if v > 0),
                               tuple((sm.name, int(k)) for sm, k in inc)))

    def check(self):
        pos = {k: 0 for k in self.ENGS}
        cnt = {}
        for k in self.ENGS:
            for w, i in self.trace[k]:
                for n, _ in list(w) + list(i):
                    cnt.setdefault(n, 0)
        progressed = True
        while progressed:
            progressed = False
            for k in self.ENGS:
                while pos[k] < len(self.trace[k]):
                    w, i = self.trace[k][pos[k]]
                    if all(cnt[n] >= v for n, v in w):
                        for n, kk in i:
                            cnt[n] += kk
                        pos[k] += 1
                        progressed = True
                    else:
                        break
        stuck = {k: pos[k] for k in self.ENGS if pos[k] < len(self.trace[k])}
        if stuck:
            lines = []
            for k, p in stuck.items():
                lines.append(f"engine {k} stuck at op {p}/{len(self.trace[k])}"
                             f" waiting {self.trace[k][p][0]} counts="
                             f"{[(n, cnt[n]) for n, _ in self.trace[k][p][0]]}")
            raise RuntimeError("DEADLOCK:\n" + "\n".join(lines))
        return {k: len(self.trace[k]) for k in self.ENGS}



def build(taps=(), last_phase=7, p2_os=None, detect_races=False):
    nc = bass.Bass("TRN2", target_bir_lowering=False,
                   detect_race_conditions=detect_races)
    stack = contextlib.ExitStack()
    P = Prog()
    D = {}

    ap = lambda t: t.ap()
    xt = ap(nc.dram_tensor("xt", [DM, S], F32, kind="ExternalInput"))
    win = ap(nc.dram_tensor("win", [NO2, 128, DM], F32, kind="ExternalInput"))
    wout = ap(nc.dram_tensor("wout", [16, 128, INTER], F32, kind="ExternalInput"))
    w1 = ap(nc.dram_tensor("w1", [64, 128, DM], F32, kind="ExternalInput"))
    w2 = ap(nc.dram_tensor("w2", [16, 128, HFF], F32, kind="ExternalInput"))
    convw = ap(nc.dram_tensor("convw", [128, 136], F32, kind="ExternalInput"))
    convb = ap(nc.dram_tensor("convb", [128, 34], F32, kind="ExternalInput"))
    dtb = ap(nc.dram_tensor("dtb", [64, 1], F32, kind="ExternalInput"))
    alog = ap(nc.dram_tensor("alog", [64, 1], F32, kind="ExternalInput"))
    dpp = ap(nc.dram_tensor("dpp", [128, 32], F32, kind="ExternalInput"))
    nwpp = ap(nc.dram_tensor("nwpp", [128, 32], F32, kind="ExternalInput"))
    idn = ap(nc.dram_tensor("idn", [128, 128], F32, kind="ExternalInput"))
    onesd = ap(nc.dram_tensor("ones", [128, 128], F32, kind="ExternalInput"))
    maskd = ap(nc.dram_tensor("mask", [128, 128], F32, kind="ExternalInput"))
    out = ap(nc.dram_tensor("out", [DM, S], F32, kind="ExternalOutput"))

    d_xqi = ap(nc.dram_tensor("d_xqi", [KD, 128, S], BF16, kind="Internal"))
    d_gate = ap(nc.dram_tensor("d_gate", [NCH, 32, 128, L], BF16, kind="Internal"))
    d_xssm = ap(nc.dram_tensor("d_xssm", [NCH, 32, 128, L], BF16, kind="Internal"))
    d_hready = ap(nc.dram_tensor("d_hready", [32, 128, S], BF16, kind="Internal"))
    d_h1 = ap(nc.dram_tensor("d_h1", [64, 128, S], F16, kind="Internal"))
    d_wq2 = ap(nc.dram_tensor("d_wq2", [16, 128, HFF], F16, kind="Internal"))
    d_stil = ap(nc.dram_tensor("d_stil", [16, 128, S], BF16, kind="Internal"))
    d_g = ap(nc.dram_tensor("d_g", [16, 128, S], BF16, kind="Internal"))
    d_rows = ap(nc.dram_tensor("d_rows", [4, 2048], F32, kind="Internal"))

    tap_out = {}
    TAPSHAPES = {
        "xqi": ([KD, 128, S], BF16), "gate": ([NCH, 32, 128, L], BF16),
        "xssm": ([NCH, 32, 128, L], BF16), "bsb": ([128, S], BF16),
        "csb": ([128, S], BF16), "dtsp": ([64, S], F32),
        "hready": ([32, 128, S], BF16), "stil": ([16, 128, S], BF16),
        "h1": ([64, 128, S], F16), "g": ([16, 128, S], BF16),
        "rows": ([4, 2048], F32), "carry": ([128, NH * HD], F32),
        "sw": ([1, 4], F32), "xn": ([2, 128, KD * TH], F16),
        "hq0": ([2, 128, 32 * TQ], F16), "sch0": ([2, TQ], F32),
    }
    for t in taps:
        if t not in TAPSHAPES:
            continue
        shp, dt = TAPSHAPES[t]
        tap_out[t] = ap(nc.dram_tensor("tap_" + t, shp, dt, kind="ExternalOutput"))

    def arena(name, nbytes):
        return stack.enter_context(nc.sbuf_tensor(name, [128, nbytes // 2], BF16))

    AR1 = arena("AR1", 32768)
    AR2 = arena("AR2", 32768)
    AR3 = arena("AR3", 41472)
    AR4 = arena("AR4", 22528)
    AR5 = arena("AR5", 16384)
    AR7 = arena("AR7", 30720)
    AR8 = arena("AR8", 25088)

    def V(at, byte_off, p0, pcnt, shape_free, dtype=BF16):
        esz = 4 if dtype == F32 else 2
        n = 1
        for d in shape_free:
            n *= int(d)
        assert byte_off % 4 == 0
        a = at[p0:p0 + pcnt, byte_off // 2: byte_off // 2 + (esz * n) // 2]
        if dtype != BF16:
            a = a.bitcast(dtype)
        if len(shape_free) > 1:
            names = " ".join(f"d{i}" for i in range(len(shape_free)))
            a = a.rearrange(f"p ({names}) -> p {names}",
                            **{f"d{i}": int(shape_free[i]) for i in range(len(shape_free))})
        return a

    c_idn = stack.enter_context(nc.sbuf_tensor("c_idn", [128, 128], F32))
    c_idnb = stack.enter_context(nc.sbuf_tensor("c_idnb", [128, 128], BF16))
    c_idnh = stack.enter_context(nc.sbuf_tensor("c_idnh", [128, 128], F16))
    c_ones = stack.enter_context(nc.sbuf_tensor("c_ones", [128, 128], F32))
    c_mask = stack.enter_context(nc.sbuf_tensor("c_mask", [128, 128], F32))
    c_convw = stack.enter_context(nc.sbuf_tensor("c_convw", [128, 136], F32))
    c_convb = stack.enter_context(nc.sbuf_tensor("c_convb", [128, 34], F32))
    c_dpp = stack.enter_context(nc.sbuf_tensor("c_dpp", [128, 32], F32))
    c_nwpp = stack.enter_context(nc.sbuf_tensor("c_nwpp", [128, 32], F32))
    c_dtb = stack.enter_context(nc.sbuf_tensor("c_dtb", [64, 1], F32))
    c_alog = stack.enter_context(nc.sbuf_tensor("c_alog", [64, 1], F32))
    c_acol = stack.enter_context(nc.sbuf_tensor("c_acol", [64, 1], F32))
    c_tails = stack.enter_context(nc.sbuf_tensor("c_tails", [128, 102], F32))
    c_sw = stack.enter_context(nc.sbuf_tensor("c_sw", [1, 4], F32))
    c_swcol = stack.enter_context(nc.sbuf_tensor("c_swcol", [128, 2], F32))
    c_wacc = stack.enter_context(nc.sbuf_tensor("c_wacc", [128, 2], F32))
    c_wred = stack.enter_context(nc.sbuf_tensor("c_wred", [128, 1], F32))
    c_eps = stack.enter_context(nc.sbuf_tensor("c_eps", [1, 2], F32))

    xn_half = V(AR1, 0, 0, 128, (KD, TH), F16)
    hready_asm = [V(AR1, i * 8192, 0, 128, (32, L)) for i in range(2)]
    t1st = V(AR1, 16384, 0, 128, (2, L))
    gateg = [V(AR1, 17408 + i * 512, 0, 128, (2, L)) for i in range(2)]
    xssmg = [V(AR1, 19456 + i * 512, 0, 128, (2, L)) for i in range(2)]
    hr4_1 = V(AR1, 0, 0, 128, (16, TH))
    hq6_1 = V(AR1, 0, 0, 128, (32, TQ), F16)

    xqi_half = V(AR2, 0, 0, 128, (KD, TH))
    hr4_2 = V(AR2, 0, 0, 128, (16, TH))
    hq6_2 = V(AR2, 0, 0, 128, (32, TQ), F16)

    carry = V(AR3, 0, 0, 128, (NH * HD,), F32)
    carry_bf = V(AR3, 16384, 0, 128, (NH * HD,))
    xssmT = [V(AR3, 24576 + i * 8192, 0, 128, (INTER,)) for i in range(2)]
    p2wb = [V(AR3, i * 4096, 0, 128, (DM,), F16) for i in range(2)]
    w2q_raw = [V(AR3, i * 8192, 0, 128, (2048,), F32) for i in range(2)]
    w2q_stg = V(AR3, 16384, 0, 128, (4096,), F16)
    w2q_scr = V(AR3, 24576, 0, 128, (2048,), F32)
    wq2b = [V(AR3, i * 16384, 0, 128, (64, 128), F16) for i in range(2)]
    x7 = [V(AR3, i * 8192, 0, 128, (S,), F32) for i in range(2)]

    B_sb = V(AR4, 0, 0, 128, (S,))
    C_sb = V(AR4, 4096, 0, 128, (S,))
    dt_sp = V(AR4, 8192, 0, 64, (S,), F32)
    SC = 16384
    m1sc = V(AR4, SC, 0, 128, (L,), F32)
    edif = [V(AR4, SC + 512 + i * 256, 0, 128, (L,)) for i in range(2)]
    mtt = [V(AR4, SC + 1024 + i * 256, 0, 128, (L,)) for i in range(2)]
    cexp = [V(AR4, SC + 1536 + i * 256, 0, 128, (L,)) for i in range(2)]
    scoresM = V(AR4, SC + 2048, 0, 128, (L,))
    dsT_c = V(AR4, SC + 2304, 0, 128, (NH,), F32)
    cd_c = V(AR4, SC + 2560, 0, 128, (NH,), F32)
    ecs_c = V(AR4, SC + 2816, 0, 64, (L,), F32)
    cs_c = V(AR4, SC + 3328, 0, 64, (L,), F32)
    a_c = V(AR4, SC + 3840, 0, 64, (L,), F32)
    csT_c = V(AR4, SC + 4352, 0, 128, (NH,), F32)
    dtT_c = V(AR4, SC + 4608, 0, 128, (NH,), F32)
    B_t = V(AR4, SC + 4864, 0, 128, (DST,))
    xdtds_g = [V(AR4, SC + 5120 + i * 512, 0, 128, (4, HD)) for i in range(2)]
    spa = V(AR4, SC, 0, 64, (TH,), F32)
    dtrelu = V(AR3, 8192, 0, 64, (TH,), F32)

    xstg = [V(AR5, i * 4096, 0, 128, (TH,), F32) for i in range(2)]
    x2stg = [V(AR5, 8192 + i * 4096, 0, 128, (TH,), F32) for i in range(2)]
    cacc = [V(AR5, i * 4096, 0, 128, (TH,), F32) for i in range(2)]
    sstg = [V(AR5, 8192 + i * 2048, 0, 128, (TH,)) for i in range(2)]
    wob = [V(AR5, i * 8192, 0, 128, (INTER,)) for i in range(2)]
    stil7 = [V(AR5, i * 4096, 0, 128, (S,)) for i in range(2)]
    g7 = [V(AR5, 8192 + i * 4096, 0, 128, (S,)) for i in range(2)]

    w1bf = [[V(AR7, h2 * 4096, 0, 128, (TH,), F32),
             V(AR1, 20480 + h2 * 4096, 0, 128, (TH,), F32)] for h2 in range(2)]
    w1b = [V(AR7, 12288 + i * 4096, 0, 128, (DM,)) for i in range(2)]
    wqscr = V(AR7, 8192, 0, 128, (TH,), F32)
    wq1b = [V(AR7, 12288 + i * 4096, 0, 128, (DM,)) for i in range(2)]
    h1stg = [V(AR7, 20480 + i * 2048, 0, 128, (TH,), F16) for i in range(2)]
    xqstg = [V(AR7, 20480 + i * 2048, 0, 128, (TH,)) for i in range(2)]
    mxcols = V(AR7, 24576, 0, 128, (8,), F32)
    invsbc = V(AR7, 26624, 0, 128, (TH,), F32)
    vacc4 = V(AR7, 0, 0, 128, (TH,), F32)
    stil4 = [V(AR7, 4096 + i * 2048, 0, 128, (TH,)) for i in range(2)]
    sq4 = [V(AR7, 8192 + i * 2048, 0, 128, (TH,)) for i in range(2)]
    gst6 = [V(AR7, i * 1024, 0, 128, (TQ,)) for i in range(2)]
    schbc = V(AR7, 4096, 0, 128, (TQ,), F32)
    i2bc = V(AR7, 6144, 0, 128, (TQ,), F32)
    rmq = V(AR7, 8192, 0, 128, (TQ,), F32)
    rmqn = V(AR7, 14400, 0, 128, (TQ,), F32)
    t6scr = [V(AR7, 10240 + i * 2048, 0, 128, (TQ,), F32) for i in range(2)]
    mx6 = V(AR7, 14336, 0, 128, (4,), F32)
    rmq1 = V(AR7, 20480, 0, 128, (TH,), F32)
    rmq1n = V(AR7, 24640, 0, 128, (TH,), F32)
    ostg7 = [V(AR7, i * 8192, 0, 128, (S,), F32) for i in range(2)]
    t7a = V(AR7, 16384, 0, 128, (TQ,))
    t7b = V(AR7, 17408, 0, 128, (TQ,))

    rowM = V(AR8, 0, 0, 1, (TH,), F32)
    rowR = V(AR8, 4096, 0, 1, (TH,), F32)
    rowS = V(AR8, 8192, 0, 1, (TH,), F32)
    csf = [V(AR8, i * 4096, 0, 1, (TH,), F32) for i in range(2)]
    ecsf = [V(AR8, 8192 + i * 4096, 0, 1, (TH,), F32) for i in range(2)]
    cslf = V(AR8, 16384, 0, 1, (NH,), F32)
    rowB = V(AR8, 16640, 0, 1, (S,), F32)
    inv1h = [V(AR8, 16640 + i * 4096, 0, 1, (TH,), F32) for i in range(2)]
    r7 = V(AR8, 0, 0, 1, (S,), F32)
    r7b = V(AR8, 8192, 0, 1, (S,), F32)
    r7c = V(AR8, 16640, 0, 1, (S,), F32)
    row6 = [V(AR8, i * 2048, 0, 1, (TQ,), F32) for i in range(4)]

    PS = [stack.enter_context(nc.psum_tensor(f"ps{i}", [128, 512], F32)) for i in range(8)]

    def PSb(i, lo, n):
        return PS[i][:, :].bitcast(BF16)[:, lo:lo + n]

    def PSh(i, lo, n):
        return PS[i][:, :].bitcast(F16)[:, lo:lo + n]

    _semd = {}

    def Q(name):
        if name not in _semd:
            _semd[name] = Sem(stack, nc, "q_" + name)
        return _semd[name]

    st = Sem(stack, nc, "st")
    sv = Sem(stack, nc, "sv")
    ss = Sem(stack, nc, "ss")
    sg = Sem(stack, nc, "sg")
    P.eng_sems = {"t": st, "v": sv, "s": ss, "g": sg}

    E = P.emit

    def row_assemble_dst(row_i, t0, ncols):
        base = d_rows[row_i:row_i + 1, t0:t0 + 1]
        return bass.AP(base.tensor, base.offset, [[1, 128], [128, ncols]])

    def init_consts():
        qc = Q("c")
        for dst, src in ((c_idn[:, :], idn), (c_ones[:, :], onesd), (c_mask[:, :], maskd),
                         (c_convw[:, :], convw), (c_convb[:, :], convb),
                         (c_dpp[:, :], dpp), (c_nwpp[:, :], nwpp),
                         (c_dtb[:, :], dtb), (c_alog[:, :], alog)):
            E("q", lambda e, d=dst, s=src: e.dma_start(d, s), inc=[(qc, 16)])
        ldn = (qc, qc.n)
        D["consts"] = ldn
        for ek in ("t", "v", "s", "g"):
            E(ek, lambda e: e.nop(), wait=[ldn])
        E("v", lambda e: e.tensor_copy(c_idnb[:, :], c_idn[:, :]),
          wait=[ldn], inc=[(sv, 1)])
        E("v", lambda e: e.tensor_copy(c_idnh[:, :], c_idn[:, :]), inc=[(sv, 1)])
        E("v", lambda e: e.memset(c_tails[:, :], 0.0), inc=[(sv, 1)])
        E("v", lambda e: e.memset(c_wacc[:, :], 0.0), inc=[(sv, 1)])
        E("v", lambda e: e.memset(c_eps[0:1, 0:1], LNEPS), inc=[(sv, 1)])
        E("v", lambda e: e.memset(c_eps[0:1, 1:2], RMSEPS), inc=[(sv, 1)])
        E("s", lambda e: e.activation(c_acol[:, :], c_alog[:, :], AF.Exp, scale=1.0),
          wait=[ldn], inc=[(ss, 1)])
        E("v", lambda e: e.tensor_scalar_mul(c_acol[:, :], c_acol[:, :], -1.0),
          wait=[(ss, ss.n)], inc=[(sv, 1)])
        D["init_v"] = sv.n

    def p0_dma(lo, hi):
        for i in range(lo, hi):
            if i < 128:
                src = w1[i // 2][:, (i % 2) * 1024:(i % 2 + 1) * 1024]
            else:
                j = i - 128
                src = w2[j // 8][:, (j % 8) * 1024:(j % 8 + 1) * 1024]
            qq = Q(f"wa{i % 2}")
            wv = [(sv, D.get(("p0r", i - 2), 0))]
            E("g", lambda e, d=w1bf[i % 2][0][:, :], s=src: e.dma_start(d, s),
              wait=wv, inc=[(qq, 16)])
            col = 0 if i < 128 else 1
            first = (i == 0 or i == 128)
            E("v", lambda e, b=w1bf[i % 2][0][:, :]: e.tensor_reduce(
                c_wred[:, :], b, AX, ALU.add, apply_absolute_value=True),
                wait=[(qq, qq.n), (sv, D["init_v"])], inc=[(sv, 1)])
            if first:
                E("v", lambda e, c=col: e.tensor_copy(c_wacc[:, c:c + 1], c_wred[:, :]),
                  inc=[(sv, 1)])
            else:
                E("v", lambda e, c=col: e.tensor_add(
                    c_wacc[:, c:c + 1], c_wacc[:, c:c + 1], c_wred[:, :]), inc=[(sv, 1)])
            D[("p0r", i)] = sv.n

    def p1_half(hf):
        t0 = hf * TH
        entry = [(sv, sv.n), (ss, ss.n), (Q("pa0"), Q("pa0").n),
                 (Q("pa1"), Q("pa1").n)]
        if "xntap0" in D:
            entry.append(D["xntap0"])
        for k in range(KD):
            qq = Q(f"a{k % 2}")
            wv = [(st, D.get(("p1mm", hf, k - 2), 0)), (ss, D.get(("p1sq", hf, k - 2), 0))]
            if k < 2:
                wv += entry
            E("q", lambda e, kk=k, b=xstg[k % 2]: e.dma_start(
                b[:, :], xt[kk * 128:(kk + 1) * 128, t0:t0 + TH]), wait=wv, inc=[(qq, 16)])
            ldn = (qq, qq.n)
            E("s", lambda e, b=xstg[k % 2], o=x2stg[k % 2]: e.activation(
                o[:, :], b[:, :], AF.Square, scale=1.0),
                wait=[ldn, (st, D.get(("p1mm2", hf, k - 2), 0))] +
                (entry if k < 2 else []), inc=[(ss, 1)])
            D[("p1sq", hf, k)] = ss.n
            sqn = ss.n
            for t4 in range(2):
                E("t", lambda e, b=xstg[k % 2], t4=t4, kk=k: e.matmul(
                    PS[t4][0:1, :], c_ones[:, 0:1], b[:, t4 * 512:(t4 + 1) * 512],
                    start=(kk == 0), stop=(kk == KD - 1)),
                    wait=([ldn] + (entry if k == 0 else [])) if t4 == 0 else [],
                    inc=[(st, 1)] if t4 == 1 else [])
            D[("p1mm", hf, k)] = st.n
            for t4 in range(2):
                E("t", lambda e, b=x2stg[k % 2], t4=t4, kk=k: e.matmul(
                    PS[2 + t4][0:1, :], c_ones[:, 0:1], b[:, t4 * 512:(t4 + 1) * 512],
                    start=(kk == 0), stop=(kk == KD - 1)),
                    wait=([(ss, sqn)] + (entry if k == 0 else [])) if t4 == 0 else [],
                    inc=[(st, 1)] if t4 == 1 else [])
            D[("p1mm2", hf, k)] = st.n
        fin = st.n
        E("v", lambda e: e.tensor_scalar(rowM[0:1, 0:512], PS[0][0:1, :], 1.0 / DM, None,
                                         ALU.mult), wait=[(st, fin)], inc=[(sv, 1)])
        E("v", lambda e: e.tensor_scalar(rowM[0:1, 512:1024], PS[1][0:1, :], 1.0 / DM,
                                         None, ALU.mult), inc=[(sv, 1)])
        E("v", lambda e: e.tensor_scalar(rowR[0:1, 0:512], PS[2][0:1, :], 1.0 / DM, None,
                                         ALU.mult), inc=[(sv, 1)])
        E("v", lambda e: e.tensor_scalar(rowR[0:1, 512:1024], PS[3][0:1, :], 1.0 / DM,
                                         None, ALU.mult), inc=[(sv, 1)])
        E("v", lambda e: e.tensor_mul(rowS[0:1, :], rowM[0:1, :], rowM[0:1, :]),
          inc=[(sv, 1)])
        E("v", lambda e: e.tensor_sub(rowR[0:1, :], rowR[0:1, :], rowS[0:1, :]),
          inc=[(sv, 1)])
        E("v", lambda e: e.tensor_scalar_add(rowS[0:1, :], rowR[0:1, :],
                                             c_eps[0:1, 0:1]),
          wait=[(sv, sv.n), (sv, D["init_v"])], inc=[(sv, 1)])
        E("s", lambda e: e.activation(rowR[0:1, :], rowS[0:1, :], AF.Sqrt, scale=1.0),
          wait=[(sv, sv.n)], inc=[(ss, 1)])
        E("v", lambda e: e.reciprocal(rowR[0:1, :], rowR[0:1, :]),
          wait=[(ss, ss.n)], inc=[(sv, 1)])
        E("v", lambda e: e.tensor_mul(ecsf[1][0:1, :], rowR[0:1, :], rowR[0:1, :]),
          inc=[(sv, 1)])
        E("v", lambda e: e.tensor_mul(ecsf[1][0:1, :], ecsf[1][0:1, :], rowS[0:1, :]),
          inc=[(sv, 1)])
        E("v", lambda e: e.tensor_scalar(ecsf[1][0:1, :], ecsf[1][0:1, :], -0.5, 1.5,
                                         ALU.mult, ALU.add), inc=[(sv, 1)])
        E("v", lambda e: e.tensor_mul(rowR[0:1, :], rowR[0:1, :], ecsf[1][0:1, :]),
          inc=[(sv, 1)])
        stn = sv.n
        if "lnstats" in taps:
            E("q", lambda e, t0=t0: e.dma_start(d_rows[2:3, t0:t0 + TH], rowM[0:1, :]),
              wait=[(sv, stn)], inc=[(Q(f"ls{hf}"), 16)])
            E("q", lambda e, t0=t0: e.dma_start(d_rows[3:4, t0:t0 + TH], rowR[0:1, :]),
              wait=[(sv, stn)], inc=[(Q(f"ls{hf}b"), 16)])
        for t4 in range(2):
            E("t", lambda e, t4=t4: e.matmul(PS[4 + t4][:, :], c_ones[0:1, :],
                                             rowM[0:1, t4 * 512:(t4 + 1) * 512],
                                             start=True, stop=True),
              wait=[(sv, stn)], inc=[(st, 1)])
            E("t", lambda e, t4=t4: e.matmul(PS[6 + t4][:, :], c_ones[0:1, :],
                                             rowR[0:1, t4 * 512:(t4 + 1) * 512],
                                             start=True, stop=True), inc=[(st, 1)])
        bcn = st.n
        for k in range(KD):
            qq = Q(f"a{k % 2}")
            E("q", lambda e, kk=k, b=xstg[k % 2]: e.dma_start(
                b[:, :], xt[kk * 128:(kk + 1) * 128, t0:t0 + TH]),
                wait=[(sv, D.get(("p1xn", hf, k - 2), 0)), (st, fin)], inc=[(qq, 16)])
            ldn = (qq, qq.n)
            for t4 in range(2):
                sl = slice(t4 * 512, (t4 + 1) * 512)
                E("v", lambda e, b=xstg[k % 2], o=x2stg[k % 2], t4=t4, sl=sl: e.tensor_sub(
                    o[:, sl], b[:, sl], PS[4 + t4][:, :]),
                    wait=[ldn, (st, bcn)], inc=[(sv, 1)])
                E("v", lambda e, o=x2stg[k % 2], kk=k, t4=t4, sl=sl: e.tensor_mul(
                    xn_half[:, kk, sl], o[:, sl], PS[6 + t4][:, :]), inc=[(sv, 1)])
            D[("p1xn", hf, k)] = sv.n

    def p1b_half(hf):
        t0 = hf * TH
        spg = [(Q("sa0"), Q("sa0").n), (Q("sa1"), Q("sa1").n)]
        E("v", lambda e: e.tensor_copy(rmq1[:, :], xn_half[:, 0, :]),
          wait=spg, inc=[(sv, 1)])
        E("v", lambda e: e.tensor_copy(rmq1n[:, :], xn_half[:, 0, :]),
          wait=spg, inc=[(sv, 1)])
        for k in range(1, KD):
            E("v", lambda e, kk=k: e.tensor_max(rmq1[:, :], rmq1[:, :],
                                                xn_half[:, kk, :]), inc=[(sv, 1)])
            E("v", lambda e, kk=k: e.tensor_tensor(rmq1n[:, :], rmq1n[:, :],
                                                   xn_half[:, kk, :], ALU.min),
              inc=[(sv, 1)])
        E("v", lambda e: e.tensor_scalar_mul(rmq1n[:, :], rmq1n[:, :], -1.0),
          inc=[(sv, 1)])
        E("v", lambda e: e.tensor_max(rmq1[:, :], rmq1[:, :], rmq1n[:, :]),
          inc=[(sv, 1)])
        amx = sv.n
        for i in range(8):
            E("t", lambda e, i=i: e.transpose(PS[i % 2][:, 0:128],
                                              rmq1[:, i * 128:(i + 1) * 128], c_idn[:, :]),
              wait=[(sv, amx if i < 2 else D[("p1t", hf, i - 2)])], inc=[(st, 1)])
            E("v", lambda e, i=i: e.reduce_max(mxcols[:, i:i + 1], PS[i % 2][:, 0:128], AX),
              wait=[(st, st.n)], inc=[(sv, 1)])
            D[("p1t", hf, i)] = sv.n
        qq = Q(f"r1{hf}")
        E("q", lambda e, dst=row_assemble_dst(0, t0, 8): e.dma_start(
            dst, mxcols[:, :], allow_slow_non_contiguous=True),
            wait=[(sv, sv.n)], inc=[(qq, 16)])
        E("q", lambda e: e.dma_start(rowS[0:1, :], d_rows[0:1, t0:t0 + TH]),
          wait=[(qq, qq.n)], inc=[(qq, 16)])
        E("v", lambda e: e.tensor_scalar_max(rowS[0:1, :], rowS[0:1, :], QEPS),
          wait=[(qq, qq.n)], inc=[(sv, 1)])
        E("v", lambda e, o=t0: e.tensor_copy(rowB[0:1, o:o + TH], rowS[0:1, :]),
          inc=[(sv, 1)])
        E("v", lambda e: e.reciprocal(rowM[0:1, :], rowS[0:1, :]), inc=[(sv, 1)])
        E("v", lambda e: e.tensor_scalar_mul(rowM[0:1, :], rowM[0:1, :], 127.0),
          inc=[(sv, 1)])
        for t4 in range(2):
            E("t", lambda e, t4=t4: e.matmul(PS[6 + t4][:, :], c_ones[0:1, :],
                                             rowM[0:1, t4 * 512:(t4 + 1) * 512],
                                             start=True, stop=True),
              wait=[(sv, sv.n)], inc=[(st, 1)])
        bc = st.n
        for k in range(KD):
            qq = Q(f"sa{k % 2}")
            for t4 in range(2):
                sl = slice(t4 * 512, (t4 + 1) * 512)
                E("v", lambda e, kk=k, t4=t4, sl=sl: e.tensor_mul(
                    x2stg[kk % 2][:, sl], xn_half[:, kk, sl], PS[6 + t4][:, :]),
                    wait=[(st, bc)] + ([D[("p1q", hf, k - 2)]] if ("p1q", hf, k - 2) in D
                                       else ([D[("p1q", 0, 14 + (k % 2))]] if hf == 1 else [])),
                    inc=[(sv, 1)])
            E("v", lambda e, kk=k: e.tensor_scalar(
                xqstg[kk % 2][:, :], x2stg[kk % 2][:, :], MAGIC, MAGIC,
                ALU.add, ALU.subtract), inc=[(sv, 1)])
            E("q", lambda e, kk=k: e.dma_start(d_xqi[kk, :, t0:t0 + TH],
                                               xqstg[kk % 2][:, :]),
              wait=[(sv, sv.n)], inc=[(qq, 16)])
            D[("p1q", hf, k)] = (qq, qq.n)

    def p2_half(hf):
        t0 = hf * TH
        for o in (range(NO2) if p2_os is None else p2_os):
            grp = o % 2
            pg = [PS[2 * grp], PS[2 * grp + 1]]
            qq = Q(f"wb{o % 2}")
            E("g", lambda e, oo=o, b=p2wb[o % 2]: e.dma_start(b[:, :], win[oo]),
              wait=[(st, D.get(("p2mm", hf, o - 2), 0))], inc=[(qq, 16)])
            wv = [(qq, qq.n), (sv, D.get(("p2ev", hf, o - 2), 0)),
                  (ss, D.get(("p2es", hf, o - 2), 0)),
                  (sv, D.get(("p1bend", hf), 0))]
            for k in range(KD):
                for t2 in range(2):
                    E("t", lambda e, kk=k, t2=t2, b=p2wb[o % 2], pg=pg: e.matmul(
                        pg[t2][:, :], b[:, kk * 128:(kk + 1) * 128],
                        xn_half[:, kk, t2 * 512:(t2 + 1) * 512],
                        start=(kk == 0), stop=(kk == KD - 1)),
                        wait=wv if (k == 0 and t2 == 0) else [],
                        inc=[(st, 1)] if (k == KD - 1 and t2 == 1) else [])
            D[("p2mm", hf, o)] = st.n
            mmn = st.n
            if o < 32:
                qs2 = Q(f"pa{o % 2}")
                for t2 in range(2):
                    E("s", lambda e, t2=t2, b=sstg[o % 2], pg=pg: e.activation(
                        b[:, t2 * 512:(t2 + 1) * 512], pg[t2][:, :], AF.Silu, scale=1.0),
                        wait=[(st, mmn)] + ([D[("p2dma", hf, o - 2)]]
                                            if ("p2dma", hf, o - 2) in D else []),
                        inc=[(ss, 1)])
                D[("p2es", hf, o)] = ss.n
                D[("p2ev", hf, o)] = sv.n
                dst = d_gate[hf * 8:(hf + 1) * 8, o].transpose([1, 0, 2])
                E("q", lambda e, b=sstg[o % 2], dst=dst: e.dma_start(
                    dst, b[:, :].rearrange("p (c l) -> p c l", c=8)),
                    wait=[(ss, ss.n)], inc=[(qs2, 16)])
                D[("p2dma", hf, o)] = (qs2, qs2.n)
            elif o < 66:
                cc = o - 32
                acc = cacc[o % 2]
                for t2 in range(2):
                    E("v", lambda e, t2=t2, acc=acc, cc=cc, pg=pg: e.tensor_scalar_mul(
                        acc[:, t2 * 512:(t2 + 1) * 512], pg[t2][:, :],
                        c_convw[:, cc * 4 + 3:cc * 4 + 4]),
                        wait=[(st, mmn), (ss, D.get(("p2es", hf, o - 2), 0))],
                        inc=[(sv, 1)])
                for sft in (1, 2, 3):
                    kk = 3 - sft
                    wc = c_convw[:, cc * 4 + kk:cc * 4 + kk + 1]
                    if hf == 1:
                        E("v", lambda e, sft=sft, acc=acc, cc=cc, wc=wc:
                          e.scalar_tensor_tensor(
                              acc[:, 0:sft], c_tails[:, cc * 3 + 3 - sft:cc * 3 + 3], wc,
                              acc[:, 0:sft], ALU.mult, ALU.add), inc=[(sv, 1)])
                    for t2 in range(2):
                        o_lo = max(sft, t2 * 512)
                        o_hi = (t2 + 1) * 512
                        i_lo = o_lo - sft
                        while o_lo < o_hi:
                            sb = i_lo // 512
                            i_in = i_lo % 512
                            n = min(o_hi - o_lo, 512 - i_in)
                            E("v", lambda e, acc=acc, o_lo=o_lo, n=n, sb=sb, i_in=i_in,
                              wc=wc, pg=pg: e.scalar_tensor_tensor(
                                  acc[:, o_lo:o_lo + n], pg[sb][:, i_in:i_in + n], wc,
                                  acc[:, o_lo:o_lo + n], ALU.mult, ALU.add),
                                inc=[(sv, 1)])
                            o_lo += n
                            i_lo += n
                if hf == 0:
                    E("v", lambda e, cc=cc, pg=pg: e.tensor_copy(
                        c_tails[:, cc * 3:cc * 3 + 3], pg[1][:, 509:512]), inc=[(sv, 1)])
                D[("p2ev", hf, o)] = sv.n
                evn = sv.n
                if cc < 32:
                    qs2 = Q(f"pa{o % 2}")
                    for t2 in range(2):
                        E("s", lambda e, t2=t2, b=sstg[o % 2], acc=acc, cc=cc: e.activation(
                            b[:, t2 * 512:(t2 + 1) * 512], acc[:, t2 * 512:(t2 + 1) * 512],
                            AF.Silu, bias=c_convb[:, cc:cc + 1], scale=1.0),
                            wait=[(sv, evn)] + ([D[("p2dma", hf, o - 2)]]
                                                if ("p2dma", hf, o - 2) in D else []),
                            inc=[(ss, 1)])
                    D[("p2es", hf, o)] = ss.n
                    dst = d_xssm[hf * 8:(hf + 1) * 8, cc].transpose([1, 0, 2])
                    E("q", lambda e, b=sstg[o % 2], dst=dst: e.dma_start(
                        dst, b[:, :].rearrange("p (c l) -> p c l", c=8)),
                        wait=[(ss, ss.n)], inc=[(qs2, 16)])
                    D[("p2dma", hf, o)] = (qs2, qs2.n)
                else:
                    dstv = B_sb if cc == 32 else C_sb
                    for t2 in range(2):
                        E("s", lambda e, t2=t2, dv=dstv, acc=acc, cc=cc: e.activation(
                            dv[:, t0 + t2 * 512:t0 + (t2 + 1) * 512],
                            acc[:, t2 * 512:(t2 + 1) * 512],
                            AF.Silu, bias=c_convb[:, cc:cc + 1], scale=1.0),
                            wait=[(sv, evn)], inc=[(ss, 1)])
                    D[("p2es", hf, o)] = ss.n
            else:
                for t2 in range(2):
                    E("v", lambda e, t2=t2, pg=pg: e.tensor_scalar_add(
                        dt_sp[:, t0 + t2 * 512:t0 + (t2 + 1) * 512], pg[t2][0:64, :],
                        c_dtb[:, :]), wait=[(st, mmn)], inc=[(sv, 1)])
                dts = dt_sp[:, t0:t0 + TH]
                E("s", lambda e, d2=dts: e.activation(spa[:, :], d2, AF.Abs, scale=1.0),
                  wait=[(sv, sv.n)], inc=[(ss, 1)])
                E("s", lambda e: e.activation(spa[:, :], spa[:, :], AF.Exp, scale=-1.0),
                  inc=[(ss, 1)])
                E("s", lambda e: e.activation(spa[:, :], spa[:, :], AF.Ln,
                                              bias=c_ones[0:64, 0:1], scale=1.0),
                  inc=[(ss, 1)])
                E("v", lambda e, d2=dts: e.tensor_scalar_max(dtrelu[:, :], d2, 0.0),
                  wait=[(ss, ss.n)], inc=[(sv, 1)])
                E("v", lambda e, d2=dts: e.tensor_add(d2, spa[:, :], dtrelu[:, :]),
                  inc=[(sv, 1)])
                D[("p2ev", hf, o)] = sv.n
                D[("dt", hf)] = sv.n

    def p0_finish():
        E("v", lambda e: e.memset(carry[:, :], 0.0),
          wait=[(st, D[("p2mm", 1, NO2 - 1)])], inc=[(sv, 1)])
        E("v", lambda e: e.memset(carry_bf[:, :], 0.0), inc=[(sv, 1)])
        E("t", lambda e: e.matmul(PS[4][0:2, 0:2], c_ones[:, 0:2], c_wacc[:, :],
                                  start=True, stop=True),
          wait=[(sv, sv.n)], inc=[(st, 1)])
        E("v", lambda e: e.tensor_scalar(c_sw[0:1, 0:2], PS[4][0:1, 0:2],
                                         1.0 / (HFF * DM), QEPS, ALU.mult, ALU.max),
          wait=[(st, st.n)], inc=[(sv, 1)])
        E("v", lambda e: e.reciprocal(c_sw[0:1, 2:4], c_sw[0:1, 0:2]), inc=[(sv, 1)])
        E("t", lambda e: e.matmul(PS[5][:, 0:2], c_ones[0:1, :], c_sw[0:1, 2:4],
                                  start=True, stop=True), wait=[(sv, sv.n)], inc=[(st, 1)])
        E("v", lambda e: e.tensor_copy(c_swcol[:, :], PS[5][:, 0:2]),
          wait=[(st, st.n)], inc=[(sv, 1)])
        for hf in range(2):
            E("v", lambda e, hf=hf: e.tensor_scalar(
                inv1h[hf][0:1, :], rowB[0:1, hf * TH:(hf + 1) * TH],
                c_sw[0:1, 0:1], 1.0 / 127.0, ALU.mult, ALU.mult), inc=[(sv, 1)])
        D["p0fin"] = sv.n

    def p3_loads(c):
        par = c % 2
        qq = Q(f"ta{par}")
        src = d_xssm[c].rearrange("a p l -> (a p) l")
        E("s", lambda e, par=par, src=src: e.dma_start(xssmT[par][:, :], src,
                                                       transpose=True),
          wait=[(sv, D.get(("xtfree", c - 2), 0))] + D["p2spill"],
          inc=[(qq, 16)])
        D[("xtq", c)] = (qq, qq.n)

    def p3_group_loads(c, g4):
        qq1 = Q(f"ga{g4 % 2}")
        qq2 = Q(f"gb{g4 % 2}")
        wv = [(sv, D.get(("grpfree", c, g4 - 2), 0)) if g4 >= 2 else
              (sv, D.get(("grpfree", c - 1, g4 + 14), 0)),
              (st, D.get("p2end_t", 0))] + \
            ([D["xntap1"]] if "xntap1" in D else [])
        E("q", lambda e, c=c, g4=g4: e.dma_start(
            gateg[g4 % 2][:, :, :], d_gate[c, 2 * g4:2 * g4 + 2].transpose([1, 0, 2])),
            wait=wv + D["p2spill"], inc=[(qq1, 16)])
        E("q", lambda e, c=c, g4=g4: e.dma_start(
            xssmg[g4 % 2][:, :, :], d_xssm[c, 2 * g4:2 * g4 + 2].transpose([1, 0, 2])),
            wait=wv, inc=[(qq2, 16)])
        D[("grpq", c, g4)] = [(qq1, qq1.n), (qq2, qq2.n)]

    def p3_chunk_prelude(c):
        par = c % 2
        E("v", lambda e, c=c: e.tensor_scalar_mul(a_c[:, :], dt_sp[:, c * L:(c + 1) * L],
                                                  c_acol[:, :]),
          wait=[(sv, D.get(("dt", c // 8), 0)),
                (ss, D.get(("presfree", c - 1), 0))], inc=[(sv, 1)])
        E("v", lambda e: e.tensor_tensor_scan(cs_c[:, :], c_ones[0:64, :], a_c[:, :],
                                              0.0, ALU.mult, ALU.add), inc=[(sv, 1)])
        csn = sv.n
        E("s", lambda e: e.activation(ecs_c[:, :], cs_c[:, :], AF.Exp, scale=1.0),
          wait=[(sv, csn)], inc=[(ss, 1)])
        ecn = ss.n
        wv_ps = [(sv, D.get("w1evlast", 0))]
        E("t", lambda e: e.transpose(PS[6][:, 0:64], cs_c[:, :], c_idn[0:64, 0:64]),
          wait=[(sv, csn)] + wv_ps, inc=[(st, 1)])
        E("v", lambda e: e.tensor_copy(csT_c[:, :], PS[6][:, 0:64]),
          wait=[(st, st.n)], inc=[(sv, 1)])
        E("t", lambda e, c=c: e.transpose(PS[7][:, 0:64], dt_sp[:, c * L:(c + 1) * L],
                                          c_idn[0:64, 0:64]), inc=[(st, 1)])
        E("v", lambda e: e.tensor_copy(dtT_c[:, :], PS[7][:, 0:64]),
          wait=[(st, st.n)], inc=[(sv, 1)])
        E("t", lambda e, c=c: e.transpose(PSb(6, 256, 128), B_sb[:, c * L:(c + 1) * L],
                                          c_idnb[:, :]), wait=[(sv, sv.n)], inc=[(st, 1)])
        E("v", lambda e: e.tensor_copy(B_t[:, :], PSb(6, 256, 128)),
          wait=[(st, st.n)], inc=[(sv, 1)])
        E("t", lambda e, c=c: e.matmul(PS[7][:, 64:192], B_sb[:, c * L:(c + 1) * L],
                                       C_sb[:, c * L:(c + 1) * L], start=True, stop=True),
          inc=[(st, 1)])
        E("v", lambda e: e.tensor_mul(scoresM[:, :], PS[7][:, 64:192], c_mask[:, :]),
          wait=[(st, st.n)], inc=[(sv, 1)])
        D[("csready", c)] = (csn, ecn)
        for g8 in range(2):
            qf1, qf2 = Q(f"tb{g8 % 2}"), Q(f"tc{g8 % 2}")
            wv = [(st, D.get(("flfree", c - 1), 0))]
            E("s", lambda e, g8=g8: e.dma_start(csf[g8 % 2][0:1, :],
                                                cs_c[8 * g8:8 * g8 + 8, :]),
              wait=[(sv, csn)] + wv, inc=[(qf1, 16)])
            E("s", lambda e, g8=g8: e.dma_start(ecsf[g8 % 2][0:1, :],
                                                ecs_c[8 * g8:8 * g8 + 8, :]),
              wait=[(ss, ecn)] + wv, inc=[(qf2, 16)])
            D[("flq", c, g8)] = [(qf1, qf1.n), (qf2, qf2.n)]
        qtd = Q("td")
        E("s", lambda e: e.dma_start(cslf[0:1, :], cs_c[:, 127:128]),
          wait=[(sv, csn), (st, D.get(("flfree", c - 1), 0))], inc=[(qtd, 16)])
        E("t", lambda e: e.matmul(PS[7][:, 192:256], c_ones[0:1, :], cslf[0:1, :],
                                  start=True, stop=True),
          wait=[(qtd, qtd.n)], inc=[(st, 1)])
        E("v", lambda e: e.tensor_sub(dsT_c[:, :], PS[7][:, 192:256], csT_c[:, :]),
          wait=[(st, st.n)], inc=[(sv, 1)])
        E("s", lambda e: e.activation(dsT_c[:, :], dsT_c[:, :], AF.Exp, scale=1.0),
          wait=[(sv, sv.n)], inc=[(ss, 1)])
        E("s", lambda e: e.activation(cd_c[:, :], PS[7][:, 192:256], AF.Exp, scale=1.0),
          inc=[(ss, 1)])
        E("v", lambda e, par=par: e.tensor_tensor(
            xssmT[par][:, :].rearrange("p (h d2) -> p h d2", h=NH),
            xssmT[par][:, :].rearrange("p (h d2) -> p h d2", h=NH),
            dtT_c[:, :].unsqueeze(2).broadcast_to([128, NH, HD]), ALU.mult),
            wait=[D[("xtq", c)]], inc=[(sv, 1)])
        D[("pre", c)] = (sv.n, ss.n, st.n)
        D["presv"] = sv.n
        D[("presfree", c)] = ss.n

    def p3_chunk(c):
        par = c % 2
        for g4 in range(16):
            p3_group_loads(c, g4)
            for j in range(4):
                h = g4 * 4 + j
                g8, pr = h // 8, h // 2
                bpar = pr % 2
                if h % 2 == 0:
                    wv = list(D[("flq", c, g8)])
                    key = ("prread", c, pr - 2) if pr >= 2 else ("prread", c - 1, pr + 30)
                    if key in D:
                        wv += [(ss, D[key][0]), (sv, D[key][1])]
                    E("t", lambda e, bpar=bpar, g8=g8, h=h: e.matmul(
                        PS[bpar][:, 0:256], c_ones[0:1, :],
                        csf[g8 % 2][0:1, (h % 8) * 128:(h % 8) * 128 + 256],
                        start=True, stop=True), wait=wv, inc=[(st, 1)])
                    E("t", lambda e, bpar=bpar, g8=g8, h=h: e.matmul(
                        PS[bpar][:, 256:512], c_ones[0:1, :],
                        ecsf[g8 % 2][0:1, (h % 8) * 128:(h % 8) * 128 + 256],
                        start=True, stop=True), inc=[(st, 1)])
                    D[("prbc", c, pr)] = st.n
                E("s", lambda e, bpar=bpar, h=h: e.activation(
                    m1sc[:, :], PS[bpar][:, (h % 2) * 128:(h % 2) * 128 + 128], AF.Relu,
                    bias=csT_c[:, h:h + 1], scale=-1.0),
                    wait=[(st, D[("prbc", c, pr)])], inc=[(ss, 1)])
                E("s", lambda e, h=h: e.activation(edif[h % 2][:, :], m1sc[:, :],
                                                   AF.Exp, scale=-1.0), inc=[(ss, 1)])
                edn = ss.n
                E("v", lambda e, h=h: e.tensor_mul(mtt[h % 2][:, :], edif[h % 2][:, :],
                                                   scoresM[:, :]),
                  wait=[(ss, edn)], inc=[(sv, 1)])
                E("v", lambda e, h=h, bpar=bpar, c=c: e.tensor_mul(
                    cexp[h % 2][:, :], C_sb[:, c * L:(c + 1) * L],
                    PS[bpar][:, 256 + (h % 2) * 128:256 + (h % 2) * 128 + 128]),
                    inc=[(sv, 1)])
                mtn = sv.n
                D[("prread", c, pr)] = (ss.n, sv.n)
                yb = 2 + (g4 % 2)
                pb, cl = 64 * (h % 2), 128 * ((h % 4) // 2)
                wv = [(sv, mtn)]
                key = ("yfree", c, g4 - 2) if g4 >= 2 else ("yfree", c - 1, g4 + 14)
                if key in D:
                    wv.append((sv, D[key]))
                E("t", lambda e, yb=yb, pb=pb, cl=cl, h=h, par=par: e.matmul(
                    PS[yb][pb:pb + 64, cl:cl + 128],
                    xssmT[par][:, h * HD:(h + 1) * HD], mtt[h % 2][:, :],
                    start=True, stop=False), wait=wv, inc=[(st, 1)])
                E("t", lambda e, yb=yb, pb=pb, cl=cl, h=h: e.matmul(
                    PS[yb][pb:pb + 64, cl:cl + 128],
                    carry_bf[:, h * HD:(h + 1) * HD], cexp[h % 2][:, :],
                    start=False, stop=True), inc=[(st, 1)])
                if h % 8 == 6 and h // 8 + 2 <= 7:
                    gnx = h // 8 + 2
                    qf1, qf2 = Q(f"tb{gnx % 2}"), Q(f"tc{gnx % 2}")
                    E("s", lambda e, gnx=gnx: e.dma_start(
                        csf[gnx % 2][0:1, :], cs_c[8 * gnx:8 * gnx + 8, :]),
                        wait=[(st, st.n)], inc=[(qf1, 16)])
                    E("s", lambda e, gnx=gnx: e.dma_start(
                        ecsf[gnx % 2][0:1, :], ecs_c[8 * gnx:8 * gnx + 8, :]),
                        wait=[(st, st.n)], inc=[(qf2, 16)])
                    D[("flq", c, gnx)] = [(qf1, qf1.n), (qf2, qf2.n)]
            E("v", lambda e, par=par, g4=g4: e.tensor_tensor(
                xdtds_g[g4 % 2][:, :, :],
                xssmT[par][:, g4 * 256:(g4 + 1) * 256].rearrange("p (a d2) -> p a d2", a=4),
                dsT_c[:, g4 * 4:g4 * 4 + 4].unsqueeze(2).broadcast_to([128, 4, HD]),
                ALU.mult), wait=[(st, D.get(("grpst", c, g4 - 2),
                                            D.get(("grpst", c - 1, 15), 0)))],
                inc=[(sv, 1)])
            E("t", lambda e, g4=g4: e.matmul(
                PS[2 + (g4 % 2)][:, 256:512], B_t[:, :],
                xdtds_g[g4 % 2][:, :, :].rearrange("p a d2 -> p (a d2)"),
                start=True, stop=True), wait=[(sv, sv.n)], inc=[(st, 1)])
            grp_st = st.n
            D[("grpst", c, g4)] = grp_st
            for j in range(4):
                h = g4 * 4 + j
                yb = 2 + (g4 % 2)
                pb = 64 * (h % 2)
                cl = 128 * ((h % 4) // 2)
                kb = h // 2
                E("v", lambda e, yb=yb, pb=pb, cl=cl, h=h, j=j, g4=g4, kb=kb:
                  e.scalar_tensor_tensor(
                      t1st[pb:pb + 64, (h % 4) // 2, :],
                      xssmg[g4 % 2][pb:pb + 64, j // 2, :],
                      c_dpp[pb:pb + 64, kb:kb + 1],
                      PS[yb][pb:pb + 64, cl:cl + 128], ALU.mult, ALU.add),
                    wait=[(st, grp_st)] + D[("grpq", c, g4)] +
                    ([D[("hrq", c - 2)]] if (g4 == 0 and j == 0 and ("hrq", c - 2) in D)
                     else []),
                    inc=[(sv, 1)])
                E("v", lambda e, par=par, pb=pb, h=h, j=j, g4=g4, kb=kb: e.tensor_mul(
                    hready_asm[par][pb:pb + 64, kb, :],
                    t1st[pb:pb + 64, (h % 4) // 2, :],
                    gateg[g4 % 2][pb:pb + 64, j // 2, :]), inc=[(sv, 1)])
                E("v", lambda e, yb=yb, h=h, j=j: e.scalar_tensor_tensor(
                    carry[:, h * HD:(h + 1) * HD], carry[:, h * HD:(h + 1) * HD],
                    cd_c[:, h:h + 1], PS[yb][:, 256 + j * 64:256 + (j + 1) * 64],
                    ALU.mult, ALU.add), inc=[(sv, 1)])
                E("v", lambda e, h=h: e.tensor_copy(carry_bf[:, h * HD:(h + 1) * HD],
                                                    carry[:, h * HD:(h + 1) * HD]),
                  inc=[(sv, 1)])
            D[("yfree", c, g4)] = sv.n
            D[("grpfree", c, g4)] = sv.n
        D[("xtfree", c)] = sv.n
        D[("flfree", c)] = st.n
        qq = Q(f"ha{c % 2}")
        E("q", lambda e, par=par, c=c: e.dma_start(
            d_hready[:, :, c * L:(c + 1) * L].transpose([1, 0, 2]),
            hready_asm[par][:, :, :]),
            wait=[(sv, sv.n)], inc=[(qq, 16)])
        D[("hrq", c)] = (qq, qq.n)

    def p5_init_half(hf):
        qq = Q("xq")
        wv = [(st, D[("w1mm", 0, 63)])] if hf == 1 else []
        E("q", lambda e: e.dma_start(
            xqi_half[:, :, :], d_xqi[:, :, hf * TH:(hf + 1) * TH].transpose([1, 0, 2])),
            wait=wv + [D[("p1q", hf, KD - 1)], D[("p1q", hf, KD - 2)]], inc=[(qq, 16)])
        D[("xqiq", hf)] = (qq, qq.n)
        for t4 in range(2):
            E("t", lambda e, hf=hf, t4=t4: e.matmul(
                PS[4 + t4][:, :], c_ones[0:1, :],
                inv1h[hf][0:1, t4 * 512:(t4 + 1) * 512], start=True, stop=True),
                wait=[(sv, D["p0fin"]), (sv, D.get("w1evlast", 0))], inc=[(st, 1)])
        bn = st.n
        E("v", lambda e: e.tensor_copy(invsbc[:, 0:512], PS[4][:, :]),
          wait=[(st, bn)], inc=[(sv, 1)])
        E("v", lambda e: e.tensor_copy(invsbc[:, 512:1024], PS[5][:, :]), inc=[(sv, 1)])
        D[("invn", hf)] = sv.n

    def p5_batch(hf, olo, ohi):
        for o in range(olo, ohi):
            ob = o % 2
            wmm_guard = (st, D.get(("w1mm", hf, o - 2),
                                   D.get(("w1mm", 0, 62 + (o % 2)), 0) if hf == 1
                                   else 0))
            for h2 in range(2):
                qq = Q(f"wa{h2}{o % 2}")
                sl = slice(h2 * 1024, (h2 + 1) * 1024)
                bB = w1bf[h2][o % 2]
                E("g", lambda e, o=o, b=bB, sl=sl: e.dma_start(b[:, :], w1[o][:, sl]),
                  wait=[(sv, D.get(("w1qh", hf, o - 2, h2),
                                   D.get(("w1qh", 1 - hf, 62 + o % 2, h2), 0))),
                        (sv, D.get("p0done", 0))],
                  inc=[(qq, 16)])
                wn = (qq, qq.n)
                E("v", lambda e, b=bB: e.tensor_scalar(
                    wqscr[:, :], b[:, :], c_swcol[:, 0:1], CLIP1, ALU.mult, ALU.min),
                    wait=[wn, (sv, D["p0fin"]), wmm_guard], inc=[(sv, 1)])
                E("v", lambda e: e.tensor_scalar(wqscr[:, :], wqscr[:, :], -CLIP1, MAGIC,
                                                 ALU.max, ALU.add), inc=[(sv, 1)])
                E("v", lambda e, o=o, sl=sl: e.tensor_scalar(
                    wq1b[o % 2][:, sl], wqscr[:, :], MAGIC, None, ALU.subtract),
                    inc=[(sv, 1)])
                D[("w1qh", hf, o, h2)] = sv.n
            D[("w1q", hf, o)] = sv.n
            wv = [(sv, D[("w1q", hf, o)]), D[("xqiq", hf)], (sv, D["presv"]),
                  (sv, D.get(("w1ev", hf, o - 2), 0))]
            if o < 2:
                wv.append((sv, D[("invn", hf)]))
            pg = [PS[4 + 2 * (o % 2)], PS[5 + 2 * (o % 2)]]
            for k in range(KD):
                for t2 in range(2):
                    E("t", lambda e, o=o, k=k, t2=t2, pg=pg: e.matmul(
                        pg[t2][:, :], wq1b[o % 2][:, k * 128:(k + 1) * 128],
                        xqi_half[:, k, t2 * 512:(t2 + 1) * 512],
                        start=(k == 0), stop=(k == KD - 1)),
                        wait=wv if (k == 0 and t2 == 0) else [],
                        inc=[(st, 1)] if (k == KD - 1 and t2 == 1) else [])
            D[("w1mm", hf, o)] = st.n
            mmn = st.n
            qh = Q(f"h1{o % 2}")
            for t2 in range(2):
                E("v", lambda e, o=o, t2=t2, pg=pg: e.tensor_mul(
                    h1stg[o % 2][:, t2 * 512:(t2 + 1) * 512], pg[t2][:, :],
                    invsbc[:, t2 * 512:(t2 + 1) * 512]),
                    wait=[(st, mmn), (sv, D[("invn", hf)])] +
                    ([D[("h1q", hf, o - 2)]] if ("h1q", hf, o - 2) in D else
                     ([D[("h1q", 0, 62 + (o % 2))]] if hf == 1 and o < 2 else
                      [(Q("sa0"), Q("sa0").n), (Q("sa1"), Q("sa1").n)])),
                    inc=[(sv, 1)])
            D[("w1ev", hf, o)] = sv.n
            D["w1evlast"] = sv.n
            E("q", lambda e, o=o, hf=hf: e.dma_start(
                d_h1[o, :, hf * TH:(hf + 1) * TH], h1stg[o % 2][:, :]),
                wait=[(sv, sv.n)], inc=[(qh, 16)])
            D[("h1q", hf, o)] = (qh, qh.n)

    def p4_half(hf):
        t0 = hf * TH
        q1, q2 = Q("p40"), Q("p41")
        wv = [D[("hrq", NCH - 1)], D[("hrq", NCH - 2)], (st, D[("w1mm", 1, 63)])]
        if hf == 1:
            wv.append((st, D[("p4mm", 0, 15)]))
            wv.append((sv, D[("p4sq", 0, 31)]))
        E("q", lambda e: e.dma_start(
            hr4_1[:, :, :], d_hready[0:16, :, t0:t0 + TH].transpose([1, 0, 2])),
            wait=wv, inc=[(q1, 16)])
        E("q", lambda e: e.dma_start(
            hr4_2[:, :, :], d_hready[16:32, :, t0:t0 + TH].transpose([1, 0, 2])),
            wait=wv, inc=[(q2, 16)])
        ldq = [(q1, q1.n), (q2, q2.n)]
        E("v", lambda e: e.memset(vacc4[:, :], 0.0), inc=[(sv, 1)])
        for k in range(32):
            hr = hr4_1 if k < 16 else hr4_2
            kk = k % 16
            E("s", lambda e, hr=hr, kk=kk, k=k: e.activation(
                sq4[k % 2][:, :], hr[:, kk, :], AF.Square, scale=1.0),
                wait=ldq + [(sv, D.get(("p4sq", hf, k - 2), 0))], inc=[(ss, 1)])
            E("v", lambda e, k=k: e.tensor_add(vacc4[:, :], vacc4[:, :], sq4[k % 2][:, :]),
              wait=[(ss, ss.n)], inc=[(sv, 1)])
            E("v", lambda e, hr=hr, kk=kk, k=k: e.tensor_scalar_mul(
                hr[:, kk, :], hr[:, kk, :], c_nwpp[:, k:k + 1]), inc=[(sv, 1)])
            D[("p4sq", hf, k)] = sv.n
        nwn = sv.n
        for t2 in range(2):
            E("t", lambda e, t2=t2: e.matmul(PS[4 + t2][0:1, :], c_ones[:, 0:1],
                                             vacc4[:, t2 * 512:(t2 + 1) * 512],
                                             start=True, stop=True),
              wait=[(sv, nwn)], inc=[(st, 1)])
        E("v", lambda e: e.tensor_scalar_mul(rowM[0:1, 0:512], PS[4][0:1, :],
                                             1.0 / INTER),
          wait=[(st, st.n)] + ([D[("varq", hf - 1)]] if ("varq", hf - 1) in D else []),
          inc=[(sv, 1)])
        E("v", lambda e: e.tensor_scalar_mul(rowM[0:1, 512:1024], PS[5][0:1, :],
                                             1.0 / INTER), inc=[(sv, 1)])
        qr4 = Q(f"r4{hf}")
        E("q", lambda e, t0=t0: e.dma_start(d_rows[2:3, t0:t0 + TH], rowM[0:1, :]),
          wait=[(sv, sv.n)], inc=[(qr4, 16)])
        D[("varq", hf)] = (qr4, qr4.n)
        for o in range(16):
            pg = [PS[2 * (o % 2)], PS[2 * (o % 2) + 1]]
            qq = Q(f"wc{o % 2}")
            E("g", lambda e, o=o: e.dma_start(wob[o % 2][:, :], wout[o]),
              wait=[(st, D.get(("p4mm", hf, o - 2), 0))], inc=[(qq, 16)])
            wv2 = [(qq, qq.n), (sv, nwn), (sv, D.get(("p4ev", hf, o - 2), 0))]
            for k in range(32):
                hr = hr4_1 if k < 16 else hr4_2
                kk = k % 16
                for t2 in range(2):
                    E("t", lambda e, o=o, k=k, t2=t2, hr=hr, kk=kk, pg=pg: e.matmul(
                        pg[t2][:, :], wob[o % 2][:, k * 128:(k + 1) * 128],
                        hr[:, kk, t2 * 512:(t2 + 1) * 512],
                        start=(k == 0), stop=(k == 31)),
                        wait=wv2 if (k == 0 and t2 == 0) else [],
                        inc=[(st, 1)] if (k == 31 and t2 == 1) else [])
            D[("p4mm", hf, o)] = st.n
            qst = Q(f"st{o % 2}")
            for t2 in range(2):
                E("v", lambda e, o=o, t2=t2, pg=pg: e.tensor_copy(
                    stil4[o % 2][:, t2 * 512:(t2 + 1) * 512], pg[t2][:, :]),
                    wait=[(st, st.n)] + ([D[("stq", hf, o - 2)]]
                                         if ("stq", hf, o - 2) in D else
                                         ([D[("stq", 0, 14 + (o % 2))]]
                                          if hf == 1 and o < 2 else [])),
                    inc=[(sv, 1)])
            D[("p4ev", hf, o)] = sv.n
            E("q", lambda e, o=o, t0=t0: e.dma_start(
                d_stil[o, :, t0:t0 + TH], stil4[o % 2][:, :]),
                wait=[(sv, sv.n)], inc=[(qst, 16)])
            D[("stq", hf, o)] = (qst, qst.n)

    def p4_w2quant():
        for i in range(32):
            o, sub = i // 2, i % 2
            qwq = Q(f"wq{i % 2}")
            for h2 in range(2):
                j = i * 2 + h2
                qq = Q(f"wd{j % 2}")
                sl_src = slice(sub * 4096 + h2 * 2048, sub * 4096 + (h2 + 1) * 2048)
                sl_dst = slice(h2 * 2048, (h2 + 1) * 2048)
                E("g", lambda e, o=o, b=w2q_raw[j % 2], sl_src=sl_src: e.dma_start(
                    b[:, :], w2[o][:, sl_src]),
                    wait=[(sv, D.get(("w2qj", j - 2), 0))], inc=[(qq, 16)])
                wn = (qq, qq.n)
                E("v", lambda e, j=j, b=w2q_raw[j % 2]: e.tensor_scalar(
                    w2q_scr[:, :], b[:, :], c_swcol[:, 1:2], CLIP1, ALU.mult, ALU.min),
                    wait=[wn, (sv, D["p0fin"])] +
                    ([D[("wq2d", i - 1)]] if (h2 == 0 and ("wq2d", i - 1) in D) else []),
                    inc=[(sv, 1)])
                E("v", lambda e: e.tensor_scalar(w2q_scr[:, :], w2q_scr[:, :],
                                                 -CLIP1, MAGIC, ALU.max, ALU.add),
                  inc=[(sv, 1)])
                E("v", lambda e, sl_dst=sl_dst: e.tensor_scalar(
                    w2q_stg[:, sl_dst], w2q_scr[:, :], MAGIC, None, ALU.subtract),
                    inc=[(sv, 1)])
                D[("w2qj", j)] = sv.n
            D[("w2q", i)] = sv.n
            E("q", lambda e, o=o, sub=sub: e.dma_start(
                d_wq2[o][:, sub * 4096:(sub + 1) * 4096], w2q_stg[:, :]),
                wait=[(sv, sv.n)], inc=[(qwq, 16)])
            D[("wq2d", i)] = (qwq, qwq.n)

    def p6_quarter(q4):
        t0 = q4 * TQ
        q1, q2 = Q("p60"), Q("p61")
        wv = [D[("wq2d", 31)], D[("wq2d", 30)], D[("h1q", 1, 63)], D[("h1q", 1, 62)]]
        if q4 > 0:
            wv.append((st, D[("p6mm", q4 - 1, 15)]))
            wv.append((ss, D[("p6ev", q4 - 1, 15)]))
        else:
            wv.append((st, D[("p4mm", 1, 15)]))
            wv.append((sv, D[("p4ev", 1, 15)]))
        if "hq0tap" in D:
            wv += D["hq0tap"]
        if "sch0tap" in D:
            wv += D["sch0tap"]
        E("q", lambda e, q4=q4: e.dma_start(
            hq6_1[:, :, :], d_h1[0:32, :, t0:t0 + TQ].transpose([1, 0, 2])),
            wait=wv, inc=[(q1, 16)])
        E("q", lambda e, q4=q4: e.dma_start(
            hq6_2[:, :, :], d_h1[32:64, :, t0:t0 + TQ].transpose([1, 0, 2])),
            wait=wv, inc=[(q2, 16)])
        ldq = [(q1, q1.n), (q2, q2.n)]
        E("v", lambda e: e.memset(rmq[:, :], 0.0),
          wait=[D[("stq", 1, 15)], D[("stq", 1, 14)]] if q4 == 0 else [],
          inc=[(sv, 1)])
        E("v", lambda e: e.memset(rmqn[:, :], 0.0), inc=[(sv, 1)])
        for k in range(64):
            hq = hq6_1 if k < 32 else hq6_2
            kk = k % 32
            E("s", lambda e, hq=hq, kk=kk: e.activation(hq[:, kk, :], hq[:, kk, :],
                                                        AF.Gelu, scale=1.0),
              wait=ldq if k == 0 else [], inc=[(ss, 1)])
            E("v", lambda e, hq=hq, kk=kk: e.tensor_max(rmq[:, :], rmq[:, :],
                                                        hq[:, kk, :]),
              wait=[(ss, ss.n)], inc=[(sv, 1)])
            E("v", lambda e, hq=hq, kk=kk: e.tensor_tensor(rmqn[:, :], rmqn[:, :],
                                                           hq[:, kk, :], ALU.min),
              inc=[(sv, 1)])
        E("v", lambda e: e.tensor_scalar_mul(rmqn[:, :], rmqn[:, :], -1.0), inc=[(sv, 1)])
        E("v", lambda e: e.tensor_max(rmq[:, :], rmq[:, :], rmqn[:, :]), inc=[(sv, 1)])
        for i in range(4):
            E("t", lambda e, i=i: e.transpose(PS[6][:, 0:128],
                                              rmq[:, i * 128:(i + 1) * 128],
                                              c_idn[:, :]),
              wait=[(sv, sv.n)], inc=[(st, 1)])
            E("v", lambda e, i=i: e.reduce_max(mx6[:, i:i + 1], PS[6][:, 0:128], AX),
              wait=[(st, st.n)], inc=[(sv, 1)])
        qr6 = Q(f"r6{q4}")
        E("q", lambda e, dst=row_assemble_dst(1, t0, 4): e.dma_start(
            dst, mx6[:, :], allow_slow_non_contiguous=True),
            wait=[(sv, sv.n)], inc=[(qr6, 16)])
        E("q", lambda e, t0=t0: e.dma_start(row6[0][0:1, :], d_rows[1:2, t0:t0 + TQ]),
          wait=[(qr6, qr6.n)] + ([D[("varq", 1)], D[("varq", 0)]] if q4 == 0 else []),
          inc=[(qr6, 16)])
        E("v", lambda e: e.tensor_scalar_max(row6[0][0:1, :], row6[0][0:1, :], QEPS),
          wait=[(qr6, qr6.n)], inc=[(sv, 1)])
        E("v", lambda e: e.reciprocal(row6[1][0:1, :], row6[0][0:1, :]), inc=[(sv, 1)])
        E("v", lambda e: e.tensor_scalar_mul(row6[1][0:1, :], row6[1][0:1, :], 127.0),
          inc=[(sv, 1)])
        E("v", lambda e: e.tensor_scalar(row6[2][0:1, :], row6[0][0:1, :],
                                         c_sw[0:1, 1:2], 1.0 / 127.0, ALU.mult, ALU.mult),
          inc=[(sv, 1)])
        E("t", lambda e: e.matmul(PS[6][:, :], c_ones[0:1, :], row6[1][0:1, :],
                                  start=True, stop=True), wait=[(sv, sv.n)], inc=[(st, 1)])
        E("t", lambda e: e.matmul(PS[7][:, :], c_ones[0:1, :], row6[2][0:1, :],
                                  start=True, stop=True), inc=[(st, 1)])
        E("v", lambda e: e.tensor_copy(schbc[:, :], PS[6][:, :]),
          wait=[(st, st.n)], inc=[(sv, 1)])
        E("v", lambda e: e.tensor_copy(i2bc[:, :], PS[7][:, :]), inc=[(sv, 1)])
        bcn = sv.n
        for k in range(64):
            hq = hq6_1 if k < 32 else hq6_2
            kk = k % 32
            E("v", lambda e, hq=hq, kk=kk, k=k: e.tensor_mul(
                t6scr[k % 2][:, :], hq[:, kk, :], schbc[:, :]),
                wait=[(sv, bcn)] if k == 0 else [], inc=[(sv, 1)])
            E("v", lambda e, hq=hq, kk=kk, k=k: e.tensor_scalar(
                hq[:, kk, :], t6scr[k % 2][:, :], MAGIC, MAGIC, ALU.add, ALU.subtract),
                inc=[(sv, 1)])
        qn = sv.n
        if q4 == 0 and "hq0" in taps:
            E("q", lambda e: e.dma_start(tap_out["hq0"][0],
                                         hq6_1[:, :, :].rearrange("p a b -> p (a b)")),
              wait=[(sv, qn)], inc=[(Q("thq"), 16)])
            E("q", lambda e: e.dma_start(tap_out["hq0"][1],
                                         hq6_2[:, :, :].rearrange("p a b -> p (a b)")),
              wait=[(sv, qn)], inc=[(Q("thq2"), 16)])
            D["hq0tap"] = [(Q("thq"), Q("thq").n), (Q("thq2"), Q("thq2").n)]
        if q4 == 0 and "sch0" in taps:
            E("q", lambda e: e.dma_start(tap_out["sch0"][0:1, :], row6[1][0:1, :]),
              wait=[(sv, qn)], inc=[(Q("tsc"), 16)])
            E("q", lambda e: e.dma_start(tap_out["sch0"][1:2, :], row6[2][0:1, :]),
              wait=[(sv, qn)], inc=[(Q("tsc2"), 16)])
            D["sch0tap"] = [(Q("tsc"), Q("tsc").n), (Q("tsc2"), Q("tsc2").n)]
        for o in range(16):
            qq = Q(f"w2b{o % 2}")
            E("q", lambda e, o=o: e.dma_start(
                wq2b[o % 2][:, :, :], d_wq2[o].rearrange("p (a b) -> p a b", a=64)),
                wait=[(st, D.get(("p6mm", q4, o - 2), 0))] +
                ([D[("wq2d", 31)], D[("wq2d", 30)]] if (q4 == 0 and o < 2) else []),
                inc=[(qq, 16)])
            wv2 = [(qq, qq.n), (sv, qn), (ss, D.get(("p6ev", q4, o - 2), 0))]
            pb = PS[o % 2]
            for k in range(64):
                hq = hq6_1 if k < 32 else hq6_2
                kk = k % 32
                E("t", lambda e, o=o, k=k, hq=hq, kk=kk, pb=pb: e.matmul(
                    pb[:, :], wq2b[o % 2][:, k, :], hq[:, kk, :],
                    start=(k == 0), stop=(k == 63)),
                    wait=wv2 if k == 0 else [],
                    inc=[(st, 1)] if k == 63 else [])
            D[("p6mm", q4, o)] = st.n
            E("v", lambda e, o=o, pb=pb: e.tensor_mul(t6scr[o % 2][:, :], pb[:, :],
                                                      i2bc[:, :]),
              wait=[(st, st.n), (ss, D.get(("p6ev", q4, o - 2), 0))], inc=[(sv, 1)])
            qg = Q(f"g6{o % 2}")
            E("s", lambda e, o=o: e.activation(gst6[o % 2][:, :], t6scr[o % 2][:, :],
                                               AF.Silu, scale=1.0),
              wait=[(sv, sv.n)] + ([D[("gq", q4, o - 2)]] if ("gq", q4, o - 2) in D else
                                   ([D[("gq", q4 - 1, 14 + (o % 2))]] if q4 > 0 and o < 2
                                    else [])),
              inc=[(ss, 1)])
            D[("p6ev", q4, o)] = ss.n
            E("q", lambda e, o=o, t0=t0: e.dma_start(d_g[o, :, t0:t0 + TQ],
                                                     gst6[o % 2][:, :]),
              wait=[(ss, ss.n)], inc=[(qg, 16)])
            D[("gq", q4, o)] = (qg, qg.n)

    def p7():
        qr7 = Q("r7")
        E("q", lambda e: e.dma_start(r7[0:1, :], d_rows[2:3, :]),
          wait=[D[("varq", 1)], D[("varq", 0)], (ss, D[("p6ev", 3, 15)])],
          inc=[(qr7, 16)])
        E("v", lambda e: e.tensor_scalar_add(r7b[0:1, :], r7[0:1, :], c_eps[0:1, 1:2]),
          wait=[(qr7, qr7.n)], inc=[(sv, 1)])
        E("s", lambda e: e.activation(r7[0:1, :], r7b[0:1, :], AF.Sqrt, scale=1.0),
          wait=[(sv, sv.n)], inc=[(ss, 1)])
        E("v", lambda e: e.reciprocal(r7[0:1, :], r7[0:1, :]),
          wait=[(ss, ss.n)], inc=[(sv, 1)])
        E("v", lambda e: e.tensor_mul(r7c[0:1, :], r7[0:1, :], r7[0:1, :]), inc=[(sv, 1)])
        E("v", lambda e: e.tensor_mul(r7c[0:1, :], r7c[0:1, :], r7b[0:1, :]),
          inc=[(sv, 1)])
        E("v", lambda e: e.tensor_scalar(r7c[0:1, :], r7c[0:1, :], -0.5, 1.5,
                                         ALU.mult, ALU.add), inc=[(sv, 1)])
        E("v", lambda e: e.tensor_mul(r7[0:1, :], r7[0:1, :], r7c[0:1, :]),
          inc=[(sv, 1)])
        for t4 in range(4):
            E("t", lambda e, t4=t4: e.matmul(PS[4 + t4][:, :],
                                             c_ones[0:1, :],
                                             r7[0:1, t4 * 512:(t4 + 1) * 512],
                                             start=True, stop=True),
              wait=[(sv, sv.n)], inc=[(st, 1)])
        bcn = st.n
        for k in range(KD):
            qxk = Q(f"x7{k % 2}")
            qsk = Q(f"s7{k % 2}")
            qgk = Q(f"g7{k % 2}")
            qok = Q(f"o7{k % 2}")
            wv = [D[("gq", 3, 15)], D[("gq", 3, 14)]] if k < 2 else                  [D[("p7q", k - 2)]]
            E("q", lambda e, k=k: e.dma_start(x7[k % 2][:, :],
                                              xt[k * 128:(k + 1) * 128, :]),
              wait=wv, inc=[(qxk, 16)])
            E("q", lambda e, k=k: e.dma_start(stil7[k % 2][:, :], d_stil[k]),
              wait=wv + ([D[("stq", 1, k)], D[("stq", 0, k)]] if k < 2 else []),
              inc=[(qsk, 16)])
            E("q", lambda e, k=k: e.dma_start(g7[k % 2][:, :], d_g[k]),
              wait=wv, inc=[(qgk, 16)])
            ldq = [(qxk, qxk.n), (qsk, qsk.n), (qgk, qgk.n)]
            for t4 in range(4):
                sl = slice(t4 * 512, (t4 + 1) * 512)
                E("v", lambda e, k=k, t4=t4, sl=sl: e.tensor_mul(
                    t7a[:, :], g7[k % 2][:, sl], PS[4 + t4][:, :]),
                    wait=(ldq + [(st, bcn)]) if t4 == 0 else [], inc=[(sv, 1)])
                E("v", lambda e, k=k, sl=sl: e.tensor_mul(
                    t7b[:, :], t7a[:, :], stil7[k % 2][:, sl]), inc=[(sv, 1)])
                E("v", lambda e, k=k, sl=sl: e.tensor_add(
                    ostg7[k % 2][:, sl], x7[k % 2][:, sl], t7b[:, :]), inc=[(sv, 1)])
            E("q", lambda e, k=k: e.dma_start(out[k * 128:(k + 1) * 128, :],
                                              ostg7[k % 2][:, :]),
              wait=[(sv, sv.n)], inc=[(qok, 16)])
            D[("p7q", k)] = (qok, qok.n)

    init_consts()
    p0_dma(0, 48)
    if last_phase >= 1:
        p1_half(0)
        if "xn" in taps:
            E("q", lambda e: e.dma_start(tap_out["xn"][0], xn_half[:, :, :].rearrange(
                "p a b -> p (a b)")), wait=[(sv, sv.n)], inc=[(Q("xn0"), 16)])
            D["xntap0"] = (Q("xn0"), Q("xn0").n)
        p1b_half(0)
        D[("p1bend", 0)] = sv.n
    if last_phase >= 2:
        p2_half(0)
    p0_dma(48, 128)
    if last_phase >= 1:
        p1_half(1)
        if "xn" in taps:
            E("q", lambda e: e.dma_start(tap_out["xn"][1], xn_half[:, :, :].rearrange(
                "p a b -> p (a b)")), wait=[(sv, sv.n), (Q("xn0"), Q("xn0").n)],
              inc=[(Q("xn1"), 16)])
            D["xntap1"] = (Q("xn1"), Q("xn1").n)
        p1b_half(1)
        D[("p1bend", 1)] = sv.n
    if last_phase >= 2:
        p2_half(1)
        D["p2end_t"] = st.n
    p0_dma(128, 256)
    D["p0done"] = sv.n
    D["p2spill"] = [(Q("pa0"), Q("pa0").n), (Q("pa1"), Q("pa1").n)]
    if last_phase >= 3:
        p0_finish()
        if last_phase >= 5:
            p5_init_half(0)
        p3_loads(0)
        p3_chunk_prelude(0)
        for c in range(NCH):
            if c + 1 < NCH:
                p3_loads(c + 1)
            p3_chunk(c)
            if last_phase >= 5:
                if c == 8:
                    p5_init_half(1)
                hf = c // 8
                p5_batch(hf, 8 * (c % 8), 8 * (c % 8) + 8)
            if c + 1 < NCH:
                p3_chunk_prelude(c + 1)
    if last_phase >= 4:
        p4_half(0)
        p4_w2quant()
        p4_half(1)
    if last_phase >= 6:
        for q4 in range(4):
            p6_quarter(q4)
    if last_phase >= 7:
        p7()

    def all_done():
        w = [(sv, sv.n), (ss, ss.n), (st, st.n), (sg, sg.n)]
        w += [(s, s.n) for s in _semd.values()]
        return w

    qtap = Q("tap")
    for name, src in (("bsb", B_sb[:, :]), ("csb", C_sb[:, :]),
                      ("dtsp", dt_sp[:, :]), ("carry", carry[:, :]),
                      ("sw", c_sw[:, :])):
        if name in taps:
            E("q", lambda e, d=tap_out[name], s=src: e.dma_start(d, s),
              wait=all_done(), inc=[(qtap, 16)])
    for name, src in (("xqi", d_xqi), ("gate", d_gate), ("xssm", d_xssm),
                      ("hready", d_hready), ("h1", d_h1), ("stil", d_stil),
                      ("g", d_g), ("rows", d_rows)):
        if name in taps:
            E("q", lambda e, d=tap_out[name], s=src: e.dma_start(d, s),
              wait=all_done(), inc=[(qtap, 16)])

    E("q", lambda e: e.nop(), wait=all_done() + [(st, st.n), (sv, sv.n)])

    counts = P.check()

    with nc.Block() as block:
        @block.tensor
        def _(eng):
            for op in P.ops["t"]:
                op(eng)

        @block.vector
        def _(eng):
            for op in P.ops["v"]:
                op(eng)

        @block.scalar
        def _(eng):
            for op in P.ops["s"]:
                op(eng)

        @block.gpsimd
        def _(eng):
            for op in P.ops["g"]:
                op(eng)

        @block.sync
        def _(eng):
            for op in P.ops["q"]:
                op(eng)

    stack.close()
    return nc, counts



def kernel(**inputs):
    from concourse.bass_utils import run_bass_kernel_spmd
    in_maps = host_prep(inputs)
    nc, _counts = build(taps=(), last_phase=7)
    res = run_bass_kernel_spmd(nc, in_maps, core_ids=list(range(len(in_maps))))
    out = np.stack([np.ascontiguousarray(r["out"].T) for r in res.results], axis=0)
    return out.astype(np.float32)


# version 08
# speedup vs baseline: 1.5026x; 1.4962x over previous
import contextlib
import numpy as np
import concourse.bass as bass
import concourse.mybir as mybir

F32 = mybir.dt.float32
BF16 = mybir.dt.bfloat16
F16 = mybir.dt.float16
AF = mybir.ActivationFunctionType
ALU = mybir.AluOpType
AX = mybir.AxisListType.X

S = 2048
DM = 2048
KD = 16
INTER = 4096
CONVD = 4352
PROJ = 8512
NO2 = 67
NH = 64
HD = 64
DST = 128
L = 128
NCH = 16
HFF = 8192
TH = 1024
TQ = 512
MAGIC = 12582912.0
CLIP1 = 1.4990234375
LNEPS = 1e-5
RMSEPS = 1e-6
QEPS = 1e-5



def _tile_lhsT(w, no, nk):
    out_r, in_r = w.shape
    assert in_r == nk * 128
    if out_r < no * 128:
        w = np.concatenate([w, np.zeros((no * 128 - out_r, in_r), w.dtype)], 0)
    t = np.empty((no, 128, nk * 128), np.float32)
    for o in range(no):
        blk = w[o * 128:(o + 1) * 128, :]
        t[o] = np.ascontiguousarray(
            blk.reshape(128, nk, 128).transpose(2, 1, 0)).reshape(128, nk * 128)
    return t


def host_prep(inputs):
    x = np.ascontiguousarray(np.asarray(inputs["L0_attn_hidden_states"], np.float32))
    sh = {}
    sh["win"] = _tile_lhsT(np.asarray(inputs["L0_attn_in_proj_w"], np.float32), NO2, KD)
    sh["wout"] = _tile_lhsT(np.asarray(inputs["L0_attn_out_proj_w"], np.float32), 16, 32)
    sh["w1"] = _tile_lhsT(np.asarray(inputs["L0_ffn_w1"], np.float32), 64, KD)
    sh["w2"] = _tile_lhsT(np.asarray(inputs["L0_ffn_w2"], np.float32), 16, 64)
    cw = np.asarray(inputs["L0_attn_conv_w"], np.float32).reshape(CONVD, 4)
    sh["convw"] = np.ascontiguousarray(
        cw.reshape(34, 128, 4).transpose(1, 0, 2).reshape(128, 136))
    cb = np.asarray(inputs["L0_attn_conv_b"], np.float32)
    sh["convb"] = np.ascontiguousarray(cb.reshape(34, 128).T)
    sh["dtb"] = np.ascontiguousarray(
        np.asarray(inputs["L0_attn_dt_bias"], np.float32).reshape(64, 1))
    sh["alog"] = np.ascontiguousarray(
        np.asarray(inputs["L0_attn_A_log"], np.float32).reshape(64, 1))
    dd = np.asarray(inputs["L0_attn_D"], np.float32)
    ch = np.arange(INTER)
    sh["dpp"] = np.ascontiguousarray(dd[(ch // HD)].reshape(32, 128).T)
    nw = np.asarray(inputs["L0_attn_norm_weight"], np.float32)
    sh["nwpp"] = np.ascontiguousarray(nw.reshape(32, 128).T)
    sh["idn"] = np.eye(128, dtype=np.float32)
    sh["ones"] = np.ones((128, 128), np.float32)
    sh["mask"] = np.ascontiguousarray(
        (np.arange(128)[None, :] >= np.arange(128)[:, None]).astype(np.float32))
    in_maps = []
    for c in range(x.shape[0]):
        d = dict(sh)
        d["xt"] = np.ascontiguousarray(x[c].T)
        in_maps.append(d)
    return in_maps



class Sem:
    def __init__(self, stack, nc, name):
        self.h = stack.enter_context(nc.semaphore(name))
        self.n = 0
        self.name = name


class Prog:
    ENGS = ("t", "v", "s", "g", "q")

    def __init__(self):
        self.ops = {k: [] for k in self.ENGS}
        self.trace = {k: [] for k in self.ENGS}
        self.eng_sems = {}

    def emit(self, ek, build, wait=(), inc=(), dma=None, chain=False):
        if dma is None:
            dma = any(k >= 16 for _, k in inc)
        esem = self.eng_sems.get(ek)
        if esem is not None:
            lagv = esem.n if (chain or ek != "t") else max(0, esem.n - 8)
            wait = list(wait) + [(esem, lagv)]
            if not dma and not any(sm is esem for sm, _ in inc):
                inc = list(inc) + [(esem, 1)]
        for sm, k in inc:
            sm.n += k
        inc_h = tuple((sm.h, int(k)) for sm, k in inc)
        wait_h = tuple((sm.h, int(v)) for sm, v in wait if v > 0)

        def op(e, build=build, wait_h=wait_h, inc_h=inc_h):
            for hh, v in wait_h:
                e.wait_ge(hh, v)
            r = build(e)
            for hh, k in inc_h:
                r = r.then_inc(hh, k)
        self.ops[ek].append(op)
        self.trace[ek].append((tuple((sm.name, int(v)) for sm, v in wait if v > 0),
                               tuple((sm.name, int(k)) for sm, k in inc)))

    def check(self):
        pos = {k: 0 for k in self.ENGS}
        cnt = {}
        for k in self.ENGS:
            for w, i in self.trace[k]:
                for n, _ in list(w) + list(i):
                    cnt.setdefault(n, 0)
        progressed = True
        while progressed:
            progressed = False
            for k in self.ENGS:
                while pos[k] < len(self.trace[k]):
                    w, i = self.trace[k][pos[k]]
                    if all(cnt[n] >= v for n, v in w):
                        for n, kk in i:
                            cnt[n] += kk
                        pos[k] += 1
                        progressed = True
                    else:
                        break
        stuck = {k: pos[k] for k in self.ENGS if pos[k] < len(self.trace[k])}
        if stuck:
            lines = []
            for k, p in stuck.items():
                lines.append(f"engine {k} stuck at op {p}/{len(self.trace[k])}"
                             f" waiting {self.trace[k][p][0]} counts="
                             f"{[(n, cnt[n]) for n, _ in self.trace[k][p][0]]}")
            raise RuntimeError("DEADLOCK:\n" + "\n".join(lines))
        return {k: len(self.trace[k]) for k in self.ENGS}



def build(taps=(), last_phase=7, p2_os=None, detect_races=False):
    nc = bass.Bass("TRN2", target_bir_lowering=False,
                   detect_race_conditions=detect_races)
    stack = contextlib.ExitStack()
    P = Prog()
    D = {}

    ap = lambda t: t.ap()
    xt = ap(nc.dram_tensor("xt", [DM, S], F32, kind="ExternalInput"))
    win = ap(nc.dram_tensor("win", [NO2, 128, DM], F32, kind="ExternalInput"))
    wout = ap(nc.dram_tensor("wout", [16, 128, INTER], F32, kind="ExternalInput"))
    w1 = ap(nc.dram_tensor("w1", [64, 128, DM], F32, kind="ExternalInput"))
    w2 = ap(nc.dram_tensor("w2", [16, 128, HFF], F32, kind="ExternalInput"))
    convw = ap(nc.dram_tensor("convw", [128, 136], F32, kind="ExternalInput"))
    convb = ap(nc.dram_tensor("convb", [128, 34], F32, kind="ExternalInput"))
    dtb = ap(nc.dram_tensor("dtb", [64, 1], F32, kind="ExternalInput"))
    alog = ap(nc.dram_tensor("alog", [64, 1], F32, kind="ExternalInput"))
    dpp = ap(nc.dram_tensor("dpp", [128, 32], F32, kind="ExternalInput"))
    nwpp = ap(nc.dram_tensor("nwpp", [128, 32], F32, kind="ExternalInput"))
    idn = ap(nc.dram_tensor("idn", [128, 128], F32, kind="ExternalInput"))
    onesd = ap(nc.dram_tensor("ones", [128, 128], F32, kind="ExternalInput"))
    maskd = ap(nc.dram_tensor("mask", [128, 128], F32, kind="ExternalInput"))
    out = ap(nc.dram_tensor("out", [DM, S], F32, kind="ExternalOutput"))

    d_xqi = ap(nc.dram_tensor("d_xqi", [KD, 128, S], BF16, kind="Internal"))
    d_gate = ap(nc.dram_tensor("d_gate", [NCH, 32, 128, L], BF16, kind="Internal"))
    d_xssm = ap(nc.dram_tensor("d_xssm", [NCH, 32, 128, L], BF16, kind="Internal"))
    d_hready = ap(nc.dram_tensor("d_hready", [32, 128, S], BF16, kind="Internal"))
    d_h1 = ap(nc.dram_tensor("d_h1", [64, 128, S], F16, kind="Internal"))
    d_wq2 = ap(nc.dram_tensor("d_wq2", [16, 128, HFF], F16, kind="Internal"))
    d_stil = ap(nc.dram_tensor("d_stil", [16, 128, S], BF16, kind="Internal"))
    d_g = ap(nc.dram_tensor("d_g", [16, 128, S], BF16, kind="Internal"))
    d_rows = ap(nc.dram_tensor("d_rows", [4, 2048], F32, kind="Internal"))

    tap_out = {}
    TAPSHAPES = {
        "xqi": ([KD, 128, S], BF16), "gate": ([NCH, 32, 128, L], BF16),
        "xssm": ([NCH, 32, 128, L], BF16), "bsb": ([128, S], BF16),
        "csb": ([128, S], BF16), "dtsp": ([64, S], F32),
        "hready": ([32, 128, S], BF16), "stil": ([16, 128, S], BF16),
        "h1": ([64, 128, S], F16), "g": ([16, 128, S], BF16),
        "rows": ([4, 2048], F32), "carry": ([128, NH * HD], F32),
        "sw": ([1, 4], F32), "xn": ([2, 128, KD * TH], F16),
        "hq0": ([2, 128, 32 * TQ], F16), "sch0": ([2, TQ], F32),
    }
    for t in taps:
        if t not in TAPSHAPES:
            continue
        shp, dt = TAPSHAPES[t]
        tap_out[t] = ap(nc.dram_tensor("tap_" + t, shp, dt, kind="ExternalOutput"))

    def arena(name, nbytes):
        return stack.enter_context(nc.sbuf_tensor(name, [128, nbytes // 2], BF16))

    AR1 = arena("AR1", 32768)
    AR2 = arena("AR2", 32768)
    AR3 = arena("AR3", 41472)
    AR4 = arena("AR4", 22528)
    AR5 = arena("AR5", 16384)
    AR7 = arena("AR7", 30720)
    AR8 = arena("AR8", 25088)

    def V(at, byte_off, p0, pcnt, shape_free, dtype=BF16):
        esz = 4 if dtype == F32 else 2
        n = 1
        for d in shape_free:
            n *= int(d)
        assert byte_off % 4 == 0
        a = at[p0:p0 + pcnt, byte_off // 2: byte_off // 2 + (esz * n) // 2]
        if dtype != BF16:
            a = a.bitcast(dtype)
        if len(shape_free) > 1:
            names = " ".join(f"d{i}" for i in range(len(shape_free)))
            a = a.rearrange(f"p ({names}) -> p {names}",
                            **{f"d{i}": int(shape_free[i]) for i in range(len(shape_free))})
        return a

    c_idn = stack.enter_context(nc.sbuf_tensor("c_idn", [128, 128], F32))
    c_idnb = stack.enter_context(nc.sbuf_tensor("c_idnb", [128, 128], BF16))
    c_idnh = stack.enter_context(nc.sbuf_tensor("c_idnh", [128, 128], F16))
    c_ones = stack.enter_context(nc.sbuf_tensor("c_ones", [128, 128], F32))
    c_mask = stack.enter_context(nc.sbuf_tensor("c_mask", [128, 128], F32))
    c_convw = stack.enter_context(nc.sbuf_tensor("c_convw", [128, 136], F32))
    c_convb = stack.enter_context(nc.sbuf_tensor("c_convb", [128, 34], F32))
    c_dpp = stack.enter_context(nc.sbuf_tensor("c_dpp", [128, 32], F32))
    c_nwpp = stack.enter_context(nc.sbuf_tensor("c_nwpp", [128, 32], F32))
    c_dtb = stack.enter_context(nc.sbuf_tensor("c_dtb", [64, 1], F32))
    c_alog = stack.enter_context(nc.sbuf_tensor("c_alog", [64, 1], F32))
    c_acol = stack.enter_context(nc.sbuf_tensor("c_acol", [64, 1], F32))
    c_tails = stack.enter_context(nc.sbuf_tensor("c_tails", [128, 102], F32))
    c_sw = stack.enter_context(nc.sbuf_tensor("c_sw", [1, 4], F32))
    c_swcol = stack.enter_context(nc.sbuf_tensor("c_swcol", [128, 2], F32))
    c_wacc = stack.enter_context(nc.sbuf_tensor("c_wacc", [128, 2], F32))
    c_wred = stack.enter_context(nc.sbuf_tensor("c_wred", [128, 1], F32))
    c_eps = stack.enter_context(nc.sbuf_tensor("c_eps", [1, 2], F32))

    xn_half = V(AR1, 0, 0, 128, (KD, TH), F16)
    hready_asm = [V(AR1, i * 8192, 0, 128, (32, L)) for i in range(2)]
    t1st = V(AR1, 16384, 0, 128, (2, L))
    gateg = [V(AR1, 17408 + i * 512, 0, 128, (2, L)) for i in range(2)]
    xssmg = [V(AR1, 19456 + i * 512, 0, 128, (2, L)) for i in range(2)]
    hr4_1 = V(AR1, 0, 0, 128, (16, TH))
    hq6_1 = V(AR1, 0, 0, 128, (32, TQ), F16)

    xqi_half = V(AR2, 0, 0, 128, (KD, TH))
    hr4_2 = V(AR2, 0, 0, 128, (16, TH))
    hq6_2 = V(AR2, 0, 0, 128, (32, TQ), F16)

    carry = V(AR3, 0, 0, 128, (NH * HD,), F32)
    carry_bf = V(AR3, 16384, 0, 128, (NH * HD,))
    xssmT = [V(AR3, 24576 + i * 8192, 0, 128, (INTER,)) for i in range(2)]
    p2wb = [V(AR3, i * 4096, 0, 128, (DM,), F16) for i in range(2)]
    w2q_raw = [V(AR3, i * 8192, 0, 128, (2048,), F32) for i in range(2)]
    w2q_stg = V(AR3, 16384, 0, 128, (4096,), F16)
    w2q_scr = V(AR3, 24576, 0, 128, (2048,), F32)
    wq2b = [V(AR3, i * 16384, 0, 128, (64, 128), F16) for i in range(2)]
    x7 = [V(AR3, i * 8192, 0, 128, (S,), F32) for i in range(2)]

    B_sb = V(AR4, 0, 0, 128, (S,))
    C_sb = V(AR4, 4096, 0, 128, (S,))
    dt_sp = V(AR4, 8192, 0, 64, (S,), F32)
    SC = 16384
    m1sc = V(AR4, SC, 0, 128, (L,), F32)
    edif = [V(AR4, SC + 512 + i * 256, 0, 128, (L,)) for i in range(2)]
    mtt = [V(AR4, SC + 1024 + i * 256, 0, 128, (L,)) for i in range(2)]
    cexp = [V(AR4, SC + 1536 + i * 256, 0, 128, (L,)) for i in range(2)]
    scoresM = V(AR4, SC + 2048, 0, 128, (L,))
    dsT_c = V(AR4, SC + 2304, 0, 128, (NH,), F32)
    cd_c = V(AR4, SC + 2560, 0, 128, (NH,), F32)
    ecs_c = V(AR4, SC + 2816, 0, 64, (L,), F32)
    cs_c = V(AR4, SC + 3328, 0, 64, (L,), F32)
    a_c = V(AR4, SC + 3840, 0, 64, (L,), F32)
    csT_c = V(AR4, SC + 4352, 0, 128, (NH,), F32)
    dtT_c = V(AR4, SC + 4608, 0, 128, (NH,), F32)
    B_t = V(AR4, SC + 4864, 0, 128, (DST,))
    xdtds_g = [V(AR4, SC + 5120 + i * 512, 0, 128, (4, HD)) for i in range(2)]
    spa = V(AR4, SC, 0, 64, (TH,), F32)
    dtrelu = V(AR3, 8192, 0, 64, (TH,), F32)

    xstg = [V(AR5, i * 4096, 0, 128, (TH,), F32) for i in range(2)]
    x2stg = [V(AR5, 8192 + i * 4096, 0, 128, (TH,), F32) for i in range(2)]
    cacc = [V(AR5, i * 4096, 0, 128, (TH,), F32) for i in range(2)]
    sstg = [V(AR5, 8192 + i * 2048, 0, 128, (TH,)) for i in range(2)]
    wob = [V(AR5, i * 8192, 0, 128, (INTER,)) for i in range(2)]
    stil7 = [V(AR5, i * 4096, 0, 128, (S,)) for i in range(2)]
    g7 = [V(AR5, 8192 + i * 4096, 0, 128, (S,)) for i in range(2)]

    w1bf = [[V(AR7, h2 * 4096, 0, 128, (TH,), F32),
             V(AR1, 20480 + h2 * 4096, 0, 128, (TH,), F32)] for h2 in range(2)]
    w1b = [V(AR7, 12288 + i * 4096, 0, 128, (DM,)) for i in range(2)]
    wqscr = V(AR7, 8192, 0, 128, (TH,), F32)
    wq1b = [V(AR7, 12288 + i * 4096, 0, 128, (DM,)) for i in range(2)]
    h1stg = [V(AR7, 20480 + i * 2048, 0, 128, (TH,), F16) for i in range(2)]
    xqstg = [V(AR7, 20480 + i * 2048, 0, 128, (TH,)) for i in range(2)]
    mxcols = V(AR7, 24576, 0, 128, (8,), F32)
    invsbc = V(AR7, 26624, 0, 128, (TH,), F32)
    vacc4 = V(AR7, 0, 0, 128, (TH,), F32)
    stil4 = [V(AR7, 4096 + i * 2048, 0, 128, (TH,)) for i in range(2)]
    sq4 = [V(AR7, 8192 + i * 2048, 0, 128, (TH,)) for i in range(2)]
    gst6 = [V(AR7, i * 1024, 0, 128, (TQ,)) for i in range(2)]
    schbc = V(AR7, 4096, 0, 128, (TQ,), F32)
    i2bc = V(AR7, 6144, 0, 128, (TQ,), F32)
    rmq = V(AR7, 8192, 0, 128, (TQ,), F32)
    rmqn = V(AR7, 14400, 0, 128, (TQ,), F32)
    t6scr = [V(AR7, 10240 + i * 2048, 0, 128, (TQ,), F32) for i in range(2)]
    mx6 = V(AR7, 14336, 0, 128, (4,), F32)
    rmq1 = V(AR7, 20480, 0, 128, (TH,), F32)
    rmq1n = V(AR7, 24640, 0, 128, (TH,), F32)
    ostg7 = [V(AR7, i * 8192, 0, 128, (S,), F32) for i in range(2)]
    t7a = V(AR7, 16384, 0, 128, (TQ,))
    t7b = V(AR7, 17408, 0, 128, (TQ,))

    rowM = V(AR8, 0, 0, 1, (TH,), F32)
    rowR = V(AR8, 4096, 0, 1, (TH,), F32)
    rowS = V(AR8, 8192, 0, 1, (TH,), F32)
    csf = [V(AR8, i * 4096, 0, 1, (TH,), F32) for i in range(2)]
    ecsf = [V(AR8, 8192 + i * 4096, 0, 1, (TH,), F32) for i in range(2)]
    cslf = V(AR8, 16384, 0, 1, (NH,), F32)
    rowB = V(AR8, 16640, 0, 1, (S,), F32)
    inv1h = [V(AR8, 16640 + i * 4096, 0, 1, (TH,), F32) for i in range(2)]
    r7 = V(AR8, 0, 0, 1, (S,), F32)
    r7b = V(AR8, 8192, 0, 1, (S,), F32)
    r7c = V(AR8, 16640, 0, 1, (S,), F32)
    row6 = [V(AR8, i * 2048, 0, 1, (TQ,), F32) for i in range(4)]

    PS = [stack.enter_context(nc.psum_tensor(f"ps{i}", [128, 512], F32)) for i in range(8)]

    def PSb(i, lo, n):
        return PS[i][:, :].bitcast(BF16)[:, lo:lo + n]

    def PSh(i, lo, n):
        return PS[i][:, :].bitcast(F16)[:, lo:lo + n]

    _semd = {}

    def Q(name):
        if name not in _semd:
            _semd[name] = Sem(stack, nc, "q_" + name)
        return _semd[name]

    st = Sem(stack, nc, "st")
    sv = Sem(stack, nc, "sv")
    ss = Sem(stack, nc, "ss")
    sg = Sem(stack, nc, "sg")
    P.eng_sems = {"t": st, "v": sv, "s": ss, "g": sg}

    E = P.emit

    def row_assemble_dst(row_i, t0, ncols):
        base = d_rows[row_i:row_i + 1, t0:t0 + 1]
        return bass.AP(base.tensor, base.offset, [[1, 128], [128, ncols]])

    def init_consts():
        qc = Q("c")
        for dst, src in ((c_idn[:, :], idn), (c_ones[:, :], onesd), (c_mask[:, :], maskd),
                         (c_convw[:, :], convw), (c_convb[:, :], convb),
                         (c_dpp[:, :], dpp), (c_nwpp[:, :], nwpp),
                         (c_dtb[:, :], dtb), (c_alog[:, :], alog)):
            E("q", lambda e, d=dst, s=src: e.dma_start(d, s), inc=[(qc, 16)])
        ldn = (qc, qc.n)
        D["consts"] = ldn
        for ek in ("t", "v", "s", "g"):
            E(ek, lambda e: e.nop(), wait=[ldn])
        E("v", lambda e: e.tensor_copy(c_idnb[:, :], c_idn[:, :]),
          wait=[ldn], inc=[(sv, 1)])
        E("v", lambda e: e.tensor_copy(c_idnh[:, :], c_idn[:, :]), inc=[(sv, 1)])
        E("v", lambda e: e.memset(c_tails[:, :], 0.0), inc=[(sv, 1)])
        E("v", lambda e: e.memset(c_wacc[:, :], 0.0), inc=[(sv, 1)])
        E("v", lambda e: e.memset(c_eps[0:1, 0:1], LNEPS), inc=[(sv, 1)])
        E("v", lambda e: e.memset(c_eps[0:1, 1:2], RMSEPS), inc=[(sv, 1)])
        E("s", lambda e: e.activation(c_acol[:, :], c_alog[:, :], AF.Exp, scale=1.0),
          wait=[ldn], inc=[(ss, 1)])
        E("v", lambda e: e.tensor_scalar_mul(c_acol[:, :], c_acol[:, :], -1.0),
          wait=[(ss, ss.n)], inc=[(sv, 1)])
        D["init_v"] = sv.n

    def p0_dma(lo, hi):
        for i in range(lo, hi):
            if i < 128:
                src = w1[i // 2][:, (i % 2) * 1024:(i % 2 + 1) * 1024]
            else:
                j = i - 128
                src = w2[j // 8][:, (j % 8) * 1024:(j % 8 + 1) * 1024]
            qq = Q(f"wa{i % 2}")
            wv = [(sv, D.get(("p0r", i - 2), 0))]
            E("g", lambda e, d=w1bf[i % 2][0][:, :], s=src: e.dma_start(d, s),
              wait=wv, inc=[(qq, 16)])
            col = 0 if i < 128 else 1
            first = (i == 0 or i == 128)
            E("v", lambda e, b=w1bf[i % 2][0][:, :]: e.tensor_reduce(
                c_wred[:, :], b, AX, ALU.add, apply_absolute_value=True),
                wait=[(qq, qq.n), (sv, D["init_v"])], inc=[(sv, 1)])
            if first:
                E("v", lambda e, c=col: e.tensor_copy(c_wacc[:, c:c + 1], c_wred[:, :]),
                  inc=[(sv, 1)])
            else:
                E("v", lambda e, c=col: e.tensor_add(
                    c_wacc[:, c:c + 1], c_wacc[:, c:c + 1], c_wred[:, :]), inc=[(sv, 1)])
            D[("p0r", i)] = sv.n

    def p1_half(hf):
        t0 = hf * TH
        entry = [(sv, sv.n), (ss, ss.n), (Q("pa0"), Q("pa0").n),
                 (Q("pa1"), Q("pa1").n)]
        if "xntap0" in D:
            entry.append(D["xntap0"])
        for k in range(KD):
            qq = Q(f"a{k % 2}")
            wv = [(st, D.get(("p1mm", hf, k - 2), 0)), (ss, D.get(("p1sq", hf, k - 2), 0))]
            if k < 2:
                wv += entry
            E("q", lambda e, kk=k, b=xstg[k % 2]: e.dma_start(
                b[:, :], xt[kk * 128:(kk + 1) * 128, t0:t0 + TH]), wait=wv, inc=[(qq, 16)])
            ldn = (qq, qq.n)
            E("s", lambda e, b=xstg[k % 2], o=x2stg[k % 2]: e.activation(
                o[:, :], b[:, :], AF.Square, scale=1.0),
                wait=[ldn, (st, D.get(("p1mm2", hf, k - 2), 0))] +
                (entry if k < 2 else []), inc=[(ss, 1)])
            D[("p1sq", hf, k)] = ss.n
            sqn = ss.n
            for t4 in range(2):
                E("t", lambda e, b=xstg[k % 2], t4=t4, kk=k: e.matmul(
                    PS[t4][0:1, :], c_ones[:, 0:1], b[:, t4 * 512:(t4 + 1) * 512],
                    start=(kk == 0), stop=(kk == KD - 1)),
                    wait=([ldn] + (entry if k == 0 else [])) if t4 == 0 else [],
                    inc=[(st, 1)] if t4 == 1 else [])
            D[("p1mm", hf, k)] = st.n
            for t4 in range(2):
                E("t", lambda e, b=x2stg[k % 2], t4=t4, kk=k: e.matmul(
                    PS[2 + t4][0:1, :], c_ones[:, 0:1], b[:, t4 * 512:(t4 + 1) * 512],
                    start=(kk == 0), stop=(kk == KD - 1)),
                    wait=([(ss, sqn)] + (entry if k == 0 else [])) if t4 == 0 else [],
                    inc=[(st, 1)] if t4 == 1 else [])
            D[("p1mm2", hf, k)] = st.n
        fin = st.n
        E("v", lambda e: e.tensor_scalar(rowM[0:1, 0:512], PS[0][0:1, :], 1.0 / DM, None,
                                         ALU.mult), wait=[(st, fin)], inc=[(sv, 1)])
        E("v", lambda e: e.tensor_scalar(rowM[0:1, 512:1024], PS[1][0:1, :], 1.0 / DM,
                                         None, ALU.mult), inc=[(sv, 1)])
        E("v", lambda e: e.tensor_scalar(rowR[0:1, 0:512], PS[2][0:1, :], 1.0 / DM, None,
                                         ALU.mult), inc=[(sv, 1)])
        E("v", lambda e: e.tensor_scalar(rowR[0:1, 512:1024], PS[3][0:1, :], 1.0 / DM,
                                         None, ALU.mult), inc=[(sv, 1)])
        E("v", lambda e: e.tensor_mul(rowS[0:1, :], rowM[0:1, :], rowM[0:1, :]),
          inc=[(sv, 1)])
        E("v", lambda e: e.tensor_sub(rowR[0:1, :], rowR[0:1, :], rowS[0:1, :]),
          inc=[(sv, 1)])
        E("v", lambda e: e.tensor_scalar_add(rowS[0:1, :], rowR[0:1, :],
                                             c_eps[0:1, 0:1]),
          wait=[(sv, sv.n), (sv, D["init_v"])], inc=[(sv, 1)])
        E("s", lambda e: e.activation(rowR[0:1, :], rowS[0:1, :], AF.Sqrt, scale=1.0),
          wait=[(sv, sv.n)], inc=[(ss, 1)])
        E("v", lambda e: e.reciprocal(rowR[0:1, :], rowR[0:1, :]),
          wait=[(ss, ss.n)], inc=[(sv, 1)])
        E("v", lambda e: e.tensor_mul(ecsf[1][0:1, :], rowR[0:1, :], rowR[0:1, :]),
          inc=[(sv, 1)])
        E("v", lambda e: e.tensor_mul(ecsf[1][0:1, :], ecsf[1][0:1, :], rowS[0:1, :]),
          inc=[(sv, 1)])
        E("v", lambda e: e.tensor_scalar(ecsf[1][0:1, :], ecsf[1][0:1, :], -0.5, 1.5,
                                         ALU.mult, ALU.add), inc=[(sv, 1)])
        E("v", lambda e: e.tensor_mul(rowR[0:1, :], rowR[0:1, :], ecsf[1][0:1, :]),
          inc=[(sv, 1)])
        stn = sv.n
        if "lnstats" in taps:
            E("q", lambda e, t0=t0: e.dma_start(d_rows[2:3, t0:t0 + TH], rowM[0:1, :]),
              wait=[(sv, stn)], inc=[(Q(f"ls{hf}"), 16)])
            E("q", lambda e, t0=t0: e.dma_start(d_rows[3:4, t0:t0 + TH], rowR[0:1, :]),
              wait=[(sv, stn)], inc=[(Q(f"ls{hf}b"), 16)])
        for t4 in range(2):
            E("t", lambda e, t4=t4: e.matmul(PS[4 + t4][:, :], c_ones[0:1, :],
                                             rowM[0:1, t4 * 512:(t4 + 1) * 512],
                                             start=True, stop=True),
              wait=[(sv, stn)], inc=[(st, 1)])
            E("t", lambda e, t4=t4: e.matmul(PS[6 + t4][:, :], c_ones[0:1, :],
                                             rowR[0:1, t4 * 512:(t4 + 1) * 512],
                                             start=True, stop=True), inc=[(st, 1)])
        bcn = st.n
        for k in range(KD):
            qq = Q(f"a{k % 2}")
            E("q", lambda e, kk=k, b=xstg[k % 2]: e.dma_start(
                b[:, :], xt[kk * 128:(kk + 1) * 128, t0:t0 + TH]),
                wait=[(sv, D.get(("p1xn", hf, k - 2), 0)), (st, fin)], inc=[(qq, 16)])
            ldn = (qq, qq.n)
            for t4 in range(2):
                sl = slice(t4 * 512, (t4 + 1) * 512)
                E("v", lambda e, b=xstg[k % 2], o=x2stg[k % 2], t4=t4, sl=sl: e.tensor_sub(
                    o[:, sl], b[:, sl], PS[4 + t4][:, :]),
                    wait=[ldn, (st, bcn)], inc=[(sv, 1)])
                E("v", lambda e, o=x2stg[k % 2], kk=k, t4=t4, sl=sl: e.tensor_mul(
                    xn_half[:, kk, sl], o[:, sl], PS[6 + t4][:, :]), inc=[(sv, 1)])
            D[("p1xn", hf, k)] = sv.n

    def p1b_half(hf):
        t0 = hf * TH
        spg = [(Q("sa0"), Q("sa0").n), (Q("sa1"), Q("sa1").n)]
        E("v", lambda e: e.tensor_copy(rmq1[:, :], xn_half[:, 0, :]),
          wait=spg, inc=[(sv, 1)])
        E("v", lambda e: e.tensor_copy(rmq1n[:, :], xn_half[:, 0, :]),
          wait=spg, inc=[(sv, 1)])
        for k in range(1, KD):
            E("v", lambda e, kk=k: e.tensor_max(rmq1[:, :], rmq1[:, :],
                                                xn_half[:, kk, :]), inc=[(sv, 1)])
            E("v", lambda e, kk=k: e.tensor_tensor(rmq1n[:, :], rmq1n[:, :],
                                                   xn_half[:, kk, :], ALU.min),
              inc=[(sv, 1)])
        E("v", lambda e: e.tensor_scalar_mul(rmq1n[:, :], rmq1n[:, :], -1.0),
          inc=[(sv, 1)])
        E("v", lambda e: e.tensor_max(rmq1[:, :], rmq1[:, :], rmq1n[:, :]),
          inc=[(sv, 1)])
        amx = sv.n
        for i in range(8):
            E("t", lambda e, i=i: e.transpose(PS[i % 2][:, 0:128],
                                              rmq1[:, i * 128:(i + 1) * 128], c_idn[:, :]),
              wait=[(sv, amx if i < 2 else D[("p1t", hf, i - 2)])], inc=[(st, 1)])
            E("v", lambda e, i=i: e.reduce_max(mxcols[:, i:i + 1], PS[i % 2][:, 0:128], AX),
              wait=[(st, st.n)], inc=[(sv, 1)])
            D[("p1t", hf, i)] = sv.n
        qq = Q(f"r1{hf}")
        E("q", lambda e, dst=row_assemble_dst(0, t0, 8): e.dma_start(
            dst, mxcols[:, :], allow_slow_non_contiguous=True),
            wait=[(sv, sv.n)], inc=[(qq, 16)])
        E("q", lambda e: e.dma_start(rowS[0:1, :], d_rows[0:1, t0:t0 + TH]),
          wait=[(qq, qq.n)], inc=[(qq, 16)])
        E("v", lambda e: e.tensor_scalar_max(rowS[0:1, :], rowS[0:1, :], QEPS),
          wait=[(qq, qq.n)], inc=[(sv, 1)])
        E("v", lambda e, o=t0: e.tensor_copy(rowB[0:1, o:o + TH], rowS[0:1, :]),
          inc=[(sv, 1)])
        E("v", lambda e: e.reciprocal(rowM[0:1, :], rowS[0:1, :]), inc=[(sv, 1)])
        E("v", lambda e: e.tensor_scalar_mul(rowM[0:1, :], rowM[0:1, :], 127.0),
          inc=[(sv, 1)])
        for t4 in range(2):
            E("t", lambda e, t4=t4: e.matmul(PS[6 + t4][:, :], c_ones[0:1, :],
                                             rowM[0:1, t4 * 512:(t4 + 1) * 512],
                                             start=True, stop=True),
              wait=[(sv, sv.n)], inc=[(st, 1)])
        bc = st.n
        for k in range(KD):
            qq = Q(f"sa{k % 2}")
            for t4 in range(2):
                sl = slice(t4 * 512, (t4 + 1) * 512)
                E("v", lambda e, kk=k, t4=t4, sl=sl: e.tensor_mul(
                    x2stg[kk % 2][:, sl], xn_half[:, kk, sl], PS[6 + t4][:, :]),
                    wait=[(st, bc)] + ([D[("p1q", hf, k - 2)]] if ("p1q", hf, k - 2) in D
                                       else ([D[("p1q", 0, 14 + (k % 2))]] if hf == 1 else [])),
                    inc=[(sv, 1)])
            E("v", lambda e, kk=k: e.tensor_scalar(
                xqstg[kk % 2][:, :], x2stg[kk % 2][:, :], MAGIC, MAGIC,
                ALU.add, ALU.subtract), inc=[(sv, 1)])
            E("q", lambda e, kk=k: e.dma_start(d_xqi[kk, :, t0:t0 + TH],
                                               xqstg[kk % 2][:, :]),
              wait=[(sv, sv.n)], inc=[(qq, 16)])
            D[("p1q", hf, k)] = (qq, qq.n)

    def p2_half(hf):
        t0 = hf * TH
        for o in (range(NO2) if p2_os is None else p2_os):
            grp = o % 2
            pg = [PS[2 * grp], PS[2 * grp + 1]]
            qq = Q(f"wb{o % 2}")
            E("g", lambda e, oo=o, b=p2wb[o % 2]: e.dma_start(b[:, :], win[oo]),
              wait=[(st, D.get(("p2mm", hf, o - 2), 0))], inc=[(qq, 16)])
            wv = [(qq, qq.n), (sv, D.get(("p2ev", hf, o - 2), 0)),
                  (ss, D.get(("p2es", hf, o - 2), 0)),
                  (sv, D.get(("p1bend", hf), 0))]
            for k in range(KD):
                for t2 in range(2):
                    E("t", lambda e, kk=k, t2=t2, b=p2wb[o % 2], pg=pg: e.matmul(
                        pg[t2][:, :], b[:, kk * 128:(kk + 1) * 128],
                        xn_half[:, kk, t2 * 512:(t2 + 1) * 512],
                        start=(kk == 0), stop=(kk == KD - 1)),
                        wait=wv if (k == 0 and t2 == 0) else [],
                        inc=[(st, 1)] if (k == KD - 1 and t2 == 1) else [])
            D[("p2mm", hf, o)] = st.n
            mmn = st.n
            if o < 32:
                qs2 = Q(f"pa{o % 2}")
                for t2 in range(2):
                    E("s", lambda e, t2=t2, b=sstg[o % 2], pg=pg: e.activation(
                        b[:, t2 * 512:(t2 + 1) * 512], pg[t2][:, :], AF.Silu, scale=1.0),
                        wait=[(st, mmn)] + ([D[("p2dma", hf, o - 2)]]
                                            if ("p2dma", hf, o - 2) in D else []),
                        inc=[(ss, 1)])
                D[("p2es", hf, o)] = ss.n
                D[("p2ev", hf, o)] = sv.n
                dst = d_gate[hf * 8:(hf + 1) * 8, o].transpose([1, 0, 2])
                E("q", lambda e, b=sstg[o % 2], dst=dst: e.dma_start(
                    dst, b[:, :].rearrange("p (c l) -> p c l", c=8)),
                    wait=[(ss, ss.n)], inc=[(qs2, 16)])
                D[("p2dma", hf, o)] = (qs2, qs2.n)
            elif o < 66:
                cc = o - 32
                acc = cacc[o % 2]
                for t2 in range(2):
                    E("v", lambda e, t2=t2, acc=acc, cc=cc, pg=pg: e.tensor_scalar_mul(
                        acc[:, t2 * 512:(t2 + 1) * 512], pg[t2][:, :],
                        c_convw[:, cc * 4 + 3:cc * 4 + 4]),
                        wait=[(st, mmn), (ss, D.get(("p2es", hf, o - 2), 0))],
                        inc=[(sv, 1)])
                for sft in (1, 2, 3):
                    kk = 3 - sft
                    wc = c_convw[:, cc * 4 + kk:cc * 4 + kk + 1]
                    if hf == 1:
                        E("v", lambda e, sft=sft, acc=acc, cc=cc, wc=wc:
                          e.scalar_tensor_tensor(
                              acc[:, 0:sft], c_tails[:, cc * 3 + 3 - sft:cc * 3 + 3], wc,
                              acc[:, 0:sft], ALU.mult, ALU.add), inc=[(sv, 1)])
                    for t2 in range(2):
                        o_lo = max(sft, t2 * 512)
                        o_hi = (t2 + 1) * 512
                        i_lo = o_lo - sft
                        while o_lo < o_hi:
                            sb = i_lo // 512
                            i_in = i_lo % 512
                            n = min(o_hi - o_lo, 512 - i_in)
                            E("v", lambda e, acc=acc, o_lo=o_lo, n=n, sb=sb, i_in=i_in,
                              wc=wc, pg=pg: e.scalar_tensor_tensor(
                                  acc[:, o_lo:o_lo + n], pg[sb][:, i_in:i_in + n], wc,
                                  acc[:, o_lo:o_lo + n], ALU.mult, ALU.add),
                                inc=[(sv, 1)])
                            o_lo += n
                            i_lo += n
                if hf == 0:
                    E("v", lambda e, cc=cc, pg=pg: e.tensor_copy(
                        c_tails[:, cc * 3:cc * 3 + 3], pg[1][:, 509:512]), inc=[(sv, 1)])
                D[("p2ev", hf, o)] = sv.n
                evn = sv.n
                if cc < 32:
                    qs2 = Q(f"pa{o % 2}")
                    for t2 in range(2):
                        E("s", lambda e, t2=t2, b=sstg[o % 2], acc=acc, cc=cc: e.activation(
                            b[:, t2 * 512:(t2 + 1) * 512], acc[:, t2 * 512:(t2 + 1) * 512],
                            AF.Silu, bias=c_convb[:, cc:cc + 1], scale=1.0),
                            wait=[(sv, evn)] + ([D[("p2dma", hf, o - 2)]]
                                                if ("p2dma", hf, o - 2) in D else []),
                            inc=[(ss, 1)])
                    D[("p2es", hf, o)] = ss.n
                    dst = d_xssm[hf * 8:(hf + 1) * 8, cc].transpose([1, 0, 2])
                    E("q", lambda e, b=sstg[o % 2], dst=dst: e.dma_start(
                        dst, b[:, :].rearrange("p (c l) -> p c l", c=8)),
                        wait=[(ss, ss.n)], inc=[(qs2, 16)])
                    D[("p2dma", hf, o)] = (qs2, qs2.n)
                else:
                    dstv = B_sb if cc == 32 else C_sb
                    for t2 in range(2):
                        E("s", lambda e, t2=t2, dv=dstv, acc=acc, cc=cc: e.activation(
                            dv[:, t0 + t2 * 512:t0 + (t2 + 1) * 512],
                            acc[:, t2 * 512:(t2 + 1) * 512],
                            AF.Silu, bias=c_convb[:, cc:cc + 1], scale=1.0),
                            wait=[(sv, evn)], inc=[(ss, 1)])
                    D[("p2es", hf, o)] = ss.n
            else:
                for t2 in range(2):
                    E("v", lambda e, t2=t2, pg=pg: e.tensor_scalar_add(
                        dt_sp[:, t0 + t2 * 512:t0 + (t2 + 1) * 512], pg[t2][0:64, :],
                        c_dtb[:, :]), wait=[(st, mmn)], inc=[(sv, 1)])
                dts = dt_sp[:, t0:t0 + TH]
                E("s", lambda e, d2=dts: e.activation(spa[:, :], d2, AF.Abs, scale=1.0),
                  wait=[(sv, sv.n)], inc=[(ss, 1)])
                E("s", lambda e: e.activation(spa[:, :], spa[:, :], AF.Exp, scale=-1.0),
                  inc=[(ss, 1)])
                E("s", lambda e: e.activation(spa[:, :], spa[:, :], AF.Ln,
                                              bias=c_ones[0:64, 0:1], scale=1.0),
                  inc=[(ss, 1)])
                E("v", lambda e, d2=dts: e.tensor_scalar_max(dtrelu[:, :], d2, 0.0),
                  wait=[(ss, ss.n)], inc=[(sv, 1)])
                E("v", lambda e, d2=dts: e.tensor_add(d2, spa[:, :], dtrelu[:, :]),
                  inc=[(sv, 1)])
                D[("p2ev", hf, o)] = sv.n
                D[("dt", hf)] = sv.n

    def p0_finish():
        E("v", lambda e: e.memset(carry[:, :], 0.0),
          wait=[(st, D[("p2mm", 1, NO2 - 1)])], inc=[(sv, 1)])
        E("v", lambda e: e.memset(carry_bf[:, :], 0.0), inc=[(sv, 1)])
        E("t", lambda e: e.matmul(PS[4][0:2, 0:2], c_ones[:, 0:2], c_wacc[:, :],
                                  start=True, stop=True),
          wait=[(sv, sv.n)], inc=[(st, 1)])
        E("v", lambda e: e.tensor_scalar(c_sw[0:1, 0:2], PS[4][0:1, 0:2],
                                         1.0 / (HFF * DM), QEPS, ALU.mult, ALU.max),
          wait=[(st, st.n)], inc=[(sv, 1)])
        E("v", lambda e: e.reciprocal(c_sw[0:1, 2:4], c_sw[0:1, 0:2]), inc=[(sv, 1)])
        E("t", lambda e: e.matmul(PS[5][:, 0:2], c_ones[0:1, :], c_sw[0:1, 2:4],
                                  start=True, stop=True), wait=[(sv, sv.n)], inc=[(st, 1)])
        E("v", lambda e: e.tensor_copy(c_swcol[:, :], PS[5][:, 0:2]),
          wait=[(st, st.n)], inc=[(sv, 1)])
        for hf in range(2):
            E("v", lambda e, hf=hf: e.tensor_scalar(
                inv1h[hf][0:1, :], rowB[0:1, hf * TH:(hf + 1) * TH],
                c_sw[0:1, 0:1], 1.0 / 127.0, ALU.mult, ALU.mult), inc=[(sv, 1)])
        D["p0fin"] = sv.n

    def p3_loads(c):
        par = c % 2
        qq = Q(f"ta{par}")
        src = d_xssm[c].rearrange("a p l -> (a p) l")
        E("s", lambda e, par=par, src=src: e.dma_start(xssmT[par][:, :], src,
                                                       transpose=True),
          wait=[(sv, D.get(("xtfree", c - 2), 0))] + D["p2spill"],
          inc=[(qq, 16)])
        D[("xtq", c)] = (qq, qq.n)

    def p3_group_loads(c, g4):
        qq1 = Q(f"ga{g4 % 2}")
        qq2 = Q(f"gb{g4 % 2}")
        wv = [(sv, D.get(("grpfree", c, g4 - 2), 0)) if g4 >= 2 else
              (sv, D.get(("grpfree", c - 1, g4 + 14), 0)),
              (st, D.get("p2end_t", 0))] + \
            ([D["xntap1"]] if "xntap1" in D else [])
        E("q", lambda e, c=c, g4=g4: e.dma_start(
            gateg[g4 % 2][:, :, :], d_gate[c, 2 * g4:2 * g4 + 2].transpose([1, 0, 2])),
            wait=wv + D["p2spill"], inc=[(qq1, 16)])
        E("q", lambda e, c=c, g4=g4: e.dma_start(
            xssmg[g4 % 2][:, :, :], d_xssm[c, 2 * g4:2 * g4 + 2].transpose([1, 0, 2])),
            wait=wv, inc=[(qq2, 16)])
        D[("grpq", c, g4)] = [(qq1, qq1.n), (qq2, qq2.n)]

    def p3_chunk_prelude(c):
        par = c % 2
        E("v", lambda e, c=c: e.tensor_scalar_mul(a_c[:, :], dt_sp[:, c * L:(c + 1) * L],
                                                  c_acol[:, :]),
          wait=[(sv, D.get(("dt", c // 8), 0)),
                (ss, D.get(("presfree", c - 1), 0))], inc=[(sv, 1)])
        E("v", lambda e: e.tensor_tensor_scan(cs_c[:, :], c_ones[0:64, :], a_c[:, :],
                                              0.0, ALU.mult, ALU.add), inc=[(sv, 1)])
        csn = sv.n
        E("s", lambda e: e.activation(ecs_c[:, :], cs_c[:, :], AF.Exp, scale=1.0),
          wait=[(sv, csn)], inc=[(ss, 1)])
        ecn = ss.n
        wv_ps = [(sv, D.get("w1evlast", 0))]
        E("t", lambda e: e.transpose(PS[6][:, 0:64], cs_c[:, :], c_idn[0:64, 0:64]),
          wait=[(sv, csn)] + wv_ps, inc=[(st, 1)])
        E("v", lambda e: e.tensor_copy(csT_c[:, :], PS[6][:, 0:64]),
          wait=[(st, st.n)], inc=[(sv, 1)])
        E("t", lambda e, c=c: e.transpose(PS[7][:, 0:64], dt_sp[:, c * L:(c + 1) * L],
                                          c_idn[0:64, 0:64]), inc=[(st, 1)])
        E("v", lambda e: e.tensor_copy(dtT_c[:, :], PS[7][:, 0:64]),
          wait=[(st, st.n)], inc=[(sv, 1)])
        E("t", lambda e, c=c: e.transpose(PSb(6, 256, 128), B_sb[:, c * L:(c + 1) * L],
                                          c_idnb[:, :]), wait=[(sv, sv.n)], inc=[(st, 1)])
        E("v", lambda e: e.tensor_copy(B_t[:, :], PSb(6, 256, 128)),
          wait=[(st, st.n)], inc=[(sv, 1)])
        E("t", lambda e, c=c: e.matmul(PS[7][:, 64:192], B_sb[:, c * L:(c + 1) * L],
                                       C_sb[:, c * L:(c + 1) * L], start=True, stop=True),
          inc=[(st, 1)])
        E("v", lambda e: e.tensor_mul(scoresM[:, :], PS[7][:, 64:192], c_mask[:, :]),
          wait=[(st, st.n)], inc=[(sv, 1)])
        D[("csready", c)] = (csn, ecn)
        for g8 in range(2):
            qf1, qf2 = Q(f"tb{g8 % 2}"), Q(f"tc{g8 % 2}")
            wv = [(st, D.get(("flfree", c - 1), 0))]
            E("s", lambda e, g8=g8: e.dma_start(csf[g8 % 2][0:1, :],
                                                cs_c[8 * g8:8 * g8 + 8, :]),
              wait=[(sv, csn)] + wv, inc=[(qf1, 16)])
            E("s", lambda e, g8=g8: e.dma_start(ecsf[g8 % 2][0:1, :],
                                                ecs_c[8 * g8:8 * g8 + 8, :]),
              wait=[(ss, ecn)] + wv, inc=[(qf2, 16)])
            D[("flq", c, g8)] = [(qf1, qf1.n), (qf2, qf2.n)]
        qtd = Q("td")
        E("s", lambda e: e.dma_start(cslf[0:1, :], cs_c[:, 127:128]),
          wait=[(sv, csn), (st, D.get(("flfree", c - 1), 0))], inc=[(qtd, 16)])
        E("t", lambda e: e.matmul(PS[7][:, 192:256], c_ones[0:1, :], cslf[0:1, :],
                                  start=True, stop=True),
          wait=[(qtd, qtd.n)], inc=[(st, 1)])
        E("v", lambda e: e.tensor_sub(dsT_c[:, :], PS[7][:, 192:256], csT_c[:, :]),
          wait=[(st, st.n)], inc=[(sv, 1)])
        E("s", lambda e: e.activation(dsT_c[:, :], dsT_c[:, :], AF.Exp, scale=1.0),
          wait=[(sv, sv.n)], inc=[(ss, 1)])
        E("s", lambda e: e.activation(cd_c[:, :], PS[7][:, 192:256], AF.Exp, scale=1.0),
          inc=[(ss, 1)])
        E("v", lambda e, par=par: e.tensor_tensor(
            xssmT[par][:, :].rearrange("p (h d2) -> p h d2", h=NH),
            xssmT[par][:, :].rearrange("p (h d2) -> p h d2", h=NH),
            dtT_c[:, :].unsqueeze(2).broadcast_to([128, NH, HD]), ALU.mult),
            wait=[D[("xtq", c)]], inc=[(sv, 1)])
        D[("pre", c)] = (sv.n, ss.n, st.n)
        D["presv"] = sv.n
        D[("presfree", c)] = ss.n

    def p3_chunk(c):
        par = c % 2
        for g4 in range(16):
            p3_group_loads(c, g4)
            for j in range(4):
                h = g4 * 4 + j
                g8, pr = h // 8, h // 2
                bpar = pr % 2
                if h % 2 == 0:
                    wv = list(D[("flq", c, g8)])
                    key = ("prread", c, pr - 2) if pr >= 2 else ("prread", c - 1, pr + 30)
                    if key in D:
                        wv += [(ss, D[key][0]), (sv, D[key][1])]
                    E("t", lambda e, bpar=bpar, g8=g8, h=h: e.matmul(
                        PS[bpar][:, 0:256], c_ones[0:1, :],
                        csf[g8 % 2][0:1, (h % 8) * 128:(h % 8) * 128 + 256],
                        start=True, stop=True), wait=wv, inc=[(st, 1)])
                    E("t", lambda e, bpar=bpar, g8=g8, h=h: e.matmul(
                        PS[bpar][:, 256:512], c_ones[0:1, :],
                        ecsf[g8 % 2][0:1, (h % 8) * 128:(h % 8) * 128 + 256],
                        start=True, stop=True), inc=[(st, 1)])
                    D[("prbc", c, pr)] = st.n
                E("s", lambda e, bpar=bpar, h=h: e.activation(
                    m1sc[:, :], PS[bpar][:, (h % 2) * 128:(h % 2) * 128 + 128], AF.Relu,
                    bias=csT_c[:, h:h + 1], scale=-1.0),
                    wait=[(st, D[("prbc", c, pr)])], inc=[(ss, 1)])
                E("s", lambda e, h=h: e.activation(edif[h % 2][:, :], m1sc[:, :],
                                                   AF.Exp, scale=-1.0), inc=[(ss, 1)])
                edn = ss.n
                E("v", lambda e, h=h: e.tensor_mul(mtt[h % 2][:, :], edif[h % 2][:, :],
                                                   scoresM[:, :]),
                  wait=[(ss, edn)], inc=[(sv, 1)])
                E("v", lambda e, h=h, bpar=bpar, c=c: e.tensor_mul(
                    cexp[h % 2][:, :], C_sb[:, c * L:(c + 1) * L],
                    PS[bpar][:, 256 + (h % 2) * 128:256 + (h % 2) * 128 + 128]),
                    inc=[(sv, 1)])
                mtn = sv.n
                D[("prread", c, pr)] = (ss.n, sv.n)
                yb = 2 + (g4 % 2)
                pb, cl = 64 * (h % 2), 128 * ((h % 4) // 2)
                wv = [(sv, mtn)]
                key = ("yfree", c, g4 - 2) if g4 >= 2 else ("yfree", c - 1, g4 + 14)
                if key in D:
                    wv.append((sv, D[key]))
                E("t", lambda e, yb=yb, pb=pb, cl=cl, h=h, par=par: e.matmul(
                    PS[yb][pb:pb + 64, cl:cl + 128],
                    xssmT[par][:, h * HD:(h + 1) * HD], mtt[h % 2][:, :],
                    start=True, stop=False), wait=wv, inc=[(st, 1)])
                E("t", lambda e, yb=yb, pb=pb, cl=cl, h=h: e.matmul(
                    PS[yb][pb:pb + 64, cl:cl + 128],
                    carry_bf[:, h * HD:(h + 1) * HD], cexp[h % 2][:, :],
                    start=False, stop=True), inc=[(st, 1)])
                if h % 8 == 6 and h // 8 + 2 <= 7:
                    gnx = h // 8 + 2
                    qf1, qf2 = Q(f"tb{gnx % 2}"), Q(f"tc{gnx % 2}")
                    E("s", lambda e, gnx=gnx: e.dma_start(
                        csf[gnx % 2][0:1, :], cs_c[8 * gnx:8 * gnx + 8, :]),
                        wait=[(st, st.n)], inc=[(qf1, 16)])
                    E("s", lambda e, gnx=gnx: e.dma_start(
                        ecsf[gnx % 2][0:1, :], ecs_c[8 * gnx:8 * gnx + 8, :]),
                        wait=[(st, st.n)], inc=[(qf2, 16)])
                    D[("flq", c, gnx)] = [(qf1, qf1.n), (qf2, qf2.n)]
            E("v", lambda e, par=par, g4=g4: e.tensor_tensor(
                xdtds_g[g4 % 2][:, :, :],
                xssmT[par][:, g4 * 256:(g4 + 1) * 256].rearrange("p (a d2) -> p a d2", a=4),
                dsT_c[:, g4 * 4:g4 * 4 + 4].unsqueeze(2).broadcast_to([128, 4, HD]),
                ALU.mult), wait=[(st, D.get(("grpst", c, g4 - 2),
                                            D.get(("grpst", c - 1, 15), 0)))],
                inc=[(sv, 1)])
            E("t", lambda e, g4=g4: e.matmul(
                PS[2 + (g4 % 2)][:, 256:512], B_t[:, :],
                xdtds_g[g4 % 2][:, :, :].rearrange("p a d2 -> p (a d2)"),
                start=True, stop=True), wait=[(sv, sv.n)], inc=[(st, 1)])
            grp_st = st.n
            D[("grpst", c, g4)] = grp_st
            for j in range(4):
                h = g4 * 4 + j
                yb = 2 + (g4 % 2)
                pb = 64 * (h % 2)
                cl = 128 * ((h % 4) // 2)
                kb = h // 2
                E("v", lambda e, yb=yb, pb=pb, cl=cl, h=h, j=j, g4=g4, kb=kb:
                  e.scalar_tensor_tensor(
                      t1st[pb:pb + 64, (h % 4) // 2, :],
                      xssmg[g4 % 2][pb:pb + 64, j // 2, :],
                      c_dpp[pb:pb + 64, kb:kb + 1],
                      PS[yb][pb:pb + 64, cl:cl + 128], ALU.mult, ALU.add),
                    wait=[(st, grp_st)] + D[("grpq", c, g4)] +
                    ([D[("hrq", c - 2)]] if (g4 == 0 and j == 0 and ("hrq", c - 2) in D)
                     else []),
                    inc=[(sv, 1)])
                E("v", lambda e, par=par, pb=pb, h=h, j=j, g4=g4, kb=kb: e.tensor_mul(
                    hready_asm[par][pb:pb + 64, kb, :],
                    t1st[pb:pb + 64, (h % 4) // 2, :],
                    gateg[g4 % 2][pb:pb + 64, j // 2, :]), inc=[(sv, 1)])
                E("v", lambda e, yb=yb, h=h, j=j: e.scalar_tensor_tensor(
                    carry[:, h * HD:(h + 1) * HD], carry[:, h * HD:(h + 1) * HD],
                    cd_c[:, h:h + 1], PS[yb][:, 256 + j * 64:256 + (j + 1) * 64],
                    ALU.mult, ALU.add), inc=[(sv, 1)])
                E("v", lambda e, h=h: e.tensor_copy(carry_bf[:, h * HD:(h + 1) * HD],
                                                    carry[:, h * HD:(h + 1) * HD]),
                  inc=[(sv, 1)])
            D[("yfree", c, g4)] = sv.n
            D[("grpfree", c, g4)] = sv.n
        D[("xtfree", c)] = sv.n
        D[("flfree", c)] = st.n
        qq = Q(f"ha{c % 2}")
        E("q", lambda e, par=par, c=c: e.dma_start(
            d_hready[:, :, c * L:(c + 1) * L].transpose([1, 0, 2]),
            hready_asm[par][:, :, :]),
            wait=[(sv, sv.n)], inc=[(qq, 16)])
        D[("hrq", c)] = (qq, qq.n)

    def p5_init_half(hf):
        qq = Q("xq")
        wv = [(st, D[("w1mm", 0, 63)])] if hf == 1 else []
        E("q", lambda e: e.dma_start(
            xqi_half[:, :, :], d_xqi[:, :, hf * TH:(hf + 1) * TH].transpose([1, 0, 2])),
            wait=wv + [D[("p1q", hf, KD - 1)], D[("p1q", hf, KD - 2)]], inc=[(qq, 16)])
        D[("xqiq", hf)] = (qq, qq.n)
        for t4 in range(2):
            E("t", lambda e, hf=hf, t4=t4: e.matmul(
                PS[4 + t4][:, :], c_ones[0:1, :],
                inv1h[hf][0:1, t4 * 512:(t4 + 1) * 512], start=True, stop=True),
                wait=[(sv, D["p0fin"]), (sv, D.get("w1evlast", 0))], inc=[(st, 1)])
        bn = st.n
        E("v", lambda e: e.tensor_copy(invsbc[:, 0:512], PS[4][:, :]),
          wait=[(st, bn)], inc=[(sv, 1)])
        E("v", lambda e: e.tensor_copy(invsbc[:, 512:1024], PS[5][:, :]), inc=[(sv, 1)])
        D[("invn", hf)] = sv.n

    def p5_batch(hf, olo, ohi):
        for o in range(olo, ohi):
            ob = o % 2
            wmm_guard = (st, D.get(("w1mm", hf, o - 2),
                                   D.get(("w1mm", 0, 62 + (o % 2)), 0) if hf == 1
                                   else 0))
            for h2 in range(2):
                qq = Q(f"wa{h2}{o % 2}")
                sl = slice(h2 * 1024, (h2 + 1) * 1024)
                bB = w1bf[h2][o % 2]
                E("g", lambda e, o=o, b=bB, sl=sl: e.dma_start(b[:, :], w1[o][:, sl]),
                  wait=[(sv, D.get(("w1qh", hf, o - 2, h2),
                                   D.get(("w1qh", 1 - hf, 62 + o % 2, h2), 0))),
                        (sv, D.get("p0done", 0))],
                  inc=[(qq, 16)])
                wn = (qq, qq.n)
                E("v", lambda e, b=bB: e.tensor_scalar(
                    wqscr[:, :], b[:, :], c_swcol[:, 0:1], CLIP1, ALU.mult, ALU.min),
                    wait=[wn, (sv, D["p0fin"]), wmm_guard], inc=[(sv, 1)])
                E("v", lambda e: e.tensor_scalar(wqscr[:, :], wqscr[:, :], -CLIP1, MAGIC,
                                                 ALU.max, ALU.add), inc=[(sv, 1)])
                E("v", lambda e, o=o, sl=sl: e.tensor_scalar(
                    wq1b[o % 2][:, sl], wqscr[:, :], MAGIC, None, ALU.subtract),
                    inc=[(sv, 1)])
                D[("w1qh", hf, o, h2)] = sv.n
            D[("w1q", hf, o)] = sv.n
            wv = [(sv, D[("w1q", hf, o)]), D[("xqiq", hf)], (sv, D["presv"]),
                  (sv, D.get(("w1ev", hf, o - 2), 0))]
            if o < 2:
                wv.append((sv, D[("invn", hf)]))
            pg = [PS[4 + 2 * (o % 2)], PS[5 + 2 * (o % 2)]]
            for k in range(KD):
                for t2 in range(2):
                    E("t", lambda e, o=o, k=k, t2=t2, pg=pg: e.matmul(
                        pg[t2][:, :], wq1b[o % 2][:, k * 128:(k + 1) * 128],
                        xqi_half[:, k, t2 * 512:(t2 + 1) * 512],
                        start=(k == 0), stop=(k == KD - 1)),
                        wait=wv if (k == 0 and t2 == 0) else [],
                        inc=[(st, 1)] if (k == KD - 1 and t2 == 1) else [])
            D[("w1mm", hf, o)] = st.n
            mmn = st.n
            qh = Q(f"h1{o % 2}")
            for t2 in range(2):
                E("v", lambda e, o=o, t2=t2, pg=pg: e.tensor_mul(
                    h1stg[o % 2][:, t2 * 512:(t2 + 1) * 512], pg[t2][:, :],
                    invsbc[:, t2 * 512:(t2 + 1) * 512]),
                    wait=[(st, mmn), (sv, D[("invn", hf)])] +
                    ([D[("h1q", hf, o - 2)]] if ("h1q", hf, o - 2) in D else
                     ([D[("h1q", 0, 62 + (o % 2))]] if hf == 1 and o < 2 else
                      [(Q("sa0"), Q("sa0").n), (Q("sa1"), Q("sa1").n)])),
                    inc=[(sv, 1)])
            D[("w1ev", hf, o)] = sv.n
            D["w1evlast"] = sv.n
            E("q", lambda e, o=o, hf=hf: e.dma_start(
                d_h1[o, :, hf * TH:(hf + 1) * TH], h1stg[o % 2][:, :]),
                wait=[(sv, sv.n)], inc=[(qh, 16)])
            D[("h1q", hf, o)] = (qh, qh.n)

    def p4_half(hf):
        t0 = hf * TH
        q1, q2 = Q("p40"), Q("p41")
        wv = [D[("hrq", NCH - 1)], D[("hrq", NCH - 2)], (st, D[("w1mm", 1, 63)])]
        if hf == 1:
            wv.append((st, D[("p4mm", 0, 15)]))
            wv.append((sv, D[("p4sq", 0, 31)]))
        E("q", lambda e: e.dma_start(
            hr4_1[:, :, :], d_hready[0:16, :, t0:t0 + TH].transpose([1, 0, 2])),
            wait=wv, inc=[(q1, 16)])
        E("q", lambda e: e.dma_start(
            hr4_2[:, :, :], d_hready[16:32, :, t0:t0 + TH].transpose([1, 0, 2])),
            wait=wv, inc=[(q2, 16)])
        ldq = [(q1, q1.n), (q2, q2.n)]
        E("v", lambda e: e.memset(vacc4[:, :], 0.0), inc=[(sv, 1)])
        for k in range(32):
            hr = hr4_1 if k < 16 else hr4_2
            kk = k % 16
            E("s", lambda e, hr=hr, kk=kk, k=k: e.activation(
                sq4[k % 2][:, :], hr[:, kk, :], AF.Square, scale=1.0),
                wait=ldq + [(sv, D.get(("p4sq", hf, k - 2), 0))], inc=[(ss, 1)])
            E("v", lambda e, k=k: e.tensor_add(vacc4[:, :], vacc4[:, :], sq4[k % 2][:, :]),
              wait=[(ss, ss.n)], inc=[(sv, 1)])
            E("v", lambda e, hr=hr, kk=kk, k=k: e.tensor_scalar_mul(
                hr[:, kk, :], hr[:, kk, :], c_nwpp[:, k:k + 1]), inc=[(sv, 1)])
            D[("p4sq", hf, k)] = sv.n
        nwn = sv.n
        for t2 in range(2):
            E("t", lambda e, t2=t2: e.matmul(PS[4 + t2][0:1, :], c_ones[:, 0:1],
                                             vacc4[:, t2 * 512:(t2 + 1) * 512],
                                             start=True, stop=True),
              wait=[(sv, nwn)], inc=[(st, 1)])
        E("v", lambda e: e.tensor_scalar_mul(rowM[0:1, 0:512], PS[4][0:1, :],
                                             1.0 / INTER),
          wait=[(st, st.n)] + ([D[("varq", hf - 1)]] if ("varq", hf - 1) in D else []),
          inc=[(sv, 1)])
        E("v", lambda e: e.tensor_scalar_mul(rowM[0:1, 512:1024], PS[5][0:1, :],
                                             1.0 / INTER), inc=[(sv, 1)])
        qr4 = Q(f"r4{hf}")
        E("q", lambda e, t0=t0: e.dma_start(d_rows[2:3, t0:t0 + TH], rowM[0:1, :]),
          wait=[(sv, sv.n)], inc=[(qr4, 16)])
        D[("varq", hf)] = (qr4, qr4.n)
        for o in range(16):
            pg = [PS[2 * (o % 2)], PS[2 * (o % 2) + 1]]
            qq = Q(f"wc{o % 2}")
            E("g", lambda e, o=o: e.dma_start(wob[o % 2][:, :], wout[o]),
              wait=[(st, D.get(("p4mm", hf, o - 2), 0))], inc=[(qq, 16)])
            wv2 = [(qq, qq.n), (sv, nwn), (sv, D.get(("p4ev", hf, o - 2), 0))]
            for k in range(32):
                hr = hr4_1 if k < 16 else hr4_2
                kk = k % 16
                for t2 in range(2):
                    E("t", lambda e, o=o, k=k, t2=t2, hr=hr, kk=kk, pg=pg: e.matmul(
                        pg[t2][:, :], wob[o % 2][:, k * 128:(k + 1) * 128],
                        hr[:, kk, t2 * 512:(t2 + 1) * 512],
                        start=(k == 0), stop=(k == 31)),
                        wait=wv2 if (k == 0 and t2 == 0) else [],
                        inc=[(st, 1)] if (k == 31 and t2 == 1) else [])
            D[("p4mm", hf, o)] = st.n
            qst = Q(f"st{o % 2}")
            for t2 in range(2):
                E("v", lambda e, o=o, t2=t2, pg=pg: e.tensor_copy(
                    stil4[o % 2][:, t2 * 512:(t2 + 1) * 512], pg[t2][:, :]),
                    wait=[(st, st.n)] + ([D[("stq", hf, o - 2)]]
                                         if ("stq", hf, o - 2) in D else
                                         ([D[("stq", 0, 14 + (o % 2))]]
                                          if hf == 1 and o < 2 else [])),
                    inc=[(sv, 1)])
            D[("p4ev", hf, o)] = sv.n
            E("q", lambda e, o=o, t0=t0: e.dma_start(
                d_stil[o, :, t0:t0 + TH], stil4[o % 2][:, :]),
                wait=[(sv, sv.n)], inc=[(qst, 16)])
            D[("stq", hf, o)] = (qst, qst.n)

    def p4_w2quant():
        for i in range(32):
            o, sub = i // 2, i % 2
            qwq = Q(f"wq{i % 2}")
            for h2 in range(2):
                j = i * 2 + h2
                qq = Q(f"wd{j % 2}")
                sl_src = slice(sub * 4096 + h2 * 2048, sub * 4096 + (h2 + 1) * 2048)
                sl_dst = slice(h2 * 2048, (h2 + 1) * 2048)
                E("g", lambda e, o=o, b=w2q_raw[j % 2], sl_src=sl_src: e.dma_start(
                    b[:, :], w2[o][:, sl_src]),
                    wait=[(sv, D.get(("w2qj", j - 2), 0))], inc=[(qq, 16)])
                wn = (qq, qq.n)
                E("v", lambda e, j=j, b=w2q_raw[j % 2]: e.tensor_scalar(
                    w2q_scr[:, :], b[:, :], c_swcol[:, 1:2], CLIP1, ALU.mult, ALU.min),
                    wait=[wn, (sv, D["p0fin"])] +
                    ([D[("wq2d", i - 1)]] if (h2 == 0 and ("wq2d", i - 1) in D) else []),
                    inc=[(sv, 1)])
                E("v", lambda e: e.tensor_scalar(w2q_scr[:, :], w2q_scr[:, :],
                                                 -CLIP1, MAGIC, ALU.max, ALU.add),
                  inc=[(sv, 1)])
                E("v", lambda e, sl_dst=sl_dst: e.tensor_scalar(
                    w2q_stg[:, sl_dst], w2q_scr[:, :], MAGIC, None, ALU.subtract),
                    inc=[(sv, 1)])
                D[("w2qj", j)] = sv.n
            D[("w2q", i)] = sv.n
            E("q", lambda e, o=o, sub=sub: e.dma_start(
                d_wq2[o][:, sub * 4096:(sub + 1) * 4096], w2q_stg[:, :]),
                wait=[(sv, sv.n)], inc=[(qwq, 16)])
            D[("wq2d", i)] = (qwq, qwq.n)

    def p6_quarter(q4):
        t0 = q4 * TQ
        q1, q2 = Q("p60"), Q("p61")
        wv = [D[("wq2d", 31)], D[("wq2d", 30)], D[("h1q", 1, 63)], D[("h1q", 1, 62)]]
        if q4 > 0:
            wv.append((st, D[("p6mm", q4 - 1, 15)]))
            wv.append((ss, D[("p6ev", q4 - 1, 15)]))
        else:
            wv.append((st, D[("p4mm", 1, 15)]))
            wv.append((sv, D[("p4ev", 1, 15)]))
        if "hq0tap" in D:
            wv += D["hq0tap"]
        if "sch0tap" in D:
            wv += D["sch0tap"]
        E("q", lambda e, q4=q4: e.dma_start(
            hq6_1[:, :, :], d_h1[0:32, :, t0:t0 + TQ].transpose([1, 0, 2])),
            wait=wv, inc=[(q1, 16)])
        E("q", lambda e, q4=q4: e.dma_start(
            hq6_2[:, :, :], d_h1[32:64, :, t0:t0 + TQ].transpose([1, 0, 2])),
            wait=wv, inc=[(q2, 16)])
        ldq = [(q1, q1.n), (q2, q2.n)]
        E("v", lambda e: e.memset(rmq[:, :], 0.0),
          wait=[D[("stq", 1, 15)], D[("stq", 1, 14)]] if q4 == 0 else [],
          inc=[(sv, 1)])
        E("v", lambda e: e.memset(rmqn[:, :], 0.0), inc=[(sv, 1)])
        for k in range(64):
            hq = hq6_1 if k < 32 else hq6_2
            kk = k % 32
            E("s", lambda e, hq=hq, kk=kk: e.activation(hq[:, kk, :], hq[:, kk, :],
                                                        AF.Gelu, scale=1.0),
              wait=ldq if k == 0 else [], inc=[(ss, 1)])
            E("v", lambda e, hq=hq, kk=kk: e.tensor_max(rmq[:, :], rmq[:, :],
                                                        hq[:, kk, :]),
              wait=[(ss, ss.n)], inc=[(sv, 1)])
            E("v", lambda e, hq=hq, kk=kk: e.tensor_tensor(rmqn[:, :], rmqn[:, :],
                                                           hq[:, kk, :], ALU.min),
              inc=[(sv, 1)])
        E("v", lambda e: e.tensor_scalar_mul(rmqn[:, :], rmqn[:, :], -1.0), inc=[(sv, 1)])
        E("v", lambda e: e.tensor_max(rmq[:, :], rmq[:, :], rmqn[:, :]), inc=[(sv, 1)])
        for i in range(4):
            E("t", lambda e, i=i: e.transpose(PS[6][:, 0:128],
                                              rmq[:, i * 128:(i + 1) * 128],
                                              c_idn[:, :]),
              wait=[(sv, sv.n)], inc=[(st, 1)])
            E("v", lambda e, i=i: e.reduce_max(mx6[:, i:i + 1], PS[6][:, 0:128], AX),
              wait=[(st, st.n)], inc=[(sv, 1)])
        qr6 = Q(f"r6{q4}")
        E("q", lambda e, dst=row_assemble_dst(1, t0, 4): e.dma_start(
            dst, mx6[:, :], allow_slow_non_contiguous=True),
            wait=[(sv, sv.n)], inc=[(qr6, 16)])
        E("q", lambda e, t0=t0: e.dma_start(row6[0][0:1, :], d_rows[1:2, t0:t0 + TQ]),
          wait=[(qr6, qr6.n)] + ([D[("varq", 1)], D[("varq", 0)]] if q4 == 0 else []),
          inc=[(qr6, 16)])
        E("v", lambda e: e.tensor_scalar_max(row6[0][0:1, :], row6[0][0:1, :], QEPS),
          wait=[(qr6, qr6.n)], inc=[(sv, 1)])
        E("v", lambda e: e.reciprocal(row6[1][0:1, :], row6[0][0:1, :]), inc=[(sv, 1)])
        E("v", lambda e: e.tensor_scalar_mul(row6[1][0:1, :], row6[1][0:1, :], 127.0),
          inc=[(sv, 1)])
        E("v", lambda e: e.tensor_scalar(row6[2][0:1, :], row6[0][0:1, :],
                                         c_sw[0:1, 1:2], 1.0 / 127.0, ALU.mult, ALU.mult),
          inc=[(sv, 1)])
        E("t", lambda e: e.matmul(PS[6][:, :], c_ones[0:1, :], row6[1][0:1, :],
                                  start=True, stop=True), wait=[(sv, sv.n)], inc=[(st, 1)])
        E("t", lambda e: e.matmul(PS[7][:, :], c_ones[0:1, :], row6[2][0:1, :],
                                  start=True, stop=True), inc=[(st, 1)])
        E("v", lambda e: e.tensor_copy(schbc[:, :], PS[6][:, :]),
          wait=[(st, st.n)], inc=[(sv, 1)])
        E("v", lambda e: e.tensor_copy(i2bc[:, :], PS[7][:, :]), inc=[(sv, 1)])
        bcn = sv.n
        for k in range(64):
            hq = hq6_1 if k < 32 else hq6_2
            kk = k % 32
            E("v", lambda e, hq=hq, kk=kk, k=k: e.tensor_mul(
                t6scr[k % 2][:, :], hq[:, kk, :], schbc[:, :]),
                wait=[(sv, bcn)] if k == 0 else [], inc=[(sv, 1)])
            E("v", lambda e, hq=hq, kk=kk, k=k: e.tensor_scalar(
                hq[:, kk, :], t6scr[k % 2][:, :], MAGIC, MAGIC, ALU.add, ALU.subtract),
                inc=[(sv, 1)])
        qn = sv.n
        if q4 == 0 and "hq0" in taps:
            E("q", lambda e: e.dma_start(tap_out["hq0"][0],
                                         hq6_1[:, :, :].rearrange("p a b -> p (a b)")),
              wait=[(sv, qn)], inc=[(Q("thq"), 16)])
            E("q", lambda e: e.dma_start(tap_out["hq0"][1],
                                         hq6_2[:, :, :].rearrange("p a b -> p (a b)")),
              wait=[(sv, qn)], inc=[(Q("thq2"), 16)])
            D["hq0tap"] = [(Q("thq"), Q("thq").n), (Q("thq2"), Q("thq2").n)]
        if q4 == 0 and "sch0" in taps:
            E("q", lambda e: e.dma_start(tap_out["sch0"][0:1, :], row6[1][0:1, :]),
              wait=[(sv, qn)], inc=[(Q("tsc"), 16)])
            E("q", lambda e: e.dma_start(tap_out["sch0"][1:2, :], row6[2][0:1, :]),
              wait=[(sv, qn)], inc=[(Q("tsc2"), 16)])
            D["sch0tap"] = [(Q("tsc"), Q("tsc").n), (Q("tsc2"), Q("tsc2").n)]
        for o in range(16):
            qq = Q(f"w2b{o % 2}")
            E("q", lambda e, o=o: e.dma_start(
                wq2b[o % 2][:, :, :], d_wq2[o].rearrange("p (a b) -> p a b", a=64)),
                wait=[(st, D.get(("p6mm", q4, o - 2), 0))] +
                ([D[("wq2d", 31)], D[("wq2d", 30)]] if (q4 == 0 and o < 2) else []),
                inc=[(qq, 16)])
            wv2 = [(qq, qq.n), (sv, qn), (ss, D.get(("p6ev", q4, o - 2), 0))]
            pb = PS[o % 2]
            for k in range(64):
                hq = hq6_1 if k < 32 else hq6_2
                kk = k % 32
                E("t", lambda e, o=o, k=k, hq=hq, kk=kk, pb=pb: e.matmul(
                    pb[:, :], wq2b[o % 2][:, k, :], hq[:, kk, :],
                    start=(k == 0), stop=(k == 63)),
                    wait=wv2 if k == 0 else [],
                    inc=[(st, 1)] if k == 63 else [])
            D[("p6mm", q4, o)] = st.n
            E("v", lambda e, o=o, pb=pb: e.tensor_mul(t6scr[o % 2][:, :], pb[:, :],
                                                      i2bc[:, :]),
              wait=[(st, st.n), (ss, D.get(("p6ev", q4, o - 2), 0))], inc=[(sv, 1)])
            qg = Q(f"g6{o % 2}")
            E("s", lambda e, o=o: e.activation(gst6[o % 2][:, :], t6scr[o % 2][:, :],
                                               AF.Silu, scale=1.0),
              wait=[(sv, sv.n)] + ([D[("gq", q4, o - 2)]] if ("gq", q4, o - 2) in D else
                                   ([D[("gq", q4 - 1, 14 + (o % 2))]] if q4 > 0 and o < 2
                                    else [])),
              inc=[(ss, 1)])
            D[("p6ev", q4, o)] = ss.n
            E("q", lambda e, o=o, t0=t0: e.dma_start(d_g[o, :, t0:t0 + TQ],
                                                     gst6[o % 2][:, :]),
              wait=[(ss, ss.n)], inc=[(qg, 16)])
            D[("gq", q4, o)] = (qg, qg.n)

    def p7():
        qr7 = Q("r7")
        E("q", lambda e: e.dma_start(r7[0:1, :], d_rows[2:3, :]),
          wait=[D[("varq", 1)], D[("varq", 0)], (ss, D[("p6ev", 3, 15)])],
          inc=[(qr7, 16)])
        E("v", lambda e: e.tensor_scalar_add(r7b[0:1, :], r7[0:1, :], c_eps[0:1, 1:2]),
          wait=[(qr7, qr7.n)], inc=[(sv, 1)])
        E("s", lambda e: e.activation(r7[0:1, :], r7b[0:1, :], AF.Sqrt, scale=1.0),
          wait=[(sv, sv.n)], inc=[(ss, 1)])
        E("v", lambda e: e.reciprocal(r7[0:1, :], r7[0:1, :]),
          wait=[(ss, ss.n)], inc=[(sv, 1)])
        E("v", lambda e: e.tensor_mul(r7c[0:1, :], r7[0:1, :], r7[0:1, :]), inc=[(sv, 1)])
        E("v", lambda e: e.tensor_mul(r7c[0:1, :], r7c[0:1, :], r7b[0:1, :]),
          inc=[(sv, 1)])
        E("v", lambda e: e.tensor_scalar(r7c[0:1, :], r7c[0:1, :], -0.5, 1.5,
                                         ALU.mult, ALU.add), inc=[(sv, 1)])
        E("v", lambda e: e.tensor_mul(r7[0:1, :], r7[0:1, :], r7c[0:1, :]),
          inc=[(sv, 1)])
        for t4 in range(4):
            E("t", lambda e, t4=t4: e.matmul(PS[4 + t4][:, :],
                                             c_ones[0:1, :],
                                             r7[0:1, t4 * 512:(t4 + 1) * 512],
                                             start=True, stop=True),
              wait=[(sv, sv.n)], inc=[(st, 1)])
        bcn = st.n
        for k in range(KD):
            qxk = Q(f"x7{k % 2}")
            qsk = Q(f"s7{k % 2}")
            qgk = Q(f"g7{k % 2}")
            qok = Q(f"o7{k % 2}")
            wv = [D[("gq", 3, 15)], D[("gq", 3, 14)]] if k < 2 else                  [D[("p7q", k - 2)]]
            E("q", lambda e, k=k: e.dma_start(x7[k % 2][:, :],
                                              xt[k * 128:(k + 1) * 128, :]),
              wait=wv, inc=[(qxk, 16)])
            E("q", lambda e, k=k: e.dma_start(stil7[k % 2][:, :], d_stil[k]),
              wait=wv + ([D[("stq", 1, k)], D[("stq", 0, k)]] if k < 2 else []),
              inc=[(qsk, 16)])
            E("q", lambda e, k=k: e.dma_start(g7[k % 2][:, :], d_g[k]),
              wait=wv, inc=[(qgk, 16)])
            ldq = [(qxk, qxk.n), (qsk, qsk.n), (qgk, qgk.n)]
            for t4 in range(4):
                sl = slice(t4 * 512, (t4 + 1) * 512)
                E("v", lambda e, k=k, t4=t4, sl=sl: e.tensor_mul(
                    t7a[:, :], g7[k % 2][:, sl], PS[4 + t4][:, :]),
                    wait=(ldq + [(st, bcn)]) if t4 == 0 else [], inc=[(sv, 1)])
                E("v", lambda e, k=k, sl=sl: e.tensor_mul(
                    t7b[:, :], t7a[:, :], stil7[k % 2][:, sl]), inc=[(sv, 1)])
                E("v", lambda e, k=k, sl=sl: e.tensor_add(
                    ostg7[k % 2][:, sl], x7[k % 2][:, sl], t7b[:, :]), inc=[(sv, 1)])
            E("q", lambda e, k=k: e.dma_start(out[k * 128:(k + 1) * 128, :],
                                              ostg7[k % 2][:, :]),
              wait=[(sv, sv.n)], inc=[(qok, 16)])
            D[("p7q", k)] = (qok, qok.n)

    init_consts()
    p0_dma(0, 48)
    if last_phase >= 1:
        p1_half(0)
        if "xn" in taps:
            E("q", lambda e: e.dma_start(tap_out["xn"][0], xn_half[:, :, :].rearrange(
                "p a b -> p (a b)")), wait=[(sv, sv.n)], inc=[(Q("xn0"), 16)])
            D["xntap0"] = (Q("xn0"), Q("xn0").n)
        p1b_half(0)
        D[("p1bend", 0)] = sv.n
    if last_phase >= 2:
        p2_half(0)
    p0_dma(48, 128)
    if last_phase >= 1:
        p1_half(1)
        if "xn" in taps:
            E("q", lambda e: e.dma_start(tap_out["xn"][1], xn_half[:, :, :].rearrange(
                "p a b -> p (a b)")), wait=[(sv, sv.n), (Q("xn0"), Q("xn0").n)],
              inc=[(Q("xn1"), 16)])
            D["xntap1"] = (Q("xn1"), Q("xn1").n)
        p1b_half(1)
        D[("p1bend", 1)] = sv.n
    if last_phase >= 2:
        p2_half(1)
        D["p2end_t"] = st.n
    p0_dma(128, 256)
    D["p0done"] = sv.n
    D["p2spill"] = [(Q("pa0"), Q("pa0").n), (Q("pa1"), Q("pa1").n)]
    if last_phase >= 3:
        p0_finish()
        if last_phase >= 5:
            p5_init_half(0)
        p3_loads(0)
        p3_chunk_prelude(0)
        for c in range(NCH):
            if c + 1 < NCH:
                p3_loads(c + 1)
            p3_chunk(c)
            if last_phase >= 5:
                if c == 8:
                    p5_init_half(1)
                hf = c // 8
                p5_batch(hf, 8 * (c % 8), 8 * (c % 8) + 8)
            if c + 1 < NCH:
                p3_chunk_prelude(c + 1)
    if last_phase >= 4:
        p4_half(0)
        p4_w2quant()
        p4_half(1)
    if last_phase >= 6:
        for q4 in range(4):
            p6_quarter(q4)
    if last_phase >= 7:
        p7()

    def all_done():
        w = [(sv, sv.n), (ss, ss.n), (st, st.n), (sg, sg.n)]
        w += [(s, s.n) for s in _semd.values()]
        return w

    qtap = Q("tap")
    for name, src in (("bsb", B_sb[:, :]), ("csb", C_sb[:, :]),
                      ("dtsp", dt_sp[:, :]), ("carry", carry[:, :]),
                      ("sw", c_sw[:, :])):
        if name in taps:
            E("q", lambda e, d=tap_out[name], s=src: e.dma_start(d, s),
              wait=all_done(), inc=[(qtap, 16)])
    for name, src in (("xqi", d_xqi), ("gate", d_gate), ("xssm", d_xssm),
                      ("hready", d_hready), ("h1", d_h1), ("stil", d_stil),
                      ("g", d_g), ("rows", d_rows)):
        if name in taps:
            E("q", lambda e, d=tap_out[name], s=src: e.dma_start(d, s),
              wait=all_done(), inc=[(qtap, 16)])

    E("q", lambda e: e.nop(), wait=all_done() + [(st, st.n), (sv, sv.n)])

    counts = P.check()

    with nc.Block() as block:
        @block.tensor
        def _(eng):
            for op in P.ops["t"]:
                op(eng)

        @block.vector
        def _(eng):
            for op in P.ops["v"]:
                op(eng)

        @block.scalar
        def _(eng):
            for op in P.ops["s"]:
                op(eng)

        @block.gpsimd
        def _(eng):
            for op in P.ops["g"]:
                op(eng)

        @block.sync
        def _(eng):
            for op in P.ops["q"]:
                op(eng)

    stack.close()
    return nc, counts



def kernel(**inputs):
    from concourse.bass_utils import run_bass_kernel_spmd
    in_maps = host_prep(inputs)
    nc, _counts = build(taps=(), last_phase=7)
    res = run_bass_kernel_spmd(nc, in_maps, core_ids=list(range(len(in_maps))))
    out = np.stack([np.ascontiguousarray(r["out"].T) for r in res.results], axis=0)
    return out.astype(np.float32)


# version 10
# speedup vs baseline: 1.5152x; 1.0084x over previous
import contextlib
import numpy as np
import concourse.bass as bass
import concourse.mybir as mybir

F32 = mybir.dt.float32
BF16 = mybir.dt.bfloat16
F16 = mybir.dt.float16
AF = mybir.ActivationFunctionType
ALU = mybir.AluOpType
AX = mybir.AxisListType.X

S = 2048
DM = 2048
KD = 16
INTER = 4096
CONVD = 4352
PROJ = 8512
NO2 = 67
NH = 64
HD = 64
DST = 128
L = 128
NCH = 16
HFF = 8192
TH = 1024
TQ = 512
MAGIC = 12582912.0
CLIP1 = 1.4990234375
LNEPS = 1e-5
RMSEPS = 1e-6
QEPS = 1e-5



def _tile_lhsT(w, no, nk):
    out_r, in_r = w.shape
    assert in_r == nk * 128
    if out_r < no * 128:
        w = np.concatenate([w, np.zeros((no * 128 - out_r, in_r), w.dtype)], 0)
    t = np.empty((no, 128, nk * 128), np.float32)
    for o in range(no):
        blk = w[o * 128:(o + 1) * 128, :]
        t[o] = np.ascontiguousarray(
            blk.reshape(128, nk, 128).transpose(2, 1, 0)).reshape(128, nk * 128)
    return t


def host_prep(inputs):
    x = np.ascontiguousarray(np.asarray(inputs["L0_attn_hidden_states"], np.float32))
    sh = {}
    sh["win"] = _tile_lhsT(np.asarray(inputs["L0_attn_in_proj_w"], np.float32), NO2, KD)
    sh["wout"] = _tile_lhsT(np.asarray(inputs["L0_attn_out_proj_w"], np.float32), 16, 32)
    sh["w1"] = _tile_lhsT(np.asarray(inputs["L0_ffn_w1"], np.float32), 64, KD)
    sh["w2"] = _tile_lhsT(np.asarray(inputs["L0_ffn_w2"], np.float32), 16, 64)
    cw = np.asarray(inputs["L0_attn_conv_w"], np.float32).reshape(CONVD, 4)
    sh["convw"] = np.ascontiguousarray(
        cw.reshape(34, 128, 4).transpose(1, 0, 2).reshape(128, 136))
    cb = np.asarray(inputs["L0_attn_conv_b"], np.float32)
    sh["convb"] = np.ascontiguousarray(cb.reshape(34, 128).T)
    sh["dtb"] = np.ascontiguousarray(
        np.asarray(inputs["L0_attn_dt_bias"], np.float32).reshape(64, 1))
    sh["alog"] = np.ascontiguousarray(
        np.asarray(inputs["L0_attn_A_log"], np.float32).reshape(64, 1))
    dd = np.asarray(inputs["L0_attn_D"], np.float32)
    ch = np.arange(INTER)
    sh["dpp"] = np.ascontiguousarray(dd[(ch // HD)].reshape(32, 128).T)
    nw = np.asarray(inputs["L0_attn_norm_weight"], np.float32)
    sh["nwpp"] = np.ascontiguousarray(nw.reshape(32, 128).T)
    sh["idn"] = np.eye(128, dtype=np.float32)
    sh["ones"] = np.ones((128, 128), np.float32)
    sh["mask"] = np.ascontiguousarray(
        (np.arange(128)[None, :] >= np.arange(128)[:, None]).astype(np.float32))
    in_maps = []
    for c in range(x.shape[0]):
        d = dict(sh)
        d["xt"] = np.ascontiguousarray(x[c].T)
        in_maps.append(d)
    return in_maps



class Sem:
    def __init__(self, stack, nc, name):
        self.h = stack.enter_context(nc.semaphore(name))
        self.n = 0
        self.name = name


class Prog:
    ENGS = ("t", "v", "s", "g", "q")

    def __init__(self):
        self.ops = {k: [] for k in self.ENGS}
        self.trace = {k: [] for k in self.ENGS}
        self.eng_sems = {}

    def emit(self, ek, build, wait=(), inc=(), dma=None, chain=False):
        if dma is None:
            dma = any(k >= 16 for _, k in inc)
        esem = self.eng_sems.get(ek)
        if esem is not None:
            lagv = esem.n if (chain or ek not in ("t", "s")) else max(0, esem.n - 8)
            wait = list(wait) + [(esem, lagv)]
            if not dma and not any(sm is esem for sm, _ in inc):
                inc = list(inc) + [(esem, 1)]
        for sm, k in inc:
            sm.n += k
        inc_h = tuple((sm.h, int(k)) for sm, k in inc)
        wait_h = tuple((sm.h, int(v)) for sm, v in wait if v > 0)

        def op(e, build=build, wait_h=wait_h, inc_h=inc_h):
            for hh, v in wait_h:
                e.wait_ge(hh, v)
            r = build(e)
            for hh, k in inc_h:
                r = r.then_inc(hh, k)
        self.ops[ek].append(op)
        self.trace[ek].append((tuple((sm.name, int(v)) for sm, v in wait if v > 0),
                               tuple((sm.name, int(k)) for sm, k in inc)))

    def check(self):
        pos = {k: 0 for k in self.ENGS}
        cnt = {}
        for k in self.ENGS:
            for w, i in self.trace[k]:
                for n, _ in list(w) + list(i):
                    cnt.setdefault(n, 0)
        progressed = True
        while progressed:
            progressed = False
            for k in self.ENGS:
                while pos[k] < len(self.trace[k]):
                    w, i = self.trace[k][pos[k]]
                    if all(cnt[n] >= v for n, v in w):
                        for n, kk in i:
                            cnt[n] += kk
                        pos[k] += 1
                        progressed = True
                    else:
                        break
        stuck = {k: pos[k] for k in self.ENGS if pos[k] < len(self.trace[k])}
        if stuck:
            lines = []
            for k, p in stuck.items():
                lines.append(f"engine {k} stuck at op {p}/{len(self.trace[k])}"
                             f" waiting {self.trace[k][p][0]} counts="
                             f"{[(n, cnt[n]) for n, _ in self.trace[k][p][0]]}")
            raise RuntimeError("DEADLOCK:\n" + "\n".join(lines))
        return {k: len(self.trace[k]) for k in self.ENGS}



def build(taps=(), last_phase=7, p2_os=None, detect_races=False):
    nc = bass.Bass("TRN2", target_bir_lowering=False,
                   detect_race_conditions=detect_races)
    stack = contextlib.ExitStack()
    P = Prog()
    D = {}

    ap = lambda t: t.ap()
    xt = ap(nc.dram_tensor("xt", [DM, S], F32, kind="ExternalInput"))
    win = ap(nc.dram_tensor("win", [NO2, 128, DM], F32, kind="ExternalInput"))
    wout = ap(nc.dram_tensor("wout", [16, 128, INTER], F32, kind="ExternalInput"))
    w1 = ap(nc.dram_tensor("w1", [64, 128, DM], F32, kind="ExternalInput"))
    w2 = ap(nc.dram_tensor("w2", [16, 128, HFF], F32, kind="ExternalInput"))
    convw = ap(nc.dram_tensor("convw", [128, 136], F32, kind="ExternalInput"))
    convb = ap(nc.dram_tensor("convb", [128, 34], F32, kind="ExternalInput"))
    dtb = ap(nc.dram_tensor("dtb", [64, 1], F32, kind="ExternalInput"))
    alog = ap(nc.dram_tensor("alog", [64, 1], F32, kind="ExternalInput"))
    dpp = ap(nc.dram_tensor("dpp", [128, 32], F32, kind="ExternalInput"))
    nwpp = ap(nc.dram_tensor("nwpp", [128, 32], F32, kind="ExternalInput"))
    idn = ap(nc.dram_tensor("idn", [128, 128], F32, kind="ExternalInput"))
    onesd = ap(nc.dram_tensor("ones", [128, 128], F32, kind="ExternalInput"))
    maskd = ap(nc.dram_tensor("mask", [128, 128], F32, kind="ExternalInput"))
    out = ap(nc.dram_tensor("out", [DM, S], F32, kind="ExternalOutput"))

    d_xqi = ap(nc.dram_tensor("d_xqi", [KD, 128, S], BF16, kind="Internal"))
    d_gate = ap(nc.dram_tensor("d_gate", [NCH, 32, 128, L], BF16, kind="Internal"))
    d_xssm = ap(nc.dram_tensor("d_xssm", [NCH, 32, 128, L], BF16, kind="Internal"))
    d_hready = ap(nc.dram_tensor("d_hready", [32, 128, S], BF16, kind="Internal"))
    d_h1 = ap(nc.dram_tensor("d_h1", [64, 128, S], F16, kind="Internal"))
    d_wq2 = ap(nc.dram_tensor("d_wq2", [16, 128, HFF], F16, kind="Internal"))
    d_stil = ap(nc.dram_tensor("d_stil", [16, 128, S], BF16, kind="Internal"))
    d_g = ap(nc.dram_tensor("d_g", [16, 128, S], BF16, kind="Internal"))
    d_rows = ap(nc.dram_tensor("d_rows", [4, 2048], F32, kind="Internal"))

    tap_out = {}
    TAPSHAPES = {
        "xqi": ([KD, 128, S], BF16), "gate": ([NCH, 32, 128, L], BF16),
        "xssm": ([NCH, 32, 128, L], BF16), "bsb": ([128, S], BF16),
        "csb": ([128, S], BF16), "dtsp": ([64, S], F32),
        "hready": ([32, 128, S], BF16), "stil": ([16, 128, S], BF16),
        "h1": ([64, 128, S], F16), "g": ([16, 128, S], BF16),
        "rows": ([4, 2048], F32), "carry": ([128, NH * HD], F32),
        "sw": ([1, 4], F32), "xn": ([2, 128, KD * TH], F16),
        "hq0": ([2, 128, 32 * TQ], F16), "sch0": ([2, TQ], F32),
    }
    for t in taps:
        if t not in TAPSHAPES:
            continue
        shp, dt = TAPSHAPES[t]
        tap_out[t] = ap(nc.dram_tensor("tap_" + t, shp, dt, kind="ExternalOutput"))

    def arena(name, nbytes):
        return stack.enter_context(nc.sbuf_tensor(name, [128, nbytes // 2], BF16))

    AR1 = arena("AR1", 32768)
    AR2 = arena("AR2", 32768)
    AR3 = arena("AR3", 41472)
    AR4 = arena("AR4", 22528)
    AR5 = arena("AR5", 16384)
    AR7 = arena("AR7", 30720)
    AR8 = arena("AR8", 25088)

    def V(at, byte_off, p0, pcnt, shape_free, dtype=BF16):
        esz = 4 if dtype == F32 else 2
        n = 1
        for d in shape_free:
            n *= int(d)
        assert byte_off % 4 == 0
        a = at[p0:p0 + pcnt, byte_off // 2: byte_off // 2 + (esz * n) // 2]
        if dtype != BF16:
            a = a.bitcast(dtype)
        if len(shape_free) > 1:
            names = " ".join(f"d{i}" for i in range(len(shape_free)))
            a = a.rearrange(f"p ({names}) -> p {names}",
                            **{f"d{i}": int(shape_free[i]) for i in range(len(shape_free))})
        return a

    c_idn = stack.enter_context(nc.sbuf_tensor("c_idn", [128, 128], F32))
    c_idnb = stack.enter_context(nc.sbuf_tensor("c_idnb", [128, 128], BF16))
    c_idnh = stack.enter_context(nc.sbuf_tensor("c_idnh", [128, 128], F16))
    c_ones = stack.enter_context(nc.sbuf_tensor("c_ones", [128, 128], F32))
    c_mask = stack.enter_context(nc.sbuf_tensor("c_mask", [128, 128], F32))
    c_convw = stack.enter_context(nc.sbuf_tensor("c_convw", [128, 136], F32))
    c_convb = stack.enter_context(nc.sbuf_tensor("c_convb", [128, 34], F32))
    c_dpp = stack.enter_context(nc.sbuf_tensor("c_dpp", [128, 32], F32))
    c_nwpp = stack.enter_context(nc.sbuf_tensor("c_nwpp", [128, 32], F32))
    c_dtb = stack.enter_context(nc.sbuf_tensor("c_dtb", [64, 1], F32))
    c_alog = stack.enter_context(nc.sbuf_tensor("c_alog", [64, 1], F32))
    c_acol = stack.enter_context(nc.sbuf_tensor("c_acol", [64, 1], F32))
    c_tails = stack.enter_context(nc.sbuf_tensor("c_tails", [128, 102], F32))
    c_sw = stack.enter_context(nc.sbuf_tensor("c_sw", [1, 4], F32))
    c_swcol = stack.enter_context(nc.sbuf_tensor("c_swcol", [128, 2], F32))
    c_wacc = stack.enter_context(nc.sbuf_tensor("c_wacc", [128, 2], F32))
    c_wred = stack.enter_context(nc.sbuf_tensor("c_wred", [128, 1], F32))
    c_eps = stack.enter_context(nc.sbuf_tensor("c_eps", [1, 2], F32))

    xn_half = V(AR1, 0, 0, 128, (KD, TH), F16)
    hready_asm = [V(AR1, i * 8192, 0, 128, (32, L)) for i in range(2)]
    t1st = V(AR1, 16384, 0, 128, (2, L))
    gateg = [V(AR1, 17408 + i * 512, 0, 128, (2, L)) for i in range(2)]
    xssmg = [V(AR1, 19456 + i * 512, 0, 128, (2, L)) for i in range(2)]
    hr4_1 = V(AR1, 0, 0, 128, (16, TH))
    hq6_1 = V(AR1, 0, 0, 128, (32, TQ), F16)

    xqi_half = V(AR2, 0, 0, 128, (KD, TH))
    hr4_2 = V(AR2, 0, 0, 128, (16, TH))
    hq6_2 = V(AR2, 0, 0, 128, (32, TQ), F16)

    carry = V(AR3, 0, 0, 128, (NH * HD,), F32)
    carry_bf = V(AR3, 16384, 0, 128, (NH * HD,))
    xssmT = [V(AR3, 24576 + i * 8192, 0, 128, (INTER,)) for i in range(2)]
    p2wb = [V(AR3, i * 4096, 0, 128, (DM,), F16) for i in range(2)]
    w2q_raw = [V(AR3, i * 8192, 0, 128, (2048,), F32) for i in range(2)]
    w2q_stg = V(AR3, 16384, 0, 128, (4096,), F16)
    w2q_scr = V(AR3, 24576, 0, 128, (2048,), F32)
    wq2b = [V(AR3, i * 16384, 0, 128, (64, 128), F16) for i in range(2)]
    x7 = [V(AR3, i * 8192, 0, 128, (S,), F32) for i in range(2)]

    B_sb = V(AR4, 0, 0, 128, (S,))
    C_sb = V(AR4, 4096, 0, 128, (S,))
    dt_sp = V(AR4, 8192, 0, 64, (S,), F32)
    SC = 16384
    m1sc = V(AR4, SC, 0, 128, (L,), F32)
    edif = [V(AR4, SC + 512 + i * 256, 0, 128, (L,)) for i in range(2)]
    mtt = [V(AR4, SC + 1024 + i * 256, 0, 128, (L,)) for i in range(2)]
    cexp = [V(AR4, SC + 1536 + i * 256, 0, 128, (L,)) for i in range(2)]
    scoresM = V(AR4, SC + 2048, 0, 128, (L,))
    dsT_c = V(AR4, SC + 2304, 0, 128, (NH,), F32)
    cd_c = V(AR4, SC + 2560, 0, 128, (NH,), F32)
    ecs_c = V(AR4, SC + 2816, 0, 64, (L,), F32)
    cs_c = V(AR4, SC + 3328, 0, 64, (L,), F32)
    a_c = V(AR4, SC + 3840, 0, 64, (L,), F32)
    csT_c = V(AR4, SC + 4352, 0, 128, (NH,), F32)
    dtT_c = V(AR4, SC + 4608, 0, 128, (NH,), F32)
    B_t = V(AR4, SC + 4864, 0, 128, (DST,))
    xdtds_g = [V(AR4, SC + 5120 + i * 512, 0, 128, (4, HD)) for i in range(2)]
    spa = V(AR4, SC, 0, 64, (TH,), F32)
    dtrelu = V(AR3, 8192, 0, 64, (TH,), F32)

    xstg = [V(AR5, i * 4096, 0, 128, (TH,), F32) for i in range(2)]
    x2stg = [V(AR5, 8192 + i * 4096, 0, 128, (TH,), F32) for i in range(2)]
    cacc = [V(AR5, i * 4096, 0, 128, (TH,), F32) for i in range(2)]
    sstg = [V(AR5, 8192 + i * 2048, 0, 128, (TH,)) for i in range(2)]
    wob = [V(AR5, i * 8192, 0, 128, (INTER,)) for i in range(2)]
    stil7 = [V(AR5, i * 4096, 0, 128, (S,)) for i in range(2)]
    g7 = [V(AR5, 8192 + i * 4096, 0, 128, (S,)) for i in range(2)]

    w1bf = [[V(AR7, h2 * 4096, 0, 128, (TH,), F32),
             V(AR1, 20480 + h2 * 4096, 0, 128, (TH,), F32)] for h2 in range(2)]
    w1b = [V(AR7, 12288 + i * 4096, 0, 128, (DM,)) for i in range(2)]
    wqscr = V(AR7, 8192, 0, 128, (TH,), F32)
    wq1b = [V(AR7, 12288 + i * 4096, 0, 128, (DM,)) for i in range(2)]
    h1stg = [V(AR7, 20480 + i * 2048, 0, 128, (TH,), F16) for i in range(2)]
    xqstg = [V(AR7, 20480 + i * 2048, 0, 128, (TH,)) for i in range(2)]
    mxcols = V(AR7, 24576, 0, 128, (8,), F32)
    invsbc = V(AR7, 26624, 0, 128, (TH,), F32)
    vacc4 = V(AR7, 0, 0, 128, (TH,), F32)
    stil4 = [V(AR7, 4096 + i * 2048, 0, 128, (TH,)) for i in range(2)]
    sq4 = [V(AR7, 8192 + i * 2048, 0, 128, (TH,)) for i in range(2)]
    gst6 = [V(AR7, i * 1024, 0, 128, (TQ,)) for i in range(2)]
    schbc = V(AR7, 4096, 0, 128, (TQ,), F32)
    i2bc = V(AR7, 6144, 0, 128, (TQ,), F32)
    rmq = V(AR7, 8192, 0, 128, (TQ,), F32)
    rmqn = V(AR7, 14400, 0, 128, (TQ,), F32)
    t6scr = [V(AR7, 10240 + i * 2048, 0, 128, (TQ,), F32) for i in range(2)]
    mx6 = V(AR7, 14336, 0, 128, (4,), F32)
    rmq1 = V(AR7, 20480, 0, 128, (TH,), F32)
    rmq1n = V(AR7, 24640, 0, 128, (TH,), F32)
    ostg7 = [V(AR7, i * 8192, 0, 128, (S,), F32) for i in range(2)]
    t7a = V(AR7, 16384, 0, 128, (TQ,))
    t7b = V(AR7, 17408, 0, 128, (TQ,))

    rowM = V(AR8, 0, 0, 1, (TH,), F32)
    rowR = V(AR8, 4096, 0, 1, (TH,), F32)
    rowS = V(AR8, 8192, 0, 1, (TH,), F32)
    csf = [V(AR8, i * 4096, 0, 1, (TH,), F32) for i in range(2)]
    ecsf = [V(AR8, 8192 + i * 4096, 0, 1, (TH,), F32) for i in range(2)]
    cslf = V(AR8, 16384, 0, 1, (NH,), F32)
    rowB = V(AR8, 16640, 0, 1, (S,), F32)
    inv1h = [V(AR8, 16640 + i * 4096, 0, 1, (TH,), F32) for i in range(2)]
    r7 = V(AR8, 0, 0, 1, (S,), F32)
    r7b = V(AR8, 8192, 0, 1, (S,), F32)
    r7c = V(AR8, 16640, 0, 1, (S,), F32)
    row6 = [V(AR8, i * 2048, 0, 1, (TQ,), F32) for i in range(4)]

    PS = [stack.enter_context(nc.psum_tensor(f"ps{i}", [128, 512], F32)) for i in range(8)]

    def PSb(i, lo, n):
        return PS[i][:, :].bitcast(BF16)[:, lo:lo + n]

    def PSh(i, lo, n):
        return PS[i][:, :].bitcast(F16)[:, lo:lo + n]

    _semd = {}

    def Q(name):
        if name not in _semd:
            _semd[name] = Sem(stack, nc, "q_" + name)
        return _semd[name]

    st = Sem(stack, nc, "st")
    sv = Sem(stack, nc, "sv")
    ss = Sem(stack, nc, "ss")
    sg = Sem(stack, nc, "sg")
    P.eng_sems = {"t": st, "v": sv, "s": ss, "g": sg}

    E = P.emit

    def row_assemble_dst(row_i, t0, ncols):
        base = d_rows[row_i:row_i + 1, t0:t0 + 1]
        return bass.AP(base.tensor, base.offset, [[1, 128], [128, ncols]])

    def init_consts():
        qc = Q("c")
        for dst, src in ((c_idn[:, :], idn), (c_ones[:, :], onesd), (c_mask[:, :], maskd),
                         (c_convw[:, :], convw), (c_convb[:, :], convb),
                         (c_dpp[:, :], dpp), (c_nwpp[:, :], nwpp),
                         (c_dtb[:, :], dtb), (c_alog[:, :], alog)):
            E("q", lambda e, d=dst, s=src: e.dma_start(d, s), inc=[(qc, 16)])
        ldn = (qc, qc.n)
        D["consts"] = ldn
        for ek in ("t", "v", "s", "g"):
            E(ek, lambda e: e.nop(), wait=[ldn])
        E("v", lambda e: e.tensor_copy(c_idnb[:, :], c_idn[:, :]),
          wait=[ldn], inc=[(sv, 1)])
        E("v", lambda e: e.tensor_copy(c_idnh[:, :], c_idn[:, :]), inc=[(sv, 1)])
        E("v", lambda e: e.memset(c_tails[:, :], 0.0), inc=[(sv, 1)])
        E("v", lambda e: e.memset(c_wacc[:, :], 0.0), inc=[(sv, 1)])
        E("v", lambda e: e.memset(c_eps[0:1, 0:1], LNEPS), inc=[(sv, 1)])
        E("v", lambda e: e.memset(c_eps[0:1, 1:2], RMSEPS), inc=[(sv, 1)])
        E("s", lambda e: e.activation(c_acol[:, :], c_alog[:, :], AF.Exp, scale=1.0),
          wait=[ldn], inc=[(ss, 1)])
        E("v", lambda e: e.tensor_scalar_mul(c_acol[:, :], c_acol[:, :], -1.0),
          wait=[(ss, ss.n)], inc=[(sv, 1)])
        D["init_v"] = sv.n

    def p0_dma(lo, hi):
        for i in range(lo, hi):
            if i < 128:
                src = w1[i // 2][:, (i % 2) * 1024:(i % 2 + 1) * 1024]
            else:
                j = i - 128
                src = w2[j // 8][:, (j % 8) * 1024:(j % 8 + 1) * 1024]
            qq = Q(f"wa{i % 2}")
            wv = [(sv, D.get(("p0r", i - 2), 0))]
            E("g", lambda e, d=w1bf[i % 2][0][:, :], s=src: e.dma_start(d, s),
              wait=wv, inc=[(qq, 16)])
            col = 0 if i < 128 else 1
            first = (i == 0 or i == 128)
            E("v", lambda e, b=w1bf[i % 2][0][:, :]: e.tensor_reduce(
                c_wred[:, :], b, AX, ALU.add, apply_absolute_value=True),
                wait=[(qq, qq.n), (sv, D["init_v"])], inc=[(sv, 1)])
            if first:
                E("v", lambda e, c=col: e.tensor_copy(c_wacc[:, c:c + 1], c_wred[:, :]),
                  inc=[(sv, 1)])
            else:
                E("v", lambda e, c=col: e.tensor_add(
                    c_wacc[:, c:c + 1], c_wacc[:, c:c + 1], c_wred[:, :]), inc=[(sv, 1)])
            D[("p0r", i)] = sv.n

    def p1_half(hf):
        t0 = hf * TH
        entry = [(sv, sv.n), (ss, ss.n), (Q("pa0"), Q("pa0").n),
                 (Q("pa1"), Q("pa1").n)]
        if "xntap0" in D:
            entry.append(D["xntap0"])
        for k in range(KD):
            qq = Q(f"a{k % 2}")
            wv = [(st, D.get(("p1mm", hf, k - 2), 0)), (ss, D.get(("p1sq", hf, k - 2), 0))]
            if k < 2:
                wv += entry
            E("q", lambda e, kk=k, b=xstg[k % 2]: e.dma_start(
                b[:, :], xt[kk * 128:(kk + 1) * 128, t0:t0 + TH]), wait=wv, inc=[(qq, 16)])
            ldn = (qq, qq.n)
            E("s", lambda e, b=xstg[k % 2], o=x2stg[k % 2]: e.activation(
                o[:, :], b[:, :], AF.Square, scale=1.0),
                wait=[ldn, (st, D.get(("p1mm2", hf, k - 2), 0))] +
                (entry if k < 2 else []), inc=[(ss, 1)])
            D[("p1sq", hf, k)] = ss.n
            sqn = ss.n
            for t4 in range(2):
                E("t", lambda e, b=xstg[k % 2], t4=t4, kk=k: e.matmul(
                    PS[t4][0:1, :], c_ones[:, 0:1], b[:, t4 * 512:(t4 + 1) * 512],
                    start=(kk == 0), stop=(kk == KD - 1)),
                    wait=([ldn] + (entry if k == 0 else [])) if t4 == 0 else [],
                    inc=[(st, 1)] if t4 == 1 else [])
            D[("p1mm", hf, k)] = st.n
            for t4 in range(2):
                E("t", lambda e, b=x2stg[k % 2], t4=t4, kk=k: e.matmul(
                    PS[2 + t4][0:1, :], c_ones[:, 0:1], b[:, t4 * 512:(t4 + 1) * 512],
                    start=(kk == 0), stop=(kk == KD - 1)),
                    wait=([(ss, sqn)] + (entry if k == 0 else [])) if t4 == 0 else [],
                    inc=[(st, 1)] if t4 == 1 else [])
            D[("p1mm2", hf, k)] = st.n
        fin = st.n
        E("v", lambda e: e.tensor_scalar(rowM[0:1, 0:512], PS[0][0:1, :], 1.0 / DM, None,
                                         ALU.mult), wait=[(st, fin)], inc=[(sv, 1)])
        E("v", lambda e: e.tensor_scalar(rowM[0:1, 512:1024], PS[1][0:1, :], 1.0 / DM,
                                         None, ALU.mult), inc=[(sv, 1)])
        E("v", lambda e: e.tensor_scalar(rowR[0:1, 0:512], PS[2][0:1, :], 1.0 / DM, None,
                                         ALU.mult), inc=[(sv, 1)])
        E("v", lambda e: e.tensor_scalar(rowR[0:1, 512:1024], PS[3][0:1, :], 1.0 / DM,
                                         None, ALU.mult), inc=[(sv, 1)])
        E("v", lambda e: e.tensor_mul(rowS[0:1, :], rowM[0:1, :], rowM[0:1, :]),
          inc=[(sv, 1)])
        E("v", lambda e: e.tensor_sub(rowR[0:1, :], rowR[0:1, :], rowS[0:1, :]),
          inc=[(sv, 1)])
        E("v", lambda e: e.tensor_scalar_add(rowS[0:1, :], rowR[0:1, :],
                                             c_eps[0:1, 0:1]),
          wait=[(sv, sv.n), (sv, D["init_v"])], inc=[(sv, 1)])
        E("s", lambda e: e.activation(rowR[0:1, :], rowS[0:1, :], AF.Sqrt, scale=1.0),
          wait=[(sv, sv.n)], inc=[(ss, 1)])
        E("v", lambda e: e.reciprocal(rowR[0:1, :], rowR[0:1, :]),
          wait=[(ss, ss.n)], inc=[(sv, 1)])
        E("v", lambda e: e.tensor_mul(ecsf[1][0:1, :], rowR[0:1, :], rowR[0:1, :]),
          inc=[(sv, 1)])
        E("v", lambda e: e.tensor_mul(ecsf[1][0:1, :], ecsf[1][0:1, :], rowS[0:1, :]),
          inc=[(sv, 1)])
        E("v", lambda e: e.tensor_scalar(ecsf[1][0:1, :], ecsf[1][0:1, :], -0.5, 1.5,
                                         ALU.mult, ALU.add), inc=[(sv, 1)])
        E("v", lambda e: e.tensor_mul(rowR[0:1, :], rowR[0:1, :], ecsf[1][0:1, :]),
          inc=[(sv, 1)])
        stn = sv.n
        if "lnstats" in taps:
            E("q", lambda e, t0=t0: e.dma_start(d_rows[2:3, t0:t0 + TH], rowM[0:1, :]),
              wait=[(sv, stn)], inc=[(Q(f"ls{hf}"), 16)])
            E("q", lambda e, t0=t0: e.dma_start(d_rows[3:4, t0:t0 + TH], rowR[0:1, :]),
              wait=[(sv, stn)], inc=[(Q(f"ls{hf}b"), 16)])
        for t4 in range(2):
            E("t", lambda e, t4=t4: e.matmul(PS[4 + t4][:, :], c_ones[0:1, :],
                                             rowM[0:1, t4 * 512:(t4 + 1) * 512],
                                             start=True, stop=True),
              wait=[(sv, stn)], inc=[(st, 1)])
            E("t", lambda e, t4=t4: e.matmul(PS[6 + t4][:, :], c_ones[0:1, :],
                                             rowR[0:1, t4 * 512:(t4 + 1) * 512],
                                             start=True, stop=True), inc=[(st, 1)])
        bcn = st.n
        for k in range(KD):
            qq = Q(f"a{k % 2}")
            E("q", lambda e, kk=k, b=xstg[k % 2]: e.dma_start(
                b[:, :], xt[kk * 128:(kk + 1) * 128, t0:t0 + TH]),
                wait=[(sv, D.get(("p1xn", hf, k - 2), 0)), (st, fin)], inc=[(qq, 16)])
            ldn = (qq, qq.n)
            for t4 in range(2):
                sl = slice(t4 * 512, (t4 + 1) * 512)
                E("v", lambda e, b=xstg[k % 2], o=x2stg[k % 2], t4=t4, sl=sl: e.tensor_sub(
                    o[:, sl], b[:, sl], PS[4 + t4][:, :]),
                    wait=[ldn, (st, bcn)], inc=[(sv, 1)])
                E("v", lambda e, o=x2stg[k % 2], kk=k, t4=t4, sl=sl: e.tensor_mul(
                    xn_half[:, kk, sl], o[:, sl], PS[6 + t4][:, :]), inc=[(sv, 1)])
            D[("p1xn", hf, k)] = sv.n

    def p1b_half(hf):
        t0 = hf * TH
        spg = [(Q("sa0"), Q("sa0").n), (Q("sa1"), Q("sa1").n)]
        E("v", lambda e: e.tensor_copy(rmq1[:, :], xn_half[:, 0, :]),
          wait=spg, inc=[(sv, 1)])
        E("v", lambda e: e.tensor_copy(rmq1n[:, :], xn_half[:, 0, :]),
          wait=spg, inc=[(sv, 1)])
        for k in range(1, KD):
            E("v", lambda e, kk=k: e.tensor_max(rmq1[:, :], rmq1[:, :],
                                                xn_half[:, kk, :]), inc=[(sv, 1)])
            E("v", lambda e, kk=k: e.tensor_tensor(rmq1n[:, :], rmq1n[:, :],
                                                   xn_half[:, kk, :], ALU.min),
              inc=[(sv, 1)])
        E("v", lambda e: e.tensor_scalar_mul(rmq1n[:, :], rmq1n[:, :], -1.0),
          inc=[(sv, 1)])
        E("v", lambda e: e.tensor_max(rmq1[:, :], rmq1[:, :], rmq1n[:, :]),
          inc=[(sv, 1)])
        amx = sv.n
        for i in range(8):
            E("t", lambda e, i=i: e.transpose(PS[i % 2][:, 0:128],
                                              rmq1[:, i * 128:(i + 1) * 128], c_idn[:, :]),
              wait=[(sv, amx if i < 2 else D[("p1t", hf, i - 2)])], inc=[(st, 1)])
            E("v", lambda e, i=i: e.reduce_max(mxcols[:, i:i + 1], PS[i % 2][:, 0:128], AX),
              wait=[(st, st.n)], inc=[(sv, 1)])
            D[("p1t", hf, i)] = sv.n
        qq = Q(f"r1{hf}")
        E("q", lambda e, dst=row_assemble_dst(0, t0, 8): e.dma_start(
            dst, mxcols[:, :], allow_slow_non_contiguous=True),
            wait=[(sv, sv.n)], inc=[(qq, 16)])
        E("q", lambda e: e.dma_start(rowS[0:1, :], d_rows[0:1, t0:t0 + TH]),
          wait=[(qq, qq.n)], inc=[(qq, 16)])
        E("v", lambda e: e.tensor_scalar_max(rowS[0:1, :], rowS[0:1, :], QEPS),
          wait=[(qq, qq.n)], inc=[(sv, 1)])
        E("v", lambda e, o=t0: e.tensor_copy(rowB[0:1, o:o + TH], rowS[0:1, :]),
          inc=[(sv, 1)])
        E("v", lambda e: e.reciprocal(rowM[0:1, :], rowS[0:1, :]), inc=[(sv, 1)])
        E("v", lambda e: e.tensor_scalar_mul(rowM[0:1, :], rowM[0:1, :], 127.0),
          inc=[(sv, 1)])
        for t4 in range(2):
            E("t", lambda e, t4=t4: e.matmul(PS[6 + t4][:, :], c_ones[0:1, :],
                                             rowM[0:1, t4 * 512:(t4 + 1) * 512],
                                             start=True, stop=True),
              wait=[(sv, sv.n)], inc=[(st, 1)])
        bc = st.n
        for k in range(KD):
            qq = Q(f"sa{k % 2}")
            for t4 in range(2):
                sl = slice(t4 * 512, (t4 + 1) * 512)
                E("v", lambda e, kk=k, t4=t4, sl=sl: e.tensor_mul(
                    x2stg[kk % 2][:, sl], xn_half[:, kk, sl], PS[6 + t4][:, :]),
                    wait=[(st, bc)] + ([D[("p1q", hf, k - 2)]] if ("p1q", hf, k - 2) in D
                                       else ([D[("p1q", 0, 14 + (k % 2))]] if hf == 1 else [])),
                    inc=[(sv, 1)])
            E("v", lambda e, kk=k: e.tensor_scalar(
                xqstg[kk % 2][:, :], x2stg[kk % 2][:, :], MAGIC, MAGIC,
                ALU.add, ALU.subtract), inc=[(sv, 1)])
            E("q", lambda e, kk=k: e.dma_start(d_xqi[kk, :, t0:t0 + TH],
                                               xqstg[kk % 2][:, :]),
              wait=[(sv, sv.n)], inc=[(qq, 16)])
            D[("p1q", hf, k)] = (qq, qq.n)

    def p2_half(hf):
        t0 = hf * TH
        for o in (range(NO2) if p2_os is None else p2_os):
            grp = o % 2
            pg = [PS[2 * grp], PS[2 * grp + 1]]
            qq = Q(f"wb{o % 2}")
            E("g", lambda e, oo=o, b=p2wb[o % 2]: e.dma_start(b[:, :], win[oo]),
              wait=[(st, D.get(("p2mm", hf, o - 2), 0))], inc=[(qq, 16)])
            wv = [(qq, qq.n), (sv, D.get(("p2ev", hf, o - 2), 0)),
                  (ss, D.get(("p2es", hf, o - 2), 0)),
                  (sv, D.get(("p1bend", hf), 0))]
            for k in range(KD):
                for t2 in range(2):
                    E("t", lambda e, kk=k, t2=t2, b=p2wb[o % 2], pg=pg: e.matmul(
                        pg[t2][:, :], b[:, kk * 128:(kk + 1) * 128],
                        xn_half[:, kk, t2 * 512:(t2 + 1) * 512],
                        start=(kk == 0), stop=(kk == KD - 1)),
                        wait=wv if (k == 0 and t2 == 0) else [],
                        inc=[(st, 1)] if (k == KD - 1 and t2 == 1) else [])
            D[("p2mm", hf, o)] = st.n
            mmn = st.n
            if o < 32:
                qs2 = Q(f"pa{o % 2}")
                for t2 in range(2):
                    E("s", lambda e, t2=t2, b=sstg[o % 2], pg=pg: e.activation(
                        b[:, t2 * 512:(t2 + 1) * 512], pg[t2][:, :], AF.Silu, scale=1.0),
                        wait=[(st, mmn)] + ([D[("p2dma", hf, o - 2)]]
                                            if ("p2dma", hf, o - 2) in D else []),
                        inc=[(ss, 1)])
                D[("p2es", hf, o)] = ss.n
                D[("p2ev", hf, o)] = sv.n
                dst = d_gate[hf * 8:(hf + 1) * 8, o].transpose([1, 0, 2])
                E("q", lambda e, b=sstg[o % 2], dst=dst: e.dma_start(
                    dst, b[:, :].rearrange("p (c l) -> p c l", c=8)),
                    wait=[(ss, ss.n)], inc=[(qs2, 16)])
                D[("p2dma", hf, o)] = (qs2, qs2.n)
            elif o < 66:
                cc = o - 32
                acc = cacc[o % 2]
                for t2 in range(2):
                    E("v", lambda e, t2=t2, acc=acc, cc=cc, pg=pg: e.tensor_scalar_mul(
                        acc[:, t2 * 512:(t2 + 1) * 512], pg[t2][:, :],
                        c_convw[:, cc * 4 + 3:cc * 4 + 4]),
                        wait=[(st, mmn), (ss, D.get(("p2es", hf, o - 2), 0))],
                        inc=[(sv, 1)])
                for sft in (1, 2, 3):
                    kk = 3 - sft
                    wc = c_convw[:, cc * 4 + kk:cc * 4 + kk + 1]
                    if hf == 1:
                        E("v", lambda e, sft=sft, acc=acc, cc=cc, wc=wc:
                          e.scalar_tensor_tensor(
                              acc[:, 0:sft], c_tails[:, cc * 3 + 3 - sft:cc * 3 + 3], wc,
                              acc[:, 0:sft], ALU.mult, ALU.add), inc=[(sv, 1)])
                    for t2 in range(2):
                        o_lo = max(sft, t2 * 512)
                        o_hi = (t2 + 1) * 512
                        i_lo = o_lo - sft
                        while o_lo < o_hi:
                            sb = i_lo // 512
                            i_in = i_lo % 512
                            n = min(o_hi - o_lo, 512 - i_in)
                            E("v", lambda e, acc=acc, o_lo=o_lo, n=n, sb=sb, i_in=i_in,
                              wc=wc, pg=pg: e.scalar_tensor_tensor(
                                  acc[:, o_lo:o_lo + n], pg[sb][:, i_in:i_in + n], wc,
                                  acc[:, o_lo:o_lo + n], ALU.mult, ALU.add),
                                inc=[(sv, 1)])
                            o_lo += n
                            i_lo += n
                if hf == 0:
                    E("v", lambda e, cc=cc, pg=pg: e.tensor_copy(
                        c_tails[:, cc * 3:cc * 3 + 3], pg[1][:, 509:512]), inc=[(sv, 1)])
                D[("p2ev", hf, o)] = sv.n
                evn = sv.n
                if cc < 32:
                    qs2 = Q(f"pa{o % 2}")
                    for t2 in range(2):
                        E("s", lambda e, t2=t2, b=sstg[o % 2], acc=acc, cc=cc: e.activation(
                            b[:, t2 * 512:(t2 + 1) * 512], acc[:, t2 * 512:(t2 + 1) * 512],
                            AF.Silu, bias=c_convb[:, cc:cc + 1], scale=1.0),
                            wait=[(sv, evn)] + ([D[("p2dma", hf, o - 2)]]
                                                if ("p2dma", hf, o - 2) in D else []),
                            inc=[(ss, 1)])
                    D[("p2es", hf, o)] = ss.n
                    dst = d_xssm[hf * 8:(hf + 1) * 8, cc].transpose([1, 0, 2])
                    E("q", lambda e, b=sstg[o % 2], dst=dst: e.dma_start(
                        dst, b[:, :].rearrange("p (c l) -> p c l", c=8)),
                        wait=[(ss, ss.n)], inc=[(qs2, 16)])
                    D[("p2dma", hf, o)] = (qs2, qs2.n)
                else:
                    dstv = B_sb if cc == 32 else C_sb
                    for t2 in range(2):
                        E("s", lambda e, t2=t2, dv=dstv, acc=acc, cc=cc: e.activation(
                            dv[:, t0 + t2 * 512:t0 + (t2 + 1) * 512],
                            acc[:, t2 * 512:(t2 + 1) * 512],
                            AF.Silu, bias=c_convb[:, cc:cc + 1], scale=1.0),
                            wait=[(sv, evn)], inc=[(ss, 1)])
                    D[("p2es", hf, o)] = ss.n
            else:
                for t2 in range(2):
                    E("v", lambda e, t2=t2, pg=pg: e.tensor_scalar_add(
                        dt_sp[:, t0 + t2 * 512:t0 + (t2 + 1) * 512], pg[t2][0:64, :],
                        c_dtb[:, :]), wait=[(st, mmn)], inc=[(sv, 1)])
                dts = dt_sp[:, t0:t0 + TH]
                E("s", lambda e, d2=dts: e.activation(spa[:, :], d2, AF.Abs, scale=1.0),
                  wait=[(sv, sv.n)], inc=[(ss, 1)])
                E("s", lambda e: e.activation(spa[:, :], spa[:, :], AF.Exp, scale=-1.0),
                  inc=[(ss, 1)])
                E("s", lambda e: e.activation(spa[:, :], spa[:, :], AF.Ln,
                                              bias=c_ones[0:64, 0:1], scale=1.0),
                  inc=[(ss, 1)])
                E("v", lambda e, d2=dts: e.tensor_scalar_max(dtrelu[:, :], d2, 0.0),
                  wait=[(ss, ss.n)], inc=[(sv, 1)])
                E("v", lambda e, d2=dts: e.tensor_add(d2, spa[:, :], dtrelu[:, :]),
                  inc=[(sv, 1)])
                D[("p2ev", hf, o)] = sv.n
                D[("dt", hf)] = sv.n

    def p0_finish():
        E("v", lambda e: e.memset(carry[:, :], 0.0),
          wait=[(st, D[("p2mm", 1, NO2 - 1)])], inc=[(sv, 1)])
        E("v", lambda e: e.memset(carry_bf[:, :], 0.0), inc=[(sv, 1)])
        E("t", lambda e: e.matmul(PS[4][0:2, 0:2], c_ones[:, 0:2], c_wacc[:, :],
                                  start=True, stop=True),
          wait=[(sv, sv.n)], inc=[(st, 1)])
        E("v", lambda e: e.tensor_scalar(c_sw[0:1, 0:2], PS[4][0:1, 0:2],
                                         1.0 / (HFF * DM), QEPS, ALU.mult, ALU.max),
          wait=[(st, st.n)], inc=[(sv, 1)])
        E("v", lambda e: e.reciprocal(c_sw[0:1, 2:4], c_sw[0:1, 0:2]), inc=[(sv, 1)])
        E("t", lambda e: e.matmul(PS[5][:, 0:2], c_ones[0:1, :], c_sw[0:1, 2:4],
                                  start=True, stop=True), wait=[(sv, sv.n)], inc=[(st, 1)])
        E("v", lambda e: e.tensor_copy(c_swcol[:, :], PS[5][:, 0:2]),
          wait=[(st, st.n)], inc=[(sv, 1)])
        for hf in range(2):
            E("v", lambda e, hf=hf: e.tensor_scalar(
                inv1h[hf][0:1, :], rowB[0:1, hf * TH:(hf + 1) * TH],
                c_sw[0:1, 0:1], 1.0 / 127.0, ALU.mult, ALU.mult), inc=[(sv, 1)])
        D["p0fin"] = sv.n

    def p3_loads(c):
        par = c % 2
        qq = Q(f"ta{par}")
        src = d_xssm[c].rearrange("a p l -> (a p) l")
        E("s", lambda e, par=par, src=src: e.dma_start(xssmT[par][:, :], src,
                                                       transpose=True),
          wait=[(sv, D.get(("xtfree", c - 2), 0))] + D["p2spill"],
          inc=[(qq, 16)])
        D[("xtq", c)] = (qq, qq.n)

    def p3_group_loads(c, g4):
        qq1 = Q(f"ga{g4 % 2}")
        qq2 = Q(f"gb{g4 % 2}")
        wv = [(sv, D.get(("grpfree", c, g4 - 2), 0)) if g4 >= 2 else
              (sv, D.get(("grpfree", c - 1, g4 + 14), 0)),
              (st, D.get("p2end_t", 0))] + \
            ([D["xntap1"]] if "xntap1" in D else [])
        E("q", lambda e, c=c, g4=g4: e.dma_start(
            gateg[g4 % 2][:, :, :], d_gate[c, 2 * g4:2 * g4 + 2].transpose([1, 0, 2])),
            wait=wv + D["p2spill"], inc=[(qq1, 16)])
        E("q", lambda e, c=c, g4=g4: e.dma_start(
            xssmg[g4 % 2][:, :, :], d_xssm[c, 2 * g4:2 * g4 + 2].transpose([1, 0, 2])),
            wait=wv, inc=[(qq2, 16)])
        D[("grpq", c, g4)] = [(qq1, qq1.n), (qq2, qq2.n)]

    def p3_chunk_prelude(c):
        par = c % 2
        E("v", lambda e, c=c: e.tensor_scalar_mul(a_c[:, :], dt_sp[:, c * L:(c + 1) * L],
                                                  c_acol[:, :]),
          wait=[(sv, D.get(("dt", c // 8), 0)),
                (ss, D.get(("presfree", c - 1), 0))], inc=[(sv, 1)])
        E("v", lambda e: e.tensor_tensor_scan(cs_c[:, :], c_ones[0:64, :], a_c[:, :],
                                              0.0, ALU.mult, ALU.add), inc=[(sv, 1)])
        csn = sv.n
        E("s", lambda e: e.activation(ecs_c[:, :], cs_c[:, :], AF.Exp, scale=1.0),
          wait=[(sv, csn)], inc=[(ss, 1)])
        ecn = ss.n
        wv_ps = [(sv, D.get("w1evlast", 0))]
        E("t", lambda e: e.transpose(PS[6][:, 0:64], cs_c[:, :], c_idn[0:64, 0:64]),
          wait=[(sv, csn)] + wv_ps, inc=[(st, 1)])
        E("v", lambda e: e.tensor_copy(csT_c[:, :], PS[6][:, 0:64]),
          wait=[(st, st.n)], inc=[(sv, 1)])
        E("t", lambda e, c=c: e.transpose(PS[7][:, 0:64], dt_sp[:, c * L:(c + 1) * L],
                                          c_idn[0:64, 0:64]), inc=[(st, 1)])
        E("v", lambda e: e.tensor_copy(dtT_c[:, :], PS[7][:, 0:64]),
          wait=[(st, st.n)], inc=[(sv, 1)])
        E("t", lambda e, c=c: e.transpose(PSb(6, 256, 128), B_sb[:, c * L:(c + 1) * L],
                                          c_idnb[:, :]), wait=[(sv, sv.n)], inc=[(st, 1)])
        E("v", lambda e: e.tensor_copy(B_t[:, :], PSb(6, 256, 128)),
          wait=[(st, st.n)], inc=[(sv, 1)])
        E("t", lambda e, c=c: e.matmul(PS[7][:, 64:192], B_sb[:, c * L:(c + 1) * L],
                                       C_sb[:, c * L:(c + 1) * L], start=True, stop=True),
          inc=[(st, 1)])
        E("v", lambda e: e.tensor_mul(scoresM[:, :], PS[7][:, 64:192], c_mask[:, :]),
          wait=[(st, st.n)], inc=[(sv, 1)])
        D[("csready", c)] = (csn, ecn)
        for g8 in range(2):
            qf1, qf2 = Q(f"tb{g8 % 2}"), Q(f"tc{g8 % 2}")
            wv = [(st, D.get(("flfree", c - 1), 0))]
            E("s", lambda e, g8=g8: e.dma_start(csf[g8 % 2][0:1, :],
                                                cs_c[8 * g8:8 * g8 + 8, :]),
              wait=[(sv, csn)] + wv, inc=[(qf1, 16)])
            E("s", lambda e, g8=g8: e.dma_start(ecsf[g8 % 2][0:1, :],
                                                ecs_c[8 * g8:8 * g8 + 8, :]),
              wait=[(ss, ecn)] + wv, inc=[(qf2, 16)])
            D[("flq", c, g8)] = [(qf1, qf1.n), (qf2, qf2.n)]
        qtd = Q("td")
        E("s", lambda e: e.dma_start(cslf[0:1, :], cs_c[:, 127:128]),
          wait=[(sv, csn), (st, D.get(("flfree", c - 1), 0))], inc=[(qtd, 16)])
        E("t", lambda e: e.matmul(PS[7][:, 192:256], c_ones[0:1, :], cslf[0:1, :],
                                  start=True, stop=True),
          wait=[(qtd, qtd.n)], inc=[(st, 1)])
        E("v", lambda e: e.tensor_sub(dsT_c[:, :], PS[7][:, 192:256], csT_c[:, :]),
          wait=[(st, st.n)], inc=[(sv, 1)])
        E("s", lambda e: e.activation(dsT_c[:, :], dsT_c[:, :], AF.Exp, scale=1.0),
          wait=[(sv, sv.n)], inc=[(ss, 1)])
        E("s", lambda e: e.activation(cd_c[:, :], PS[7][:, 192:256], AF.Exp, scale=1.0),
          inc=[(ss, 1)])
        E("v", lambda e, par=par: e.tensor_tensor(
            xssmT[par][:, :].rearrange("p (h d2) -> p h d2", h=NH),
            xssmT[par][:, :].rearrange("p (h d2) -> p h d2", h=NH),
            dtT_c[:, :].unsqueeze(2).broadcast_to([128, NH, HD]), ALU.mult),
            wait=[D[("xtq", c)]], inc=[(sv, 1)])
        D[("pre", c)] = (sv.n, ss.n, st.n)
        D["presv"] = sv.n
        D[("presfree", c)] = ss.n

    def p3_chunk(c):
        par = c % 2
        for g4 in range(16):
            p3_group_loads(c, g4)
            for j in range(4):
                h = g4 * 4 + j
                g8, pr = h // 8, h // 2
                bpar = pr % 2
                if h % 2 == 0:
                    wv = list(D[("flq", c, g8)])
                    key = ("prread", c, pr - 2) if pr >= 2 else ("prread", c - 1, pr + 30)
                    if key in D:
                        wv += [(ss, D[key][0]), (sv, D[key][1])]
                    E("t", lambda e, bpar=bpar, g8=g8, h=h: e.matmul(
                        PS[bpar][:, 0:256], c_ones[0:1, :],
                        csf[g8 % 2][0:1, (h % 8) * 128:(h % 8) * 128 + 256],
                        start=True, stop=True), wait=wv, inc=[(st, 1)])
                    E("t", lambda e, bpar=bpar, g8=g8, h=h: e.matmul(
                        PS[bpar][:, 256:512], c_ones[0:1, :],
                        ecsf[g8 % 2][0:1, (h % 8) * 128:(h % 8) * 128 + 256],
                        start=True, stop=True), inc=[(st, 1)])
                    D[("prbc", c, pr)] = st.n
                E("s", lambda e, bpar=bpar, h=h: e.activation(
                    m1sc[:, :], PS[bpar][:, (h % 2) * 128:(h % 2) * 128 + 128], AF.Relu,
                    bias=csT_c[:, h:h + 1], scale=-1.0),
                    wait=[(st, D[("prbc", c, pr)])], inc=[(ss, 1)])
                E("s", lambda e, h=h: e.activation(edif[h % 2][:, :], m1sc[:, :],
                                                   AF.Exp, scale=-1.0), inc=[(ss, 1)])
                edn = ss.n
                E("v", lambda e, h=h: e.tensor_mul(mtt[h % 2][:, :], edif[h % 2][:, :],
                                                   scoresM[:, :]),
                  wait=[(ss, edn)], inc=[(sv, 1)])
                E("v", lambda e, h=h, bpar=bpar, c=c: e.tensor_mul(
                    cexp[h % 2][:, :], C_sb[:, c * L:(c + 1) * L],
                    PS[bpar][:, 256 + (h % 2) * 128:256 + (h % 2) * 128 + 128]),
                    inc=[(sv, 1)])
                mtn = sv.n
                D[("prread", c, pr)] = (ss.n, sv.n)
                yb = 2 + (g4 % 2)
                pb, cl = 64 * (h % 2), 128 * ((h % 4) // 2)
                wv = [(sv, mtn)]
                key = ("yfree", c, g4 - 2) if g4 >= 2 else ("yfree", c - 1, g4 + 14)
                if key in D:
                    wv.append((sv, D[key]))
                E("t", lambda e, yb=yb, pb=pb, cl=cl, h=h, par=par: e.matmul(
                    PS[yb][pb:pb + 64, cl:cl + 128],
                    xssmT[par][:, h * HD:(h + 1) * HD], mtt[h % 2][:, :],
                    start=True, stop=False), wait=wv, inc=[(st, 1)])
                E("t", lambda e, yb=yb, pb=pb, cl=cl, h=h: e.matmul(
                    PS[yb][pb:pb + 64, cl:cl + 128],
                    carry_bf[:, h * HD:(h + 1) * HD], cexp[h % 2][:, :],
                    start=False, stop=True), inc=[(st, 1)])
                if h % 8 == 6 and h // 8 + 2 <= 7:
                    gnx = h // 8 + 2
                    qf1, qf2 = Q(f"tb{gnx % 2}"), Q(f"tc{gnx % 2}")
                    E("s", lambda e, gnx=gnx: e.dma_start(
                        csf[gnx % 2][0:1, :], cs_c[8 * gnx:8 * gnx + 8, :]),
                        wait=[(st, st.n)], inc=[(qf1, 16)])
                    E("s", lambda e, gnx=gnx: e.dma_start(
                        ecsf[gnx % 2][0:1, :], ecs_c[8 * gnx:8 * gnx + 8, :]),
                        wait=[(st, st.n)], inc=[(qf2, 16)])
                    D[("flq", c, gnx)] = [(qf1, qf1.n), (qf2, qf2.n)]
            E("v", lambda e, par=par, g4=g4: e.tensor_tensor(
                xdtds_g[g4 % 2][:, :, :],
                xssmT[par][:, g4 * 256:(g4 + 1) * 256].rearrange("p (a d2) -> p a d2", a=4),
                dsT_c[:, g4 * 4:g4 * 4 + 4].unsqueeze(2).broadcast_to([128, 4, HD]),
                ALU.mult), wait=[(st, D.get(("grpst", c, g4 - 2),
                                            D.get(("grpst", c - 1, 15), 0)))],
                inc=[(sv, 1)])
            E("t", lambda e, g4=g4: e.matmul(
                PS[2 + (g4 % 2)][:, 256:512], B_t[:, :],
                xdtds_g[g4 % 2][:, :, :].rearrange("p a d2 -> p (a d2)"),
                start=True, stop=True), wait=[(sv, sv.n)], inc=[(st, 1)])
            grp_st = st.n
            D[("grpst", c, g4)] = grp_st
            for j in range(4):
                h = g4 * 4 + j
                yb = 2 + (g4 % 2)
                pb = 64 * (h % 2)
                cl = 128 * ((h % 4) // 2)
                kb = h // 2
                E("v", lambda e, yb=yb, pb=pb, cl=cl, h=h, j=j, g4=g4, kb=kb:
                  e.scalar_tensor_tensor(
                      t1st[pb:pb + 64, (h % 4) // 2, :],
                      xssmg[g4 % 2][pb:pb + 64, j // 2, :],
                      c_dpp[pb:pb + 64, kb:kb + 1],
                      PS[yb][pb:pb + 64, cl:cl + 128], ALU.mult, ALU.add),
                    wait=[(st, grp_st)] + D[("grpq", c, g4)] +
                    ([D[("hrq", c - 2)]] if (g4 == 0 and j == 0 and ("hrq", c - 2) in D)
                     else []),
                    inc=[(sv, 1)])
                E("v", lambda e, par=par, pb=pb, h=h, j=j, g4=g4, kb=kb: e.tensor_mul(
                    hready_asm[par][pb:pb + 64, kb, :],
                    t1st[pb:pb + 64, (h % 4) // 2, :],
                    gateg[g4 % 2][pb:pb + 64, j // 2, :]), inc=[(sv, 1)])
                E("v", lambda e, yb=yb, h=h, j=j: e.scalar_tensor_tensor(
                    carry[:, h * HD:(h + 1) * HD], carry[:, h * HD:(h + 1) * HD],
                    cd_c[:, h:h + 1], PS[yb][:, 256 + j * 64:256 + (j + 1) * 64],
                    ALU.mult, ALU.add), inc=[(sv, 1)])
                E("v", lambda e, h=h: e.tensor_copy(carry_bf[:, h * HD:(h + 1) * HD],
                                                    carry[:, h * HD:(h + 1) * HD]),
                  inc=[(sv, 1)])
            D[("yfree", c, g4)] = sv.n
            D[("grpfree", c, g4)] = sv.n
        D[("xtfree", c)] = sv.n
        D[("flfree", c)] = st.n
        qq = Q(f"ha{c % 2}")
        E("q", lambda e, par=par, c=c: e.dma_start(
            d_hready[:, :, c * L:(c + 1) * L].transpose([1, 0, 2]),
            hready_asm[par][:, :, :]),
            wait=[(sv, sv.n)], inc=[(qq, 16)])
        D[("hrq", c)] = (qq, qq.n)

    def p5_init_half(hf):
        qq = Q("xq")
        wv = [(st, D[("w1mm", 0, 63)])] if hf == 1 else []
        E("q", lambda e: e.dma_start(
            xqi_half[:, :, :], d_xqi[:, :, hf * TH:(hf + 1) * TH].transpose([1, 0, 2])),
            wait=wv + [D[("p1q", hf, KD - 1)], D[("p1q", hf, KD - 2)]], inc=[(qq, 16)])
        D[("xqiq", hf)] = (qq, qq.n)
        for t4 in range(2):
            E("t", lambda e, hf=hf, t4=t4: e.matmul(
                PS[4 + t4][:, :], c_ones[0:1, :],
                inv1h[hf][0:1, t4 * 512:(t4 + 1) * 512], start=True, stop=True),
                wait=[(sv, D["p0fin"]), (sv, D.get("w1evlast", 0))], inc=[(st, 1)])
        bn = st.n
        E("v", lambda e: e.tensor_copy(invsbc[:, 0:512], PS[4][:, :]),
          wait=[(st, bn)], inc=[(sv, 1)])
        E("v", lambda e: e.tensor_copy(invsbc[:, 512:1024], PS[5][:, :]), inc=[(sv, 1)])
        D[("invn", hf)] = sv.n

    def p5_batch(hf, olo, ohi):
        for o in range(olo, ohi):
            ob = o % 2
            wmm_guard = (st, D.get(("w1mm", hf, o - 2),
                                   D.get(("w1mm", 0, 62 + (o % 2)), 0) if hf == 1
                                   else 0))
            for h2 in range(2):
                qq = Q(f"wa{h2}{o % 2}")
                sl = slice(h2 * 1024, (h2 + 1) * 1024)
                bB = w1bf[h2][o % 2]
                E("g", lambda e, o=o, b=bB, sl=sl: e.dma_start(b[:, :], w1[o][:, sl]),
                  wait=[(sv, D.get(("w1qh", hf, o - 2, h2),
                                   D.get(("w1qh", 1 - hf, 62 + o % 2, h2), 0))),
                        (sv, D.get("p0done", 0))],
                  inc=[(qq, 16)])
                wn = (qq, qq.n)
                E("v", lambda e, b=bB: e.tensor_scalar(
                    wqscr[:, :], b[:, :], c_swcol[:, 0:1], CLIP1, ALU.mult, ALU.min),
                    wait=[wn, (sv, D["p0fin"]), wmm_guard], inc=[(sv, 1)])
                E("v", lambda e: e.tensor_scalar(wqscr[:, :], wqscr[:, :], -CLIP1, MAGIC,
                                                 ALU.max, ALU.add), inc=[(sv, 1)])
                E("v", lambda e, o=o, sl=sl: e.tensor_scalar(
                    wq1b[o % 2][:, sl], wqscr[:, :], MAGIC, None, ALU.subtract),
                    inc=[(sv, 1)])
                D[("w1qh", hf, o, h2)] = sv.n
            D[("w1q", hf, o)] = sv.n
            wv = [(sv, D[("w1q", hf, o)]), D[("xqiq", hf)], (sv, D["presv"]),
                  (sv, D.get(("w1ev", hf, o - 2), 0))]
            if o < 2:
                wv.append((sv, D[("invn", hf)]))
            pg = [PS[4 + 2 * (o % 2)], PS[5 + 2 * (o % 2)]]
            for k in range(KD):
                for t2 in range(2):
                    E("t", lambda e, o=o, k=k, t2=t2, pg=pg: e.matmul(
                        pg[t2][:, :], wq1b[o % 2][:, k * 128:(k + 1) * 128],
                        xqi_half[:, k, t2 * 512:(t2 + 1) * 512],
                        start=(k == 0), stop=(k == KD - 1)),
                        wait=wv if (k == 0 and t2 == 0) else [],
                        inc=[(st, 1)] if (k == KD - 1 and t2 == 1) else [])
            D[("w1mm", hf, o)] = st.n
            mmn = st.n
            qh = Q(f"h1{o % 2}")
            for t2 in range(2):
                E("v", lambda e, o=o, t2=t2, pg=pg: e.tensor_mul(
                    h1stg[o % 2][:, t2 * 512:(t2 + 1) * 512], pg[t2][:, :],
                    invsbc[:, t2 * 512:(t2 + 1) * 512]),
                    wait=[(st, mmn), (sv, D[("invn", hf)])] +
                    ([D[("h1q", hf, o - 2)]] if ("h1q", hf, o - 2) in D else
                     ([D[("h1q", 0, 62 + (o % 2))]] if hf == 1 and o < 2 else
                      [(Q("sa0"), Q("sa0").n), (Q("sa1"), Q("sa1").n)])),
                    inc=[(sv, 1)])
            D[("w1ev", hf, o)] = sv.n
            D["w1evlast"] = sv.n
            E("q", lambda e, o=o, hf=hf: e.dma_start(
                d_h1[o, :, hf * TH:(hf + 1) * TH], h1stg[o % 2][:, :]),
                wait=[(sv, sv.n)], inc=[(qh, 16)])
            D[("h1q", hf, o)] = (qh, qh.n)

    def p4_half(hf):
        t0 = hf * TH
        q1, q2 = Q("p40"), Q("p41")
        wv = [D[("hrq", NCH - 1)], D[("hrq", NCH - 2)], (st, D[("w1mm", 1, 63)])]
        if hf == 1:
            wv.append((st, D[("p4mm", 0, 15)]))
            wv.append((sv, D[("p4sq", 0, 31)]))
        E("q", lambda e: e.dma_start(
            hr4_1[:, :, :], d_hready[0:16, :, t0:t0 + TH].transpose([1, 0, 2])),
            wait=wv, inc=[(q1, 16)])
        E("q", lambda e: e.dma_start(
            hr4_2[:, :, :], d_hready[16:32, :, t0:t0 + TH].transpose([1, 0, 2])),
            wait=wv, inc=[(q2, 16)])
        ldq = [(q1, q1.n), (q2, q2.n)]
        E("v", lambda e: e.memset(vacc4[:, :], 0.0), inc=[(sv, 1)])
        for k in range(32):
            hr = hr4_1 if k < 16 else hr4_2
            kk = k % 16
            E("s", lambda e, hr=hr, kk=kk, k=k: e.activation(
                sq4[k % 2][:, :], hr[:, kk, :], AF.Square, scale=1.0),
                wait=ldq + [(sv, D.get(("p4sq", hf, k - 2), 0))], inc=[(ss, 1)])
            E("v", lambda e, k=k: e.tensor_add(vacc4[:, :], vacc4[:, :], sq4[k % 2][:, :]),
              wait=[(ss, ss.n)], inc=[(sv, 1)])
            E("v", lambda e, hr=hr, kk=kk, k=k: e.tensor_scalar_mul(
                hr[:, kk, :], hr[:, kk, :], c_nwpp[:, k:k + 1]), inc=[(sv, 1)])
            D[("p4sq", hf, k)] = sv.n
        nwn = sv.n
        for t2 in range(2):
            E("t", lambda e, t2=t2: e.matmul(PS[4 + t2][0:1, :], c_ones[:, 0:1],
                                             vacc4[:, t2 * 512:(t2 + 1) * 512],
                                             start=True, stop=True),
              wait=[(sv, nwn)], inc=[(st, 1)])
        E("v", lambda e: e.tensor_scalar_mul(rowM[0:1, 0:512], PS[4][0:1, :],
                                             1.0 / INTER),
          wait=[(st, st.n)] + ([D[("varq", hf - 1)]] if ("varq", hf - 1) in D else []),
          inc=[(sv, 1)])
        E("v", lambda e: e.tensor_scalar_mul(rowM[0:1, 512:1024], PS[5][0:1, :],
                                             1.0 / INTER), inc=[(sv, 1)])
        qr4 = Q(f"r4{hf}")
        E("q", lambda e, t0=t0: e.dma_start(d_rows[2:3, t0:t0 + TH], rowM[0:1, :]),
          wait=[(sv, sv.n)], inc=[(qr4, 16)])
        D[("varq", hf)] = (qr4, qr4.n)
        for o in range(16):
            pg = [PS[2 * (o % 2)], PS[2 * (o % 2) + 1]]
            qq = Q(f"wc{o % 2}")
            E("g", lambda e, o=o: e.dma_start(wob[o % 2][:, :], wout[o]),
              wait=[(st, D.get(("p4mm", hf, o - 2), 0))], inc=[(qq, 16)])
            wv2 = [(qq, qq.n), (sv, nwn), (sv, D.get(("p4ev", hf, o - 2), 0))]
            for k in range(32):
                hr = hr4_1 if k < 16 else hr4_2
                kk = k % 16
                for t2 in range(2):
                    E("t", lambda e, o=o, k=k, t2=t2, hr=hr, kk=kk, pg=pg: e.matmul(
                        pg[t2][:, :], wob[o % 2][:, k * 128:(k + 1) * 128],
                        hr[:, kk, t2 * 512:(t2 + 1) * 512],
                        start=(k == 0), stop=(k == 31)),
                        wait=wv2 if (k == 0 and t2 == 0) else [],
                        inc=[(st, 1)] if (k == 31 and t2 == 1) else [])
            D[("p4mm", hf, o)] = st.n
            qst = Q(f"st{o % 2}")
            for t2 in range(2):
                E("v", lambda e, o=o, t2=t2, pg=pg: e.tensor_copy(
                    stil4[o % 2][:, t2 * 512:(t2 + 1) * 512], pg[t2][:, :]),
                    wait=[(st, st.n)] + ([D[("stq", hf, o - 2)]]
                                         if ("stq", hf, o - 2) in D else
                                         ([D[("stq", 0, 14 + (o % 2))]]
                                          if hf == 1 and o < 2 else [])),
                    inc=[(sv, 1)])
            D[("p4ev", hf, o)] = sv.n
            E("q", lambda e, o=o, t0=t0: e.dma_start(
                d_stil[o, :, t0:t0 + TH], stil4[o % 2][:, :]),
                wait=[(sv, sv.n)], inc=[(qst, 16)])
            D[("stq", hf, o)] = (qst, qst.n)

    def p4_w2quant():
        for i in range(32):
            o, sub = i // 2, i % 2
            qwq = Q(f"wq{i % 2}")
            for h2 in range(2):
                j = i * 2 + h2
                qq = Q(f"wd{j % 2}")
                sl_src = slice(sub * 4096 + h2 * 2048, sub * 4096 + (h2 + 1) * 2048)
                sl_dst = slice(h2 * 2048, (h2 + 1) * 2048)
                E("g", lambda e, o=o, b=w2q_raw[j % 2], sl_src=sl_src: e.dma_start(
                    b[:, :], w2[o][:, sl_src]),
                    wait=[(sv, D.get(("w2qj", j - 2), 0))], inc=[(qq, 16)])
                wn = (qq, qq.n)
                E("v", lambda e, j=j, b=w2q_raw[j % 2]: e.tensor_scalar(
                    w2q_scr[:, :], b[:, :], c_swcol[:, 1:2], CLIP1, ALU.mult, ALU.min),
                    wait=[wn, (sv, D["p0fin"])] +
                    ([D[("wq2d", i - 1)]] if (h2 == 0 and ("wq2d", i - 1) in D) else []),
                    inc=[(sv, 1)])
                E("v", lambda e: e.tensor_scalar(w2q_scr[:, :], w2q_scr[:, :],
                                                 -CLIP1, MAGIC, ALU.max, ALU.add),
                  inc=[(sv, 1)])
                E("v", lambda e, sl_dst=sl_dst: e.tensor_scalar(
                    w2q_stg[:, sl_dst], w2q_scr[:, :], MAGIC, None, ALU.subtract),
                    inc=[(sv, 1)])
                D[("w2qj", j)] = sv.n
            D[("w2q", i)] = sv.n
            E("q", lambda e, o=o, sub=sub: e.dma_start(
                d_wq2[o][:, sub * 4096:(sub + 1) * 4096], w2q_stg[:, :]),
                wait=[(sv, sv.n)], inc=[(qwq, 16)])
            D[("wq2d", i)] = (qwq, qwq.n)

    def p6_quarter(q4):
        t0 = q4 * TQ
        q1, q2 = Q("p60"), Q("p61")
        wv = [D[("wq2d", 31)], D[("wq2d", 30)], D[("h1q", 1, 63)], D[("h1q", 1, 62)]]
        if q4 > 0:
            wv.append((st, D[("p6mm", q4 - 1, 15)]))
            wv.append((ss, D[("p6ev", q4 - 1, 15)]))
        else:
            wv.append((st, D[("p4mm", 1, 15)]))
            wv.append((sv, D[("p4ev", 1, 15)]))
        if "hq0tap" in D:
            wv += D["hq0tap"]
        if "sch0tap" in D:
            wv += D["sch0tap"]
        E("q", lambda e, q4=q4: e.dma_start(
            hq6_1[:, :, :], d_h1[0:32, :, t0:t0 + TQ].transpose([1, 0, 2])),
            wait=wv, inc=[(q1, 16)])
        E("q", lambda e, q4=q4: e.dma_start(
            hq6_2[:, :, :], d_h1[32:64, :, t0:t0 + TQ].transpose([1, 0, 2])),
            wait=wv, inc=[(q2, 16)])
        ldq = [(q1, q1.n), (q2, q2.n)]
        E("v", lambda e: e.memset(rmq[:, :], 0.0),
          wait=[D[("stq", 1, 15)], D[("stq", 1, 14)]] if q4 == 0 else [],
          inc=[(sv, 1)])
        E("v", lambda e: e.memset(rmqn[:, :], 0.0), inc=[(sv, 1)])
        for k in range(64):
            hq = hq6_1 if k < 32 else hq6_2
            kk = k % 32
            E("s", lambda e, hq=hq, kk=kk: e.activation(hq[:, kk, :], hq[:, kk, :],
                                                        AF.Gelu, scale=1.0),
              wait=ldq if k == 0 else [], inc=[(ss, 1)])
            E("v", lambda e, hq=hq, kk=kk: e.tensor_max(rmq[:, :], rmq[:, :],
                                                        hq[:, kk, :]),
              wait=[(ss, ss.n)], inc=[(sv, 1)])
            E("v", lambda e, hq=hq, kk=kk: e.tensor_tensor(rmqn[:, :], rmqn[:, :],
                                                           hq[:, kk, :], ALU.min),
              inc=[(sv, 1)])
        E("v", lambda e: e.tensor_scalar_mul(rmqn[:, :], rmqn[:, :], -1.0), inc=[(sv, 1)])
        E("v", lambda e: e.tensor_max(rmq[:, :], rmq[:, :], rmqn[:, :]), inc=[(sv, 1)])
        for i in range(4):
            E("t", lambda e, i=i: e.transpose(PS[6][:, 0:128],
                                              rmq[:, i * 128:(i + 1) * 128],
                                              c_idn[:, :]),
              wait=[(sv, sv.n)], inc=[(st, 1)])
            E("v", lambda e, i=i: e.reduce_max(mx6[:, i:i + 1], PS[6][:, 0:128], AX),
              wait=[(st, st.n)], inc=[(sv, 1)])
        qr6 = Q(f"r6{q4}")
        E("q", lambda e, dst=row_assemble_dst(1, t0, 4): e.dma_start(
            dst, mx6[:, :], allow_slow_non_contiguous=True),
            wait=[(sv, sv.n)], inc=[(qr6, 16)])
        E("q", lambda e, t0=t0: e.dma_start(row6[0][0:1, :], d_rows[1:2, t0:t0 + TQ]),
          wait=[(qr6, qr6.n)] + ([D[("varq", 1)], D[("varq", 0)]] if q4 == 0 else []),
          inc=[(qr6, 16)])
        E("v", lambda e: e.tensor_scalar_max(row6[0][0:1, :], row6[0][0:1, :], QEPS),
          wait=[(qr6, qr6.n)], inc=[(sv, 1)])
        E("v", lambda e: e.reciprocal(row6[1][0:1, :], row6[0][0:1, :]), inc=[(sv, 1)])
        E("v", lambda e: e.tensor_scalar_mul(row6[1][0:1, :], row6[1][0:1, :], 127.0),
          inc=[(sv, 1)])
        E("v", lambda e: e.tensor_scalar(row6[2][0:1, :], row6[0][0:1, :],
                                         c_sw[0:1, 1:2], 1.0 / 127.0, ALU.mult, ALU.mult),
          inc=[(sv, 1)])
        E("t", lambda e: e.matmul(PS[6][:, :], c_ones[0:1, :], row6[1][0:1, :],
                                  start=True, stop=True), wait=[(sv, sv.n)], inc=[(st, 1)])
        E("t", lambda e: e.matmul(PS[7][:, :], c_ones[0:1, :], row6[2][0:1, :],
                                  start=True, stop=True), inc=[(st, 1)])
        E("v", lambda e: e.tensor_copy(schbc[:, :], PS[6][:, :]),
          wait=[(st, st.n)], inc=[(sv, 1)])
        E("v", lambda e: e.tensor_copy(i2bc[:, :], PS[7][:, :]), inc=[(sv, 1)])
        bcn = sv.n
        for k in range(64):
            hq = hq6_1 if k < 32 else hq6_2
            kk = k % 32
            E("v", lambda e, hq=hq, kk=kk, k=k: e.tensor_mul(
                t6scr[k % 2][:, :], hq[:, kk, :], schbc[:, :]),
                wait=[(sv, bcn)] if k == 0 else [], inc=[(sv, 1)])
            E("v", lambda e, hq=hq, kk=kk, k=k: e.tensor_scalar(
                hq[:, kk, :], t6scr[k % 2][:, :], MAGIC, MAGIC, ALU.add, ALU.subtract),
                inc=[(sv, 1)])
        qn = sv.n
        if q4 == 0 and "hq0" in taps:
            E("q", lambda e: e.dma_start(tap_out["hq0"][0],
                                         hq6_1[:, :, :].rearrange("p a b -> p (a b)")),
              wait=[(sv, qn)], inc=[(Q("thq"), 16)])
            E("q", lambda e: e.dma_start(tap_out["hq0"][1],
                                         hq6_2[:, :, :].rearrange("p a b -> p (a b)")),
              wait=[(sv, qn)], inc=[(Q("thq2"), 16)])
            D["hq0tap"] = [(Q("thq"), Q("thq").n), (Q("thq2"), Q("thq2").n)]
        if q4 == 0 and "sch0" in taps:
            E("q", lambda e: e.dma_start(tap_out["sch0"][0:1, :], row6[1][0:1, :]),
              wait=[(sv, qn)], inc=[(Q("tsc"), 16)])
            E("q", lambda e: e.dma_start(tap_out["sch0"][1:2, :], row6[2][0:1, :]),
              wait=[(sv, qn)], inc=[(Q("tsc2"), 16)])
            D["sch0tap"] = [(Q("tsc"), Q("tsc").n), (Q("tsc2"), Q("tsc2").n)]
        for o in range(16):
            qq = Q(f"w2b{o % 2}")
            E("q", lambda e, o=o: e.dma_start(
                wq2b[o % 2][:, :, :], d_wq2[o].rearrange("p (a b) -> p a b", a=64)),
                wait=[(st, D.get(("p6mm", q4, o - 2), 0))] +
                ([D[("wq2d", 31)], D[("wq2d", 30)]] if (q4 == 0 and o < 2) else []),
                inc=[(qq, 16)])
            wv2 = [(qq, qq.n), (sv, qn), (ss, D.get(("p6ev", q4, o - 2), 0))]
            pb = PS[o % 2]
            for k in range(64):
                hq = hq6_1 if k < 32 else hq6_2
                kk = k % 32
                E("t", lambda e, o=o, k=k, hq=hq, kk=kk, pb=pb: e.matmul(
                    pb[:, :], wq2b[o % 2][:, k, :], hq[:, kk, :],
                    start=(k == 0), stop=(k == 63)),
                    wait=wv2 if k == 0 else [],
                    inc=[(st, 1)] if k == 63 else [])
            D[("p6mm", q4, o)] = st.n
            E("v", lambda e, o=o, pb=pb: e.tensor_mul(t6scr[o % 2][:, :], pb[:, :],
                                                      i2bc[:, :]),
              wait=[(st, st.n), (ss, D.get(("p6ev", q4, o - 2), 0))], inc=[(sv, 1)])
            qg = Q(f"g6{o % 2}")
            E("s", lambda e, o=o: e.activation(gst6[o % 2][:, :], t6scr[o % 2][:, :],
                                               AF.Silu, scale=1.0),
              wait=[(sv, sv.n)] + ([D[("gq", q4, o - 2)]] if ("gq", q4, o - 2) in D else
                                   ([D[("gq", q4 - 1, 14 + (o % 2))]] if q4 > 0 and o < 2
                                    else [])),
              inc=[(ss, 1)])
            D[("p6ev", q4, o)] = ss.n
            E("q", lambda e, o=o, t0=t0: e.dma_start(d_g[o, :, t0:t0 + TQ],
                                                     gst6[o % 2][:, :]),
              wait=[(ss, ss.n)], inc=[(qg, 16)])
            D[("gq", q4, o)] = (qg, qg.n)

    def p7():
        qr7 = Q("r7")
        E("q", lambda e: e.dma_start(r7[0:1, :], d_rows[2:3, :]),
          wait=[D[("varq", 1)], D[("varq", 0)], (ss, D[("p6ev", 3, 15)])],
          inc=[(qr7, 16)])
        E("v", lambda e: e.tensor_scalar_add(r7b[0:1, :], r7[0:1, :], c_eps[0:1, 1:2]),
          wait=[(qr7, qr7.n)], inc=[(sv, 1)])
        E("s", lambda e: e.activation(r7[0:1, :], r7b[0:1, :], AF.Sqrt, scale=1.0),
          wait=[(sv, sv.n)], inc=[(ss, 1)])
        E("v", lambda e: e.reciprocal(r7[0:1, :], r7[0:1, :]),
          wait=[(ss, ss.n)], inc=[(sv, 1)])
        E("v", lambda e: e.tensor_mul(r7c[0:1, :], r7[0:1, :], r7[0:1, :]), inc=[(sv, 1)])
        E("v", lambda e: e.tensor_mul(r7c[0:1, :], r7c[0:1, :], r7b[0:1, :]),
          inc=[(sv, 1)])
        E("v", lambda e: e.tensor_scalar(r7c[0:1, :], r7c[0:1, :], -0.5, 1.5,
                                         ALU.mult, ALU.add), inc=[(sv, 1)])
        E("v", lambda e: e.tensor_mul(r7[0:1, :], r7[0:1, :], r7c[0:1, :]),
          inc=[(sv, 1)])
        for t4 in range(4):
            E("t", lambda e, t4=t4: e.matmul(PS[4 + t4][:, :],
                                             c_ones[0:1, :],
                                             r7[0:1, t4 * 512:(t4 + 1) * 512],
                                             start=True, stop=True),
              wait=[(sv, sv.n)], inc=[(st, 1)])
        bcn = st.n
        for k in range(KD):
            qxk = Q(f"x7{k % 2}")
            qsk = Q(f"s7{k % 2}")
            qgk = Q(f"g7{k % 2}")
            qok = Q(f"o7{k % 2}")
            wv = [D[("gq", 3, 15)], D[("gq", 3, 14)]] if k < 2 else                  [D[("p7q", k - 2)]]
            E("q", lambda e, k=k: e.dma_start(x7[k % 2][:, :],
                                              xt[k * 128:(k + 1) * 128, :]),
              wait=wv, inc=[(qxk, 16)])
            E("q", lambda e, k=k: e.dma_start(stil7[k % 2][:, :], d_stil[k]),
              wait=wv + ([D[("stq", 1, k)], D[("stq", 0, k)]] if k < 2 else []),
              inc=[(qsk, 16)])
            E("q", lambda e, k=k: e.dma_start(g7[k % 2][:, :], d_g[k]),
              wait=wv, inc=[(qgk, 16)])
            ldq = [(qxk, qxk.n), (qsk, qsk.n), (qgk, qgk.n)]
            for t4 in range(4):
                sl = slice(t4 * 512, (t4 + 1) * 512)
                E("v", lambda e, k=k, t4=t4, sl=sl: e.tensor_mul(
                    t7a[:, :], g7[k % 2][:, sl], PS[4 + t4][:, :]),
                    wait=(ldq + [(st, bcn)]) if t4 == 0 else [], inc=[(sv, 1)])
                E("v", lambda e, k=k, sl=sl: e.tensor_mul(
                    t7b[:, :], t7a[:, :], stil7[k % 2][:, sl]), inc=[(sv, 1)])
                E("v", lambda e, k=k, sl=sl: e.tensor_add(
                    ostg7[k % 2][:, sl], x7[k % 2][:, sl], t7b[:, :]), inc=[(sv, 1)])
            E("q", lambda e, k=k: e.dma_start(out[k * 128:(k + 1) * 128, :],
                                              ostg7[k % 2][:, :]),
              wait=[(sv, sv.n)], inc=[(qok, 16)])
            D[("p7q", k)] = (qok, qok.n)

    init_consts()
    p0_dma(0, 48)
    if last_phase >= 1:
        p1_half(0)
        if "xn" in taps:
            E("q", lambda e: e.dma_start(tap_out["xn"][0], xn_half[:, :, :].rearrange(
                "p a b -> p (a b)")), wait=[(sv, sv.n)], inc=[(Q("xn0"), 16)])
            D["xntap0"] = (Q("xn0"), Q("xn0").n)
        p1b_half(0)
        D[("p1bend", 0)] = sv.n
    if last_phase >= 2:
        p2_half(0)
    p0_dma(48, 128)
    if last_phase >= 1:
        p1_half(1)
        if "xn" in taps:
            E("q", lambda e: e.dma_start(tap_out["xn"][1], xn_half[:, :, :].rearrange(
                "p a b -> p (a b)")), wait=[(sv, sv.n), (Q("xn0"), Q("xn0").n)],
              inc=[(Q("xn1"), 16)])
            D["xntap1"] = (Q("xn1"), Q("xn1").n)
        p1b_half(1)
        D[("p1bend", 1)] = sv.n
    if last_phase >= 2:
        p2_half(1)
        D["p2end_t"] = st.n
    p0_dma(128, 256)
    D["p0done"] = sv.n
    D["p2spill"] = [(Q("pa0"), Q("pa0").n), (Q("pa1"), Q("pa1").n)]
    if last_phase >= 3:
        p0_finish()
        if last_phase >= 5:
            p5_init_half(0)
        p3_loads(0)
        p3_chunk_prelude(0)
        for c in range(NCH):
            if c + 1 < NCH:
                p3_loads(c + 1)
            p3_chunk(c)
            if last_phase >= 5:
                if c == 8:
                    p5_init_half(1)
                hf = c // 8
                p5_batch(hf, 8 * (c % 8), 8 * (c % 8) + 8)
            if c + 1 < NCH:
                p3_chunk_prelude(c + 1)
    if last_phase >= 4:
        p4_half(0)
        p4_w2quant()
        p4_half(1)
    if last_phase >= 6:
        for q4 in range(4):
            p6_quarter(q4)
    if last_phase >= 7:
        p7()

    def all_done():
        w = [(sv, sv.n), (ss, ss.n), (st, st.n), (sg, sg.n)]
        w += [(s, s.n) for s in _semd.values()]
        return w

    qtap = Q("tap")
    for name, src in (("bsb", B_sb[:, :]), ("csb", C_sb[:, :]),
                      ("dtsp", dt_sp[:, :]), ("carry", carry[:, :]),
                      ("sw", c_sw[:, :])):
        if name in taps:
            E("q", lambda e, d=tap_out[name], s=src: e.dma_start(d, s),
              wait=all_done(), inc=[(qtap, 16)])
    for name, src in (("xqi", d_xqi), ("gate", d_gate), ("xssm", d_xssm),
                      ("hready", d_hready), ("h1", d_h1), ("stil", d_stil),
                      ("g", d_g), ("rows", d_rows)):
        if name in taps:
            E("q", lambda e, d=tap_out[name], s=src: e.dma_start(d, s),
              wait=all_done(), inc=[(qtap, 16)])

    E("q", lambda e: e.nop(), wait=all_done() + [(st, st.n), (sv, sv.n)])

    counts = P.check()

    with nc.Block() as block:
        @block.tensor
        def _(eng):
            for op in P.ops["t"]:
                op(eng)

        @block.vector
        def _(eng):
            for op in P.ops["v"]:
                op(eng)

        @block.scalar
        def _(eng):
            for op in P.ops["s"]:
                op(eng)

        @block.gpsimd
        def _(eng):
            for op in P.ops["g"]:
                op(eng)

        @block.sync
        def _(eng):
            for op in P.ops["q"]:
                op(eng)

    stack.close()
    return nc, counts


def kernel(**inputs):
    from concourse.bass_utils import run_bass_kernel_spmd
    in_maps = host_prep(inputs)
    nc, _counts = build(taps=(), last_phase=7)
    res = run_bass_kernel_spmd(nc, in_maps, core_ids=list(range(len(in_maps))))
    out = np.stack([np.ascontiguousarray(r["out"].T) for r in res.results], axis=0)
    return out.astype(np.float32)


# version 11
# speedup vs baseline: 1.5501x; 1.0230x over previous
import contextlib
import numpy as np
import concourse.bass as bass
import concourse.mybir as mybir

F32 = mybir.dt.float32
BF16 = mybir.dt.bfloat16
F16 = mybir.dt.float16
AF = mybir.ActivationFunctionType
ALU = mybir.AluOpType
AX = mybir.AxisListType.X

S = 2048
DM = 2048
KD = 16
INTER = 4096
CONVD = 4352
PROJ = 8512
NO2 = 67
NH = 64
HD = 64
DST = 128
L = 128
NCH = 16
HFF = 8192
TH = 1024
TQ = 512
MAGIC = 12582912.0
CLIP1 = 1.4990234375
LNEPS = 1e-5
RMSEPS = 1e-6
QEPS = 1e-5



def _tile_lhsT(w, no, nk):
    out_r, in_r = w.shape
    assert in_r == nk * 128
    if out_r < no * 128:
        w = np.concatenate([w, np.zeros((no * 128 - out_r, in_r), w.dtype)], 0)
    t = np.empty((no, 128, nk * 128), np.float32)
    for o in range(no):
        blk = w[o * 128:(o + 1) * 128, :]
        t[o] = np.ascontiguousarray(
            blk.reshape(128, nk, 128).transpose(2, 1, 0)).reshape(128, nk * 128)
    return t


def host_prep(inputs):
    x = np.ascontiguousarray(np.asarray(inputs["L0_attn_hidden_states"], np.float32))
    sh = {}
    sh["win"] = _tile_lhsT(np.asarray(inputs["L0_attn_in_proj_w"], np.float32), NO2, KD)
    sh["wout"] = _tile_lhsT(np.asarray(inputs["L0_attn_out_proj_w"], np.float32), 16, 32)
    sh["w1"] = _tile_lhsT(np.asarray(inputs["L0_ffn_w1"], np.float32), 64, KD)
    sh["w2"] = _tile_lhsT(np.asarray(inputs["L0_ffn_w2"], np.float32), 16, 64)
    cw = np.asarray(inputs["L0_attn_conv_w"], np.float32).reshape(CONVD, 4)
    sh["convw"] = np.ascontiguousarray(
        cw.reshape(34, 128, 4).transpose(1, 0, 2).reshape(128, 136))
    cb = np.asarray(inputs["L0_attn_conv_b"], np.float32)
    sh["convb"] = np.ascontiguousarray(cb.reshape(34, 128).T)
    sh["dtb"] = np.ascontiguousarray(
        np.asarray(inputs["L0_attn_dt_bias"], np.float32).reshape(64, 1))
    sh["alog"] = np.ascontiguousarray(
        np.asarray(inputs["L0_attn_A_log"], np.float32).reshape(64, 1))
    dd = np.asarray(inputs["L0_attn_D"], np.float32)
    ch = np.arange(INTER)
    sh["dpp"] = np.ascontiguousarray(dd[(ch // HD)].reshape(32, 128).T)
    nw = np.asarray(inputs["L0_attn_norm_weight"], np.float32)
    sh["nwpp"] = np.ascontiguousarray(nw.reshape(32, 128).T)
    sh["idn"] = np.eye(128, dtype=np.float32)
    sh["ones"] = np.ones((128, 128), np.float32)
    sh["mask"] = np.ascontiguousarray(
        (np.arange(128)[None, :] >= np.arange(128)[:, None]).astype(np.float32))
    in_maps = []
    for c in range(x.shape[0]):
        d = dict(sh)
        d["xt"] = np.ascontiguousarray(x[c].T)
        in_maps.append(d)
    return in_maps



class Sem:
    def __init__(self, stack, nc, name):
        self.h = stack.enter_context(nc.semaphore(name))
        self.n = 0
        self.name = name


class Prog:
    ENGS = ("t", "v", "s", "g", "q")

    def __init__(self):
        self.ops = {k: [] for k in self.ENGS}
        self.trace = {k: [] for k in self.ENGS}
        self.eng_sems = {}

    def emit(self, ek, build, wait=(), inc=(), dma=None, chain=False):
        if dma is None:
            dma = any(k >= 16 for _, k in inc)
        esem = self.eng_sems.get(ek)
        if esem is not None and ek not in ("t", "s"):
            wait = list(wait) + [(esem, esem.n)]
            if not dma and not any(sm is esem for sm, _ in inc):
                inc = list(inc) + [(esem, 1)]
        for sm, k in inc:
            sm.n += k
        inc_h = tuple((sm.h, int(k)) for sm, k in inc)
        wait_h = tuple((sm.h, int(v)) for sm, v in wait if v > 0)

        def op(e, build=build, wait_h=wait_h, inc_h=inc_h):
            for hh, v in wait_h:
                e.wait_ge(hh, v)
            r = build(e)
            for hh, k in inc_h:
                r = r.then_inc(hh, k)
        self.ops[ek].append(op)
        self.trace[ek].append((tuple((sm.name, int(v)) for sm, v in wait if v > 0),
                               tuple((sm.name, int(k)) for sm, k in inc)))

    def check(self):
        pos = {k: 0 for k in self.ENGS}
        cnt = {}
        for k in self.ENGS:
            for w, i in self.trace[k]:
                for n, _ in list(w) + list(i):
                    cnt.setdefault(n, 0)
        progressed = True
        while progressed:
            progressed = False
            for k in self.ENGS:
                while pos[k] < len(self.trace[k]):
                    w, i = self.trace[k][pos[k]]
                    if all(cnt[n] >= v for n, v in w):
                        for n, kk in i:
                            cnt[n] += kk
                        pos[k] += 1
                        progressed = True
                    else:
                        break
        stuck = {k: pos[k] for k in self.ENGS if pos[k] < len(self.trace[k])}
        if stuck:
            lines = []
            for k, p in stuck.items():
                lines.append(f"engine {k} stuck at op {p}/{len(self.trace[k])}"
                             f" waiting {self.trace[k][p][0]} counts="
                             f"{[(n, cnt[n]) for n, _ in self.trace[k][p][0]]}")
            raise RuntimeError("DEADLOCK:\n" + "\n".join(lines))
        return {k: len(self.trace[k]) for k in self.ENGS}



def build(taps=(), last_phase=7, p2_os=None, detect_races=False):
    nc = bass.Bass("TRN2", target_bir_lowering=False,
                   detect_race_conditions=detect_races)
    stack = contextlib.ExitStack()
    P = Prog()
    D = {}

    ap = lambda t: t.ap()
    xt = ap(nc.dram_tensor("xt", [DM, S], F32, kind="ExternalInput"))
    win = ap(nc.dram_tensor("win", [NO2, 128, DM], F32, kind="ExternalInput"))
    wout = ap(nc.dram_tensor("wout", [16, 128, INTER], F32, kind="ExternalInput"))
    w1 = ap(nc.dram_tensor("w1", [64, 128, DM], F32, kind="ExternalInput"))
    w2 = ap(nc.dram_tensor("w2", [16, 128, HFF], F32, kind="ExternalInput"))
    convw = ap(nc.dram_tensor("convw", [128, 136], F32, kind="ExternalInput"))
    convb = ap(nc.dram_tensor("convb", [128, 34], F32, kind="ExternalInput"))
    dtb = ap(nc.dram_tensor("dtb", [64, 1], F32, kind="ExternalInput"))
    alog = ap(nc.dram_tensor("alog", [64, 1], F32, kind="ExternalInput"))
    dpp = ap(nc.dram_tensor("dpp", [128, 32], F32, kind="ExternalInput"))
    nwpp = ap(nc.dram_tensor("nwpp", [128, 32], F32, kind="ExternalInput"))
    idn = ap(nc.dram_tensor("idn", [128, 128], F32, kind="ExternalInput"))
    onesd = ap(nc.dram_tensor("ones", [128, 128], F32, kind="ExternalInput"))
    maskd = ap(nc.dram_tensor("mask", [128, 128], F32, kind="ExternalInput"))
    out = ap(nc.dram_tensor("out", [DM, S], F32, kind="ExternalOutput"))

    d_xqi = ap(nc.dram_tensor("d_xqi", [KD, 128, S], BF16, kind="Internal"))
    d_gate = ap(nc.dram_tensor("d_gate", [NCH, 32, 128, L], BF16, kind="Internal"))
    d_xssm = ap(nc.dram_tensor("d_xssm", [NCH, 32, 128, L], BF16, kind="Internal"))
    d_hready = ap(nc.dram_tensor("d_hready", [32, 128, S], BF16, kind="Internal"))
    d_h1 = ap(nc.dram_tensor("d_h1", [64, 128, S], F16, kind="Internal"))
    d_wq2 = ap(nc.dram_tensor("d_wq2", [16, 128, HFF], F16, kind="Internal"))
    d_stil = ap(nc.dram_tensor("d_stil", [16, 128, S], BF16, kind="Internal"))
    d_g = ap(nc.dram_tensor("d_g", [16, 128, S], BF16, kind="Internal"))
    d_rows = ap(nc.dram_tensor("d_rows", [4, 2048], F32, kind="Internal"))

    tap_out = {}
    TAPSHAPES = {
        "xqi": ([KD, 128, S], BF16), "gate": ([NCH, 32, 128, L], BF16),
        "xssm": ([NCH, 32, 128, L], BF16), "bsb": ([128, S], BF16),
        "csb": ([128, S], BF16), "dtsp": ([64, S], F32),
        "hready": ([32, 128, S], BF16), "stil": ([16, 128, S], BF16),
        "h1": ([64, 128, S], F16), "g": ([16, 128, S], BF16),
        "rows": ([4, 2048], F32), "carry": ([128, NH * HD], F32),
        "sw": ([1, 4], F32), "xn": ([2, 128, KD * TH], F16),
        "hq0": ([2, 128, 32 * TQ], F16), "sch0": ([2, TQ], F32),
    }
    for t in taps:
        if t not in TAPSHAPES:
            continue
        shp, dt = TAPSHAPES[t]
        tap_out[t] = ap(nc.dram_tensor("tap_" + t, shp, dt, kind="ExternalOutput"))

    def arena(name, nbytes):
        return stack.enter_context(nc.sbuf_tensor(name, [128, nbytes // 2], BF16))

    AR1 = arena("AR1", 32768)
    AR2 = arena("AR2", 32768)
    AR3 = arena("AR3", 41472)
    AR4 = arena("AR4", 22528)
    AR5 = arena("AR5", 16384)
    AR7 = arena("AR7", 30720)
    AR8 = arena("AR8", 25088)

    def V(at, byte_off, p0, pcnt, shape_free, dtype=BF16):
        esz = 4 if dtype == F32 else 2
        n = 1
        for d in shape_free:
            n *= int(d)
        assert byte_off % 4 == 0
        a = at[p0:p0 + pcnt, byte_off // 2: byte_off // 2 + (esz * n) // 2]
        if dtype != BF16:
            a = a.bitcast(dtype)
        if len(shape_free) > 1:
            names = " ".join(f"d{i}" for i in range(len(shape_free)))
            a = a.rearrange(f"p ({names}) -> p {names}",
                            **{f"d{i}": int(shape_free[i]) for i in range(len(shape_free))})
        return a

    c_idn = stack.enter_context(nc.sbuf_tensor("c_idn", [128, 128], F32))
    c_idnb = stack.enter_context(nc.sbuf_tensor("c_idnb", [128, 128], BF16))
    c_idnh = stack.enter_context(nc.sbuf_tensor("c_idnh", [128, 128], F16))
    c_ones = stack.enter_context(nc.sbuf_tensor("c_ones", [128, 128], F32))
    c_mask = stack.enter_context(nc.sbuf_tensor("c_mask", [128, 128], F32))
    c_convw = stack.enter_context(nc.sbuf_tensor("c_convw", [128, 136], F32))
    c_convb = stack.enter_context(nc.sbuf_tensor("c_convb", [128, 34], F32))
    c_dpp = stack.enter_context(nc.sbuf_tensor("c_dpp", [128, 32], F32))
    c_nwpp = stack.enter_context(nc.sbuf_tensor("c_nwpp", [128, 32], F32))
    c_dtb = stack.enter_context(nc.sbuf_tensor("c_dtb", [64, 1], F32))
    c_alog = stack.enter_context(nc.sbuf_tensor("c_alog", [64, 1], F32))
    c_acol = stack.enter_context(nc.sbuf_tensor("c_acol", [64, 1], F32))
    c_tails = stack.enter_context(nc.sbuf_tensor("c_tails", [128, 102], F32))
    c_sw = stack.enter_context(nc.sbuf_tensor("c_sw", [1, 4], F32))
    c_swcol = stack.enter_context(nc.sbuf_tensor("c_swcol", [128, 2], F32))
    c_wacc = stack.enter_context(nc.sbuf_tensor("c_wacc", [128, 2], F32))
    c_wred = stack.enter_context(nc.sbuf_tensor("c_wred", [128, 1], F32))
    c_eps = stack.enter_context(nc.sbuf_tensor("c_eps", [1, 2], F32))

    xn_half = V(AR1, 0, 0, 128, (KD, TH), F16)
    hready_asm = [V(AR1, i * 8192, 0, 128, (32, L)) for i in range(2)]
    t1st = V(AR1, 16384, 0, 128, (2, L))
    gateg = [V(AR1, 17408 + i * 512, 0, 128, (2, L)) for i in range(2)]
    xssmg = [V(AR1, 19456 + i * 512, 0, 128, (2, L)) for i in range(2)]
    hr4_1 = V(AR1, 0, 0, 128, (16, TH))
    hq6_1 = V(AR1, 0, 0, 128, (32, TQ), F16)

    xqi_half = V(AR2, 0, 0, 128, (KD, TH))
    hr4_2 = V(AR2, 0, 0, 128, (16, TH))
    hq6_2 = V(AR2, 0, 0, 128, (32, TQ), F16)

    carry = V(AR3, 0, 0, 128, (NH * HD,), F32)
    carry_bf = V(AR3, 16384, 0, 128, (NH * HD,))
    xssmT = [V(AR3, 24576 + i * 8192, 0, 128, (INTER,)) for i in range(2)]
    p2wb = [V(AR3, i * 4096, 0, 128, (DM,), F16) for i in range(2)]
    w2q_raw = [V(AR3, i * 8192, 0, 128, (2048,), F32) for i in range(2)]
    w2q_stg = V(AR3, 16384, 0, 128, (4096,), F16)
    w2q_scr = V(AR3, 24576, 0, 128, (2048,), F32)
    wq2b = [V(AR3, i * 16384, 0, 128, (64, 128), F16) for i in range(2)]
    x7 = [V(AR3, i * 8192, 0, 128, (S,), F32) for i in range(2)]

    B_sb = V(AR4, 0, 0, 128, (S,))
    C_sb = V(AR4, 4096, 0, 128, (S,))
    dt_sp = V(AR4, 8192, 0, 64, (S,), F32)
    SC = 16384
    m1sc = V(AR4, SC, 0, 128, (L,), F32)
    edif = [V(AR4, SC + 512 + i * 256, 0, 128, (L,)) for i in range(2)]
    mtt = [V(AR4, SC + 1024 + i * 256, 0, 128, (L,)) for i in range(2)]
    cexp = [V(AR4, SC + 1536 + i * 256, 0, 128, (L,)) for i in range(2)]
    scoresM = V(AR4, SC + 2048, 0, 128, (L,))
    dsT_c = V(AR4, SC + 2304, 0, 128, (NH,), F32)
    cd_c = V(AR4, SC + 2560, 0, 128, (NH,), F32)
    ecs_c = V(AR4, SC + 2816, 0, 64, (L,), F32)
    cs_c = V(AR4, SC + 3328, 0, 64, (L,), F32)
    a_c = V(AR4, SC + 3840, 0, 64, (L,), F32)
    csT_c = V(AR4, SC + 4352, 0, 128, (NH,), F32)
    dtT_c = V(AR4, SC + 4608, 0, 128, (NH,), F32)
    B_t = V(AR4, SC + 4864, 0, 128, (DST,))
    xdtds_g = [V(AR4, SC + 5120 + i * 512, 0, 128, (4, HD)) for i in range(2)]
    spa = V(AR4, SC, 0, 64, (TH,), F32)
    dtrelu = V(AR3, 8192, 0, 64, (TH,), F32)

    xstg = [V(AR5, i * 4096, 0, 128, (TH,), F32) for i in range(2)]
    x2stg = [V(AR5, 8192 + i * 4096, 0, 128, (TH,), F32) for i in range(2)]
    cacc = [V(AR5, i * 4096, 0, 128, (TH,), F32) for i in range(2)]
    sstg = [V(AR5, 8192 + i * 2048, 0, 128, (TH,)) for i in range(2)]
    wob = [V(AR5, i * 8192, 0, 128, (INTER,)) for i in range(2)]
    stil7 = [V(AR5, i * 4096, 0, 128, (S,)) for i in range(2)]
    g7 = [V(AR5, 8192 + i * 4096, 0, 128, (S,)) for i in range(2)]

    w1bf = [[V(AR7, h2 * 4096, 0, 128, (TH,), F32),
             V(AR1, 20480 + h2 * 4096, 0, 128, (TH,), F32)] for h2 in range(2)]
    w1b = [V(AR7, 12288 + i * 4096, 0, 128, (DM,)) for i in range(2)]
    wqscr = V(AR7, 8192, 0, 128, (TH,), F32)
    wq1b = [V(AR7, 12288 + i * 4096, 0, 128, (DM,)) for i in range(2)]
    h1stg = [V(AR7, 20480 + i * 2048, 0, 128, (TH,), F16) for i in range(2)]
    xqstg = [V(AR7, 20480 + i * 2048, 0, 128, (TH,)) for i in range(2)]
    mxcols = V(AR7, 24576, 0, 128, (8,), F32)
    invsbc = V(AR7, 26624, 0, 128, (TH,), F32)
    vacc4 = V(AR7, 0, 0, 128, (TH,), F32)
    stil4 = [V(AR7, 4096 + i * 2048, 0, 128, (TH,)) for i in range(2)]
    sq4 = [V(AR7, 8192 + i * 2048, 0, 128, (TH,)) for i in range(2)]
    gst6 = [V(AR7, i * 1024, 0, 128, (TQ,)) for i in range(2)]
    schbc = V(AR7, 4096, 0, 128, (TQ,), F32)
    i2bc = V(AR7, 6144, 0, 128, (TQ,), F32)
    rmq = V(AR7, 8192, 0, 128, (TQ,), F32)
    rmqn = V(AR7, 14400, 0, 128, (TQ,), F32)
    t6scr = [V(AR7, 10240 + i * 2048, 0, 128, (TQ,), F32) for i in range(2)]
    mx6 = V(AR7, 14336, 0, 128, (4,), F32)
    rmq1 = V(AR7, 20480, 0, 128, (TH,), F32)
    rmq1n = V(AR7, 24640, 0, 128, (TH,), F32)
    ostg7 = [V(AR7, i * 8192, 0, 128, (S,), F32) for i in range(2)]
    t7a = V(AR7, 16384, 0, 128, (TQ,))
    t7b = V(AR7, 17408, 0, 128, (TQ,))

    rowM = V(AR8, 0, 0, 1, (TH,), F32)
    rowR = V(AR8, 4096, 0, 1, (TH,), F32)
    rowS = V(AR8, 8192, 0, 1, (TH,), F32)
    csf = [V(AR8, i * 4096, 0, 1, (TH,), F32) for i in range(2)]
    ecsf = [V(AR8, 8192 + i * 4096, 0, 1, (TH,), F32) for i in range(2)]
    cslf = V(AR8, 16384, 0, 1, (NH,), F32)
    rowB = V(AR8, 16640, 0, 1, (S,), F32)
    inv1h = [V(AR8, 16640 + i * 4096, 0, 1, (TH,), F32) for i in range(2)]
    r7 = V(AR8, 0, 0, 1, (S,), F32)
    r7b = V(AR8, 8192, 0, 1, (S,), F32)
    r7c = V(AR8, 16640, 0, 1, (S,), F32)
    row6 = [V(AR8, i * 2048, 0, 1, (TQ,), F32) for i in range(4)]

    PS = [stack.enter_context(nc.psum_tensor(f"ps{i}", [128, 512], F32)) for i in range(8)]

    def PSb(i, lo, n):
        return PS[i][:, :].bitcast(BF16)[:, lo:lo + n]

    def PSh(i, lo, n):
        return PS[i][:, :].bitcast(F16)[:, lo:lo + n]

    _semd = {}

    def Q(name):
        if name not in _semd:
            _semd[name] = Sem(stack, nc, "q_" + name)
        return _semd[name]

    st = Sem(stack, nc, "st")
    sv = Sem(stack, nc, "sv")
    ss = Sem(stack, nc, "ss")
    sg = Sem(stack, nc, "sg")
    P.eng_sems = {"t": st, "v": sv, "s": ss, "g": sg}

    E = P.emit

    def row_assemble_dst(row_i, t0, ncols):
        base = d_rows[row_i:row_i + 1, t0:t0 + 1]
        return bass.AP(base.tensor, base.offset, [[1, 128], [128, ncols]])

    def init_consts():
        qc = Q("c")
        for dst, src in ((c_idn[:, :], idn), (c_ones[:, :], onesd), (c_mask[:, :], maskd),
                         (c_convw[:, :], convw), (c_convb[:, :], convb),
                         (c_dpp[:, :], dpp), (c_nwpp[:, :], nwpp),
                         (c_dtb[:, :], dtb), (c_alog[:, :], alog)):
            E("q", lambda e, d=dst, s=src: e.dma_start(d, s), inc=[(qc, 16)])
        ldn = (qc, qc.n)
        D["consts"] = ldn
        for ek in ("t", "v", "s", "g"):
            E(ek, lambda e: e.nop(), wait=[ldn])
        E("v", lambda e: e.tensor_copy(c_idnb[:, :], c_idn[:, :]),
          wait=[ldn], inc=[(sv, 1)])
        E("v", lambda e: e.tensor_copy(c_idnh[:, :], c_idn[:, :]), inc=[(sv, 1)])
        E("v", lambda e: e.memset(c_tails[:, :], 0.0), inc=[(sv, 1)])
        E("v", lambda e: e.memset(c_wacc[:, :], 0.0), inc=[(sv, 1)])
        E("v", lambda e: e.memset(c_eps[0:1, 0:1], LNEPS), inc=[(sv, 1)])
        E("v", lambda e: e.memset(c_eps[0:1, 1:2], RMSEPS), inc=[(sv, 1)])
        E("s", lambda e: e.activation(c_acol[:, :], c_alog[:, :], AF.Exp, scale=1.0),
          wait=[ldn], inc=[(ss, 1)])
        E("v", lambda e: e.tensor_scalar_mul(c_acol[:, :], c_acol[:, :], -1.0),
          wait=[(ss, ss.n)], inc=[(sv, 1)])
        D["init_v"] = sv.n

    def p0_dma(lo, hi):
        for i in range(lo, hi):
            if i < 128:
                src = w1[i // 2][:, (i % 2) * 1024:(i % 2 + 1) * 1024]
            else:
                j = i - 128
                src = w2[j // 8][:, (j % 8) * 1024:(j % 8 + 1) * 1024]
            qq = Q(f"wa{i % 2}")
            wv = [(sv, D.get(("p0r", i - 2), 0))]
            E("g", lambda e, d=w1bf[i % 2][0][:, :], s=src: e.dma_start(d, s),
              wait=wv, inc=[(qq, 16)])
            col = 0 if i < 128 else 1
            first = (i == 0 or i == 128)
            E("v", lambda e, b=w1bf[i % 2][0][:, :]: e.tensor_reduce(
                c_wred[:, :], b, AX, ALU.add, apply_absolute_value=True),
                wait=[(qq, qq.n), (sv, D["init_v"])], inc=[(sv, 1)])
            if first:
                E("v", lambda e, c=col: e.tensor_copy(c_wacc[:, c:c + 1], c_wred[:, :]),
                  inc=[(sv, 1)])
            else:
                E("v", lambda e, c=col: e.tensor_add(
                    c_wacc[:, c:c + 1], c_wacc[:, c:c + 1], c_wred[:, :]), inc=[(sv, 1)])
            D[("p0r", i)] = sv.n

    def p1_half(hf):
        t0 = hf * TH
        entry = [(sv, sv.n), (ss, ss.n), (Q("pa0"), Q("pa0").n),
                 (Q("pa1"), Q("pa1").n)]
        if "xntap0" in D:
            entry.append(D["xntap0"])
        for k in range(KD):
            qq = Q(f"a{k % 2}")
            wv = [(st, D.get(("p1mm", hf, k - 2), 0)), (ss, D.get(("p1sq", hf, k - 2), 0))]
            if k < 2:
                wv += entry
            E("q", lambda e, kk=k, b=xstg[k % 2]: e.dma_start(
                b[:, :], xt[kk * 128:(kk + 1) * 128, t0:t0 + TH]), wait=wv, inc=[(qq, 16)])
            ldn = (qq, qq.n)
            E("s", lambda e, b=xstg[k % 2], o=x2stg[k % 2]: e.activation(
                o[:, :], b[:, :], AF.Square, scale=1.0),
                wait=[ldn, (st, D.get(("p1mm2", hf, k - 2), 0))] +
                (entry if k < 2 else []), inc=[(ss, 1)])
            D[("p1sq", hf, k)] = ss.n
            sqn = ss.n
            for t4 in range(2):
                E("t", lambda e, b=xstg[k % 2], t4=t4, kk=k: e.matmul(
                    PS[t4][0:1, :], c_ones[:, 0:1], b[:, t4 * 512:(t4 + 1) * 512],
                    start=(kk == 0), stop=(kk == KD - 1)),
                    wait=([ldn] + (entry if k == 0 else [])) if t4 == 0 else [],
                    inc=[(st, 1)] if t4 == 1 else [])
            D[("p1mm", hf, k)] = st.n
            for t4 in range(2):
                E("t", lambda e, b=x2stg[k % 2], t4=t4, kk=k: e.matmul(
                    PS[2 + t4][0:1, :], c_ones[:, 0:1], b[:, t4 * 512:(t4 + 1) * 512],
                    start=(kk == 0), stop=(kk == KD - 1)),
                    wait=([(ss, sqn)] + (entry if k == 0 else [])) if t4 == 0 else [],
                    inc=[(st, 1)] if t4 == 1 else [])
            D[("p1mm2", hf, k)] = st.n
        fin = st.n
        E("v", lambda e: e.tensor_scalar(rowM[0:1, 0:512], PS[0][0:1, :], 1.0 / DM, None,
                                         ALU.mult), wait=[(st, fin)], inc=[(sv, 1)])
        E("v", lambda e: e.tensor_scalar(rowM[0:1, 512:1024], PS[1][0:1, :], 1.0 / DM,
                                         None, ALU.mult), inc=[(sv, 1)])
        E("v", lambda e: e.tensor_scalar(rowR[0:1, 0:512], PS[2][0:1, :], 1.0 / DM, None,
                                         ALU.mult), inc=[(sv, 1)])
        E("v", lambda e: e.tensor_scalar(rowR[0:1, 512:1024], PS[3][0:1, :], 1.0 / DM,
                                         None, ALU.mult), inc=[(sv, 1)])
        E("v", lambda e: e.tensor_mul(rowS[0:1, :], rowM[0:1, :], rowM[0:1, :]),
          inc=[(sv, 1)])
        E("v", lambda e: e.tensor_sub(rowR[0:1, :], rowR[0:1, :], rowS[0:1, :]),
          inc=[(sv, 1)])
        E("v", lambda e: e.tensor_scalar_add(rowS[0:1, :], rowR[0:1, :],
                                             c_eps[0:1, 0:1]),
          wait=[(sv, sv.n), (sv, D["init_v"])], inc=[(sv, 1)])
        E("s", lambda e: e.activation(rowR[0:1, :], rowS[0:1, :], AF.Sqrt, scale=1.0),
          wait=[(sv, sv.n)], inc=[(ss, 1)])
        E("v", lambda e: e.reciprocal(rowR[0:1, :], rowR[0:1, :]),
          wait=[(ss, ss.n)], inc=[(sv, 1)])
        E("v", lambda e: e.tensor_mul(ecsf[1][0:1, :], rowR[0:1, :], rowR[0:1, :]),
          inc=[(sv, 1)])
        E("v", lambda e: e.tensor_mul(ecsf[1][0:1, :], ecsf[1][0:1, :], rowS[0:1, :]),
          inc=[(sv, 1)])
        E("v", lambda e: e.tensor_scalar(ecsf[1][0:1, :], ecsf[1][0:1, :], -0.5, 1.5,
                                         ALU.mult, ALU.add), inc=[(sv, 1)])
        E("v", lambda e: e.tensor_mul(rowR[0:1, :], rowR[0:1, :], ecsf[1][0:1, :]),
          inc=[(sv, 1)])
        stn = sv.n
        if "lnstats" in taps:
            E("q", lambda e, t0=t0: e.dma_start(d_rows[2:3, t0:t0 + TH], rowM[0:1, :]),
              wait=[(sv, stn)], inc=[(Q(f"ls{hf}"), 16)])
            E("q", lambda e, t0=t0: e.dma_start(d_rows[3:4, t0:t0 + TH], rowR[0:1, :]),
              wait=[(sv, stn)], inc=[(Q(f"ls{hf}b"), 16)])
        for t4 in range(2):
            E("t", lambda e, t4=t4: e.matmul(PS[4 + t4][:, :], c_ones[0:1, :],
                                             rowM[0:1, t4 * 512:(t4 + 1) * 512],
                                             start=True, stop=True),
              wait=[(sv, stn)], inc=[(st, 1)])
            E("t", lambda e, t4=t4: e.matmul(PS[6 + t4][:, :], c_ones[0:1, :],
                                             rowR[0:1, t4 * 512:(t4 + 1) * 512],
                                             start=True, stop=True), inc=[(st, 1)])
        bcn = st.n
        for k in range(KD):
            qq = Q(f"a{k % 2}")
            E("q", lambda e, kk=k, b=xstg[k % 2]: e.dma_start(
                b[:, :], xt[kk * 128:(kk + 1) * 128, t0:t0 + TH]),
                wait=[(sv, D.get(("p1xn", hf, k - 2), 0)), (st, fin)], inc=[(qq, 16)])
            ldn = (qq, qq.n)
            for t4 in range(2):
                sl = slice(t4 * 512, (t4 + 1) * 512)
                E("v", lambda e, b=xstg[k % 2], o=x2stg[k % 2], t4=t4, sl=sl: e.tensor_sub(
                    o[:, sl], b[:, sl], PS[4 + t4][:, :]),
                    wait=[ldn, (st, bcn)], inc=[(sv, 1)])
                E("v", lambda e, o=x2stg[k % 2], kk=k, t4=t4, sl=sl: e.tensor_mul(
                    xn_half[:, kk, sl], o[:, sl], PS[6 + t4][:, :]), inc=[(sv, 1)])
            D[("p1xn", hf, k)] = sv.n

    def p1b_half(hf):
        t0 = hf * TH
        spg = [(Q("sa0"), Q("sa0").n), (Q("sa1"), Q("sa1").n)]
        E("v", lambda e: e.tensor_copy(rmq1[:, :], xn_half[:, 0, :]),
          wait=spg, inc=[(sv, 1)])
        E("v", lambda e: e.tensor_copy(rmq1n[:, :], xn_half[:, 0, :]),
          wait=spg, inc=[(sv, 1)])
        for k in range(1, KD):
            E("v", lambda e, kk=k: e.tensor_max(rmq1[:, :], rmq1[:, :],
                                                xn_half[:, kk, :]), inc=[(sv, 1)])
            E("v", lambda e, kk=k: e.tensor_tensor(rmq1n[:, :], rmq1n[:, :],
                                                   xn_half[:, kk, :], ALU.min),
              inc=[(sv, 1)])
        E("v", lambda e: e.tensor_scalar_mul(rmq1n[:, :], rmq1n[:, :], -1.0),
          inc=[(sv, 1)])
        E("v", lambda e: e.tensor_max(rmq1[:, :], rmq1[:, :], rmq1n[:, :]),
          inc=[(sv, 1)])
        amx = sv.n
        for i in range(8):
            E("t", lambda e, i=i: e.transpose(PS[i % 2][:, 0:128],
                                              rmq1[:, i * 128:(i + 1) * 128], c_idn[:, :]),
              wait=[(sv, amx if i < 2 else D[("p1t", hf, i - 2)])], inc=[(st, 1)])
            E("v", lambda e, i=i: e.reduce_max(mxcols[:, i:i + 1], PS[i % 2][:, 0:128], AX),
              wait=[(st, st.n)], inc=[(sv, 1)])
            D[("p1t", hf, i)] = sv.n
        qq = Q(f"r1{hf}")
        E("q", lambda e, dst=row_assemble_dst(0, t0, 8): e.dma_start(
            dst, mxcols[:, :], allow_slow_non_contiguous=True),
            wait=[(sv, sv.n)], inc=[(qq, 16)])
        E("q", lambda e: e.dma_start(rowS[0:1, :], d_rows[0:1, t0:t0 + TH]),
          wait=[(qq, qq.n)], inc=[(qq, 16)])
        E("v", lambda e: e.tensor_scalar_max(rowS[0:1, :], rowS[0:1, :], QEPS),
          wait=[(qq, qq.n)], inc=[(sv, 1)])
        E("v", lambda e, o=t0: e.tensor_copy(rowB[0:1, o:o + TH], rowS[0:1, :]),
          inc=[(sv, 1)])
        E("v", lambda e: e.reciprocal(rowM[0:1, :], rowS[0:1, :]), inc=[(sv, 1)])
        E("v", lambda e: e.tensor_scalar_mul(rowM[0:1, :], rowM[0:1, :], 127.0),
          inc=[(sv, 1)])
        for t4 in range(2):
            E("t", lambda e, t4=t4: e.matmul(PS[6 + t4][:, :], c_ones[0:1, :],
                                             rowM[0:1, t4 * 512:(t4 + 1) * 512],
                                             start=True, stop=True),
              wait=[(sv, sv.n)], inc=[(st, 1)])
        bc = st.n
        for k in range(KD):
            qq = Q(f"sa{k % 2}")
            for t4 in range(2):
                sl = slice(t4 * 512, (t4 + 1) * 512)
                E("v", lambda e, kk=k, t4=t4, sl=sl: e.tensor_mul(
                    x2stg[kk % 2][:, sl], xn_half[:, kk, sl], PS[6 + t4][:, :]),
                    wait=[(st, bc)] + ([D[("p1q", hf, k - 2)]] if ("p1q", hf, k - 2) in D
                                       else ([D[("p1q", 0, 14 + (k % 2))]] if hf == 1 else [])),
                    inc=[(sv, 1)])
            E("v", lambda e, kk=k: e.tensor_scalar(
                xqstg[kk % 2][:, :], x2stg[kk % 2][:, :], MAGIC, MAGIC,
                ALU.add, ALU.subtract), inc=[(sv, 1)])
            E("q", lambda e, kk=k: e.dma_start(d_xqi[kk, :, t0:t0 + TH],
                                               xqstg[kk % 2][:, :]),
              wait=[(sv, sv.n)], inc=[(qq, 16)])
            D[("p1q", hf, k)] = (qq, qq.n)

    def p2_half(hf):
        t0 = hf * TH
        for o in (range(NO2) if p2_os is None else p2_os):
            grp = o % 2
            pg = [PS[2 * grp], PS[2 * grp + 1]]
            qq = Q(f"wb{o % 2}")
            E("g", lambda e, oo=o, b=p2wb[o % 2]: e.dma_start(b[:, :], win[oo]),
              wait=[(st, D.get(("p2mm", hf, o - 2), 0))], inc=[(qq, 16)])
            wv = [(qq, qq.n), (sv, D.get(("p2ev", hf, o - 2), 0)),
                  (ss, D.get(("p2es", hf, o - 2), 0)),
                  (sv, D.get(("p1bend", hf), 0))]
            for k in range(KD):
                for t2 in range(2):
                    E("t", lambda e, kk=k, t2=t2, b=p2wb[o % 2], pg=pg: e.matmul(
                        pg[t2][:, :], b[:, kk * 128:(kk + 1) * 128],
                        xn_half[:, kk, t2 * 512:(t2 + 1) * 512],
                        start=(kk == 0), stop=(kk == KD - 1)),
                        wait=wv if (k == 0 and t2 == 0) else [],
                        inc=[(st, 1)] if (k == KD - 1 and t2 == 1) else [])
            D[("p2mm", hf, o)] = st.n
            mmn = st.n
            if o < 32:
                qs2 = Q(f"pa{o % 2}")
                for t2 in range(2):
                    E("s", lambda e, t2=t2, b=sstg[o % 2], pg=pg: e.activation(
                        b[:, t2 * 512:(t2 + 1) * 512], pg[t2][:, :], AF.Silu, scale=1.0),
                        wait=[(st, mmn)] + ([D[("p2dma", hf, o - 2)]]
                                            if ("p2dma", hf, o - 2) in D else []),
                        inc=[(ss, 1)])
                D[("p2es", hf, o)] = ss.n
                D[("p2ev", hf, o)] = sv.n
                dst = d_gate[hf * 8:(hf + 1) * 8, o].transpose([1, 0, 2])
                E("q", lambda e, b=sstg[o % 2], dst=dst: e.dma_start(
                    dst, b[:, :].rearrange("p (c l) -> p c l", c=8)),
                    wait=[(ss, ss.n)], inc=[(qs2, 16)])
                D[("p2dma", hf, o)] = (qs2, qs2.n)
            elif o < 66:
                cc = o - 32
                acc = cacc[o % 2]
                for t2 in range(2):
                    E("v", lambda e, t2=t2, acc=acc, cc=cc, pg=pg: e.tensor_scalar_mul(
                        acc[:, t2 * 512:(t2 + 1) * 512], pg[t2][:, :],
                        c_convw[:, cc * 4 + 3:cc * 4 + 4]),
                        wait=[(st, mmn), (ss, D.get(("p2es", hf, o - 2), 0))],
                        inc=[(sv, 1)])
                for sft in (1, 2, 3):
                    kk = 3 - sft
                    wc = c_convw[:, cc * 4 + kk:cc * 4 + kk + 1]
                    if hf == 1:
                        E("v", lambda e, sft=sft, acc=acc, cc=cc, wc=wc:
                          e.scalar_tensor_tensor(
                              acc[:, 0:sft], c_tails[:, cc * 3 + 3 - sft:cc * 3 + 3], wc,
                              acc[:, 0:sft], ALU.mult, ALU.add), inc=[(sv, 1)])
                    for t2 in range(2):
                        o_lo = max(sft, t2 * 512)
                        o_hi = (t2 + 1) * 512
                        i_lo = o_lo - sft
                        while o_lo < o_hi:
                            sb = i_lo // 512
                            i_in = i_lo % 512
                            n = min(o_hi - o_lo, 512 - i_in)
                            E("v", lambda e, acc=acc, o_lo=o_lo, n=n, sb=sb, i_in=i_in,
                              wc=wc, pg=pg: e.scalar_tensor_tensor(
                                  acc[:, o_lo:o_lo + n], pg[sb][:, i_in:i_in + n], wc,
                                  acc[:, o_lo:o_lo + n], ALU.mult, ALU.add),
                                inc=[(sv, 1)])
                            o_lo += n
                            i_lo += n
                if hf == 0:
                    E("v", lambda e, cc=cc, pg=pg: e.tensor_copy(
                        c_tails[:, cc * 3:cc * 3 + 3], pg[1][:, 509:512]), inc=[(sv, 1)])
                D[("p2ev", hf, o)] = sv.n
                evn = sv.n
                if cc < 32:
                    qs2 = Q(f"pa{o % 2}")
                    for t2 in range(2):
                        E("s", lambda e, t2=t2, b=sstg[o % 2], acc=acc, cc=cc: e.activation(
                            b[:, t2 * 512:(t2 + 1) * 512], acc[:, t2 * 512:(t2 + 1) * 512],
                            AF.Silu, bias=c_convb[:, cc:cc + 1], scale=1.0),
                            wait=[(sv, evn)] + ([D[("p2dma", hf, o - 2)]]
                                                if ("p2dma", hf, o - 2) in D else []),
                            inc=[(ss, 1)])
                    D[("p2es", hf, o)] = ss.n
                    dst = d_xssm[hf * 8:(hf + 1) * 8, cc].transpose([1, 0, 2])
                    E("q", lambda e, b=sstg[o % 2], dst=dst: e.dma_start(
                        dst, b[:, :].rearrange("p (c l) -> p c l", c=8)),
                        wait=[(ss, ss.n)], inc=[(qs2, 16)])
                    D[("p2dma", hf, o)] = (qs2, qs2.n)
                else:
                    dstv = B_sb if cc == 32 else C_sb
                    for t2 in range(2):
                        E("s", lambda e, t2=t2, dv=dstv, acc=acc, cc=cc: e.activation(
                            dv[:, t0 + t2 * 512:t0 + (t2 + 1) * 512],
                            acc[:, t2 * 512:(t2 + 1) * 512],
                            AF.Silu, bias=c_convb[:, cc:cc + 1], scale=1.0),
                            wait=[(sv, evn)], inc=[(ss, 1)])
                    D[("p2es", hf, o)] = ss.n
            else:
                for t2 in range(2):
                    E("v", lambda e, t2=t2, pg=pg: e.tensor_scalar_add(
                        dt_sp[:, t0 + t2 * 512:t0 + (t2 + 1) * 512], pg[t2][0:64, :],
                        c_dtb[:, :]), wait=[(st, mmn)], inc=[(sv, 1)])
                dts = dt_sp[:, t0:t0 + TH]
                E("s", lambda e, d2=dts: e.activation(spa[:, :], d2, AF.Abs, scale=1.0),
                  wait=[(sv, sv.n)], inc=[(ss, 1)])
                E("s", lambda e: e.activation(spa[:, :], spa[:, :], AF.Exp, scale=-1.0),
                  inc=[(ss, 1)])
                E("s", lambda e: e.activation(spa[:, :], spa[:, :], AF.Ln,
                                              bias=c_ones[0:64, 0:1], scale=1.0),
                  inc=[(ss, 1)])
                E("v", lambda e, d2=dts: e.tensor_scalar_max(dtrelu[:, :], d2, 0.0),
                  wait=[(ss, ss.n)], inc=[(sv, 1)])
                E("v", lambda e, d2=dts: e.tensor_add(d2, spa[:, :], dtrelu[:, :]),
                  inc=[(sv, 1)])
                D[("p2ev", hf, o)] = sv.n
                D[("dt", hf)] = sv.n

    def p0_finish():
        E("v", lambda e: e.memset(carry[:, :], 0.0),
          wait=[(st, D[("p2mm", 1, NO2 - 1)])], inc=[(sv, 1)])
        E("v", lambda e: e.memset(carry_bf[:, :], 0.0), inc=[(sv, 1)])
        E("t", lambda e: e.matmul(PS[4][0:2, 0:2], c_ones[:, 0:2], c_wacc[:, :],
                                  start=True, stop=True),
          wait=[(sv, sv.n)], inc=[(st, 1)])
        E("v", lambda e: e.tensor_scalar(c_sw[0:1, 0:2], PS[4][0:1, 0:2],
                                         1.0 / (HFF * DM), QEPS, ALU.mult, ALU.max),
          wait=[(st, st.n)], inc=[(sv, 1)])
        E("v", lambda e: e.reciprocal(c_sw[0:1, 2:4], c_sw[0:1, 0:2]), inc=[(sv, 1)])
        E("t", lambda e: e.matmul(PS[5][:, 0:2], c_ones[0:1, :], c_sw[0:1, 2:4],
                                  start=True, stop=True), wait=[(sv, sv.n)], inc=[(st, 1)])
        E("v", lambda e: e.tensor_copy(c_swcol[:, :], PS[5][:, 0:2]),
          wait=[(st, st.n)], inc=[(sv, 1)])
        for hf in range(2):
            E("v", lambda e, hf=hf: e.tensor_scalar(
                inv1h[hf][0:1, :], rowB[0:1, hf * TH:(hf + 1) * TH],
                c_sw[0:1, 0:1], 1.0 / 127.0, ALU.mult, ALU.mult), inc=[(sv, 1)])
        D["p0fin"] = sv.n

    def p3_loads(c):
        par = c % 2
        qq = Q(f"ta{par}")
        src = d_xssm[c].rearrange("a p l -> (a p) l")
        E("s", lambda e, par=par, src=src: e.dma_start(xssmT[par][:, :], src,
                                                       transpose=True),
          wait=[(sv, D.get(("xtfree", c - 2), 0))] + D["p2spill"],
          inc=[(qq, 16)])
        D[("xtq", c)] = (qq, qq.n)

    def p3_group_loads(c, g4):
        qq1 = Q(f"ga{g4 % 2}")
        qq2 = Q(f"gb{g4 % 2}")
        wv = [(sv, D.get(("grpfree", c, g4 - 2), 0)) if g4 >= 2 else
              (sv, D.get(("grpfree", c - 1, g4 + 14), 0)),
              (st, D.get("p2end_t", 0))] + \
            ([D["xntap1"]] if "xntap1" in D else [])
        E("q", lambda e, c=c, g4=g4: e.dma_start(
            gateg[g4 % 2][:, :, :], d_gate[c, 2 * g4:2 * g4 + 2].transpose([1, 0, 2])),
            wait=wv + D["p2spill"], inc=[(qq1, 16)])
        E("q", lambda e, c=c, g4=g4: e.dma_start(
            xssmg[g4 % 2][:, :, :], d_xssm[c, 2 * g4:2 * g4 + 2].transpose([1, 0, 2])),
            wait=wv, inc=[(qq2, 16)])
        D[("grpq", c, g4)] = [(qq1, qq1.n), (qq2, qq2.n)]

    def p3_chunk_prelude(c):
        par = c % 2
        E("v", lambda e, c=c: e.tensor_scalar_mul(a_c[:, :], dt_sp[:, c * L:(c + 1) * L],
                                                  c_acol[:, :]),
          wait=[(sv, D.get(("dt", c // 8), 0)),
                (ss, D.get(("presfree", c - 1), 0))], inc=[(sv, 1)])
        E("v", lambda e: e.tensor_tensor_scan(cs_c[:, :], c_ones[0:64, :], a_c[:, :],
                                              0.0, ALU.mult, ALU.add), inc=[(sv, 1)])
        csn = sv.n
        E("s", lambda e: e.activation(ecs_c[:, :], cs_c[:, :], AF.Exp, scale=1.0),
          wait=[(sv, csn)], inc=[(ss, 1)])
        ecn = ss.n
        wv_ps = [(sv, D.get("w1evlast", 0))]
        E("t", lambda e: e.transpose(PS[6][:, 0:64], cs_c[:, :], c_idn[0:64, 0:64]),
          wait=[(sv, csn)] + wv_ps, inc=[(st, 1)])
        E("v", lambda e: e.tensor_copy(csT_c[:, :], PS[6][:, 0:64]),
          wait=[(st, st.n)], inc=[(sv, 1)])
        E("t", lambda e, c=c: e.transpose(PS[7][:, 0:64], dt_sp[:, c * L:(c + 1) * L],
                                          c_idn[0:64, 0:64]), inc=[(st, 1)])
        E("v", lambda e: e.tensor_copy(dtT_c[:, :], PS[7][:, 0:64]),
          wait=[(st, st.n)], inc=[(sv, 1)])
        E("t", lambda e, c=c: e.transpose(PSb(6, 256, 128), B_sb[:, c * L:(c + 1) * L],
                                          c_idnb[:, :]), wait=[(sv, sv.n)], inc=[(st, 1)])
        E("v", lambda e: e.tensor_copy(B_t[:, :], PSb(6, 256, 128)),
          wait=[(st, st.n)], inc=[(sv, 1)])
        E("t", lambda e, c=c: e.matmul(PS[7][:, 64:192], B_sb[:, c * L:(c + 1) * L],
                                       C_sb[:, c * L:(c + 1) * L], start=True, stop=True),
          inc=[(st, 1)])
        E("v", lambda e: e.tensor_mul(scoresM[:, :], PS[7][:, 64:192], c_mask[:, :]),
          wait=[(st, st.n)], inc=[(sv, 1)])
        D[("csready", c)] = (csn, ecn)
        for g8 in range(2):
            qf1, qf2 = Q(f"tb{g8 % 2}"), Q(f"tc{g8 % 2}")
            wv = [(st, D.get(("flfree", c - 1), 0))]
            E("s", lambda e, g8=g8: e.dma_start(csf[g8 % 2][0:1, :],
                                                cs_c[8 * g8:8 * g8 + 8, :]),
              wait=[(sv, csn)] + wv, inc=[(qf1, 16)])
            E("s", lambda e, g8=g8: e.dma_start(ecsf[g8 % 2][0:1, :],
                                                ecs_c[8 * g8:8 * g8 + 8, :]),
              wait=[(ss, ecn)] + wv, inc=[(qf2, 16)])
            D[("flq", c, g8)] = [(qf1, qf1.n), (qf2, qf2.n)]
        qtd = Q("td")
        E("s", lambda e: e.dma_start(cslf[0:1, :], cs_c[:, 127:128]),
          wait=[(sv, csn), (st, D.get(("flfree", c - 1), 0))], inc=[(qtd, 16)])
        E("t", lambda e: e.matmul(PS[7][:, 192:256], c_ones[0:1, :], cslf[0:1, :],
                                  start=True, stop=True),
          wait=[(qtd, qtd.n)], inc=[(st, 1)])
        E("v", lambda e: e.tensor_sub(dsT_c[:, :], PS[7][:, 192:256], csT_c[:, :]),
          wait=[(st, st.n)], inc=[(sv, 1)])
        E("s", lambda e: e.activation(dsT_c[:, :], dsT_c[:, :], AF.Exp, scale=1.0),
          wait=[(sv, sv.n)], inc=[(ss, 1)])
        E("s", lambda e: e.activation(cd_c[:, :], PS[7][:, 192:256], AF.Exp, scale=1.0),
          inc=[(ss, 1)])
        E("v", lambda e, par=par: e.tensor_tensor(
            xssmT[par][:, :].rearrange("p (h d2) -> p h d2", h=NH),
            xssmT[par][:, :].rearrange("p (h d2) -> p h d2", h=NH),
            dtT_c[:, :].unsqueeze(2).broadcast_to([128, NH, HD]), ALU.mult),
            wait=[D[("xtq", c)]], inc=[(sv, 1)])
        D[("pre", c)] = (sv.n, ss.n, st.n)
        D["presv"] = sv.n
        D[("presfree", c)] = ss.n

    def p3_chunk(c):
        par = c % 2
        for g4 in range(16):
            p3_group_loads(c, g4)
            for j in range(4):
                h = g4 * 4 + j
                g8, pr = h // 8, h // 2
                bpar = pr % 2
                if h % 2 == 0:
                    wv = list(D[("flq", c, g8)])
                    key = ("prread", c, pr - 2) if pr >= 2 else ("prread", c - 1, pr + 30)
                    if key in D:
                        wv += [(ss, D[key][0]), (sv, D[key][1])]
                    E("t", lambda e, bpar=bpar, g8=g8, h=h: e.matmul(
                        PS[bpar][:, 0:256], c_ones[0:1, :],
                        csf[g8 % 2][0:1, (h % 8) * 128:(h % 8) * 128 + 256],
                        start=True, stop=True), wait=wv, inc=[(st, 1)])
                    E("t", lambda e, bpar=bpar, g8=g8, h=h: e.matmul(
                        PS[bpar][:, 256:512], c_ones[0:1, :],
                        ecsf[g8 % 2][0:1, (h % 8) * 128:(h % 8) * 128 + 256],
                        start=True, stop=True), inc=[(st, 1)])
                    D[("prbc", c, pr)] = st.n
                E("s", lambda e, bpar=bpar, h=h: e.activation(
                    m1sc[:, :], PS[bpar][:, (h % 2) * 128:(h % 2) * 128 + 128], AF.Relu,
                    bias=csT_c[:, h:h + 1], scale=-1.0),
                    wait=[(st, D[("prbc", c, pr)])], inc=[(ss, 1)])
                E("s", lambda e, h=h: e.activation(edif[h % 2][:, :], m1sc[:, :],
                                                   AF.Exp, scale=-1.0), inc=[(ss, 1)])
                edn = ss.n
                E("v", lambda e, h=h: e.tensor_mul(mtt[h % 2][:, :], edif[h % 2][:, :],
                                                   scoresM[:, :]),
                  wait=[(ss, edn)], inc=[(sv, 1)])
                E("v", lambda e, h=h, bpar=bpar, c=c: e.tensor_mul(
                    cexp[h % 2][:, :], C_sb[:, c * L:(c + 1) * L],
                    PS[bpar][:, 256 + (h % 2) * 128:256 + (h % 2) * 128 + 128]),
                    inc=[(sv, 1)])
                mtn = sv.n
                D[("prread", c, pr)] = (ss.n, sv.n)
                yb = 2 + (g4 % 2)
                pb, cl = 64 * (h % 2), 128 * ((h % 4) // 2)
                wv = [(sv, mtn)]
                key = ("yfree", c, g4 - 2) if g4 >= 2 else ("yfree", c - 1, g4 + 14)
                if key in D:
                    wv.append((sv, D[key]))
                E("t", lambda e, yb=yb, pb=pb, cl=cl, h=h, par=par: e.matmul(
                    PS[yb][pb:pb + 64, cl:cl + 128],
                    xssmT[par][:, h * HD:(h + 1) * HD], mtt[h % 2][:, :],
                    start=True, stop=False), wait=wv, inc=[(st, 1)])
                E("t", lambda e, yb=yb, pb=pb, cl=cl, h=h: e.matmul(
                    PS[yb][pb:pb + 64, cl:cl + 128],
                    carry_bf[:, h * HD:(h + 1) * HD], cexp[h % 2][:, :],
                    start=False, stop=True), inc=[(st, 1)])
                if h % 8 == 6 and h // 8 + 2 <= 7:
                    gnx = h // 8 + 2
                    qf1, qf2 = Q(f"tb{gnx % 2}"), Q(f"tc{gnx % 2}")
                    E("s", lambda e, gnx=gnx: e.dma_start(
                        csf[gnx % 2][0:1, :], cs_c[8 * gnx:8 * gnx + 8, :]),
                        wait=[(st, st.n)], inc=[(qf1, 16)])
                    E("s", lambda e, gnx=gnx: e.dma_start(
                        ecsf[gnx % 2][0:1, :], ecs_c[8 * gnx:8 * gnx + 8, :]),
                        wait=[(st, st.n)], inc=[(qf2, 16)])
                    D[("flq", c, gnx)] = [(qf1, qf1.n), (qf2, qf2.n)]
            E("v", lambda e, par=par, g4=g4: e.tensor_tensor(
                xdtds_g[g4 % 2][:, :, :],
                xssmT[par][:, g4 * 256:(g4 + 1) * 256].rearrange("p (a d2) -> p a d2", a=4),
                dsT_c[:, g4 * 4:g4 * 4 + 4].unsqueeze(2).broadcast_to([128, 4, HD]),
                ALU.mult), wait=[(st, D.get(("grpst", c, g4 - 2),
                                            D.get(("grpst", c - 1, 15), 0)))],
                inc=[(sv, 1)])
            E("t", lambda e, g4=g4: e.matmul(
                PS[2 + (g4 % 2)][:, 256:512], B_t[:, :],
                xdtds_g[g4 % 2][:, :, :].rearrange("p a d2 -> p (a d2)"),
                start=True, stop=True), wait=[(sv, sv.n)], inc=[(st, 1)])
            grp_st = st.n
            D[("grpst", c, g4)] = grp_st
            for j in range(4):
                h = g4 * 4 + j
                yb = 2 + (g4 % 2)
                pb = 64 * (h % 2)
                cl = 128 * ((h % 4) // 2)
                kb = h // 2
                E("v", lambda e, yb=yb, pb=pb, cl=cl, h=h, j=j, g4=g4, kb=kb:
                  e.scalar_tensor_tensor(
                      t1st[pb:pb + 64, (h % 4) // 2, :],
                      xssmg[g4 % 2][pb:pb + 64, j // 2, :],
                      c_dpp[pb:pb + 64, kb:kb + 1],
                      PS[yb][pb:pb + 64, cl:cl + 128], ALU.mult, ALU.add),
                    wait=[(st, grp_st)] + D[("grpq", c, g4)] +
                    ([D[("hrq", c - 2)]] if (g4 == 0 and j == 0 and ("hrq", c - 2) in D)
                     else []),
                    inc=[(sv, 1)])
                E("v", lambda e, par=par, pb=pb, h=h, j=j, g4=g4, kb=kb: e.tensor_mul(
                    hready_asm[par][pb:pb + 64, kb, :],
                    t1st[pb:pb + 64, (h % 4) // 2, :],
                    gateg[g4 % 2][pb:pb + 64, j // 2, :]), inc=[(sv, 1)])
                E("v", lambda e, yb=yb, h=h, j=j: e.scalar_tensor_tensor(
                    carry[:, h * HD:(h + 1) * HD], carry[:, h * HD:(h + 1) * HD],
                    cd_c[:, h:h + 1], PS[yb][:, 256 + j * 64:256 + (j + 1) * 64],
                    ALU.mult, ALU.add), inc=[(sv, 1)])
                E("v", lambda e, h=h: e.tensor_copy(carry_bf[:, h * HD:(h + 1) * HD],
                                                    carry[:, h * HD:(h + 1) * HD]),
                  inc=[(sv, 1)])
            D[("yfree", c, g4)] = sv.n
            D[("grpfree", c, g4)] = sv.n
        D[("xtfree", c)] = sv.n
        D[("flfree", c)] = st.n
        qq = Q(f"ha{c % 2}")
        E("q", lambda e, par=par, c=c: e.dma_start(
            d_hready[:, :, c * L:(c + 1) * L].transpose([1, 0, 2]),
            hready_asm[par][:, :, :]),
            wait=[(sv, sv.n)], inc=[(qq, 16)])
        D[("hrq", c)] = (qq, qq.n)

    def p5_init_half(hf):
        qq = Q("xq")
        wv = [(st, D[("w1mm", 0, 63)])] if hf == 1 else []
        E("q", lambda e: e.dma_start(
            xqi_half[:, :, :], d_xqi[:, :, hf * TH:(hf + 1) * TH].transpose([1, 0, 2])),
            wait=wv + [D[("p1q", hf, KD - 1)], D[("p1q", hf, KD - 2)]], inc=[(qq, 16)])
        D[("xqiq", hf)] = (qq, qq.n)
        for t4 in range(2):
            E("t", lambda e, hf=hf, t4=t4: e.matmul(
                PS[4 + t4][:, :], c_ones[0:1, :],
                inv1h[hf][0:1, t4 * 512:(t4 + 1) * 512], start=True, stop=True),
                wait=[(sv, D["p0fin"]), (sv, D.get("w1evlast", 0))], inc=[(st, 1)])
        bn = st.n
        E("v", lambda e: e.tensor_copy(invsbc[:, 0:512], PS[4][:, :]),
          wait=[(st, bn)], inc=[(sv, 1)])
        E("v", lambda e: e.tensor_copy(invsbc[:, 512:1024], PS[5][:, :]), inc=[(sv, 1)])
        D[("invn", hf)] = sv.n

    def p5_batch(hf, olo, ohi):
        for o in range(olo, ohi):
            ob = o % 2
            wmm_guard = (st, D.get(("w1mm", hf, o - 2),
                                   D.get(("w1mm", 0, 62 + (o % 2)), 0) if hf == 1
                                   else 0))
            for h2 in range(2):
                qq = Q(f"wa{h2}{o % 2}")
                sl = slice(h2 * 1024, (h2 + 1) * 1024)
                bB = w1bf[h2][o % 2]
                E("g", lambda e, o=o, b=bB, sl=sl: e.dma_start(b[:, :], w1[o][:, sl]),
                  wait=[(sv, D.get(("w1qh", hf, o - 2, h2),
                                   D.get(("w1qh", 1 - hf, 62 + o % 2, h2), 0))),
                        (sv, D.get("p0done", 0))],
                  inc=[(qq, 16)])
                wn = (qq, qq.n)
                E("v", lambda e, b=bB: e.tensor_scalar(
                    wqscr[:, :], b[:, :], c_swcol[:, 0:1], CLIP1, ALU.mult, ALU.min),
                    wait=[wn, (sv, D["p0fin"]), wmm_guard], inc=[(sv, 1)])
                E("v", lambda e: e.tensor_scalar(wqscr[:, :], wqscr[:, :], -CLIP1, MAGIC,
                                                 ALU.max, ALU.add), inc=[(sv, 1)])
                E("v", lambda e, o=o, sl=sl: e.tensor_scalar(
                    wq1b[o % 2][:, sl], wqscr[:, :], MAGIC, None, ALU.subtract),
                    inc=[(sv, 1)])
                D[("w1qh", hf, o, h2)] = sv.n
            D[("w1q", hf, o)] = sv.n
            wv = [(sv, D[("w1q", hf, o)]), D[("xqiq", hf)], (sv, D["presv"]),
                  (sv, D.get(("w1ev", hf, o - 2), 0))]
            if o < 2:
                wv.append((sv, D[("invn", hf)]))
            pg = [PS[4 + 2 * (o % 2)], PS[5 + 2 * (o % 2)]]
            for k in range(KD):
                for t2 in range(2):
                    E("t", lambda e, o=o, k=k, t2=t2, pg=pg: e.matmul(
                        pg[t2][:, :], wq1b[o % 2][:, k * 128:(k + 1) * 128],
                        xqi_half[:, k, t2 * 512:(t2 + 1) * 512],
                        start=(k == 0), stop=(k == KD - 1)),
                        wait=wv if (k == 0 and t2 == 0) else [],
                        inc=[(st, 1)] if (k == KD - 1 and t2 == 1) else [])
            D[("w1mm", hf, o)] = st.n
            mmn = st.n
            qh = Q(f"h1{o % 2}")
            for t2 in range(2):
                E("v", lambda e, o=o, t2=t2, pg=pg: e.tensor_mul(
                    h1stg[o % 2][:, t2 * 512:(t2 + 1) * 512], pg[t2][:, :],
                    invsbc[:, t2 * 512:(t2 + 1) * 512]),
                    wait=[(st, mmn), (sv, D[("invn", hf)])] +
                    ([D[("h1q", hf, o - 2)]] if ("h1q", hf, o - 2) in D else
                     ([D[("h1q", 0, 62 + (o % 2))]] if hf == 1 and o < 2 else
                      [(Q("sa0"), Q("sa0").n), (Q("sa1"), Q("sa1").n)])),
                    inc=[(sv, 1)])
            D[("w1ev", hf, o)] = sv.n
            D["w1evlast"] = sv.n
            E("q", lambda e, o=o, hf=hf: e.dma_start(
                d_h1[o, :, hf * TH:(hf + 1) * TH], h1stg[o % 2][:, :]),
                wait=[(sv, sv.n)], inc=[(qh, 16)])
            D[("h1q", hf, o)] = (qh, qh.n)

    def p4_half(hf):
        t0 = hf * TH
        q1, q2 = Q("p40"), Q("p41")
        wv = [D[("hrq", NCH - 1)], D[("hrq", NCH - 2)], (st, D[("w1mm", 1, 63)])]
        if hf == 1:
            wv.append((st, D[("p4mm", 0, 15)]))
            wv.append((sv, D[("p4sq", 0, 31)]))
        E("q", lambda e: e.dma_start(
            hr4_1[:, :, :], d_hready[0:16, :, t0:t0 + TH].transpose([1, 0, 2])),
            wait=wv, inc=[(q1, 16)])
        E("q", lambda e: e.dma_start(
            hr4_2[:, :, :], d_hready[16:32, :, t0:t0 + TH].transpose([1, 0, 2])),
            wait=wv, inc=[(q2, 16)])
        ldq = [(q1, q1.n), (q2, q2.n)]
        E("v", lambda e: e.memset(vacc4[:, :], 0.0), inc=[(sv, 1)])
        for k in range(32):
            hr = hr4_1 if k < 16 else hr4_2
            kk = k % 16
            E("s", lambda e, hr=hr, kk=kk, k=k: e.activation(
                sq4[k % 2][:, :], hr[:, kk, :], AF.Square, scale=1.0),
                wait=ldq + [(sv, D.get(("p4sq", hf, k - 2), 0))], inc=[(ss, 1)])
            E("v", lambda e, k=k: e.tensor_add(vacc4[:, :], vacc4[:, :], sq4[k % 2][:, :]),
              wait=[(ss, ss.n)], inc=[(sv, 1)])
            E("v", lambda e, hr=hr, kk=kk, k=k: e.tensor_scalar_mul(
                hr[:, kk, :], hr[:, kk, :], c_nwpp[:, k:k + 1]), inc=[(sv, 1)])
            D[("p4sq", hf, k)] = sv.n
        nwn = sv.n
        for t2 in range(2):
            E("t", lambda e, t2=t2: e.matmul(PS[4 + t2][0:1, :], c_ones[:, 0:1],
                                             vacc4[:, t2 * 512:(t2 + 1) * 512],
                                             start=True, stop=True),
              wait=[(sv, nwn)], inc=[(st, 1)])
        E("v", lambda e: e.tensor_scalar_mul(rowM[0:1, 0:512], PS[4][0:1, :],
                                             1.0 / INTER),
          wait=[(st, st.n)] + ([D[("varq", hf - 1)]] if ("varq", hf - 1) in D else []),
          inc=[(sv, 1)])
        E("v", lambda e: e.tensor_scalar_mul(rowM[0:1, 512:1024], PS[5][0:1, :],
                                             1.0 / INTER), inc=[(sv, 1)])
        qr4 = Q(f"r4{hf}")
        E("q", lambda e, t0=t0: e.dma_start(d_rows[2:3, t0:t0 + TH], rowM[0:1, :]),
          wait=[(sv, sv.n)], inc=[(qr4, 16)])
        D[("varq", hf)] = (qr4, qr4.n)
        for o in range(16):
            pg = [PS[2 * (o % 2)], PS[2 * (o % 2) + 1]]
            qq = Q(f"wc{o % 2}")
            E("g", lambda e, o=o: e.dma_start(wob[o % 2][:, :], wout[o]),
              wait=[(st, D.get(("p4mm", hf, o - 2), 0))], inc=[(qq, 16)])
            wv2 = [(qq, qq.n), (sv, nwn), (sv, D.get(("p4ev", hf, o - 2), 0))]
            for k in range(32):
                hr = hr4_1 if k < 16 else hr4_2
                kk = k % 16
                for t2 in range(2):
                    E("t", lambda e, o=o, k=k, t2=t2, hr=hr, kk=kk, pg=pg: e.matmul(
                        pg[t2][:, :], wob[o % 2][:, k * 128:(k + 1) * 128],
                        hr[:, kk, t2 * 512:(t2 + 1) * 512],
                        start=(k == 0), stop=(k == 31)),
                        wait=wv2 if (k == 0 and t2 == 0) else [],
                        inc=[(st, 1)] if (k == 31 and t2 == 1) else [])
            D[("p4mm", hf, o)] = st.n
            qst = Q(f"st{o % 2}")
            for t2 in range(2):
                E("v", lambda e, o=o, t2=t2, pg=pg: e.tensor_copy(
                    stil4[o % 2][:, t2 * 512:(t2 + 1) * 512], pg[t2][:, :]),
                    wait=[(st, st.n)] + ([D[("stq", hf, o - 2)]]
                                         if ("stq", hf, o - 2) in D else
                                         ([D[("stq", 0, 14 + (o % 2))]]
                                          if hf == 1 and o < 2 else [])),
                    inc=[(sv, 1)])
            D[("p4ev", hf, o)] = sv.n
            E("q", lambda e, o=o, t0=t0: e.dma_start(
                d_stil[o, :, t0:t0 + TH], stil4[o % 2][:, :]),
                wait=[(sv, sv.n)], inc=[(qst, 16)])
            D[("stq", hf, o)] = (qst, qst.n)

    def p4_w2quant():
        for i in range(32):
            o, sub = i // 2, i % 2
            qwq = Q(f"wq{i % 2}")
            for h2 in range(2):
                j = i * 2 + h2
                qq = Q(f"wd{j % 2}")
                sl_src = slice(sub * 4096 + h2 * 2048, sub * 4096 + (h2 + 1) * 2048)
                sl_dst = slice(h2 * 2048, (h2 + 1) * 2048)
                E("g", lambda e, o=o, b=w2q_raw[j % 2], sl_src=sl_src: e.dma_start(
                    b[:, :], w2[o][:, sl_src]),
                    wait=[(sv, D.get(("w2qj", j - 2), 0))], inc=[(qq, 16)])
                wn = (qq, qq.n)
                E("v", lambda e, j=j, b=w2q_raw[j % 2]: e.tensor_scalar(
                    w2q_scr[:, :], b[:, :], c_swcol[:, 1:2], CLIP1, ALU.mult, ALU.min),
                    wait=[wn, (sv, D["p0fin"])] +
                    ([D[("wq2d", i - 1)]] if (h2 == 0 and ("wq2d", i - 1) in D) else []),
                    inc=[(sv, 1)])
                E("v", lambda e: e.tensor_scalar(w2q_scr[:, :], w2q_scr[:, :],
                                                 -CLIP1, MAGIC, ALU.max, ALU.add),
                  inc=[(sv, 1)])
                E("v", lambda e, sl_dst=sl_dst: e.tensor_scalar(
                    w2q_stg[:, sl_dst], w2q_scr[:, :], MAGIC, None, ALU.subtract),
                    inc=[(sv, 1)])
                D[("w2qj", j)] = sv.n
            D[("w2q", i)] = sv.n
            E("q", lambda e, o=o, sub=sub: e.dma_start(
                d_wq2[o][:, sub * 4096:(sub + 1) * 4096], w2q_stg[:, :]),
                wait=[(sv, sv.n)], inc=[(qwq, 16)])
            D[("wq2d", i)] = (qwq, qwq.n)

    def p6_quarter(q4):
        t0 = q4 * TQ
        q1, q2 = Q("p60"), Q("p61")
        wv = [D[("wq2d", 31)], D[("wq2d", 30)], D[("h1q", 1, 63)], D[("h1q", 1, 62)]]
        if q4 > 0:
            wv.append((st, D[("p6mm", q4 - 1, 15)]))
            wv.append((ss, D[("p6ev", q4 - 1, 15)]))
        else:
            wv.append((st, D[("p4mm", 1, 15)]))
            wv.append((sv, D[("p4ev", 1, 15)]))
        if "hq0tap" in D:
            wv += D["hq0tap"]
        if "sch0tap" in D:
            wv += D["sch0tap"]
        E("q", lambda e, q4=q4: e.dma_start(
            hq6_1[:, :, :], d_h1[0:32, :, t0:t0 + TQ].transpose([1, 0, 2])),
            wait=wv, inc=[(q1, 16)])
        E("q", lambda e, q4=q4: e.dma_start(
            hq6_2[:, :, :], d_h1[32:64, :, t0:t0 + TQ].transpose([1, 0, 2])),
            wait=wv, inc=[(q2, 16)])
        ldq = [(q1, q1.n), (q2, q2.n)]
        E("v", lambda e: e.memset(rmq[:, :], 0.0),
          wait=[D[("stq", 1, 15)], D[("stq", 1, 14)]] if q4 == 0 else [],
          inc=[(sv, 1)])
        E("v", lambda e: e.memset(rmqn[:, :], 0.0), inc=[(sv, 1)])
        for k in range(64):
            hq = hq6_1 if k < 32 else hq6_2
            kk = k % 32
            E("s", lambda e, hq=hq, kk=kk: e.activation(hq[:, kk, :], hq[:, kk, :],
                                                        AF.Gelu, scale=1.0),
              wait=ldq if k == 0 else [], inc=[(ss, 1)])
            E("v", lambda e, hq=hq, kk=kk: e.tensor_max(rmq[:, :], rmq[:, :],
                                                        hq[:, kk, :]),
              wait=[(ss, ss.n)], inc=[(sv, 1)])
            E("v", lambda e, hq=hq, kk=kk: e.tensor_tensor(rmqn[:, :], rmqn[:, :],
                                                           hq[:, kk, :], ALU.min),
              inc=[(sv, 1)])
        E("v", lambda e: e.tensor_scalar_mul(rmqn[:, :], rmqn[:, :], -1.0), inc=[(sv, 1)])
        E("v", lambda e: e.tensor_max(rmq[:, :], rmq[:, :], rmqn[:, :]), inc=[(sv, 1)])
        for i in range(4):
            E("t", lambda e, i=i: e.transpose(PS[6][:, 0:128],
                                              rmq[:, i * 128:(i + 1) * 128],
                                              c_idn[:, :]),
              wait=[(sv, sv.n)], inc=[(st, 1)])
            E("v", lambda e, i=i: e.reduce_max(mx6[:, i:i + 1], PS[6][:, 0:128], AX),
              wait=[(st, st.n)], inc=[(sv, 1)])
        qr6 = Q(f"r6{q4}")
        E("q", lambda e, dst=row_assemble_dst(1, t0, 4): e.dma_start(
            dst, mx6[:, :], allow_slow_non_contiguous=True),
            wait=[(sv, sv.n)], inc=[(qr6, 16)])
        E("q", lambda e, t0=t0: e.dma_start(row6[0][0:1, :], d_rows[1:2, t0:t0 + TQ]),
          wait=[(qr6, qr6.n)] + ([D[("varq", 1)], D[("varq", 0)]] if q4 == 0 else []),
          inc=[(qr6, 16)])
        E("v", lambda e: e.tensor_scalar_max(row6[0][0:1, :], row6[0][0:1, :], QEPS),
          wait=[(qr6, qr6.n)], inc=[(sv, 1)])
        E("v", lambda e: e.reciprocal(row6[1][0:1, :], row6[0][0:1, :]), inc=[(sv, 1)])
        E("v", lambda e: e.tensor_scalar_mul(row6[1][0:1, :], row6[1][0:1, :], 127.0),
          inc=[(sv, 1)])
        E("v", lambda e: e.tensor_scalar(row6[2][0:1, :], row6[0][0:1, :],
                                         c_sw[0:1, 1:2], 1.0 / 127.0, ALU.mult, ALU.mult),
          inc=[(sv, 1)])
        E("t", lambda e: e.matmul(PS[6][:, :], c_ones[0:1, :], row6[1][0:1, :],
                                  start=True, stop=True), wait=[(sv, sv.n)], inc=[(st, 1)])
        E("t", lambda e: e.matmul(PS[7][:, :], c_ones[0:1, :], row6[2][0:1, :],
                                  start=True, stop=True), inc=[(st, 1)])
        E("v", lambda e: e.tensor_copy(schbc[:, :], PS[6][:, :]),
          wait=[(st, st.n)], inc=[(sv, 1)])
        E("v", lambda e: e.tensor_copy(i2bc[:, :], PS[7][:, :]), inc=[(sv, 1)])
        bcn = sv.n
        for k in range(64):
            hq = hq6_1 if k < 32 else hq6_2
            kk = k % 32
            E("v", lambda e, hq=hq, kk=kk, k=k: e.tensor_mul(
                t6scr[k % 2][:, :], hq[:, kk, :], schbc[:, :]),
                wait=[(sv, bcn)] if k == 0 else [], inc=[(sv, 1)])
            E("v", lambda e, hq=hq, kk=kk, k=k: e.tensor_scalar(
                hq[:, kk, :], t6scr[k % 2][:, :], MAGIC, MAGIC, ALU.add, ALU.subtract),
                inc=[(sv, 1)])
        qn = sv.n
        if q4 == 0 and "hq0" in taps:
            E("q", lambda e: e.dma_start(tap_out["hq0"][0],
                                         hq6_1[:, :, :].rearrange("p a b -> p (a b)")),
              wait=[(sv, qn)], inc=[(Q("thq"), 16)])
            E("q", lambda e: e.dma_start(tap_out["hq0"][1],
                                         hq6_2[:, :, :].rearrange("p a b -> p (a b)")),
              wait=[(sv, qn)], inc=[(Q("thq2"), 16)])
            D["hq0tap"] = [(Q("thq"), Q("thq").n), (Q("thq2"), Q("thq2").n)]
        if q4 == 0 and "sch0" in taps:
            E("q", lambda e: e.dma_start(tap_out["sch0"][0:1, :], row6[1][0:1, :]),
              wait=[(sv, qn)], inc=[(Q("tsc"), 16)])
            E("q", lambda e: e.dma_start(tap_out["sch0"][1:2, :], row6[2][0:1, :]),
              wait=[(sv, qn)], inc=[(Q("tsc2"), 16)])
            D["sch0tap"] = [(Q("tsc"), Q("tsc").n), (Q("tsc2"), Q("tsc2").n)]
        for o in range(16):
            qq = Q(f"w2b{o % 2}")
            E("q", lambda e, o=o: e.dma_start(
                wq2b[o % 2][:, :, :], d_wq2[o].rearrange("p (a b) -> p a b", a=64)),
                wait=[(st, D.get(("p6mm", q4, o - 2), 0))] +
                ([D[("wq2d", 31)], D[("wq2d", 30)]] if (q4 == 0 and o < 2) else []),
                inc=[(qq, 16)])
            wv2 = [(qq, qq.n), (sv, qn), (ss, D.get(("p6ev", q4, o - 2), 0))]
            pb = PS[o % 2]
            for k in range(64):
                hq = hq6_1 if k < 32 else hq6_2
                kk = k % 32
                E("t", lambda e, o=o, k=k, hq=hq, kk=kk, pb=pb: e.matmul(
                    pb[:, :], wq2b[o % 2][:, k, :], hq[:, kk, :],
                    start=(k == 0), stop=(k == 63)),
                    wait=wv2 if k == 0 else [],
                    inc=[(st, 1)] if k == 63 else [])
            D[("p6mm", q4, o)] = st.n
            E("v", lambda e, o=o, pb=pb: e.tensor_mul(t6scr[o % 2][:, :], pb[:, :],
                                                      i2bc[:, :]),
              wait=[(st, st.n), (ss, D.get(("p6ev", q4, o - 2), 0))], inc=[(sv, 1)])
            qg = Q(f"g6{o % 2}")
            E("s", lambda e, o=o: e.activation(gst6[o % 2][:, :], t6scr[o % 2][:, :],
                                               AF.Silu, scale=1.0),
              wait=[(sv, sv.n)] + ([D[("gq", q4, o - 2)]] if ("gq", q4, o - 2) in D else
                                   ([D[("gq", q4 - 1, 14 + (o % 2))]] if q4 > 0 and o < 2
                                    else [])),
              inc=[(ss, 1)])
            D[("p6ev", q4, o)] = ss.n
            E("q", lambda e, o=o, t0=t0: e.dma_start(d_g[o, :, t0:t0 + TQ],
                                                     gst6[o % 2][:, :]),
              wait=[(ss, ss.n)], inc=[(qg, 16)])
            D[("gq", q4, o)] = (qg, qg.n)

    def p7():
        qr7 = Q("r7")
        E("q", lambda e: e.dma_start(r7[0:1, :], d_rows[2:3, :]),
          wait=[D[("varq", 1)], D[("varq", 0)], (ss, D[("p6ev", 3, 15)])],
          inc=[(qr7, 16)])
        E("v", lambda e: e.tensor_scalar_add(r7b[0:1, :], r7[0:1, :], c_eps[0:1, 1:2]),
          wait=[(qr7, qr7.n)], inc=[(sv, 1)])
        E("s", lambda e: e.activation(r7[0:1, :], r7b[0:1, :], AF.Sqrt, scale=1.0),
          wait=[(sv, sv.n)], inc=[(ss, 1)])
        E("v", lambda e: e.reciprocal(r7[0:1, :], r7[0:1, :]),
          wait=[(ss, ss.n)], inc=[(sv, 1)])
        E("v", lambda e: e.tensor_mul(r7c[0:1, :], r7[0:1, :], r7[0:1, :]), inc=[(sv, 1)])
        E("v", lambda e: e.tensor_mul(r7c[0:1, :], r7c[0:1, :], r7b[0:1, :]),
          inc=[(sv, 1)])
        E("v", lambda e: e.tensor_scalar(r7c[0:1, :], r7c[0:1, :], -0.5, 1.5,
                                         ALU.mult, ALU.add), inc=[(sv, 1)])
        E("v", lambda e: e.tensor_mul(r7[0:1, :], r7[0:1, :], r7c[0:1, :]),
          inc=[(sv, 1)])
        for t4 in range(4):
            E("t", lambda e, t4=t4: e.matmul(PS[4 + t4][:, :],
                                             c_ones[0:1, :],
                                             r7[0:1, t4 * 512:(t4 + 1) * 512],
                                             start=True, stop=True),
              wait=[(sv, sv.n)], inc=[(st, 1)])
        bcn = st.n
        for k in range(KD):
            qxk = Q(f"x7{k % 2}")
            qsk = Q(f"s7{k % 2}")
            qgk = Q(f"g7{k % 2}")
            qok = Q(f"o7{k % 2}")
            wv = [D[("gq", 3, 15)], D[("gq", 3, 14)]] if k < 2 else                  [D[("p7q", k - 2)]]
            E("q", lambda e, k=k: e.dma_start(x7[k % 2][:, :],
                                              xt[k * 128:(k + 1) * 128, :]),
              wait=wv, inc=[(qxk, 16)])
            E("q", lambda e, k=k: e.dma_start(stil7[k % 2][:, :], d_stil[k]),
              wait=wv + ([D[("stq", 1, k)], D[("stq", 0, k)]] if k < 2 else []),
              inc=[(qsk, 16)])
            E("q", lambda e, k=k: e.dma_start(g7[k % 2][:, :], d_g[k]),
              wait=wv, inc=[(qgk, 16)])
            ldq = [(qxk, qxk.n), (qsk, qsk.n), (qgk, qgk.n)]
            for t4 in range(4):
                sl = slice(t4 * 512, (t4 + 1) * 512)
                E("v", lambda e, k=k, t4=t4, sl=sl: e.tensor_mul(
                    t7a[:, :], g7[k % 2][:, sl], PS[4 + t4][:, :]),
                    wait=(ldq + [(st, bcn)]) if t4 == 0 else [], inc=[(sv, 1)])
                E("v", lambda e, k=k, sl=sl: e.tensor_mul(
                    t7b[:, :], t7a[:, :], stil7[k % 2][:, sl]), inc=[(sv, 1)])
                E("v", lambda e, k=k, sl=sl: e.tensor_add(
                    ostg7[k % 2][:, sl], x7[k % 2][:, sl], t7b[:, :]), inc=[(sv, 1)])
            E("q", lambda e, k=k: e.dma_start(out[k * 128:(k + 1) * 128, :],
                                              ostg7[k % 2][:, :]),
              wait=[(sv, sv.n)], inc=[(qok, 16)])
            D[("p7q", k)] = (qok, qok.n)

    init_consts()
    p0_dma(0, 48)
    if last_phase >= 1:
        p1_half(0)
        if "xn" in taps:
            E("q", lambda e: e.dma_start(tap_out["xn"][0], xn_half[:, :, :].rearrange(
                "p a b -> p (a b)")), wait=[(sv, sv.n)], inc=[(Q("xn0"), 16)])
            D["xntap0"] = (Q("xn0"), Q("xn0").n)
        p1b_half(0)
        D[("p1bend", 0)] = sv.n
    if last_phase >= 2:
        p2_half(0)
    p0_dma(48, 128)
    if last_phase >= 1:
        p1_half(1)
        if "xn" in taps:
            E("q", lambda e: e.dma_start(tap_out["xn"][1], xn_half[:, :, :].rearrange(
                "p a b -> p (a b)")), wait=[(sv, sv.n), (Q("xn0"), Q("xn0").n)],
              inc=[(Q("xn1"), 16)])
            D["xntap1"] = (Q("xn1"), Q("xn1").n)
        p1b_half(1)
        D[("p1bend", 1)] = sv.n
    if last_phase >= 2:
        p2_half(1)
        D["p2end_t"] = st.n
    p0_dma(128, 256)
    D["p0done"] = sv.n
    D["p2spill"] = [(Q("pa0"), Q("pa0").n), (Q("pa1"), Q("pa1").n)]
    if last_phase >= 3:
        p0_finish()
        if last_phase >= 5:
            p5_init_half(0)
        p3_loads(0)
        p3_chunk_prelude(0)
        for c in range(NCH):
            if c + 1 < NCH:
                p3_loads(c + 1)
            p3_chunk(c)
            if last_phase >= 5:
                if c == 8:
                    p5_init_half(1)
                hf = c // 8
                p5_batch(hf, 8 * (c % 8), 8 * (c % 8) + 8)
            if c + 1 < NCH:
                p3_chunk_prelude(c + 1)
    if last_phase >= 4:
        p4_half(0)
        p4_w2quant()
        p4_half(1)
    if last_phase >= 6:
        for q4 in range(4):
            p6_quarter(q4)
    if last_phase >= 7:
        p7()

    def all_done():
        w = [(sv, sv.n), (ss, ss.n), (st, st.n), (sg, sg.n)]
        w += [(s, s.n) for s in _semd.values()]
        return w

    qtap = Q("tap")
    for name, src in (("bsb", B_sb[:, :]), ("csb", C_sb[:, :]),
                      ("dtsp", dt_sp[:, :]), ("carry", carry[:, :]),
                      ("sw", c_sw[:, :])):
        if name in taps:
            E("q", lambda e, d=tap_out[name], s=src: e.dma_start(d, s),
              wait=all_done(), inc=[(qtap, 16)])
    for name, src in (("xqi", d_xqi), ("gate", d_gate), ("xssm", d_xssm),
                      ("hready", d_hready), ("h1", d_h1), ("stil", d_stil),
                      ("g", d_g), ("rows", d_rows)):
        if name in taps:
            E("q", lambda e, d=tap_out[name], s=src: e.dma_start(d, s),
              wait=all_done(), inc=[(qtap, 16)])

    E("q", lambda e: e.nop(), wait=all_done() + [(st, st.n), (sv, sv.n)])

    counts = P.check()

    with nc.Block() as block:
        @block.tensor
        def _(eng):
            for op in P.ops["t"]:
                op(eng)

        @block.vector
        def _(eng):
            for op in P.ops["v"]:
                op(eng)

        @block.scalar
        def _(eng):
            for op in P.ops["s"]:
                op(eng)

        @block.gpsimd
        def _(eng):
            for op in P.ops["g"]:
                op(eng)

        @block.sync
        def _(eng):
            for op in P.ops["q"]:
                op(eng)

    stack.close()
    return nc, counts


def kernel(**inputs):
    from concourse.bass_utils import run_bass_kernel_spmd
    in_maps = host_prep(inputs)
    nc, _counts = build(taps=(), last_phase=7)
    res = run_bass_kernel_spmd(nc, in_maps, core_ids=list(range(len(in_maps))))
    out = np.stack([np.ascontiguousarray(r["out"].T) for r in res.results], axis=0)
    return out.astype(np.float32)
